# version 01
# baseline (speedup 1.0000x reference)
import numpy as np
import ml_dtypes
from contextlib import ExitStack
import concourse.bass as bass
import concourse.mybir as mybir
from concourse.bass_utils import run_bass_kernel_spmd

F32 = mybir.dt.float32
BF16 = mybir.dt.bfloat16
AF = mybir.ActivationFunctionType
ALU = mybir.AluOpType
AX = mybir.AxisListType

ENGINES = ["sync", "gpsimd", "vector", "scalar", "tensor"]

NCORE = 8
D = 2048
KT = 16
NBLK = 8
NHALF = 2
BW = 128
HALF_COLS = 256
TOKC = 64
NCHH = 2
NPCOL = 4096
E_RT, E_SH, FF = 16, 2, 1024
NEXP = E_RT + E_SH
FIT = 512
NFI = NPCOL // FIT
SQ128 = 11.313708498984761

DBG = False
DBG_HB = 0
DBG_TB = 0


class Sem:
    def __init__(self, handle):
        self.h = handle
        self.c = 0


class Ctx:
    def __init__(self):
        self.cur = None
        self.eng = None

    def set_pass(self, name, eng, sems):
        self.cur = name
        self.eng = eng
        for s in sems:
            s.c = 0

    def on(self, engine_name, fn):
        if self.cur == engine_name:
            return fn(self.eng)
        return None

    def inc(self, inst, sem, n=1):
        sem.c += n
        if inst is not None:
            inst.then_inc(sem.h, n)

    def wait(self, engine_name, sem, value=None):
        if value is None:
            value = sem.c
        if self.cur == engine_name:
            self.eng.wait_ge(sem.h, value)

    def dma(self, engine_name, sem, out, in_, **kw):
        sem.c += 16
        if self.cur == engine_name:
            self.eng.dma_start(out=out, in_=in_, **kw).then_inc(sem.h, 16)
        return sem.c


class NS:
    pass


DBG_TAPS = [
    ("Kb", [128, NCHH, 16, 128], BF16),
    ("Vb", [128, NCHH, 16, 128], BF16),
    ("Qb", [128, NCHH, 16, 128], BF16),
    ("gate", [128, 8, HALF_COLS], BF16),
    ("beta0", [8, HALF_COLS], F32),
    ("beta1", [8, HALF_COLS], F32),
    ("g8", [8, HALF_COLS], F32),
    ("og", [128, 8, HALF_COLS], BF16),
    ("ypre", [128, 16, HALF_COLS], BF16),
    ("ystg", [128, 16, HALF_COLS], BF16),
    ("comb", [16, FIT], BF16),
    ("affdbg", [16, FIT], F32),
    ("affT", [128, 4, 16], F32),
    ("m1", [128, 4], F32),
    ("m2", [128, 4], F32),
    ("den", [128, 4], F32),
    ("combT", [128, 4, 16], F32),
    ("zdbg", [128, 16, FIT], F32),
]


def _vb_view(t, nh, h):
    v4 = t[:].rearrange("p c (bb hh) t -> p bb c hh t", bb=2)
    return v4[:, :, :, h, nh::2]



def build_program():
    nc = bass.Bass("TRN2", target_bir_lowering=False)
    d = NS()

    def din(name, shape, dt=F32):
        setattr(d, name, nc.dram_tensor(name, list(shape), dt, kind="ExternalInput").ap())

    din("x_fm", [D, NPCOL])
    din("wq_l", [8, 128, 16, 128])
    din("wk_l", [16, 128, 16, 128])
    din("wv_l", [16, 128, 16, 128])
    din("wg_l", [8, 128, 16, 128])
    din("wo_l", [16, 128, 8, 128])
    din("wb_l", [128, 16, 16])
    din("wa_l", [128, 16, 8])
    din("wafn_l", [128, 16, 16])
    din("cq_l", [128, 8, 4])
    din("ck_l", [128, 16, 4])
    din("cv_l", [128, 16, 4])
    din("alog", [8, 1])
    din("dtb", [8, 1])
    din("onw", [128, 1])
    din("rg_l", [E_RT, 4, 128, 16, 256])
    din("rw1_l", [E_RT, 4, 128, 16, 256])
    din("rw2_l", [E_RT, 4, 128, 8, 512])
    din("sg_l", [E_SH, 4, 128, 16, 256])
    din("sw1_l", [E_SH, 4, 128, 16, 256])
    din("sw2_l", [E_SH, 4, 128, 8, 512])
    din("rgb_l", [128, E_RT, 8])
    din("rb1_l", [128, E_RT, 8])
    din("sgb_l", [128, E_SH, 8])
    din("sb1_l", [128, E_SH, 8])
    din("b2r", [E_RT, D])
    din("b2s", [E_SH, D])
    din("sel8", [8, 1024])
    din("sel16", [16, 2048], BF16)
    din("maskneg", [128, 128])
    din("identf", [128, 128])
    din("identb", [128, 128], BF16)

    d.out = nc.dram_tensor("out_fm", [D, NPCOL], F32, kind="ExternalOutput").ap()

    def dint(name, shape, dt):
        setattr(d, name, nc.dram_tensor(name, list(shape), dt, kind="Internal").ap())

    dint("wbf_q", [8, 128, 16, 128], BF16)
    dint("wbf_k", [16, 128, 16, 128], BF16)
    dint("wbf_v", [16, 128, 16, 128], BF16)
    dint("wbf_g", [8, 128, 16, 128], BF16)
    dint("wbf_o", [16, 128, 8, 128], BF16)
    dint("rg_bf", [E_RT, 4, 128, 16, 256], BF16)
    dint("rw1_bf", [E_RT, 4, 128, 16, 256], BF16)
    dint("rw2_bf", [E_RT, 4, 128, 8, 512], BF16)
    dint("sg_bf", [E_SH, 4, 128, 16, 256], BF16)
    dint("sw1_bf", [E_SH, 4, 128, 16, 256], BF16)
    dint("sw2_bf", [E_SH, 4, 128, 8, 512], BF16)
    dint("ybf", [D, NPCOL], BF16)

    dbg_outs = {}
    if DBG:
        for nm, shape, dt in DBG_TAPS:
            dbg_outs[nm] = nc.dram_tensor("dbg_" + nm, list(shape), dt, kind="ExternalOutput").ap()

    stack = ExitStack()
    e = stack.enter_context
    r = NS()
    r.nc = nc

    r.sel8 = e(nc.sbuf_tensor("p_sel8", [8, 1024], F32))
    r.sel16 = e(nc.sbuf_tensor("p_sel16", [16, 2048], BF16))
    r.maskneg = e(nc.sbuf_tensor("p_mask", [128, 128], F32))
    r.identf = e(nc.sbuf_tensor("p_idf", [128, 128], F32))
    r.identb = e(nc.sbuf_tensor("p_idb", [128, 128], BF16))
    r.onesb = e(nc.sbuf_tensor("p_onesb", [128, 128], BF16))
    r.onesf = e(nc.sbuf_tensor("p_onesf", [128, 128], F32))
    r.ones2 = e(nc.sbuf_tensor("p_ones2", [2, 512], BF16))
    r.cst = e(nc.sbuf_tensor("p_cst", [128, 8], F32))
    r.wb_sb = e(nc.sbuf_tensor("p_wb", [128, 16, 16], BF16))
    r.wa_sb = e(nc.sbuf_tensor("p_wa", [128, 16, 8], BF16))
    r.wafn_sb = e(nc.sbuf_tensor("p_wafn", [128, 16, 16], BF16))
    r.b2r_sb = e(nc.sbuf_tensor("p_b2r", [16, 2048], BF16))
    r.b2s_sb = e(nc.sbuf_tensor("p_b2s", [2, 2048], BF16))
    r.cq = e(nc.sbuf_tensor("p_cq", [128, 8, 4], F32))
    r.ck = e(nc.sbuf_tensor("p_ck", [128, 16, 4], F32))
    r.cv = e(nc.sbuf_tensor("p_cv", [128, 16, 4], F32))
    r.nexpA = e(nc.sbuf_tensor("p_nexpA", [8, 1], F32))
    r.dtb = e(nc.sbuf_tensor("p_dtb", [8, 1], F32))
    r.onw = e(nc.sbuf_tensor("p_onw", [128, 1], F32))
    r.rgb = e(nc.sbuf_tensor("p_rgb", [128, E_RT, 8], F32))
    r.rb1 = e(nc.sbuf_tensor("p_rb1", [128, E_RT, 8], F32))
    r.sgb = e(nc.sbuf_tensor("p_sgb", [128, E_SH, 8], F32))
    r.sb1 = e(nc.sbuf_tensor("p_sb1", [128, E_SH, 8], F32))
    r.hist = e(nc.sbuf_tensor("p_hist", [128, 40, 2, 4], BF16))
    r.carry = e(nc.sbuf_tensor("p_carry", [8, 2], F32))
    r.carry0 = e(nc.sbuf_tensor("p_carry0", [8, 2], F32))

    nc.const_aps.register(F32, 1e-5, r.cst[:, 0:1])
    nc.const_aps.register(F32, 1.0 / 128.0, r.cst[:, 1:2])
    nc.const_aps.register(F32, 1.0, r.cst[:, 2:3])
    nc.const_aps.register(F32, -1.0, r.cst[:, 3:4])

    pa_stack = ExitStack()
    e = pa_stack.enter_context
    r.x_bf = e(nc.sbuf_tensor("a_xbf", [128, 16, HALF_COLS], BF16))
    r.slab = [e(nc.sbuf_tensor(f"a_slab{i}", [128, 16, 128], BF16)) for i in range(3)]
    r.pre = [e(nc.sbuf_tensor(f"a_pre{i}", [128, 2, 132], BF16)) for i in range(3)]
    r.cvo = [e(nc.sbuf_tensor(f"a_cvo{i}", [128, 2, 128], BF16)) for i in range(2)]
    r.qs = [e(nc.sbuf_tensor(f"a_qs{i}", [128, HALF_COLS], BF16)) for i in range(3)]
    r.sq = [e(nc.sbuf_tensor(f"a_sq{i}", [128, HALF_COLS], BF16)) for i in range(2)]
    r.srow = [e(nc.sbuf_tensor(f"a_srow{i}", [1, HALF_COLS], BF16)) for i in range(3)]
    r.Kb = e(nc.sbuf_tensor("a_Kb", [128, NCHH, 16, 128], BF16))
    r.Vb = e(nc.sbuf_tensor("a_Vb", [128, NCHH, 16, 128], BF16))
    r.Qb = e(nc.sbuf_tensor("a_Qb", [128, NCHH, 16, 128], BF16))
    r.gate = e(nc.sbuf_tensor("a_gate", [128, 8, HALF_COLS], BF16))
    r.beta_nh = [e(nc.sbuf_tensor(f"a_beta{p}", [8, HALF_COLS], F32)) for p in range(2)]
    r.g8 = e(nc.sbuf_tensor("a_g8", [8, HALF_COLS], F32))
    r.sp1 = e(nc.sbuf_tensor("a_sp1", [8, HALF_COLS], F32))
    r.sp2 = e(nc.sbuf_tensor("a_sp2", [8, HALF_COLS], F32))
    r.cumA = e(nc.sbuf_tensor("a_cumA", [8, HALF_COLS], F32))
    r.cumB = e(nc.sbuf_tensor("a_cumB", [8, HALF_COLS], F32))
    r.ogate = e(nc.sbuf_tensor("a_og", [128, 8, HALF_COLS], BF16))
    r.ypre = e(nc.sbuf_tensor("a_ypre", [128, 16, HALF_COLS], BF16))
    r.xfr = [e(nc.sbuf_tensor(f"a_xfr{i}", [128, HALF_COLS], F32)) for i in range(2)]
    r.ystg = e(nc.sbuf_tensor("a_ystg", [128, 16, HALF_COLS], BF16))
    r.lnt = e(nc.sbuf_tensor("a_lnt", [128, HALF_COLS], F32))
    r.lnr = e(nc.sbuf_tensor("a_lnr", [1, 1024], F32))
    r.S = e(nc.sbuf_tensor("s_S", [128, 2048], F32))
    r.S0bf = e(nc.sbuf_tensor("s_S0bf", [128, 2048], BF16))
    r.cumg_sub = e(nc.sbuf_tensor("s_cgs", [8, 256], F32))
    r.beta_sub = e(nc.sbuf_tensor("s_bts", [8, 256], F32))
    r.cumg_T = e(nc.sbuf_tensor("s_cgT", [128, 16], F32))
    r.beta_T = e(nc.sbuf_tensor("s_btT", [128, 16], F32))
    r.lamc = e(nc.sbuf_tensor("s_lamc", [128, 16], F32))
    r.rstd = e(nc.sbuf_tensor("s_rstd", [128, 16], F32))
    r.ssb = e(nc.sbuf_tensor("s_ssb", [128, 16], F32))
    r.edecf = e(nc.sbuf_tensor("s_edecf", [128, 16], F32))
    r.lamlast = e(nc.sbuf_tensor("s_lamlast", [128, 16], F32))
    r.lam_row = e(nc.sbuf_tensor("s_lamr", [128, 2048], BF16))
    r.E = e(nc.sbuf_tensor("s_E", [128, 2048], BF16))
    r.tmp_bf = e(nc.sbuf_tensor("s_tmpbf", [128, 2048], BF16))
    r.Nt = e(nc.sbuf_tensor("s_Nt", [128, 2048], BF16))
    r.Nn = e(nc.sbuf_tensor("s_Nn", [128, 2048], BF16))
    r.Nt2 = e(nc.sbuf_tensor("s_Nt2", [128, 2048], BF16))
    r.Nn2 = e(nc.sbuf_tensor("s_Nn2", [128, 2048], BF16))
    r.Wt = e(nc.sbuf_tensor("s_Wt", [128, 2048], BF16))
    r.Vtm = e(nc.sbuf_tensor("s_Vtm", [128, 2048], BF16))
    r.KDtm = e(nc.sbuf_tensor("s_KDtm", [128, 2048], BF16))
    r.U32 = e(nc.sbuf_tensor("s_U32", [128, 2048], F32))
    r.Ubf = e(nc.sbuf_tensor("s_Ubf", [128, 2048], BF16))
    r.t32 = e(nc.sbuf_tensor("s_t32", [128, 1024], F32))
    r.sqbf = e(nc.sbuf_tensor("s_sqbf", [128, 2048], BF16))
    r.on = e(nc.sbuf_tensor("s_on", [128, 2048], BF16))
    pa_stack.close()
    e = stack.enter_context
    r.ybf_sb = e(nc.sbuf_tensor("f_ybf", [128, 16, FIT], BF16))
    r.z = e(nc.sbuf_tensor("f_z", [128, 16, FIT], F32))
    r.u = [e(nc.sbuf_tensor(f"f_u{i}", [128, 8, FIT], BF16)) for i in range(3)]
    r.wA = [e(nc.sbuf_tensor(f"f_wA{i}", [128, 16, 256], BF16)) for i in range(3)]
    r.wB = [e(nc.sbuf_tensor(f"f_wB{i}", [128, 16, 256], BF16)) for i in range(3)]
    r.w2s = [e(nc.sbuf_tensor(f"f_w2s{i}", [128, 8, 512], BF16)) for i in range(4)]
    r.aff = e(nc.sbuf_tensor("f_aff", [16, FIT], F32))
    r.affT = e(nc.sbuf_tensor("f_affT", [128, 4, 16], F32))
    r.combT = e(nc.sbuf_tensor("f_combT", [128, 4, 16], F32))
    r.comb = e(nc.sbuf_tensor("f_comb", [16, FIT], BF16))
    r.m1 = e(nc.sbuf_tensor("f_m1", [128, 4], F32))
    r.m2 = e(nc.sbuf_tensor("f_m2", [128, 4], F32))
    r.den = e(nc.sbuf_tensor("f_den", [128, 4], F32))
    r.eq = e(nc.sbuf_tensor("f_eq", [128, 4, 16], F32))
    r.g1 = e(nc.sbuf_tensor("f_g1", [128, FIT], F32))
    r.u0 = e(nc.sbuf_tensor("f_uzero", [128, FIT], F32))
    r.ostg = [e(nc.sbuf_tensor(f"f_ostg{i}", [128, FIT], F32)) for i in range(2)]
    r.flnr = e(nc.sbuf_tensor("f_lnr", [1, 2048], F32))

    r.PSA = e(nc.psum_tensor("ps_A", [128, 2048], F32))
    r.PSB = e(nc.psum_tensor("ps_B", [128, 1024], F32))
    r.PSC = e(nc.psum_tensor("ps_C", [128, 2048], BF16))

    sems = {n: Sem(e(nc.semaphore(n))) for n in ["dS", "dG", "sP", "sV", "sA", "st"]}
    block = e(nc.Block())
    ctx = Ctx()

    def emit():
        emit_all(ctx, nc, d, r, sems, dbg_outs)

    for en in ENGINES:
        def run_pass(eng, en=en):
            ctx.set_pass(en, eng, list(sems.values()))
            emit()
        getattr(block, en)(run_pass)
    stack.close()
    return nc



def emit_all(ctx, nc, d, r, sems, dbg_outs):
    dS, dG, sP, sV, sA, st = (sems[n] for n in ["dS", "dG", "sP", "sV", "sA", "st"])

    for name, sb in [("sel8", r.sel8), ("sel16", r.sel16), ("maskneg", r.maskneg),
                     ("identf", r.identf), ("identb", r.identb),
                     ("cq_l", r.cq), ("ck_l", r.ck), ("cv_l", r.cv),
                     ("dtb", r.dtb), ("onw", r.onw),
                     ("rgb_l", r.rgb), ("rb1_l", r.rb1), ("sgb_l", r.sgb), ("sb1_l", r.sb1)]:
        ctx.dma("sync", dS, sb[:], getattr(d, name))
    ctx.dma("sync", dS, r.nexpA[:], d.alog)
    t_cl = dS.c

    ctx.dma("gpsimd", dG, r.wb_sb[:], d.wb_l)
    ctx.dma("gpsimd", dG, r.wa_sb[:], d.wa_l)
    ctx.dma("gpsimd", dG, r.wafn_sb[:], d.wafn_l)
    ctx.dma("gpsimd", dG, r.b2r_sb[:], d.b2r)
    ctx.dma("gpsimd", dG, r.b2s_sb[:], d.b2s)
    t_smallcast = dG.c
    for (src, dst) in [(d.wq_l, d.wbf_q), (d.wk_l, d.wbf_k), (d.wv_l, d.wbf_v),
                       (d.wg_l, d.wbf_g), (d.wo_l, d.wbf_o)]:
        ctx.dma("gpsimd", dG, dst.rearrange("a p k o -> (a p k o)"),
                src.rearrange("a p k o -> (a p k o)"))
    t_p0attn = dG.c
    for (src, dst, ne) in [(d.rg_l, d.rg_bf, E_RT), (d.rw1_l, d.rw1_bf, E_RT),
                           (d.rw2_l, d.rw2_bf, E_RT), (d.sg_l, d.sg_bf, E_SH),
                           (d.sw1_l, d.sw1_bf, E_SH), (d.sw2_l, d.sw2_bf, E_SH)]:
        for ee in range(ne):
            ctx.dma("gpsimd", dG, dst[ee].rearrange("a p k o -> (a p k o)"),
                    src[ee].rearrange("a p k o -> (a p k o)"))
    t_p0ffn = dG.c

    ctx.wait("vector", dS, t_cl)
    for fn in [lambda e: e.memset(r.onesb[:], 1.0),
               lambda e: e.memset(r.onesf[:], 1.0),
               lambda e: e.memset(r.ones2[:], 1.0),
               lambda e: e.memset(r.cst[:, 0:1], 1e-5),
               lambda e: e.memset(r.cst[:, 1:2], 1.0 / 128.0),
               lambda e: e.memset(r.cst[:, 2:3], 1.0),
               lambda e: e.memset(r.cst[:, 3:4], -1.0),
               lambda e: e.memset(r.S[:], 0.0),
               lambda e: e.memset(r.carry[:, 0:2], 0.0),
               lambda e: e.memset(r.carry0[:, 0:2], 0.0),
               lambda e: e.memset(r.Qb[:], 0.0),
               lambda e: e.memset(r.hist[:], 0.0)]:
        i = ctx.on("vector", fn)
    ctx.inc(i, sV)
    t_vinit = sV.c
    ctx.wait("scalar", sV, t_vinit)
    ctx.on("scalar", lambda e: e.activation(out=r.nexpA[:], in_=r.nexpA[:], func=AF.Exp))
    i = ctx.on("scalar", lambda e: e.mul(out=r.nexpA[:], in_=r.nexpA[:], mul=-1.0))
    ctx.inc(i, sA)
    ctx.wait("tensor", dS, t_cl)
    ctx.wait("tensor", dG, t_smallcast)
    ctx.wait("tensor", sV, t_vinit)

    r.spill_st = 0
    for tb in range(NBLK):
        for hb in range(NHALF):
            emit_half(ctx, d, r, sems, tb, hb, t_p0attn, dbg_outs)

    for it in range(NFI):
        emit_ffn_iter(ctx, d, r, sems, it, t_p0ffn, dbg_outs)

    ctx.wait("sync", st, st.c)



def emit_half(ctx, d, r, sems, tb, hb, t_p0attn, dbg_outs):
    dS, dG, sP, sV, sA, st = (sems[n] for n in ["dS", "dG", "sP", "sV", "sA", "st"])
    pc0 = tb * 512 + hb * 256
    t_entry_sV = sems["sV"].c

    t_xfree = sP.c
    ctx.wait("gpsimd", sP, t_xfree)
    xsrc = d.x_fm.rearrange("(k p) n -> p k n", p=128)[:, :, pc0:pc0 + 256]
    t_x = ctx.dma("gpsimd", dG, r.x_bf[:], xsrc)

    tiles = [("a", 0), ("b", 0), ("b", 1)] + [("q", j) for j in range(8)] + \
            [("k", j) for j in range(16)] + [("v", j) for j in range(16)] + \
            [("gg", j) for j in range(8)]
    slab_src = {"q": d.wbf_q, "k": d.wbf_k, "v": d.wbf_v, "gg": d.wbf_g}
    conv_w = {"q": r.cq, "k": r.ck, "v": r.cv}
    hist_base = {"q": 0, "k": 8, "v": 24}

    slot_free = [sV.c] * 4
    slot_free_a = [sA.c] * 4
    slab_free = [sP.c] * 3
    slab_ready = [0] * 3
    pre_free = [sA.c] * 3
    qs_free = [sA.c] * 3
    cvo_free = [sA.c] * 2
    sq_free = [sP.c] * 2
    srow_free = [sP.c] * 3
    nrm_q = 0
    ssrow_free = [sA.c] * 2
    sbc_free = [sV.c] * 2

    for ti, (kind, j) in enumerate(tiles):
        si = ti % 3
        pi = ti % 4
        psl = r.PSA[:, pi * 512: pi * 512 + 256]

        if kind in slab_src:
            ctx.wait("sync", dG, t_p0attn)
            ctx.wait("sync", sP, slab_free[si])
            slab_ready[si] = ctx.dma("sync", dS, r.slab[si][:], slab_src[kind][j])

        ctx.wait("tensor", dG, t_x)
        ctx.wait("tensor", sV, slot_free[pi])
        ctx.wait("tensor", sA, slot_free_a[pi])
        if kind in slab_src:
            ctx.wait("tensor", dS, slab_ready[si])
            lhs = lambda kk, si=si: r.slab[si][:, kk, :]
            M = 128
        elif kind == "a":
            lhs = lambda kk: r.wa_sb[:, kk, :]
            M = 8
        else:
            lhs = lambda kk, j=j: r.wb_sb[:, kk, j * 8:(j + 1) * 8]
            M = 8
        mm = None
        for kk in range(16):
            mm = ctx.on("tensor", lambda e, kk=kk, lhs=lhs, psl=psl, M=M: e.matmul(
                psl[0:M, :] if M != 128 else psl, lhs(kk), r.x_bf[:, kk, :],
                start=(kk == 0), stop=(kk == 15)))
        ctx.inc(mm, sP)
        if kind in slab_src:
            slab_free[si] = sP.c
        t_mm = sP.c

        if kind == "a":
            ctx.wait("vector", sP, t_mm)
            i = ctx.on("vector", lambda e, pi=pi: e.tensor_scalar(
                out=r.sp1[:], in0=r.PSA[0:8, pi * 512: pi * 512 + 256], scalar1=r.dtb[:],
                scalar2=None, op0=ALU.add))
            ctx.inc(i, sV)
            slot_free[pi] = sV.c
            ctx.wait("scalar", sV, sV.c)
            i = ctx.on("scalar", lambda e: e.activation(out=r.sp2[:], in_=r.sp1[:],
                                                        func=AF.Abs))
            ctx.inc(i, sA)
            ctx.wait("scalar", sA, sA.c)
            i = ctx.on("scalar", lambda e: e.activation(out=r.sp2[:], in_=r.sp2[:],
                                                        func=AF.Exp, scale=-1.0))
            ctx.inc(i, sA)
            ctx.wait("scalar", sA, sA.c)
            i = ctx.on("scalar", lambda e: e.activation(out=r.sp2[:], in_=r.sp2[:],
                                                        func=AF.Ln, bias=1.0))
            ctx.inc(i, sA)
            ctx.wait("vector", sA, sA.c)
            ctx.on("vector", lambda e: e.tensor_scalar(
                out=r.sp1[:], in0=r.sp1[:], scalar1=0.0, scalar2=None, op0=ALU.max))
            ctx.on("vector", lambda e: e.tensor_add(out=r.g8[:], in0=r.sp1[:], in1=r.sp2[:]))
            i = ctx.on("vector", lambda e: e.tensor_scalar(
                out=r.g8[:], in0=r.g8[:], scalar1=r.nexpA[:], scalar2=None, op0=ALU.mult))
            ctx.inc(i, sV)
        elif kind == "b":
            ctx.wait("scalar", sP, t_mm)
            ctx.wait("scalar", sV, t_entry_sV)
            i = ctx.on("scalar", lambda e, j=j, pi=pi: e.activation(
                out=r.beta_nh[j][:], in_=r.PSA[0:8, pi * 512: pi * 512 + 256], func=AF.Sigmoid))
            ctx.inc(i, sA)
            slot_free_a[pi] = sA.c
        elif kind == "gg":
            ctx.wait("scalar", sP, t_mm)
            ctx.wait("scalar", sV, t_entry_sV)
            i = ctx.on("scalar", lambda e, j=j, psl=psl: e.activation(
                out=r.gate[:, j, :], in_=psl, func=AF.Silu))
            ctx.inc(i, sA)
            slot_free_a[pi] = sA.c
        else:
            pri = ti % 3
            ci = ti % 2
            hb_i = hist_base[kind] + j
            ctx.wait("vector", sA, pre_free[pri])
            ctx.on("vector", lambda e, pri=pri, hb_i=hb_i: e.tensor_copy(
                out=r.pre[pri][:, :, 0:4], in_=r.hist[:, hb_i, :, :]))
            ctx.wait("vector", sP, t_mm)
            pv = psl.rearrange("p (b t) -> p b t", b=2)
            i = ctx.on("vector", lambda e, pri=pri, pv=pv: e.tensor_copy(
                out=r.pre[pri][:, :, 4:132], in_=pv))
            ctx.inc(i, sV)
            slot_free[pi] = sV.c
            ctx.wait("vector", sV, sV.c)
            ctx.on("vector", lambda e, pri=pri, hb_i=hb_i: e.tensor_copy(
                out=r.hist[:, hb_i, :, 1:4], in_=r.pre[pri][:, :, 129:132]))
            cw = conv_w[kind]
            ctx.wait("vector", sA, cvo_free[ci])
            ctx.on("vector", lambda e, pri=pri, ci=ci, cw=cw, j=j: e.tensor_scalar(
                out=r.cvo[ci][:], in0=r.pre[pri][:, :, 1:129], scalar1=cw[:, j, 0:1],
                scalar2=None, op0=ALU.mult))
            for tap in range(1, 4):
                i = ctx.on("vector", lambda e, pri=pri, ci=ci, cw=cw, j=j, tap=tap: e.scalar_tensor_tensor(
                    out=r.cvo[ci][:], in0=r.pre[pri][:, :, 1 + tap:129 + tap],
                    scalar=cw[:, j, tap:tap + 1], in1=r.cvo[ci][:],
                    op0=ALU.mult, op1=ALU.add))
            ctx.inc(i, sV)
            t_cv = sV.c
            ctx.wait("scalar", sV, t_cv)
            if kind == "v":
                nh, h = (0, j) if j < 8 else (1, j - 8)
                vout = _vb_view(r.Vb, nh, h)
                i = ctx.on("scalar", lambda e, ci=ci, vout=vout: e.activation(
                    out=vout, in_=r.cvo[ci][:].rearrange("p b (c i) -> p b c i", c=NCHH),
                    func=AF.Silu))
                ctx.inc(i, sA)
                pre_free[pri] = sA.c
                cvo_free[ci] = sA.c
            else:
                qi = nrm_q % 3
                ctx.wait("scalar", sV, qs_free[qi])
                i = ctx.on("scalar", lambda e, ci=ci, qi=qi: e.activation(
                    out=r.qs[qi][:].rearrange("p (b t) -> p b t", b=2), in_=r.cvo[ci][:],
                    func=AF.Silu))
                ctx.inc(i, sA)
                pre_free[pri] = sA.c
                cvo_free[ci] = sA.c
                sqi = nrm_q % 2
                ctx.wait("scalar", sP, sq_free[sqi])
                i = ctx.on("scalar", lambda e, qi=qi, sqi=sqi: e.activation(
                    out=r.sq[sqi][:], in_=r.qs[qi][:], func=AF.Square))
                ctx.inc(i, sA)
                t_sqr = sA.c
                ro = (nrm_q % 2) * 256
                ctx.wait("tensor", sA, max(t_sqr, ssrow_free[nrm_q % 2]))
                i = ctx.on("tensor", lambda e, sqi=sqi, ro=ro: e.matmul(
                    r.PSB[0:1, ro:ro + 256], r.onesb[:, 0:1], r.sq[sqi][:],
                    start=True, stop=True))
                ctx.inc(i, sP)
                sq_free[sqi] = sP.c
                t_ssm = sP.c
                sri = nrm_q % 3
                ctx.wait("scalar", sP, t_ssm)
                i = ctx.on("scalar", lambda e, ro=ro: e.activation(
                    out=r.lnr[0:1, 512:768], in_=r.PSB[0:1, ro:ro + 256], func=AF.Sqrt))
                ctx.inc(i, sA)
                ssrow_free[nrm_q % 2] = sA.c
                sc2 = SQ128 if kind == "q" else 1.0
                ctx.wait("vector", sA, sA.c)
                ctx.wait("vector", sP, srow_free[sri])
                ctx.on("vector", lambda e, sc2=sc2: e.tensor_scalar(
                    out=r.lnr[0:1, 512:768], in0=r.lnr[0:1, 512:768], scalar1=1e-12,
                    scalar2=sc2, op0=ALU.max, op1=ALU.mult))
                def _recip(e, sri=sri):
                    with r.nc.allow_low_precision(reason="bf16 norm-scale operand for bf16 matmul"):
                        return e.reciprocal(out=r.srow[sri][:], in_=r.lnr[0:1, 512:768])
                i = ctx.on("vector", _recip)
                ctx.inc(i, sV)
                ctx.wait("vector", sV, sV.c)
                i = ctx.on("vector", lambda e: e.engine_nop())
                ctx.inc(i, sV)
                t_srow = sV.c
                bo = 512 + (nrm_q % 2) * 256
                ctx.wait("tensor", sV, max(t_srow, sbc_free[nrm_q % 2]))
                i = ctx.on("tensor", lambda e, sri=sri, bo=bo: e.matmul(
                    r.PSB[:, bo:bo + 256], r.onesb[0:1, :], r.srow[sri][:],
                    start=True, stop=True))
                ctx.inc(i, sP)
                t_bc = sP.c
                srow_free[sri] = sP.c
                nh, h = (0, j) if j < 8 else (1, j - 8)
                dst = _vb_view(r.Qb, 1, j) if kind == "q" else _vb_view(r.Kb, nh, h)
                ctx.wait("vector", sP, t_bc)
                i = ctx.on("vector", lambda e, qi=qi, bo=bo, dst=dst: e.tensor_mul(
                    out=dst,
                    in0=r.qs[qi][:].rearrange("p (b c i) -> p b c i", b=2, c=NCHH),
                    in1=r.PSB[:, bo:bo + 256].rearrange("p (b c i) -> p b c i", b=2, c=NCHH)))
                ctx.inc(i, sV)
                qs_free[qi] = sV.c
                sbc_free[nrm_q % 2] = sV.c
                nrm_q += 1

    emit_cumsum_half(ctx, r, sP, sV, sA)

    consts = {"sel8": r.sel8[:], "maskneg": r.maskneg[:], "identf": r.identf[:],
              "identb": r.identb[:]}
    for c in range(NCHH):
        emit_chunk(ctx, r, c, consts, sP, sV, sA)

    emit_wo_ln(ctx, d, r, sems, pc0)

    if DBG and tb == DBG_TB and hb == DBG_HB:
        for nm, src in [("Kb", r.Kb), ("Vb", r.Vb), ("Qb", r.Qb), ("gate", r.gate),
                        ("beta0", r.beta_nh[0]), ("beta1", r.beta_nh[1]), ("g8", r.g8),
                        ("og", r.ogate), ("ypre", r.ypre), ("ystg", r.ystg)]:
            if nm in dbg_outs:
                ctx.wait("sync", sV, sV.c)
                ctx.wait("sync", sA, sA.c)
                ctx.dma("sync", st, dbg_outs[nm], src[:])



def emit_cumsum_half(ctx, r, sP, sV, sA):
    def cp(dst, src):
        ctx.on("vector", lambda e: e.tensor_copy(out=dst, in_=src))

    cp(r.carry0[:, 0:2], r.carry[:, 0:2])
    src, dst = r.cumA, r.cumB
    cp(src[:, 0:256], r.g8[:, 0:256])
    s = 1
    while s < BW:
        sv = src[:].rearrange("p (b t) -> p b t", b=2)
        dv = dst[:].rearrange("p (b t) -> p b t", b=2)
        cp(dv[:, :, 0:s], sv[:, :, 0:s])
        ctx.on("vector", lambda e, sv=sv, dv=dv, s=s: e.tensor_add(
            out=dv[:, :, s:BW], in0=sv[:, :, s:BW], in1=sv[:, :, 0:BW - s]))
        src, dst = dst, src
        s *= 2
    for b in range(2):
        ctx.on("vector", lambda e, b=b, src=src: e.tensor_scalar(
            out=src[:, b * BW:(b + 1) * BW], in0=src[:, b * BW:(b + 1) * BW],
            scalar1=r.carry[:, b:b + 1], scalar2=None, op0=ALU.add))
    if src is not r.cumA:
        cp(r.cumA[:, 0:256], src[:, 0:256])
    for b in range(2):
        cp(r.carry[:, b:b + 1], r.cumA[:, b * BW + BW - 1:b * BW + BW])
    i = ctx.on("vector", lambda e: e.engine_nop())
    ctx.inc(i, sV)



def emit_chunk(ctx, r, c, consts, sP, sV, sA):
    sel8, maskneg, identf, identb = consts["sel8"], consts["maskneg"], consts["identf"], consts["identb"]
    col0 = c * TOKC
    Kc = r.Kb[:, c]
    Vc = r.Vb[:, c]
    Qc = r.Qb[:, c]

    t_pa = sA.c
    ctx.wait("tensor", sA, t_pa)

    ctx.wait("scalar", sV, sV.c)
    i = ctx.on("scalar", lambda e: e.copy(out=r.S0bf[:], in_=r.S[:]))
    ctx.inc(i, sA)
    t_s0bf = sA.c

    for b in range(2):
        src = r.cumA[:, b * BW + col0: b * BW + col0 + TOKC]
        dst = r.cumg_sub[:, b * 128:(b + 1) * 128].rearrange("p (i two) -> p i two", two=2)
        srcb = src.unsqueeze(2).broadcast_to([8, TOKC, 2])
        ctx.on("vector", lambda e, dst=dst, srcb=srcb: e.tensor_copy(out=dst, in_=srcb))
        if col0 == 0:
            base = r.carry0[:, b:b + 1]
        else:
            base = r.cumA[:, b * BW + col0 - 1: b * BW + col0]
        ctx.on("vector", lambda e, b=b, base=base: e.tensor_scalar(
            out=r.cumg_sub[:, b * 128:(b + 1) * 128], in0=r.cumg_sub[:, b * 128:(b + 1) * 128],
            scalar1=base, scalar2=None, op0=ALU.subtract))
    for b in range(2):
        for p in range(2):
            dst = r.beta_sub[:, b * 128 + p: (b + 1) * 128: 2]
            src = r.beta_nh[p][:, b * BW + col0: b * BW + col0 + TOKC]
            ctx.on("vector", lambda e, dst=dst, src=src: e.tensor_copy(out=dst, in_=src))
    i = ctx.on("vector", lambda e: e.engine_nop())
    ctx.inc(i, sV)
    t_prep = sV.c

    ctx.wait("tensor", sV, t_prep)
    for b in range(2):
        ctx.on("tensor", lambda e, b=b: e.transpose(
            r.PSB[:, b * 8:(b + 1) * 8], r.cumg_sub[:, b * 128:(b + 1) * 128], identf[0:8, 0:8]))
    for b in range(2):
        i = ctx.on("tensor", lambda e, b=b: e.transpose(
            r.PSB[:, 16 + b * 8: 16 + (b + 1) * 8], r.beta_sub[:, b * 128:(b + 1) * 128],
            identf[0:8, 0:8]))
    ctx.inc(i, sP)
    t_tp1 = sP.c

    ctx.wait("vector", sP, t_tp1)
    ctx.on("vector", lambda e: e.tensor_copy(out=r.cumg_T[:], in_=r.PSB[:, 0:16]))
    i = ctx.on("vector", lambda e: e.tensor_copy(out=r.beta_T[:], in_=r.PSB[:, 16:32]))
    ctx.inc(i, sV)
    t_tp1e = sV.c
    ctx.wait("scalar", sV, t_tp1e)
    i = ctx.on("scalar", lambda e: e.activation(out=r.lamc[:], in_=r.cumg_T[:], func=AF.Exp))
    ctx.inc(i, sA)
    t_lamc = sA.c

    for bh in range(16):
        b, h = bh // 8, bh % 8
        i = ctx.on("tensor", lambda e, b=b, h=h, bh=bh: e.matmul(
            r.PSA[:, bh * 128:(bh + 1) * 128], sel8[:, h * 128:(h + 1) * 128],
            r.cumg_sub[:, b * 128:(b + 1) * 128], start=True, stop=True))
    ctx.inc(i, sP)
    t_cgrow = sP.c

    ctx.wait("scalar", sP, t_cgrow)
    i = ctx.on("scalar", lambda e: e.activation(out=r.lam_row[:], in_=r.PSA[:], func=AF.Exp))
    ctx.inc(i, sA)
    mview = maskneg.unsqueeze(1).broadcast_to([128, 8, 128])
    for half in range(2):
        cslice = r.cumg_T[:, half * 8:(half + 1) * 8]
        colrep = cslice.unsqueeze(2).broadcast_to([128, 8, 128])
        pa = r.PSA[:, half * 1024:(half + 1) * 1024].rearrange("p (a b) -> p a b", b=128)
        t3 = r.t32[:].rearrange("p (a b) -> p a b", b=128)
        ctx.wait("vector", sA, sA.c)
        ctx.on("vector", lambda e, pa=pa, colrep=colrep, t3=t3: e.tensor_sub(out=t3, in0=pa, in1=colrep))
        ctx.on("vector", lambda e, t3=t3, mview=mview: e.tensor_add(out=t3, in0=t3, in1=mview))
        i = ctx.on("vector", lambda e: e.engine_nop())
        ctx.inc(i, sV)
        ctx.wait("scalar", sV, sV.c)
        i = ctx.on("scalar", lambda e, half=half: e.activation(
            out=r.E[:, half * 1024:(half + 1) * 1024], in_=r.t32[:], func=AF.Exp))
        ctx.inc(i, sA)
    t_E = sA.c
    ctx.wait("vector", sA, t_E)
    ctx.on("vector", lambda e: e.tensor_copy(out=r.edecf[:], in_=r.E[:, 127::128]))
    ctx.on("vector", lambda e: e.tensor_copy(out=r.lamlast[:], in_=r.lam_row[:, 127::128]))
    t_psa_free_v = sV.c
    t_psa_free_a = sA.c

    ctx.wait("tensor", sV, t_psa_free_v)
    ctx.wait("tensor", sA, t_psa_free_a)
    for bh in range(16):
        i = ctx.on("tensor", lambda e, bh=bh: e.matmul(
            r.PSA[:, bh * 128:(bh + 1) * 128], Kc[:, bh], Kc[:, bh], start=True, stop=True))
    ctx.inc(i, sP)
    t_A = sP.c
    iview = identb.unsqueeze(1).broadcast_to([128, 16, 128])
    Ev = r.E[:].rearrange("p (a b) -> p a b", b=128)
    tbv = r.tmp_bf[:].rearrange("p (a b) -> p a b", b=128)
    ctx.wait("vector", sA, t_E)
    ctx.on("vector", lambda e: e.tensor_sub(out=tbv, in0=Ev, in1=iview))
    ctx.wait("vector", sP, t_A)
    i = ctx.on("vector", lambda e: e.tensor_mul(out=r.Nt[:], in0=r.PSA[:], in1=r.tmp_bf[:]))
    ctx.inc(i, sV)
    t_AE = sV.c
    ctx.wait("tensor", sV, t_AE)
    for bh in range(16):
        i = ctx.on("tensor", lambda e, bh=bh: e.matmul(
            r.PSA[:, bh * 128:(bh + 1) * 128], Kc[:, bh], Qc[:, bh], start=True, stop=True))
    ctx.inc(i, sP)
    t_KQ = sP.c
    ctx.wait("vector", sP, t_KQ)
    i = ctx.on("vector", lambda e: e.tensor_mul(out=r.Wt[:], in0=r.PSA[:], in1=r.E[:]))
    ctx.inc(i, sV)
    t_W = sV.c
    ctx.wait("tensor", sV, t_W)
    for bh in range(16):
        b, h = bh // 8, bh % 8
        i = ctx.on("tensor", lambda e, b=b, h=h, bh=bh: e.matmul(
            r.PSA[:, bh * 128:(bh + 1) * 128], sel8[:, h * 128:(h + 1) * 128],
            r.beta_sub[:, b * 128:(b + 1) * 128], start=True, stop=True))
    ctx.inc(i, sP)
    t_brow = sP.c
    ctx.wait("vector", sP, t_brow)
    i = ctx.on("vector", lambda e: e.scalar_tensor_tensor(
        out=r.Nt[:], in0=r.Nt[:], scalar=2.0, in1=r.PSA[:], op0=ALU.mult, op1=ALU.mult))
    ctx.inc(i, sV)
    t_Nt = sV.c

    ctx.wait("tensor", sV, t_Nt)
    for bh in range(16):
        i = ctx.on("tensor", lambda e, bh=bh: e.transpose(
            r.PSC[:, bh * 128:(bh + 1) * 128], r.Nt[:, bh * 128:(bh + 1) * 128], identb))
    ctx.inc(i, sP)
    t_Ntp = sP.c
    ctx.wait("vector", sP, t_Ntp)
    i = ctx.on("vector", lambda e: e.tensor_copy(out=r.Nn[:], in_=r.PSC[:]))
    ctx.inc(i, sV)
    t_Nn = sV.c

    ctx.wait("tensor", sV, t_Nn)
    for bh in range(16):
        i = ctx.on("tensor", lambda e, bh=bh: e.transpose(
            r.PSC[:, bh * 128:(bh + 1) * 128], Vc[:, bh], identb))
    ctx.inc(i, sP)
    t_Vtp = sP.c
    ctx.wait("vector", sP, t_Vtp)
    i = ctx.on("vector", lambda e: e.tensor_copy(out=r.Vtm[:], in_=r.PSC[:]))
    ctx.inc(i, sV)
    t_Vtm = sV.c

    ctx.wait("tensor", sA, t_s0bf)
    for bh in range(16):
        i = ctx.on("tensor", lambda e, bh=bh: e.matmul(
            r.PSA[:, bh * 128:(bh + 1) * 128], Kc[:, bh],
            r.S0bf[:, bh * 128:(bh + 1) * 128], start=True, stop=True))
    ctx.inc(i, sP)
    t_KS0 = sP.c
    ctx.wait("vector", sP, t_KS0)
    ctx.wait("vector", sA, t_lamc)
    for bh in range(16):
        sl = slice(bh * 128, (bh + 1) * 128)
        ctx.on("vector", lambda e, sl=sl, bh=bh: e.scalar_tensor_tensor(
            out=r.U32[:, sl], in0=r.PSA[:, sl], scalar=r.lamc[:, bh:bh + 1],
            in1=r.Vtm[:, sl], op0=ALU.mult, op1=ALU.subtract))
        ctx.on("vector", lambda e, sl=sl, bh=bh: e.tensor_scalar(
            out=r.U32[:, sl], in0=r.U32[:, sl], scalar1=r.beta_T[:, bh:bh + 1],
            scalar2=-2.0, op0=ALU.mult, op1=ALU.mult))
    i = ctx.on("vector", lambda e: e.engine_nop())
    ctx.inc(i, sV)
    t_R = sV.c
    ctx.wait("scalar", sV, t_R)
    i = ctx.on("scalar", lambda e: e.copy(out=r.Ubf[:], in_=r.U32[:]))
    ctx.inc(i, sA)
    t_ubf_prev = sA.c

    Acur, Bcur = r.Nt, r.Nn
    Anew, Bnew = r.Nt2, r.Nn2
    p = 1
    t_psa_free = t_R
    while p <= 64:
        ctx.wait("tensor", sA, t_ubf_prev)
        ctx.wait("tensor", sV, t_psa_free)
        for bh in range(16):
            sl = slice(bh * 128, (bh + 1) * 128)
            i = ctx.on("tensor", lambda e, sl=sl, Acur=Acur: e.matmul(
                r.PSA[:, sl], Acur[:, sl], r.Ubf[:, sl], start=True, stop=True))
        ctx.inc(i, sP)
        t_ap = sP.c
        ctx.wait("vector", sP, t_ap)
        op = ALU.subtract if p == 1 else ALU.add
        i = ctx.on("vector", lambda e, op=op: e.tensor_tensor(
            out=r.U32[:], in0=r.U32[:], in1=r.PSA[:], op=op))
        ctx.inc(i, sV)
        t_psa_free = sV.c
        ctx.wait("scalar", sV, t_psa_free)
        i = ctx.on("scalar", lambda e: e.copy(out=r.Ubf[:], in_=r.U32[:]))
        ctx.inc(i, sA)
        t_ubf_prev = sA.c
        if p < 64:
            ctx.wait("tensor", sV, t_psa_free)
            for bh in range(16):
                sl = slice(bh * 128, (bh + 1) * 128)
                i = ctx.on("tensor", lambda e, sl=sl, Acur=Acur, Bcur=Bcur: e.matmul(
                    r.PSA[:, sl], Bcur[:, sl], Acur[:, sl], start=True, stop=True))
            ctx.inc(i, sP)
            t_d1 = sP.c
            ctx.wait("vector", sP, t_d1)
            i = ctx.on("vector", lambda e, Anew=Anew: e.tensor_copy(out=Anew[:], in_=r.PSA[:]))
            ctx.inc(i, sV)
            t_Anew = sV.c
            ctx.wait("tensor", sV, t_Anew)
            for bh in range(16):
                sl = slice(bh * 128, (bh + 1) * 128)
                i = ctx.on("tensor", lambda e, sl=sl, Acur=Acur, Bcur=Bcur: e.matmul(
                    r.PSA[:, sl], Acur[:, sl], Bcur[:, sl], start=True, stop=True))
            ctx.inc(i, sP)
            t_d2 = sP.c
            ctx.wait("vector", sP, t_d2)
            i = ctx.on("vector", lambda e, Bnew=Bnew: e.tensor_copy(out=Bnew[:], in_=r.PSA[:]))
            ctx.inc(i, sV)
            t_Bnew = sV.c
            Acur, Anew = Anew, Acur
            Bcur, Bnew = Bnew, Bcur
            t_psa_free = t_Bnew
        p *= 2

    i = ctx.on("vector", lambda e: e.tensor_mul(
        out=r.tmp_bf[:].rearrange("p (a b) -> p a b", b=128), in0=Qc,
        in1=r.lam_row[:].rearrange("p (a b) -> p a b", b=128)))
    ctx.inc(i, sV)
    t_qlam = sV.c
    ctx.wait("tensor", sV, t_qlam)
    ctx.wait("tensor", sA, t_ubf_prev)
    for bh in range(16):
        sl = slice(bh * 128, (bh + 1) * 128)
        ctx.on("tensor", lambda e, sl=sl: e.matmul(
            r.PSA[:, sl], r.tmp_bf[:, sl], r.S0bf[:, sl], start=True, stop=False))
        i = ctx.on("tensor", lambda e, sl=sl: e.matmul(
            r.PSA[:, sl], r.Wt[:, sl], r.Ubf[:, sl], start=False, stop=True))
    ctx.inc(i, sP)
    t_O = sP.c

    ctx.wait("scalar", sP, t_O)
    i = ctx.on("scalar", lambda e: e.activation(out=r.sqbf[:], in_=r.PSA[:], func=AF.Square))
    ctx.inc(i, sA)
    t_sq = sA.c
    ctx.wait("vector", sA, t_sq)
    i = ctx.on("vector", lambda e: e.tensor_reduce(
        out=r.ssb[:], in_=r.sqbf[:].rearrange("p (a b) -> p a b", b=128), axis=AX.X, op=ALU.add))
    ctx.inc(i, sV)
    ctx.wait("scalar", sV, sV.c)
    i = ctx.on("scalar", lambda e: e.activation(out=r.rstd[:], in_=r.ssb[:], func=AF.Sqrt,
                                                scale=1.0 / 128.0, bias=1e-5))
    ctx.inc(i, sA)
    ctx.wait("vector", sA, sA.c)
    i = ctx.on("vector", lambda e: e.reciprocal(out=r.rstd[:], in_=r.rstd[:]))
    ctx.inc(i, sV)
    ctx.wait("vector", sV, sV.c)
    rsv = r.rstd[:].unsqueeze(2).broadcast_to([128, 16, 128])
    i = ctx.on("vector", lambda e: e.tensor_mul(
        out=r.on[:].rearrange("p (a b) -> p a b", b=128),
        in0=r.PSA[:].rearrange("p (a b) -> p a b", b=128), in1=rsv))
    ctx.inc(i, sV)
    t_on = sV.c
    ctx.wait("tensor", sV, t_on)
    for bh in range(16):
        i = ctx.on("tensor", lambda e, bh=bh: e.transpose(
            r.PSC[:, bh * 128:(bh + 1) * 128], r.on[:, bh * 128:(bh + 1) * 128], identb))
    ctx.inc(i, sP)
    t_otp = sP.c
    ctx.wait("vector", sP, t_otp)
    for bh in range(16):
        b, h = bh // 8, bh % 8
        dst = r.ogate[:, h, b * BW + col0: b * BW + col0 + TOKC]
        src = r.PSC[:, bh * 128 + 1:(bh + 1) * 128:2]
        gsl = r.gate[:, h, b * BW + col0: b * BW + col0 + TOKC]
        i = ctx.on("vector", lambda e, dst=dst, src=src, gsl=gsl: e.scalar_tensor_tensor(
            out=dst, in0=src, scalar=r.onw[:], in1=gsl, op0=ALU.mult, op1=ALU.mult))
    ctx.inc(i, sV)
    t_og = sV.c

    ctx.wait("tensor", sV, t_og)
    for bh in range(16):
        i = ctx.on("tensor", lambda e, bh=bh: e.transpose(
            r.PSC[:, bh * 128:(bh + 1) * 128], Kc[:, bh], identb))
    ctx.inc(i, sP)
    t_ktp = sP.c
    ctx.wait("vector", sP, t_ktp)
    for bh in range(16):
        sl = slice(bh * 128, (bh + 1) * 128)
        ctx.on("vector", lambda e, sl=sl, bh=bh: e.tensor_scalar(
            out=r.KDtm[:, sl], in0=r.PSC[:, sl],
            scalar1=r.edecf[:, bh:bh + 1], scalar2=None, op0=ALU.mult))
        ctx.on("vector", lambda e, sl=sl, bh=bh: e.tensor_scalar(
            out=r.S[:, sl], in0=r.S[:, sl],
            scalar1=r.lamlast[:, bh:bh + 1], scalar2=None, op0=ALU.mult))
    i = ctx.on("vector", lambda e: e.engine_nop())
    ctx.inc(i, sV)
    t_kdtm = sV.c
    ctx.wait("tensor", sV, t_kdtm)
    for bh in range(16):
        sl = slice(bh * 128, (bh + 1) * 128)
        i = ctx.on("tensor", lambda e, sl=sl: e.matmul(
            r.PSA[:, sl], r.KDtm[:, sl], r.Ubf[:, sl], start=True, stop=True))
    ctx.inc(i, sP)
    t_smm = sP.c
    ctx.wait("vector", sP, t_smm)
    i = ctx.on("vector", lambda e: e.tensor_add(out=r.S[:], in0=r.S[:], in1=r.PSA[:]))
    ctx.inc(i, sV)



def emit_wo_ln(ctx, d, r, sems, pc0):
    dS, dG, sP, sV, sA, st = (sems[n] for n in ["dS", "dG", "sP", "sV", "sA", "st"])
    t_og_all = sV.c
    slab_free = sP.c
    slot_free = [sV.c] * 4
    xf_cons = [sV.c] * 2

    for ot in range(16):
        si = ot % 2
        pi = ot % 4
        psl = r.PSA[:, pi * 512: pi * 512 + 256]
        ctx.wait("sync", sV, xf_cons[si])
        t_xf = ctx.dma("sync", dS, r.xfr[si][:], d.x_fm[ot * 128:(ot + 1) * 128, pc0:pc0 + 256])
        ctx.wait("sync", sP, slab_free)
        t_slab = ctx.dma("sync", dS, r.slab[2][:, 0:8, :], d.wbf_o[ot])
        ctx.wait("tensor", dS, t_slab)
        ctx.wait("tensor", sV, max(slot_free[pi], t_og_all))
        mm = None
        for kk in range(8):
            mm = ctx.on("tensor", lambda e, kk=kk, psl=psl: e.matmul(
                psl, r.slab[2][:, kk, :], r.ogate[:, kk, :],
                start=(kk == 0), stop=(kk == 7)))
        ctx.inc(mm, sP)
        slab_free = sP.c
        t_mm = sP.c
        ctx.wait("vector", sP, t_mm)
        ctx.wait("vector", dS, t_xf)
        i = ctx.on("vector", lambda e, psl=psl, si=si, ot=ot: e.tensor_add(
            out=r.ypre[:, ot, :], in0=psl, in1=r.xfr[si][:]))
        ctx.inc(i, sV)
        slot_free[pi] = sV.c
        xf_cons[si] = sV.c
    t_ypre = sV.c

    ctx.wait("tensor", sV, t_ypre)
    mm = None
    for ot in range(16):
        mm = ctx.on("tensor", lambda e, ot=ot: e.matmul(
            r.PSB[0:1, 0:256], r.onesb[:, 0:1], r.ypre[:, ot, :],
            start=(ot == 0), stop=(ot == 15)))
    ctx.inc(mm, sP)
    ctx.wait("scalar", sV, t_ypre)
    sq_mm = [sP.c] * 2
    for ot in range(16):
        sqi = ot % 2
        ctx.wait("scalar", sP, sq_mm[sqi])
        i = ctx.on("scalar", lambda e, ot=ot, sqi=sqi: e.activation(
            out=r.sq[sqi][:], in_=r.ypre[:, ot, :], func=AF.Square))
        ctx.inc(i, sA)
        t_sqr = sA.c
        ctx.wait("tensor", sA, t_sqr)
        i = ctx.on("tensor", lambda e, sqi=sqi, ot=ot: e.matmul(
            r.PSB[0:1, 256:512], r.onesb[:, 0:1], r.sq[sqi][:],
            start=(ot == 0), stop=(ot == 15)))
        ctx.inc(i, sP)
        sq_mm[sqi] = sP.c
    t_sumsq = sP.c
    ctx.wait("vector", sP, t_sumsq)
    ctx.on("vector", lambda e: e.tensor_scalar(
        out=r.lnr[0:1, 0:256], in0=r.PSB[0:1, 0:256], scalar1=-1.0 / 2048.0,
        scalar2=None, op0=ALU.mult))
    ctx.on("vector", lambda e: e.tensor_mul(
        out=r.lnr[0:1, 512:768], in0=r.lnr[0:1, 0:256], in1=r.lnr[0:1, 0:256]))
    i = ctx.on("vector", lambda e: e.scalar_tensor_tensor(
        out=r.lnr[0:1, 256:512], in0=r.PSB[0:1, 256:512], scalar=1.0 / 2048.0,
        in1=r.lnr[0:1, 512:768], op0=ALU.mult, op1=ALU.subtract))
    ctx.inc(i, sV)
    t_var = sV.c
    ctx.wait("scalar", sV, t_var)
    i = ctx.on("scalar", lambda e: e.activation(out=r.lnr[0:1, 256:512],
                                                in_=r.lnr[0:1, 256:512], func=AF.Sqrt, bias=1e-5))
    ctx.inc(i, sA)
    ctx.wait("vector", sA, sA.c)
    i = ctx.on("vector", lambda e: e.reciprocal(out=r.lnr[0:1, 256:512],
                                                in_=r.lnr[0:1, 256:512]))
    ctx.inc(i, sV)
    ctx.wait("vector", sV, sV.c)
    i = ctx.on("vector", lambda e: e.engine_nop())
    ctx.inc(i, sV)
    t_rs = sV.c
    ctx.wait("tensor", sV, t_rs)
    ctx.on("tensor", lambda e: e.matmul(
        r.PSB[:, 512:768], r.onesf[0:1, :], r.lnr[0:1, 0:256], start=True, stop=True))
    i = ctx.on("tensor", lambda e: e.matmul(
        r.PSB[:, 768:1024], r.onesf[0:1, :], r.lnr[0:1, 256:512], start=True, stop=True))
    ctx.inc(i, sP)
    t_bc = sP.c
    ctx.wait("vector", sP, t_bc)
    ctx.wait("vector", st, r.spill_st)
    for ot in range(16):
        ctx.on("vector", lambda e, ot=ot: e.tensor_add(
            out=r.lnt[:], in0=r.ypre[:, ot, :], in1=r.PSB[:, 512:768]))
        i = ctx.on("vector", lambda e, ot=ot: e.tensor_mul(
            out=r.ystg[:, ot, :], in0=r.lnt[:], in1=r.PSB[:, 768:1024]))
        ctx.inc(i, sV)
    t_y = sV.c
    ctx.wait("sync", sV, t_y)
    ctx.dma("sync", st, d.ybf.rearrange("(k p) n -> p k n", p=128)[:, :, pc0:pc0 + 256],
            r.ystg[:])
    r.spill_st = sems["st"].c



def emit_ffn_iter(ctx, d, r, sems, it, t_p0ffn, dbg_outs):
    dS, dG, sP, sV, sA, st = (sems[n] for n in ["dS", "dG", "sP", "sV", "sA", "st"])
    pc0 = it * FIT
    t_spills = st.c

    ctx.wait("sync", st, t_spills)
    ctx.wait("sync", sP, sP.c)
    ysrc = d.ybf.rearrange("(k p) n -> p k n", p=128)[:, :, pc0:pc0 + FIT]
    t_y = ctx.dma("sync", dS, r.ybf_sb[:], ysrc)
    ctx.wait("vector", dS, t_y)
    ctx.wait("vector", sP, sP.c)
    ctx.wait("vector", sA, sA.c)
    i = None
    for kk in range(16):
        i = ctx.on("vector", lambda e, kk=kk: e.tensor_copy(
            out=r.z[:, kk, :], in_=r.ybf_sb[:, kk, :]))
    ctx.inc(i, sV)
    t_z0 = sV.c

    ctx.wait("tensor", dS, t_y)
    ctx.wait("tensor", sV, t_z0)
    ctx.wait("tensor", sA, sA.c)
    mm = None
    for kk in range(16):
        mm = ctx.on("tensor", lambda e, kk=kk: e.matmul(
            r.PSB[0:16, 0:512], r.wafn_sb[:, kk, :], r.ybf_sb[:, kk, :],
            start=(kk == 0), stop=(kk == 15)))
    ctx.inc(mm, sP)
    t_aff = sP.c
    ctx.wait("scalar", sP, t_aff)
    i = ctx.on("scalar", lambda e: e.activation(out=r.aff[:], in_=r.PSB[0:16, 0:512],
                                                func=AF.Sigmoid))
    ctx.inc(i, sA)
    t_affs = sA.c
    ctx.wait("tensor", sA, t_affs)
    for q4 in range(4):
        i = ctx.on("tensor", lambda e, q4=q4: e.transpose(
            r.PSB[:, 512 + q4 * 16: 512 + (q4 + 1) * 16],
            r.aff[:, q4 * 128:(q4 + 1) * 128], r.identf[:][0:16, 0:16]))
    ctx.inc(i, sP)
    t_at = sP.c
    ctx.wait("vector", sP, t_at)
    i = ctx.on("vector", lambda e: e.tensor_copy(
        out=r.affT[:].rearrange("p a b -> p (a b)"), in_=r.PSB[:, 512:576]))
    ctx.inc(i, sV)
    ctx.wait("vector", sV, sV.c)
    av = r.affT[:]
    i = ctx.on("vector", lambda e: e.tensor_reduce(out=r.m1[:], in_=av, axis=AX.X, op=ALU.max))
    ctx.inc(i, sV)
    ctx.wait("vector", sV, sV.c)
    for q4 in range(4):
        i = ctx.on("vector", lambda e, q4=q4: e.tensor_scalar(
            out=r.eq[:, q4, :], in0=av[:, q4, :], scalar1=r.m1[:, q4:q4 + 1],
            scalar2=None, op0=ALU.is_ge))
    ctx.inc(i, sV)
    ctx.wait("vector", sV, sV.c)
    ctx.on("vector", lambda e: e.scalar_tensor_tensor(
        out=r.combT[:].rearrange("p a b -> p (a b)"),
        in0=r.eq[:].rearrange("p a b -> p (a b)"), scalar=-1e30,
        in1=av.rearrange("p a b -> p (a b)"), op0=ALU.mult, op1=ALU.add))
    i = ctx.on("vector", lambda e: e.engine_nop())
    ctx.inc(i, sV)
    ctx.wait("vector", sV, sV.c)
    i = ctx.on("vector", lambda e: e.tensor_reduce(
        out=r.m2[:], in_=r.combT[:], axis=AX.X, op=ALU.max))
    ctx.inc(i, sV)
    ctx.wait("vector", sV, sV.c)
    i = ctx.on("vector", lambda e: e.tensor_add(out=r.den[:], in0=r.m1[:], in1=r.m2[:]))
    ctx.inc(i, sV)
    ctx.wait("scalar", sV, sV.c)
    i = ctx.on("scalar", lambda e: e.activation(out=r.den[:], in_=r.den[:], func=AF.Ln))
    ctx.inc(i, sA)
    ctx.wait("scalar", sA, sA.c)
    i = ctx.on("scalar", lambda e: e.activation(out=r.den[:], in_=r.den[:], func=AF.Exp,
                                                scale=-1.0))
    ctx.inc(i, sA)
    ctx.wait("vector", sA, sA.c)
    for q4 in range(4):
        i = ctx.on("vector", lambda e, q4=q4: e.tensor_scalar(
            out=r.eq[:, q4, :], in0=av[:, q4, :], scalar1=r.m2[:, q4:q4 + 1],
            scalar2=None, op0=ALU.is_ge))
    ctx.inc(i, sV)
    ctx.wait("vector", sV, sV.c)
    for q4 in range(4):
        i = ctx.on("vector", lambda e, q4=q4: e.tensor_mul(
            out=r.combT[:, q4, :], in0=av[:, q4, :], in1=r.eq[:, q4, :]))
    ctx.inc(i, sV)
    ctx.wait("vector", sV, sV.c)
    for q4 in range(4):
        i = ctx.on("vector", lambda e, q4=q4: e.tensor_scalar(
            out=r.combT[:, q4, :], in0=r.combT[:, q4, :], scalar1=r.den[:, q4:q4 + 1],
            scalar2=None, op0=ALU.mult))
    ctx.inc(i, sV)
    t_combT = sV.c
    ctx.wait("tensor", sV, t_combT)
    for q4 in range(4):
        i = ctx.on("tensor", lambda e, q4=q4: e.transpose(
            r.PSB[0:16, 512 + q4 * 128: 512 + (q4 + 1) * 128],
            r.combT[:, q4, :], r.identf[:]))
    ctx.inc(i, sP)
    t_cbt = sP.c
    ctx.wait("vector", sP, t_cbt)
    i = ctx.on("vector", lambda e: e.tensor_copy(out=r.comb[:], in_=r.PSB[0:16, 512:1024]))
    ctx.inc(i, sV)
    t_comb = sV.c

    if DBG and it == 0 and "comb" in dbg_outs:
        ctx.wait("sync", sV, t_comb)
        ctx.wait("sync", sA, sA.c)
        ctx.dma("sync", st, dbg_outs["comb"], r.comb[:])
        ctx.dma("sync", st, dbg_outs["affdbg"], r.aff[:])
        ctx.dma("sync", st, dbg_outs["affT"], r.affT[:])
        ctx.dma("sync", st, dbg_outs["m1"], r.m1[:])
        ctx.dma("sync", st, dbg_outs["m2"], r.m2[:])
        ctx.dma("sync", st, dbg_outs["den"], r.den[:])
        ctx.dma("sync", st, dbg_outs["combT"], r.combT[:])
        t_tap = sems["st"].c
        ctx.wait("vector", st, t_tap)
        ctx.wait("tensor", st, t_tap)

    wsrc = lambda ee: (d.rg_bf[ee], d.rw1_bf[ee], d.rw2_bf[ee]) if ee < E_RT else \
                      (d.sg_bf[ee - E_RT], d.sw1_bf[ee - E_RT], d.sw2_bf[ee - E_RT])
    bias_g = lambda ee, ft: r.rgb[:, ee, ft:ft + 1] if ee < E_RT else r.sgb[:, ee - E_RT, ft:ft + 1]
    bias_1 = lambda ee, ft: r.rb1[:, ee, ft:ft + 1] if ee < E_RT else r.sb1[:, ee - E_RT, ft:ft + 1]

    wa_free = [sP.c] * 3
    wb_free = [sP.c] * 3
    w2_free = [sP.c] * 4
    seqA = 0
    seqW2 = 0
    t_g1free = sV.c
    pb_v = [sV.c] * 2
    pb_a = [sA.c] * 2
    z_free = [sV.c] * 4
    t_combbc_cons = sV.c
    u_free = sP.c

    for grp in range(6):
        experts = [grp * 3, grp * 3 + 1, grp * 3 + 2]
        for gi, ee in enumerate(experts):
            srcs = wsrc(ee)
            if ee < E_RT:
                ctx.wait("tensor", sV, max(t_comb, t_combbc_cons))
                i = ctx.on("tensor", lambda e, ee=ee: e.matmul(
                    r.PSB[:, 0:512], r.sel16[:, ee * 128:(ee + 1) * 128], r.comb[:],
                    start=True, stop=True))
                ctx.inc(i, sP)
                t_cbc = sP.c
            for fp in range(4):
                sa_i = seqA % 3
                seqA += 1
                ctx.wait("sync", dG, t_p0ffn)
                ctx.wait("sync", sP, wa_free[sa_i])
                t_wa = ctx.dma("sync", dS, r.wA[sa_i][:], srcs[0][fp])
                ctx.wait("sync", sP, wb_free[sa_i])
                t_wb = ctx.dma("sync", dS, r.wB[sa_i][:], srcs[1][fp])
                for fc2 in range(2):
                    fc = fp * 2 + fc2
                    psl_g = r.PSA[:, fc2 * 1024: fc2 * 1024 + 512]
                    psl_1 = r.PSA[:, fc2 * 1024 + 512: fc2 * 1024 + 1024]
                    ctx.wait("tensor", dS, max(t_wa, t_wb))
                    ctx.wait("tensor", sV, pb_v[fc2])
                    ctx.wait("tensor", sA, pb_a[fc2])
                    mm = None
                    for kk in range(16):
                        mm = ctx.on("tensor", lambda e, kk=kk, sa_i=sa_i, fc2=fc2, psl_g=psl_g: e.matmul(
                            psl_g, r.wA[sa_i][:, kk, fc2 * 128:(fc2 + 1) * 128],
                            r.ybf_sb[:, kk, :], start=(kk == 0), stop=(kk == 15)))
                    for kk in range(16):
                        mm = ctx.on("tensor", lambda e, kk=kk, sa_i=sa_i, fc2=fc2, psl_1=psl_1: e.matmul(
                            psl_1, r.wB[sa_i][:, kk, fc2 * 128:(fc2 + 1) * 128],
                            r.ybf_sb[:, kk, :], start=(kk == 0), stop=(kk == 15)))
                    ctx.inc(mm, sP)
                    t_h = sP.c
                    ctx.wait("scalar", sP, t_h)
                    ctx.wait("scalar", sV, t_g1free)
                    i = ctx.on("scalar", lambda e, psl_g=psl_g, ee=ee, fc=fc: e.activation(
                        out=r.g1[:], in_=psl_g, func=AF.Gelu, bias=bias_g(ee, fc)))
                    ctx.inc(i, sA)
                    pb_a[fc2] = sA.c
                    ctx.wait("vector", sA, pb_a[fc2])
                    ctx.wait("vector", sP, u_free)
                    if ee < E_RT:
                        ctx.wait("vector", sP, t_cbc)
                        i = ctx.on("vector", lambda e, psl_1=psl_1, ee=ee, fc=fc: e.scalar_tensor_tensor(
                            out=r.u0[:], in0=psl_1, scalar=bias_1(ee, fc), in1=r.g1[:],
                            op0=ALU.add, op1=ALU.mult))
                        ctx.inc(i, sV)
                        t_g1free = sV.c
                        i = ctx.on("vector", lambda e, gi=gi, fc=fc: e.tensor_mul(
                            out=r.u[gi][:, fc, :], in0=r.u0[:], in1=r.PSB[:, 0:512]))
                        ctx.inc(i, sV)
                        t_combbc_cons = sV.c
                    else:
                        i = ctx.on("vector", lambda e, psl_1=psl_1, ee=ee, fc=fc, gi=gi: e.scalar_tensor_tensor(
                            out=r.u[gi][:, fc, :], in0=psl_1, scalar=bias_1(ee, fc), in1=r.g1[:],
                            op0=ALU.add, op1=ALU.mult))
                        ctx.inc(i, sV)
                        t_g1free = sV.c
                    pb_v[fc2] = sV.c
                wa_free[sa_i] = sP.c
                wb_free[sa_i] = sP.c
        t_u_grp = sV.c
        t_u_grp_a = sA.c

        for ps4 in range(4):
            rdy = []
            for gi, ee in enumerate(experts):
                w2i = seqW2 % 4
                seqW2 += 1
                ctx.wait("sync", sP, w2_free[w2i])
                t_rdy = ctx.dma("sync", dS, r.w2s[w2i][:], wsrc(ee)[2][ps4])
                rdy.append((w2i, t_rdy))
            for dc4 in range(4):
                dc = ps4 * 4 + dc4
                psl = r.PSA[:, dc4 * 512:(dc4 + 1) * 512]
                ctx.wait("tensor", sV, max(t_u_grp, z_free[dc4]))
                ctx.wait("tensor", sA, t_u_grp_a)
                ntot = 24 + (2 if grp == 0 else 0)
                nmm = 0
                mm = None
                for gi, ee in enumerate(experts):
                    w2i, t_rdy = rdy[gi]
                    ctx.wait("tensor", dS, t_rdy)
                    for fc in range(8):
                        mm = ctx.on("tensor", lambda e, w2i=w2i, fc=fc, dc4=dc4, psl=psl,
                                    gi=gi, nmm=nmm, ntot=ntot: e.matmul(
                            psl, r.w2s[w2i][:, fc, dc4 * 128:(dc4 + 1) * 128],
                            r.u[gi][:, fc, :], start=(nmm == 0), stop=(nmm == ntot - 1)))
                        nmm += 1
                if grp == 0:
                    mm = ctx.on("tensor", lambda e, dc=dc, psl=psl: e.matmul(
                        psl, r.b2r_sb[:, dc * 128:(dc + 1) * 128], r.comb[:],
                        start=False, stop=False, skip_group_check=True))
                    mm = ctx.on("tensor", lambda e, dc=dc, psl=psl: e.matmul(
                        psl, r.b2s_sb[:, dc * 128:(dc + 1) * 128], r.ones2[:],
                        start=False, stop=True, skip_group_check=True))
                ctx.inc(mm, sP)
                t_w2 = sP.c
                ctx.wait("vector", sP, t_w2)
                ctx.wait("vector", sV, t_z0)
                if grp == 0:
                    i = ctx.on("vector", lambda e, dc=dc, psl=psl: e.scalar_tensor_tensor(
                        out=r.z[:, dc, :], in0=r.z[:, dc, :], scalar=2.0, in1=psl,
                        op0=ALU.mult, op1=ALU.add))
                else:
                    i = ctx.on("vector", lambda e, dc=dc, psl=psl: e.tensor_add(
                        out=r.z[:, dc, :], in0=r.z[:, dc, :], in1=psl))
                ctx.inc(i, sV)
                z_free[dc4] = sV.c
            for gi in range(3):
                w2_free[rdy[gi][0]] = sP.c
        pb_v = [sV.c] * 2
        u_free = sP.c

    t_zdone = sV.c
    if DBG and it == 0 and "zdbg" in dbg_outs:
        ctx.wait("sync", sV, t_zdone)
        ctx.dma("sync", st, dbg_outs["zdbg"], r.z[:])
    ctx.wait("tensor", sV, t_zdone)
    mm = None
    for ot in range(16):
        mm = ctx.on("tensor", lambda e, ot=ot: e.matmul(
            r.PSB[0:1, 0:512], r.onesf[:, 0:1], r.z[:, ot, :],
            start=(ot == 0), stop=(ot == 15)))
    ctx.inc(mm, sP)
    ctx.wait("scalar", sV, t_zdone)
    sq_mm = [sP.c] * 2
    scr = [r.g1, r.u0]
    for ot in range(16):
        sqi = ot % 2
        ctx.wait("scalar", sP, sq_mm[sqi])
        i = ctx.on("scalar", lambda e, ot=ot, sqi=sqi: e.activation(
            out=scr[sqi][:], in_=r.z[:, ot, :], func=AF.Square))
        ctx.inc(i, sA)
        t_sqr = sA.c
        ctx.wait("tensor", sA, t_sqr)
        i = ctx.on("tensor", lambda e, sqi=sqi, ot=ot: e.matmul(
            r.PSB[0:1, 512:1024], r.onesf[:, 0:1], scr[sqi][:],
            start=(ot == 0), stop=(ot == 15)))
        ctx.inc(i, sP)
        sq_mm[sqi] = sP.c
    t_sumsq = sP.c
    ctx.wait("vector", sP, t_sumsq)
    ctx.on("vector", lambda e: e.tensor_scalar(
        out=r.flnr[0:1, 0:512], in0=r.PSB[0:1, 0:512], scalar1=-1.0 / 2048.0,
        scalar2=None, op0=ALU.mult))
    ctx.on("vector", lambda e: e.tensor_mul(
        out=r.flnr[0:1, 1024:1536], in0=r.flnr[0:1, 0:512], in1=r.flnr[0:1, 0:512]))
    i = ctx.on("vector", lambda e: e.scalar_tensor_tensor(
        out=r.flnr[0:1, 512:1024], in0=r.PSB[0:1, 512:1024], scalar=1.0 / 2048.0,
        in1=r.flnr[0:1, 1024:1536], op0=ALU.mult, op1=ALU.subtract))
    ctx.inc(i, sV)
    t_var = sV.c
    ctx.wait("scalar", sV, t_var)
    i = ctx.on("scalar", lambda e: e.activation(out=r.flnr[0:1, 512:1024],
                                                in_=r.flnr[0:1, 512:1024], func=AF.Sqrt, bias=1e-5))
    ctx.inc(i, sA)
    ctx.wait("vector", sA, sA.c)
    i = ctx.on("vector", lambda e: e.reciprocal(out=r.flnr[0:1, 512:1024],
                                                in_=r.flnr[0:1, 512:1024]))
    ctx.inc(i, sV)
    ctx.wait("vector", sV, sV.c)
    i = ctx.on("vector", lambda e: e.engine_nop())
    ctx.inc(i, sV)
    t_rs = sV.c
    ctx.wait("tensor", sV, t_rs)
    ctx.on("tensor", lambda e: e.matmul(
        r.PSA[:, 0:512], r.onesf[0:1, :], r.flnr[0:1, 0:512], start=True, stop=True))
    i = ctx.on("tensor", lambda e: e.matmul(
        r.PSA[:, 512:1024], r.onesf[0:1, :], r.flnr[0:1, 512:1024], start=True, stop=True))
    ctx.inc(i, sP)
    t_bc = sP.c
    ctx.wait("vector", sP, t_bc)
    ost_busy = [sems["st"].c] * 2
    for ot in range(16):
        oi = ot % 2
        ctx.wait("vector", st, ost_busy[oi])
        ctx.on("vector", lambda e, ot=ot, oi=oi: e.tensor_add(
            out=r.ostg[oi][:], in0=r.z[:, ot, :], in1=r.PSA[:, 0:512]))
        i = ctx.on("vector", lambda e, ot=ot, oi=oi: e.tensor_mul(
            out=r.ostg[oi][:], in0=r.ostg[oi][:], in1=r.PSA[:, 512:1024]))
        ctx.inc(i, sV)
        t_o = sV.c
        ctx.wait("sync", sV, t_o)
        ost_busy[oi] = ctx.dma("sync", st, d.out[ot * 128:(ot + 1) * 128, pc0:pc0 + FIT],
                               r.ostg[oi][:])



def _stage_weights(inp):
    g = {}
    T = lambda a: np.ascontiguousarray(a)
    Wq = inp["L0_attn_Wq"]; Wk = inp["L0_attn_Wk"]; Wv = inp["L0_attn_Wv"]
    Wb = inp["L0_attn_Wb"]; Wa = inp["L0_attn_Wa"]; Wg = inp["L0_attn_Wg"]
    Wo = inp["L0_attn_Wo"]
    g["wq_l"] = T(Wq.T.reshape(16, 128, 8, 128).transpose(2, 1, 0, 3))
    g["wk_l"] = T(Wk.T.reshape(16, 128, 16, 128).transpose(2, 1, 0, 3))
    g["wv_l"] = T(Wv.T.reshape(16, 128, 16, 128).transpose(2, 1, 0, 3))
    g["wg_l"] = T(Wg.T.reshape(16, 128, 8, 128).transpose(2, 1, 0, 3))
    g["wo_l"] = T(Wo.T.reshape(8, 128, 16, 128).transpose(2, 1, 0, 3))
    g["wb_l"] = T(Wb.T.reshape(16, 128, 16).transpose(1, 0, 2))
    g["wa_l"] = T(Wa.T.reshape(16, 128, 8).transpose(1, 0, 2))
    g["wafn_l"] = T(inp["L0_ffn_wa"].reshape(16, 128, 16).transpose(1, 0, 2))
    g["cq_l"] = T(inp["L0_attn_conv_q"].reshape(8, 128, 4).transpose(1, 0, 2))
    g["ck_l"] = T(inp["L0_attn_conv_k"].reshape(16, 128, 4).transpose(1, 0, 2))
    g["cv_l"] = T(inp["L0_attn_conv_v"].reshape(16, 128, 4).transpose(1, 0, 2))
    g["alog"] = T(inp["L0_attn_A_log"].reshape(8, 1))
    g["dtb"] = T(inp["L0_attn_dt_bias"].reshape(8, 1))
    g["onw"] = T(inp["L0_attn_o_norm_w"].reshape(128, 1))
    g["rg_l"] = T(inp["L0_ffn_rg"].reshape(16, 16, 128, 4, 256).transpose(0, 3, 2, 1, 4))
    g["rw1_l"] = T(inp["L0_ffn_rw1"].reshape(16, 16, 128, 4, 256).transpose(0, 3, 2, 1, 4))
    g["rw2_l"] = T(inp["L0_ffn_rw2"].reshape(16, 8, 128, 4, 512).transpose(0, 3, 2, 1, 4))
    g["sg_l"] = T(inp["L0_ffn_sg"].reshape(2, 16, 128, 4, 256).transpose(0, 3, 2, 1, 4))
    g["sw1_l"] = T(inp["L0_ffn_sw1"].reshape(2, 16, 128, 4, 256).transpose(0, 3, 2, 1, 4))
    g["sw2_l"] = T(inp["L0_ffn_sw2"].reshape(2, 8, 128, 4, 512).transpose(0, 3, 2, 1, 4))
    g["rgb_l"] = T(inp["L0_ffn_rgb"].reshape(16, 8, 128).transpose(2, 0, 1))
    g["rb1_l"] = T(inp["L0_ffn_rb1"].reshape(16, 8, 128).transpose(2, 0, 1))
    g["sgb_l"] = T(inp["L0_ffn_sgb"].reshape(2, 8, 128).transpose(2, 0, 1))
    g["sb1_l"] = T(inp["L0_ffn_sb1"].reshape(2, 8, 128).transpose(2, 0, 1))
    g["b2r"] = T(inp["L0_ffn_rb2"])
    g["b2s"] = T(inp["L0_ffn_sb2"])
    sel8 = np.zeros((8, 1024), np.float32)
    for h in range(8):
        sel8[h, h * 128:(h + 1) * 128] = 1.0
    g["sel8"] = sel8
    sel16 = np.zeros((16, 2048), np.float32)
    for j in range(16):
        sel16[j, j * 128:(j + 1) * 128] = 1.0
    g["sel16"] = sel16.astype(ml_dtypes.bfloat16)
    g["maskneg"] = np.where(np.triu(np.ones((128, 128)), 0) > 0, 0.0, -1e9).astype(np.float32)
    g["identf"] = np.eye(128, dtype=np.float32)
    g["identb"] = np.eye(128).astype(ml_dtypes.bfloat16)
    return g


def _stage_x(x, core):
    xc = x[2 * core:2 * core + 2]
    xr = xc.reshape(2, NBLK, NHALF, BW, D)
    return np.ascontiguousarray(xr.transpose(4, 1, 2, 0, 3).reshape(D, NPCOL))


def _unstage_out(out_fm):
    of = out_fm.reshape(D, NBLK, NHALF, 2, BW)
    return np.ascontiguousarray(of.transpose(3, 1, 2, 4, 0).reshape(2, 2048, D))


def kernel(**inp):
    nc = build_program()
    g = _stage_weights(inp)
    x = np.asarray(inp["L0_attn_x"])
    in_maps = []
    for core in range(NCORE):
        m = dict(g)
        m["x_fm"] = _stage_x(x, core)
        in_maps.append(m)
    res = run_bass_kernel_spmd(nc, in_maps, core_ids=list(range(NCORE)))
    out = np.empty((16, 2048, 2048), np.float32)
    for core in range(NCORE):
        out[2 * core:2 * core + 2] = _unstage_out(res.results[core]["out_fm"])
    return out


# version 04
# speedup vs baseline: 1.0057x; 1.0057x over previous
import numpy as np
import ml_dtypes
from contextlib import ExitStack
import concourse.bass as bass
import concourse.mybir as mybir
from concourse.bass_utils import run_bass_kernel_spmd

F32 = mybir.dt.float32
BF16 = mybir.dt.bfloat16
AF = mybir.ActivationFunctionType
ALU = mybir.AluOpType
AX = mybir.AxisListType

ENGINES = ["sync", "gpsimd", "vector", "scalar", "tensor"]

NCORE = 8
D = 2048
KT = 16
NBLK = 8
NHALF = 2
BW = 128
HALF_COLS = 256
TOKC = 64
NCHH = 2
NPCOL = 4096
E_RT, E_SH, FF = 16, 2, 1024
NEXP = E_RT + E_SH
FIT = 512
NFI = NPCOL // FIT
SQ128 = 11.313708498984761

DBG = False
DBG_HB = 0
DBG_TB = 0


class Sem:
    def __init__(self, handle):
        self.h = handle
        self.c = 0


class Ctx:
    def __init__(self):
        self.cur = None
        self.eng = None

    def set_pass(self, name, eng, sems):
        self.cur = name
        self.eng = eng
        for s in sems:
            s.c = 0

    def on(self, engine_name, fn):
        if self.cur == engine_name:
            return fn(self.eng)
        return None

    def inc(self, inst, sem, n=1):
        sem.c += n
        if inst is not None:
            inst.then_inc(sem.h, n)

    def wait(self, engine_name, sem, value=None):
        if value is None:
            value = sem.c
        if self.cur == engine_name:
            self.eng.wait_ge(sem.h, value)

    def dma(self, engine_name, sem, out, in_, **kw):
        sem.c += 16
        if self.cur == engine_name:
            self.eng.dma_start(out=out, in_=in_, **kw).then_inc(sem.h, 16)
        return sem.c


class NS:
    pass


DBG_TAPS = [
    ("Kb", [128, NCHH, 16, 128], BF16),
    ("Vb", [128, NCHH, 16, 128], BF16),
    ("Qb", [128, NCHH, 16, 128], BF16),
    ("gate", [128, 8, HALF_COLS], BF16),
    ("beta0", [8, HALF_COLS], F32),
    ("beta1", [8, HALF_COLS], F32),
    ("g8", [8, HALF_COLS], F32),
    ("og", [128, 8, HALF_COLS], BF16),
    ("ypre", [128, 16, HALF_COLS], BF16),
    ("ystg", [128, 16, HALF_COLS], BF16),
    ("comb", [16, FIT], BF16),
    ("affdbg", [16, FIT], F32),
    ("affT", [128, 4, 16], F32),
    ("m1", [128, 4], F32),
    ("m2", [128, 4], F32),
    ("den", [128, 4], F32),
    ("combT", [128, 4, 16], F32),
    ("zdbg", [128, 16, FIT], F32),
]


def _vb_view(t, nh, h):
    v4 = t[:].rearrange("p c (bb hh) t -> p bb c hh t", bb=2)
    return v4[:, :, :, h, nh::2]



def build_program():
    nc = bass.Bass("TRN2", target_bir_lowering=False)
    d = NS()

    def din(name, shape, dt=F32):
        setattr(d, name, nc.dram_tensor(name, list(shape), dt, kind="ExternalInput").ap())

    din("x_fm", [D, NPCOL])
    din("wq_l", [8, 128, 16, 128])
    din("wk_l", [16, 128, 16, 128])
    din("wv_l", [16, 128, 16, 128])
    din("wg_l", [8, 128, 16, 128])
    din("wo_l", [16, 128, 8, 128])
    din("wb_l", [128, 16, 16])
    din("wa_l", [128, 16, 8])
    din("wafn_l", [128, 16, 16])
    din("cq_l", [128, 8, 4])
    din("ck_l", [128, 16, 4])
    din("cv_l", [128, 16, 4])
    din("alog", [8, 1])
    din("dtb", [8, 1])
    din("onw", [128, 1])
    din("rg_l", [E_RT, 4, 128, 16, 256])
    din("rw1_l", [E_RT, 4, 128, 16, 256])
    din("rw2_l", [E_RT, 4, 128, 8, 512])
    din("sg_l", [E_SH, 4, 128, 16, 256])
    din("sw1_l", [E_SH, 4, 128, 16, 256])
    din("sw2_l", [E_SH, 4, 128, 8, 512])
    din("rgb_l", [128, E_RT, 8])
    din("rb1_l", [128, E_RT, 8])
    din("sgb_l", [128, E_SH, 8])
    din("sb1_l", [128, E_SH, 8])
    din("b2r", [E_RT, D])
    din("b2s", [E_SH, D])
    din("sel8", [8, 1024])
    din("sel16", [16, 2048], BF16)
    din("maskneg", [128, 128])
    din("identf", [128, 128])
    din("identb", [128, 128], BF16)

    d.out = nc.dram_tensor("out_fm", [D, NPCOL], F32, kind="ExternalOutput").ap()

    def dint(name, shape, dt):
        setattr(d, name, nc.dram_tensor(name, list(shape), dt, kind="Internal").ap())

    dint("wbf_q", [8, 128, 16, 128], BF16)
    dint("wbf_k", [16, 128, 16, 128], BF16)
    dint("wbf_v", [16, 128, 16, 128], BF16)
    dint("wbf_g", [8, 128, 16, 128], BF16)
    dint("wbf_o", [16, 128, 8, 128], BF16)
    dint("rg_bf", [E_RT, 4, 128, 16, 256], BF16)
    dint("rw1_bf", [E_RT, 4, 128, 16, 256], BF16)
    dint("rw2_bf", [E_RT, 4, 128, 8, 512], BF16)
    dint("sg_bf", [E_SH, 4, 128, 16, 256], BF16)
    dint("sw1_bf", [E_SH, 4, 128, 16, 256], BF16)
    dint("sw2_bf", [E_SH, 4, 128, 8, 512], BF16)
    dint("ybf", [D, NPCOL], BF16)

    dbg_outs = {}
    if DBG:
        for nm, shape, dt in DBG_TAPS:
            dbg_outs[nm] = nc.dram_tensor("dbg_" + nm, list(shape), dt, kind="ExternalOutput").ap()

    stack = ExitStack()
    e = stack.enter_context
    r = NS()
    r.nc = nc

    r.sel8 = e(nc.sbuf_tensor("p_sel8", [8, 1024], F32))
    r.sel16 = e(nc.sbuf_tensor("p_sel16", [16, 2048], BF16))
    r.maskneg = e(nc.sbuf_tensor("p_mask", [128, 128], F32))
    r.identf = e(nc.sbuf_tensor("p_idf", [128, 128], F32))
    r.identb = e(nc.sbuf_tensor("p_idb", [128, 128], BF16))
    r.onesb = e(nc.sbuf_tensor("p_onesb", [128, 128], BF16))
    r.onesf = e(nc.sbuf_tensor("p_onesf", [128, 128], F32))
    r.ones2 = e(nc.sbuf_tensor("p_ones2", [2, 512], BF16))
    r.cst = e(nc.sbuf_tensor("p_cst", [128, 8], F32))
    r.wb_sb = e(nc.sbuf_tensor("p_wb", [128, 16, 16], BF16))
    r.wa_sb = e(nc.sbuf_tensor("p_wa", [128, 16, 8], BF16))
    r.wafn_sb = e(nc.sbuf_tensor("p_wafn", [128, 16, 16], BF16))
    r.b2r_sb = e(nc.sbuf_tensor("p_b2r", [16, 2048], BF16))
    r.b2s_sb = e(nc.sbuf_tensor("p_b2s", [2, 2048], BF16))
    r.cq = e(nc.sbuf_tensor("p_cq", [128, 8, 4], F32))
    r.ck = e(nc.sbuf_tensor("p_ck", [128, 16, 4], F32))
    r.cv = e(nc.sbuf_tensor("p_cv", [128, 16, 4], F32))
    r.nexpA = e(nc.sbuf_tensor("p_nexpA", [8, 1], F32))
    r.dtb = e(nc.sbuf_tensor("p_dtb", [8, 1], F32))
    r.onw = e(nc.sbuf_tensor("p_onw", [128, 1], F32))
    r.rgb = e(nc.sbuf_tensor("p_rgb", [128, E_RT, 8], F32))
    r.rb1 = e(nc.sbuf_tensor("p_rb1", [128, E_RT, 8], F32))
    r.sgb = e(nc.sbuf_tensor("p_sgb", [128, E_SH, 8], F32))
    r.sb1 = e(nc.sbuf_tensor("p_sb1", [128, E_SH, 8], F32))
    r.hist = e(nc.sbuf_tensor("p_hist", [128, 40, 2, 4], BF16))
    r.carry = e(nc.sbuf_tensor("p_carry", [8, 2], F32))
    r.carry0 = e(nc.sbuf_tensor("p_carry0", [8, 2], F32))

    nc.const_aps.register(F32, 1e-5, r.cst[:, 0:1])
    nc.const_aps.register(F32, 1.0 / 128.0, r.cst[:, 1:2])
    nc.const_aps.register(F32, 1.0, r.cst[:, 2:3])
    nc.const_aps.register(F32, -1.0, r.cst[:, 3:4])

    pa_stack = ExitStack()
    e = pa_stack.enter_context
    r.x_bf = e(nc.sbuf_tensor("a_xbf", [128, 16, HALF_COLS], BF16))
    r.slab = [e(nc.sbuf_tensor(f"a_slab{i}", [128, 16, 128], BF16)) for i in range(3)]
    r.pre = [e(nc.sbuf_tensor(f"a_pre{i}", [128, 2, 132], BF16)) for i in range(3)]
    r.cvo = [e(nc.sbuf_tensor(f"a_cvo{i}", [128, 2, 128], BF16)) for i in range(2)]
    r.qs = [e(nc.sbuf_tensor(f"a_qs{i}", [128, HALF_COLS], BF16)) for i in range(3)]
    r.sq = [e(nc.sbuf_tensor(f"a_sq{i}", [128, HALF_COLS], BF16)) for i in range(2)]
    r.srow = [e(nc.sbuf_tensor(f"a_srow{i}", [1, HALF_COLS], BF16)) for i in range(3)]
    r.Kb = e(nc.sbuf_tensor("a_Kb", [128, NCHH, 16, 128], BF16))
    r.Vb = e(nc.sbuf_tensor("a_Vb", [128, NCHH, 16, 128], BF16))
    r.Qb = e(nc.sbuf_tensor("a_Qb", [128, NCHH, 16, 128], BF16))
    r.gate = e(nc.sbuf_tensor("a_gate", [128, 8, HALF_COLS], BF16))
    r.beta_nh = [e(nc.sbuf_tensor(f"a_beta{p}", [8, HALF_COLS], F32)) for p in range(2)]
    r.g8 = e(nc.sbuf_tensor("a_g8", [8, HALF_COLS], F32))
    r.sp1 = e(nc.sbuf_tensor("a_sp1", [8, HALF_COLS], F32))
    r.sp2 = e(nc.sbuf_tensor("a_sp2", [8, HALF_COLS], F32))
    r.cumA = e(nc.sbuf_tensor("a_cumA", [8, HALF_COLS], F32))
    r.cumB = e(nc.sbuf_tensor("a_cumB", [8, HALF_COLS], F32))
    r.ogate = e(nc.sbuf_tensor("a_og", [128, 8, HALF_COLS], BF16))
    r.ypre = e(nc.sbuf_tensor("a_ypre", [128, 16, HALF_COLS], BF16))
    r.xfr = [e(nc.sbuf_tensor(f"a_xfr{i}", [128, HALF_COLS], F32)) for i in range(2)]
    r.ystg = e(nc.sbuf_tensor("a_ystg", [128, 16, HALF_COLS], BF16))
    r.lnt = e(nc.sbuf_tensor("a_lnt", [128, HALF_COLS], F32))
    r.lnr = e(nc.sbuf_tensor("a_lnr", [1, 1024], F32))
    r.S = e(nc.sbuf_tensor("s_S", [128, 2048], F32))
    r.S0bf = e(nc.sbuf_tensor("s_S0bf", [128, 2048], BF16))
    r.cumg_sub = e(nc.sbuf_tensor("s_cgs", [8, 256], F32))
    r.beta_sub = e(nc.sbuf_tensor("s_bts", [8, 256], F32))
    r.cumg_T = e(nc.sbuf_tensor("s_cgT", [128, 16], F32))
    r.beta_T = e(nc.sbuf_tensor("s_btT", [128, 16], F32))
    r.lamc = e(nc.sbuf_tensor("s_lamc", [128, 16], F32))
    r.rstd = e(nc.sbuf_tensor("s_rstd", [128, 16], F32))
    r.ssb = e(nc.sbuf_tensor("s_ssb", [128, 16], F32))
    r.edecf = e(nc.sbuf_tensor("s_edecf", [128, 16], F32))
    r.lamlast = e(nc.sbuf_tensor("s_lamlast", [128, 16], F32))
    r.lam_row = e(nc.sbuf_tensor("s_lamr", [128, 2048], BF16))
    r.E = e(nc.sbuf_tensor("s_E", [128, 2048], BF16))
    r.tmp_bf = e(nc.sbuf_tensor("s_tmpbf", [128, 2048], BF16))
    r.Nt = e(nc.sbuf_tensor("s_Nt", [128, 2048], BF16))
    r.Nn = e(nc.sbuf_tensor("s_Nn", [128, 2048], BF16))
    r.Nt2 = e(nc.sbuf_tensor("s_Nt2", [128, 2048], BF16))
    r.Nn2 = e(nc.sbuf_tensor("s_Nn2", [128, 2048], BF16))
    r.Wt = e(nc.sbuf_tensor("s_Wt", [128, 2048], BF16))
    r.Vtm = e(nc.sbuf_tensor("s_Vtm", [128, 2048], BF16))
    r.KDtm = e(nc.sbuf_tensor("s_KDtm", [128, 2048], BF16))
    r.U32 = e(nc.sbuf_tensor("s_U32", [128, 2048], F32))
    r.Ubf = e(nc.sbuf_tensor("s_Ubf", [128, 2048], BF16))
    r.t32 = e(nc.sbuf_tensor("s_t32", [128, 1024], F32))
    r.sqbf = e(nc.sbuf_tensor("s_sqbf", [128, 2048], BF16))
    r.on = e(nc.sbuf_tensor("s_on", [128, 2048], BF16))
    pa_stack.close()
    e = stack.enter_context
    r.ybf_sb2 = [e(nc.sbuf_tensor(f"f_ybf{i}", [128, 16, FIT], BF16)) for i in range(2)]
    r.z = e(nc.sbuf_tensor("f_z", [128, 16, FIT], F32))
    r.u = [e(nc.sbuf_tensor(f"f_u{i}", [128, 8, FIT], BF16)) for i in range(3)]
    r.wA = [e(nc.sbuf_tensor(f"f_wA{i}", [128, 16, 256], BF16)) for i in range(3)]
    r.wB = [e(nc.sbuf_tensor(f"f_wB{i}", [128, 16, 256], BF16)) for i in range(3)]
    r.w2s = [e(nc.sbuf_tensor(f"f_w2s{i}", [128, 8, 512], BF16)) for i in range(3)]
    r.aff = e(nc.sbuf_tensor("f_aff", [16, FIT], F32))
    r.affT = e(nc.sbuf_tensor("f_affT", [128, 4, 16], F32))
    r.combT = e(nc.sbuf_tensor("f_combT", [128, 4, 16], F32))
    r.comb = e(nc.sbuf_tensor("f_comb", [16, FIT], BF16))
    r.m1 = e(nc.sbuf_tensor("f_m1", [128, 4], F32))
    r.m2 = e(nc.sbuf_tensor("f_m2", [128, 4], F32))
    r.den = e(nc.sbuf_tensor("f_den", [128, 4], F32))
    r.eq = e(nc.sbuf_tensor("f_eq", [128, 4, 16], F32))
    r.g1 = e(nc.sbuf_tensor("f_g1", [128, FIT], F32))
    r.u0 = e(nc.sbuf_tensor("f_uzero", [128, FIT], F32))
    r.ostg = [e(nc.sbuf_tensor(f"f_ostg{i}", [128, FIT], F32)) for i in range(2)]
    r.flnr = e(nc.sbuf_tensor("f_lnr", [1, 2048], F32))

    r.PSA = e(nc.psum_tensor("ps_A", [128, 2048], F32))
    r.PSB = e(nc.psum_tensor("ps_B", [128, 1024], F32))
    r.PSC = e(nc.psum_tensor("ps_C", [128, 2048], BF16))

    sems = {n: Sem(e(nc.semaphore(n))) for n in ["dS", "dG", "sP", "sV", "sA", "st"]}
    block = e(nc.Block())
    ctx = Ctx()

    def emit():
        emit_all(ctx, nc, d, r, sems, dbg_outs)

    for en in ENGINES:
        def run_pass(eng, en=en):
            ctx.set_pass(en, eng, list(sems.values()))
            emit()
        getattr(block, en)(run_pass)
    stack.close()
    return nc



def emit_all(ctx, nc, d, r, sems, dbg_outs):
    dS, dG, sP, sV, sA, st = (sems[n] for n in ["dS", "dG", "sP", "sV", "sA", "st"])

    for name, sb in [("sel8", r.sel8), ("sel16", r.sel16), ("maskneg", r.maskneg),
                     ("identf", r.identf), ("identb", r.identb),
                     ("cq_l", r.cq), ("ck_l", r.ck), ("cv_l", r.cv),
                     ("dtb", r.dtb), ("onw", r.onw),
                     ("rgb_l", r.rgb), ("rb1_l", r.rb1), ("sgb_l", r.sgb), ("sb1_l", r.sb1)]:
        ctx.dma("sync", dS, sb[:], getattr(d, name))
    ctx.dma("sync", dS, r.nexpA[:], d.alog)
    t_cl = dS.c

    ctx.dma("gpsimd", dG, r.wb_sb[:], d.wb_l)
    ctx.dma("gpsimd", dG, r.wa_sb[:], d.wa_l)
    ctx.dma("gpsimd", dG, r.wafn_sb[:], d.wafn_l)
    ctx.dma("gpsimd", dG, r.b2r_sb[:], d.b2r)
    ctx.dma("gpsimd", dG, r.b2s_sb[:], d.b2s)
    t_smallcast = dG.c
    for (src, dst) in [(d.wq_l, d.wbf_q), (d.wk_l, d.wbf_k), (d.wv_l, d.wbf_v),
                       (d.wg_l, d.wbf_g), (d.wo_l, d.wbf_o)]:
        ctx.dma("gpsimd", dG, dst.rearrange("a p k o -> (a p k o)"),
                src.rearrange("a p k o -> (a p k o)"))
    t_p0attn = dG.c
    for (src, dst, ne) in [(d.rg_l, d.rg_bf, E_RT), (d.rw1_l, d.rw1_bf, E_RT),
                           (d.rw2_l, d.rw2_bf, E_RT), (d.sg_l, d.sg_bf, E_SH),
                           (d.sw1_l, d.sw1_bf, E_SH), (d.sw2_l, d.sw2_bf, E_SH)]:
        for ee in range(ne):
            ctx.dma("gpsimd", dG, dst[ee].rearrange("a p k o -> (a p k o)"),
                    src[ee].rearrange("a p k o -> (a p k o)"))
    t_p0ffn = dG.c

    ctx.wait("vector", dS, t_cl)
    for fn in [lambda e: e.memset(r.onesb[:], 1.0),
               lambda e: e.memset(r.onesf[:], 1.0),
               lambda e: e.memset(r.ones2[:], 1.0),
               lambda e: e.memset(r.cst[:, 0:1], 1e-5),
               lambda e: e.memset(r.cst[:, 1:2], 1.0 / 128.0),
               lambda e: e.memset(r.cst[:, 2:3], 1.0),
               lambda e: e.memset(r.cst[:, 3:4], -1.0),
               lambda e: e.memset(r.S[:], 0.0),
               lambda e: e.memset(r.carry[:, 0:2], 0.0),
               lambda e: e.memset(r.carry0[:, 0:2], 0.0),
               lambda e: e.memset(r.Qb[:], 0.0),
               lambda e: e.memset(r.hist[:], 0.0)]:
        i = ctx.on("vector", fn)
    ctx.inc(i, sV)
    t_vinit = sV.c
    ctx.wait("scalar", sV, t_vinit)
    ctx.on("scalar", lambda e: e.activation(out=r.nexpA[:], in_=r.nexpA[:], func=AF.Exp))
    i = ctx.on("scalar", lambda e: e.mul(out=r.nexpA[:], in_=r.nexpA[:], mul=-1.0))
    ctx.inc(i, sA)
    ctx.wait("tensor", dS, t_cl)
    ctx.wait("tensor", dG, t_smallcast)
    ctx.wait("tensor", sV, t_vinit)

    r.spill_st = 0
    for tb in range(NBLK):
        for hb in range(NHALF):
            emit_half(ctx, d, r, sems, tb, hb, t_p0attn, dbg_outs)

    r.y_pref = [0, 0]
    r.y_used = [0, 0]
    for it in range(NFI):
        emit_ffn_iter(ctx, d, r, sems, it, t_p0ffn, dbg_outs)

    ctx.wait("sync", st, st.c)



def emit_half(ctx, d, r, sems, tb, hb, t_p0attn, dbg_outs):
    dS, dG, sP, sV, sA, st = (sems[n] for n in ["dS", "dG", "sP", "sV", "sA", "st"])
    pc0 = tb * 512 + hb * 256
    t_entry_sV = sems["sV"].c

    t_xfree = sP.c
    ctx.wait("gpsimd", sP, t_xfree)
    xsrc = d.x_fm.rearrange("(k p) n -> p k n", p=128)[:, :, pc0:pc0 + 256]
    t_x = ctx.dma("gpsimd", dG, r.x_bf[:], xsrc)

    tiles = [("a", 0), ("b", 0), ("b", 1)] + [("q", j) for j in range(8)] + \
            [("k", j) for j in range(16)] + [("v", j) for j in range(16)] + \
            [("gg", j) for j in range(8)]
    slab_src = {"q": d.wbf_q, "k": d.wbf_k, "v": d.wbf_v, "gg": d.wbf_g}
    conv_w = {"q": r.cq, "k": r.ck, "v": r.cv}
    hist_base = {"q": 0, "k": 8, "v": 24}

    slot_free = [sV.c] * 4
    slot_free_a = [sA.c] * 4
    slab_free = [sP.c] * 3
    slab_ready = [0] * 3
    pre_free = [sA.c] * 3
    qs_free = [sA.c] * 3
    cvo_free = [sA.c] * 2
    sq_free = [sP.c] * 2
    srow_free = [sP.c] * 3
    nrm_q = 0
    ssrow_free = [sA.c] * 2
    sbc_free = [sV.c] * 2

    for ti, (kind, j) in enumerate(tiles):
        si = ti % 3
        pi = ti % 4
        psl = r.PSA[:, pi * 512: pi * 512 + 256]

        if kind in slab_src:
            ctx.wait("sync", dG, t_p0attn)
            ctx.wait("sync", sP, slab_free[si])
            slab_ready[si] = ctx.dma("sync", dS, r.slab[si][:], slab_src[kind][j])

        ctx.wait("tensor", dG, t_x)
        ctx.wait("tensor", sV, slot_free[pi])
        ctx.wait("tensor", sA, slot_free_a[pi])
        if kind in slab_src:
            ctx.wait("tensor", dS, slab_ready[si])
            lhs = lambda kk, si=si: r.slab[si][:, kk, :]
            M = 128
        elif kind == "a":
            lhs = lambda kk: r.wa_sb[:, kk, :]
            M = 8
        else:
            lhs = lambda kk, j=j: r.wb_sb[:, kk, j * 8:(j + 1) * 8]
            M = 8
        mm = None
        for kk in range(16):
            mm = ctx.on("tensor", lambda e, kk=kk, lhs=lhs, psl=psl, M=M: e.matmul(
                psl[0:M, :] if M != 128 else psl, lhs(kk), r.x_bf[:, kk, :],
                start=(kk == 0), stop=(kk == 15)))
        ctx.inc(mm, sP)
        if kind in slab_src:
            slab_free[si] = sP.c
        t_mm = sP.c

        if kind == "a":
            ctx.wait("vector", sP, t_mm)
            i = ctx.on("vector", lambda e, pi=pi: e.tensor_scalar(
                out=r.sp1[:], in0=r.PSA[0:8, pi * 512: pi * 512 + 256], scalar1=r.dtb[:],
                scalar2=None, op0=ALU.add))
            ctx.inc(i, sV)
            slot_free[pi] = sV.c
            ctx.wait("scalar", sV, sV.c)
            i = ctx.on("scalar", lambda e: e.activation(out=r.sp2[:], in_=r.sp1[:],
                                                        func=AF.Abs))
            ctx.inc(i, sA)
            ctx.wait("scalar", sA, sA.c)
            i = ctx.on("scalar", lambda e: e.activation(out=r.sp2[:], in_=r.sp2[:],
                                                        func=AF.Exp, scale=-1.0))
            ctx.inc(i, sA)
            ctx.wait("scalar", sA, sA.c)
            i = ctx.on("scalar", lambda e: e.activation(out=r.sp2[:], in_=r.sp2[:],
                                                        func=AF.Ln, bias=1.0))
            ctx.inc(i, sA)
            ctx.wait("vector", sA, sA.c)
            ctx.on("vector", lambda e: e.tensor_scalar(
                out=r.sp1[:], in0=r.sp1[:], scalar1=0.0, scalar2=None, op0=ALU.max))
            ctx.on("vector", lambda e: e.tensor_add(out=r.g8[:], in0=r.sp1[:], in1=r.sp2[:]))
            i = ctx.on("vector", lambda e: e.tensor_scalar(
                out=r.g8[:], in0=r.g8[:], scalar1=r.nexpA[:], scalar2=None, op0=ALU.mult))
            ctx.inc(i, sV)
        elif kind == "b":
            ctx.wait("scalar", sP, t_mm)
            ctx.wait("scalar", sV, t_entry_sV)
            i = ctx.on("scalar", lambda e, j=j, pi=pi: e.activation(
                out=r.beta_nh[j][:], in_=r.PSA[0:8, pi * 512: pi * 512 + 256], func=AF.Sigmoid))
            ctx.inc(i, sA)
            slot_free_a[pi] = sA.c
        elif kind == "gg":
            ctx.wait("scalar", sP, t_mm)
            ctx.wait("scalar", sV, t_entry_sV)
            i = ctx.on("scalar", lambda e, j=j, psl=psl: e.activation(
                out=r.gate[:, j, :], in_=psl, func=AF.Silu))
            ctx.inc(i, sA)
            slot_free_a[pi] = sA.c
        else:
            pri = ti % 3
            ci = ti % 2
            hb_i = hist_base[kind] + j
            ctx.wait("vector", sA, pre_free[pri])
            ctx.on("vector", lambda e, pri=pri, hb_i=hb_i: e.tensor_copy(
                out=r.pre[pri][:, :, 0:4], in_=r.hist[:, hb_i, :, :]))
            ctx.wait("vector", sP, t_mm)
            pv = psl.rearrange("p (b t) -> p b t", b=2)
            i = ctx.on("vector", lambda e, pri=pri, pv=pv: e.tensor_copy(
                out=r.pre[pri][:, :, 4:132], in_=pv))
            ctx.inc(i, sV)
            slot_free[pi] = sV.c
            ctx.wait("vector", sV, sV.c)
            ctx.on("vector", lambda e, pri=pri, hb_i=hb_i: e.tensor_copy(
                out=r.hist[:, hb_i, :, 1:4], in_=r.pre[pri][:, :, 129:132]))
            cw = conv_w[kind]
            ctx.wait("vector", sA, cvo_free[ci])
            ctx.on("vector", lambda e, pri=pri, ci=ci, cw=cw, j=j: e.tensor_scalar(
                out=r.cvo[ci][:], in0=r.pre[pri][:, :, 1:129], scalar1=cw[:, j, 0:1],
                scalar2=None, op0=ALU.mult))
            for tap in range(1, 4):
                i = ctx.on("vector", lambda e, pri=pri, ci=ci, cw=cw, j=j, tap=tap: e.scalar_tensor_tensor(
                    out=r.cvo[ci][:], in0=r.pre[pri][:, :, 1 + tap:129 + tap],
                    scalar=cw[:, j, tap:tap + 1], in1=r.cvo[ci][:],
                    op0=ALU.mult, op1=ALU.add))
            ctx.inc(i, sV)
            t_cv = sV.c
            ctx.wait("scalar", sV, t_cv)
            if kind == "v":
                nh, h = (0, j) if j < 8 else (1, j - 8)
                vout = _vb_view(r.Vb, nh, h)
                i = ctx.on("scalar", lambda e, ci=ci, vout=vout: e.activation(
                    out=vout, in_=r.cvo[ci][:].rearrange("p b (c i) -> p b c i", c=NCHH),
                    func=AF.Silu))
                ctx.inc(i, sA)
                pre_free[pri] = sA.c
                cvo_free[ci] = sA.c
            else:
                qi = nrm_q % 3
                ctx.wait("scalar", sV, qs_free[qi])
                i = ctx.on("scalar", lambda e, ci=ci, qi=qi: e.activation(
                    out=r.qs[qi][:].rearrange("p (b t) -> p b t", b=2), in_=r.cvo[ci][:],
                    func=AF.Silu))
                ctx.inc(i, sA)
                pre_free[pri] = sA.c
                cvo_free[ci] = sA.c
                sqi = nrm_q % 2
                ctx.wait("scalar", sP, sq_free[sqi])
                i = ctx.on("scalar", lambda e, qi=qi, sqi=sqi: e.activation(
                    out=r.sq[sqi][:], in_=r.qs[qi][:], func=AF.Square))
                ctx.inc(i, sA)
                t_sqr = sA.c
                ro = (nrm_q % 2) * 256
                ctx.wait("tensor", sA, max(t_sqr, ssrow_free[nrm_q % 2]))
                i = ctx.on("tensor", lambda e, sqi=sqi, ro=ro: e.matmul(
                    r.PSB[0:1, ro:ro + 256], r.onesb[:, 0:1], r.sq[sqi][:],
                    start=True, stop=True))
                ctx.inc(i, sP)
                sq_free[sqi] = sP.c
                t_ssm = sP.c
                sri = nrm_q % 3
                ctx.wait("scalar", sP, t_ssm)
                i = ctx.on("scalar", lambda e, ro=ro: e.activation(
                    out=r.lnr[0:1, 512:768], in_=r.PSB[0:1, ro:ro + 256], func=AF.Sqrt))
                ctx.inc(i, sA)
                ssrow_free[nrm_q % 2] = sA.c
                sc2 = SQ128 if kind == "q" else 1.0
                ctx.wait("vector", sA, sA.c)
                ctx.wait("vector", sP, srow_free[sri])
                ctx.on("vector", lambda e, sc2=sc2: e.tensor_scalar(
                    out=r.lnr[0:1, 512:768], in0=r.lnr[0:1, 512:768], scalar1=1e-12,
                    scalar2=sc2, op0=ALU.max, op1=ALU.mult))
                def _recip(e, sri=sri):
                    with r.nc.allow_low_precision(reason="bf16 norm-scale operand for bf16 matmul"):
                        return e.reciprocal(out=r.srow[sri][:], in_=r.lnr[0:1, 512:768])
                i = ctx.on("vector", _recip)
                ctx.inc(i, sV)
                ctx.wait("vector", sV, sV.c)
                i = ctx.on("vector", lambda e: e.engine_nop())
                ctx.inc(i, sV)
                t_srow = sV.c
                bo = 512 + (nrm_q % 2) * 256
                ctx.wait("tensor", sV, max(t_srow, sbc_free[nrm_q % 2]))
                i = ctx.on("tensor", lambda e, sri=sri, bo=bo: e.matmul(
                    r.PSB[:, bo:bo + 256], r.onesb[0:1, :], r.srow[sri][:],
                    start=True, stop=True))
                ctx.inc(i, sP)
                t_bc = sP.c
                srow_free[sri] = sP.c
                nh, h = (0, j) if j < 8 else (1, j - 8)
                dst = _vb_view(r.Qb, 1, j) if kind == "q" else _vb_view(r.Kb, nh, h)
                ctx.wait("vector", sP, t_bc)
                i = ctx.on("vector", lambda e, qi=qi, bo=bo, dst=dst: e.tensor_mul(
                    out=dst,
                    in0=r.qs[qi][:].rearrange("p (b c i) -> p b c i", b=2, c=NCHH),
                    in1=r.PSB[:, bo:bo + 256].rearrange("p (b c i) -> p b c i", b=2, c=NCHH)))
                ctx.inc(i, sV)
                qs_free[qi] = sV.c
                sbc_free[nrm_q % 2] = sV.c
                nrm_q += 1

    emit_cumsum_half(ctx, r, sP, sV, sA)

    consts = {"sel8": r.sel8[:], "maskneg": r.maskneg[:], "identf": r.identf[:],
              "identb": r.identb[:]}
    for c in range(NCHH):
        emit_chunk(ctx, r, c, consts, sP, sV, sA)

    emit_wo_ln(ctx, d, r, sems, pc0)

    if DBG and tb == DBG_TB and hb == DBG_HB:
        for nm, src in [("Kb", r.Kb), ("Vb", r.Vb), ("Qb", r.Qb), ("gate", r.gate),
                        ("beta0", r.beta_nh[0]), ("beta1", r.beta_nh[1]), ("g8", r.g8),
                        ("og", r.ogate), ("ypre", r.ypre), ("ystg", r.ystg)]:
            if nm in dbg_outs:
                ctx.wait("sync", sV, sV.c)
                ctx.wait("sync", sA, sA.c)
                ctx.dma("sync", st, dbg_outs[nm], src[:])



def emit_cumsum_half(ctx, r, sP, sV, sA):
    def cp(dst, src):
        ctx.on("vector", lambda e: e.tensor_copy(out=dst, in_=src))

    cp(r.carry0[:, 0:2], r.carry[:, 0:2])
    src, dst = r.cumA, r.cumB
    cp(src[:, 0:256], r.g8[:, 0:256])
    s = 1
    while s < BW:
        sv = src[:].rearrange("p (b t) -> p b t", b=2)
        dv = dst[:].rearrange("p (b t) -> p b t", b=2)
        cp(dv[:, :, 0:s], sv[:, :, 0:s])
        ctx.on("vector", lambda e, sv=sv, dv=dv, s=s: e.tensor_add(
            out=dv[:, :, s:BW], in0=sv[:, :, s:BW], in1=sv[:, :, 0:BW - s]))
        src, dst = dst, src
        s *= 2
    for b in range(2):
        ctx.on("vector", lambda e, b=b, src=src: e.tensor_scalar(
            out=src[:, b * BW:(b + 1) * BW], in0=src[:, b * BW:(b + 1) * BW],
            scalar1=r.carry[:, b:b + 1], scalar2=None, op0=ALU.add))
    if src is not r.cumA:
        cp(r.cumA[:, 0:256], src[:, 0:256])
    for b in range(2):
        cp(r.carry[:, b:b + 1], r.cumA[:, b * BW + BW - 1:b * BW + BW])
    i = ctx.on("vector", lambda e: e.engine_nop())
    ctx.inc(i, sV)



def emit_chunk(ctx, r, c, consts, sP, sV, sA):
    sel8, maskneg, identf, identb = consts["sel8"], consts["maskneg"], consts["identf"], consts["identb"]
    col0 = c * TOKC
    Kc = r.Kb[:, c]
    Vc = r.Vb[:, c]
    Qc = r.Qb[:, c]

    t_pa = sA.c
    ctx.wait("tensor", sA, t_pa)

    ctx.wait("scalar", sV, sV.c)
    i = ctx.on("scalar", lambda e: e.copy(out=r.S0bf[:], in_=r.S[:]))
    ctx.inc(i, sA)
    t_s0bf = sA.c

    for b in range(2):
        src = r.cumA[:, b * BW + col0: b * BW + col0 + TOKC]
        dst = r.cumg_sub[:, b * 128:(b + 1) * 128].rearrange("p (i two) -> p i two", two=2)
        srcb = src.unsqueeze(2).broadcast_to([8, TOKC, 2])
        ctx.on("vector", lambda e, dst=dst, srcb=srcb: e.tensor_copy(out=dst, in_=srcb))
        if col0 == 0:
            base = r.carry0[:, b:b + 1]
        else:
            base = r.cumA[:, b * BW + col0 - 1: b * BW + col0]
        ctx.on("vector", lambda e, b=b, base=base: e.tensor_scalar(
            out=r.cumg_sub[:, b * 128:(b + 1) * 128], in0=r.cumg_sub[:, b * 128:(b + 1) * 128],
            scalar1=base, scalar2=None, op0=ALU.subtract))
    for b in range(2):
        for p in range(2):
            dst = r.beta_sub[:, b * 128 + p: (b + 1) * 128: 2]
            src = r.beta_nh[p][:, b * BW + col0: b * BW + col0 + TOKC]
            ctx.on("vector", lambda e, dst=dst, src=src: e.tensor_copy(out=dst, in_=src))
    i = ctx.on("vector", lambda e: e.engine_nop())
    ctx.inc(i, sV)
    t_prep = sV.c

    ctx.wait("tensor", sV, t_prep)
    for b in range(2):
        ctx.on("tensor", lambda e, b=b: e.transpose(
            r.PSB[:, b * 8:(b + 1) * 8], r.cumg_sub[:, b * 128:(b + 1) * 128], identf[0:8, 0:8]))
    for b in range(2):
        i = ctx.on("tensor", lambda e, b=b: e.transpose(
            r.PSB[:, 16 + b * 8: 16 + (b + 1) * 8], r.beta_sub[:, b * 128:(b + 1) * 128],
            identf[0:8, 0:8]))
    ctx.inc(i, sP)
    t_tp1 = sP.c

    ctx.wait("vector", sP, t_tp1)
    ctx.on("vector", lambda e: e.tensor_copy(out=r.cumg_T[:], in_=r.PSB[:, 0:16]))
    i = ctx.on("vector", lambda e: e.tensor_copy(out=r.beta_T[:], in_=r.PSB[:, 16:32]))
    ctx.inc(i, sV)
    t_tp1e = sV.c
    ctx.wait("scalar", sV, t_tp1e)
    i = ctx.on("scalar", lambda e: e.activation(out=r.lamc[:], in_=r.cumg_T[:], func=AF.Exp))
    ctx.inc(i, sA)
    t_lamc = sA.c

    for bh in range(16):
        b, h = bh // 8, bh % 8
        i = ctx.on("tensor", lambda e, b=b, h=h, bh=bh: e.matmul(
            r.PSA[:, bh * 128:(bh + 1) * 128], sel8[:, h * 128:(h + 1) * 128],
            r.cumg_sub[:, b * 128:(b + 1) * 128], start=True, stop=True))
    ctx.inc(i, sP)
    t_cgrow = sP.c

    ctx.wait("scalar", sP, t_cgrow)
    i = ctx.on("scalar", lambda e: e.activation(out=r.lam_row[:], in_=r.PSA[:], func=AF.Exp))
    ctx.inc(i, sA)
    mview = maskneg.unsqueeze(1).broadcast_to([128, 8, 128])
    for half in range(2):
        cslice = r.cumg_T[:, half * 8:(half + 1) * 8]
        colrep = cslice.unsqueeze(2).broadcast_to([128, 8, 128])
        pa = r.PSA[:, half * 1024:(half + 1) * 1024].rearrange("p (a b) -> p a b", b=128)
        t3 = r.t32[:].rearrange("p (a b) -> p a b", b=128)
        ctx.wait("vector", sA, sA.c)
        ctx.on("vector", lambda e, pa=pa, colrep=colrep, t3=t3: e.tensor_sub(out=t3, in0=pa, in1=colrep))
        ctx.on("vector", lambda e, t3=t3, mview=mview: e.tensor_add(out=t3, in0=t3, in1=mview))
        i = ctx.on("vector", lambda e: e.engine_nop())
        ctx.inc(i, sV)
        ctx.wait("scalar", sV, sV.c)
        i = ctx.on("scalar", lambda e, half=half: e.activation(
            out=r.E[:, half * 1024:(half + 1) * 1024], in_=r.t32[:], func=AF.Exp))
        ctx.inc(i, sA)
    t_E = sA.c
    ctx.wait("vector", sA, t_E)
    ctx.on("vector", lambda e: e.tensor_copy(out=r.edecf[:], in_=r.E[:, 127::128]))
    ctx.on("vector", lambda e: e.tensor_copy(out=r.lamlast[:], in_=r.lam_row[:, 127::128]))
    t_psa_free_v = sV.c
    t_psa_free_a = sA.c

    ctx.wait("tensor", sV, t_psa_free_v)
    ctx.wait("tensor", sA, t_psa_free_a)
    for bh in range(16):
        i = ctx.on("tensor", lambda e, bh=bh: e.matmul(
            r.PSA[:, bh * 128:(bh + 1) * 128], Kc[:, bh], Kc[:, bh], start=True, stop=True))
    ctx.inc(i, sP)
    t_A = sP.c
    iview = identb.unsqueeze(1).broadcast_to([128, 16, 128])
    Ev = r.E[:].rearrange("p (a b) -> p a b", b=128)
    tbv = r.tmp_bf[:].rearrange("p (a b) -> p a b", b=128)
    ctx.wait("vector", sA, t_E)
    ctx.on("vector", lambda e: e.tensor_sub(out=tbv, in0=Ev, in1=iview))
    ctx.wait("vector", sP, t_A)
    i = ctx.on("vector", lambda e: e.tensor_mul(out=r.Nt[:], in0=r.PSA[:], in1=r.tmp_bf[:]))
    ctx.inc(i, sV)
    t_AE = sV.c
    ctx.wait("tensor", sV, t_AE)
    for bh in range(16):
        i = ctx.on("tensor", lambda e, bh=bh: e.matmul(
            r.PSA[:, bh * 128:(bh + 1) * 128], Kc[:, bh], Qc[:, bh], start=True, stop=True))
    ctx.inc(i, sP)
    t_KQ = sP.c
    ctx.wait("vector", sP, t_KQ)
    i = ctx.on("vector", lambda e: e.tensor_mul(out=r.Wt[:], in0=r.PSA[:], in1=r.E[:]))
    ctx.inc(i, sV)
    t_W = sV.c
    ctx.wait("tensor", sV, t_W)
    for bh in range(16):
        b, h = bh // 8, bh % 8
        i = ctx.on("tensor", lambda e, b=b, h=h, bh=bh: e.matmul(
            r.PSA[:, bh * 128:(bh + 1) * 128], sel8[:, h * 128:(h + 1) * 128],
            r.beta_sub[:, b * 128:(b + 1) * 128], start=True, stop=True))
    ctx.inc(i, sP)
    t_brow = sP.c
    ctx.wait("vector", sP, t_brow)
    i = ctx.on("vector", lambda e: e.scalar_tensor_tensor(
        out=r.Nt[:], in0=r.Nt[:], scalar=2.0, in1=r.PSA[:], op0=ALU.mult, op1=ALU.mult))
    ctx.inc(i, sV)
    t_Nt = sV.c

    ctx.wait("tensor", sV, t_Nt)
    for bh in range(16):
        i = ctx.on("tensor", lambda e, bh=bh: e.transpose(
            r.PSC[:, bh * 128:(bh + 1) * 128], r.Nt[:, bh * 128:(bh + 1) * 128], identb))
    ctx.inc(i, sP)
    t_Ntp = sP.c
    ctx.wait("vector", sP, t_Ntp)
    i = ctx.on("vector", lambda e: e.tensor_copy(out=r.Nn[:], in_=r.PSC[:]))
    ctx.inc(i, sV)
    t_Nn = sV.c

    ctx.wait("tensor", sV, t_Nn)
    for bh in range(16):
        i = ctx.on("tensor", lambda e, bh=bh: e.transpose(
            r.PSC[:, bh * 128:(bh + 1) * 128], Vc[:, bh], identb))
    ctx.inc(i, sP)
    t_Vtp = sP.c
    ctx.wait("vector", sP, t_Vtp)
    i = ctx.on("vector", lambda e: e.tensor_copy(out=r.Vtm[:], in_=r.PSC[:]))
    ctx.inc(i, sV)
    t_Vtm = sV.c

    ctx.wait("tensor", sA, t_s0bf)
    for bh in range(16):
        i = ctx.on("tensor", lambda e, bh=bh: e.matmul(
            r.PSA[:, bh * 128:(bh + 1) * 128], Kc[:, bh],
            r.S0bf[:, bh * 128:(bh + 1) * 128], start=True, stop=True))
    ctx.inc(i, sP)
    t_KS0 = sP.c
    ctx.wait("vector", sP, t_KS0)
    ctx.wait("vector", sA, t_lamc)
    for bh in range(16):
        sl = slice(bh * 128, (bh + 1) * 128)
        ctx.on("vector", lambda e, sl=sl, bh=bh: e.scalar_tensor_tensor(
            out=r.U32[:, sl], in0=r.PSA[:, sl], scalar=r.lamc[:, bh:bh + 1],
            in1=r.Vtm[:, sl], op0=ALU.mult, op1=ALU.subtract))
        ctx.on("vector", lambda e, sl=sl, bh=bh: e.tensor_scalar(
            out=r.U32[:, sl], in0=r.U32[:, sl], scalar1=r.beta_T[:, bh:bh + 1],
            scalar2=-2.0, op0=ALU.mult, op1=ALU.mult))
    i = ctx.on("vector", lambda e: e.engine_nop())
    ctx.inc(i, sV)
    t_R = sV.c
    ctx.wait("scalar", sV, t_R)
    i = ctx.on("scalar", lambda e: e.copy(out=r.Ubf[:], in_=r.U32[:]))
    ctx.inc(i, sA)
    t_ubf_prev = sA.c

    Acur, Bcur = r.Nt, r.Nn
    Anew, Bnew = r.Nt2, r.Nn2
    p = 1
    t_psa_free = t_R
    while p <= 64:
        ctx.wait("tensor", sA, t_ubf_prev)
        ctx.wait("tensor", sV, t_psa_free)
        for bh in range(16):
            sl = slice(bh * 128, (bh + 1) * 128)
            i = ctx.on("tensor", lambda e, sl=sl, Acur=Acur: e.matmul(
                r.PSA[:, sl], Acur[:, sl], r.Ubf[:, sl], start=True, stop=True))
        ctx.inc(i, sP)
        t_ap = sP.c
        ctx.wait("vector", sP, t_ap)
        op = ALU.subtract if p == 1 else ALU.add
        i = ctx.on("vector", lambda e, op=op: e.tensor_tensor(
            out=r.U32[:], in0=r.U32[:], in1=r.PSA[:], op=op))
        ctx.inc(i, sV)
        t_psa_free = sV.c
        ctx.wait("scalar", sV, t_psa_free)
        i = ctx.on("scalar", lambda e: e.copy(out=r.Ubf[:], in_=r.U32[:]))
        ctx.inc(i, sA)
        t_ubf_prev = sA.c
        if p < 64:
            ctx.wait("tensor", sV, t_psa_free)
            for bh in range(16):
                sl = slice(bh * 128, (bh + 1) * 128)
                i = ctx.on("tensor", lambda e, sl=sl, Acur=Acur, Bcur=Bcur: e.matmul(
                    r.PSA[:, sl], Bcur[:, sl], Acur[:, sl], start=True, stop=True))
            ctx.inc(i, sP)
            t_d1 = sP.c
            ctx.wait("vector", sP, t_d1)
            i = ctx.on("vector", lambda e, Anew=Anew: e.tensor_copy(out=Anew[:], in_=r.PSA[:]))
            ctx.inc(i, sV)
            t_Anew = sV.c
            ctx.wait("tensor", sV, t_Anew)
            for bh in range(16):
                sl = slice(bh * 128, (bh + 1) * 128)
                i = ctx.on("tensor", lambda e, sl=sl, Acur=Acur, Bcur=Bcur: e.matmul(
                    r.PSA[:, sl], Acur[:, sl], Bcur[:, sl], start=True, stop=True))
            ctx.inc(i, sP)
            t_d2 = sP.c
            ctx.wait("vector", sP, t_d2)
            i = ctx.on("vector", lambda e, Bnew=Bnew: e.tensor_copy(out=Bnew[:], in_=r.PSA[:]))
            ctx.inc(i, sV)
            t_Bnew = sV.c
            Acur, Anew = Anew, Acur
            Bcur, Bnew = Bnew, Bcur
            t_psa_free = t_Bnew
        p *= 2

    i = ctx.on("vector", lambda e: e.tensor_mul(
        out=r.tmp_bf[:].rearrange("p (a b) -> p a b", b=128), in0=Qc,
        in1=r.lam_row[:].rearrange("p (a b) -> p a b", b=128)))
    ctx.inc(i, sV)
    t_qlam = sV.c
    ctx.wait("tensor", sV, t_qlam)
    ctx.wait("tensor", sA, t_ubf_prev)
    for bh in range(16):
        sl = slice(bh * 128, (bh + 1) * 128)
        ctx.on("tensor", lambda e, sl=sl: e.matmul(
            r.PSA[:, sl], r.tmp_bf[:, sl], r.S0bf[:, sl], start=True, stop=False))
        i = ctx.on("tensor", lambda e, sl=sl: e.matmul(
            r.PSA[:, sl], r.Wt[:, sl], r.Ubf[:, sl], start=False, stop=True))
    ctx.inc(i, sP)
    t_O = sP.c

    ctx.wait("scalar", sP, t_O)
    i = ctx.on("scalar", lambda e: e.activation(out=r.sqbf[:], in_=r.PSA[:], func=AF.Square))
    ctx.inc(i, sA)
    t_sq = sA.c
    ctx.wait("vector", sA, t_sq)
    i = ctx.on("vector", lambda e: e.tensor_reduce(
        out=r.ssb[:], in_=r.sqbf[:].rearrange("p (a b) -> p a b", b=128), axis=AX.X, op=ALU.add))
    ctx.inc(i, sV)
    ctx.wait("scalar", sV, sV.c)
    i = ctx.on("scalar", lambda e: e.activation(out=r.rstd[:], in_=r.ssb[:], func=AF.Sqrt,
                                                scale=1.0 / 128.0, bias=1e-5))
    ctx.inc(i, sA)
    ctx.wait("vector", sA, sA.c)
    i = ctx.on("vector", lambda e: e.reciprocal(out=r.rstd[:], in_=r.rstd[:]))
    ctx.inc(i, sV)
    ctx.wait("vector", sV, sV.c)
    rsv = r.rstd[:].unsqueeze(2).broadcast_to([128, 16, 128])
    i = ctx.on("vector", lambda e: e.tensor_mul(
        out=r.on[:].rearrange("p (a b) -> p a b", b=128),
        in0=r.PSA[:].rearrange("p (a b) -> p a b", b=128), in1=rsv))
    ctx.inc(i, sV)
    t_on = sV.c
    ctx.wait("tensor", sV, t_on)
    for bh in range(16):
        i = ctx.on("tensor", lambda e, bh=bh: e.transpose(
            r.PSC[:, bh * 128:(bh + 1) * 128], r.on[:, bh * 128:(bh + 1) * 128], identb))
    ctx.inc(i, sP)
    t_otp = sP.c
    ctx.wait("vector", sP, t_otp)
    for bh in range(16):
        b, h = bh // 8, bh % 8
        dst = r.ogate[:, h, b * BW + col0: b * BW + col0 + TOKC]
        src = r.PSC[:, bh * 128 + 1:(bh + 1) * 128:2]
        gsl = r.gate[:, h, b * BW + col0: b * BW + col0 + TOKC]
        i = ctx.on("vector", lambda e, dst=dst, src=src, gsl=gsl: e.scalar_tensor_tensor(
            out=dst, in0=src, scalar=r.onw[:], in1=gsl, op0=ALU.mult, op1=ALU.mult))
    ctx.inc(i, sV)
    t_og = sV.c

    ctx.wait("tensor", sV, t_og)
    for bh in range(16):
        i = ctx.on("tensor", lambda e, bh=bh: e.transpose(
            r.PSC[:, bh * 128:(bh + 1) * 128], Kc[:, bh], identb))
    ctx.inc(i, sP)
    t_ktp = sP.c
    ctx.wait("vector", sP, t_ktp)
    for bh in range(16):
        sl = slice(bh * 128, (bh + 1) * 128)
        ctx.on("vector", lambda e, sl=sl, bh=bh: e.tensor_scalar(
            out=r.KDtm[:, sl], in0=r.PSC[:, sl],
            scalar1=r.edecf[:, bh:bh + 1], scalar2=None, op0=ALU.mult))
        ctx.on("vector", lambda e, sl=sl, bh=bh: e.tensor_scalar(
            out=r.S[:, sl], in0=r.S[:, sl],
            scalar1=r.lamlast[:, bh:bh + 1], scalar2=None, op0=ALU.mult))
    i = ctx.on("vector", lambda e: e.engine_nop())
    ctx.inc(i, sV)
    t_kdtm = sV.c
    ctx.wait("tensor", sV, t_kdtm)
    for bh in range(16):
        sl = slice(bh * 128, (bh + 1) * 128)
        i = ctx.on("tensor", lambda e, sl=sl: e.matmul(
            r.PSA[:, sl], r.KDtm[:, sl], r.Ubf[:, sl], start=True, stop=True))
    ctx.inc(i, sP)
    t_smm = sP.c
    ctx.wait("vector", sP, t_smm)
    i = ctx.on("vector", lambda e: e.tensor_add(out=r.S[:], in0=r.S[:], in1=r.PSA[:]))
    ctx.inc(i, sV)



def emit_wo_ln(ctx, d, r, sems, pc0):
    dS, dG, sP, sV, sA, st = (sems[n] for n in ["dS", "dG", "sP", "sV", "sA", "st"])
    t_og_all = sV.c
    slab_free = [sP.c] * 3
    slot_free = [sV.c] * 4
    xf_cons = [sV.c] * 2

    for ot in range(16):
        si = ot % 2
        pi = ot % 4
        sli = ot % 3
        psl = r.PSA[:, pi * 512: pi * 512 + 256]
        ctx.wait("sync", sV, xf_cons[si])
        t_xf = ctx.dma("sync", dS, r.xfr[si][:], d.x_fm[ot * 128:(ot + 1) * 128, pc0:pc0 + 256])
        ctx.wait("sync", sP, slab_free[sli])
        t_slab = ctx.dma("sync", dS, r.slab[sli][:, 0:8, :], d.wbf_o[ot])
        ctx.wait("tensor", dS, t_slab)
        ctx.wait("tensor", sV, max(slot_free[pi], t_og_all))
        mm = None
        for kk in range(8):
            mm = ctx.on("tensor", lambda e, kk=kk, psl=psl, sli=sli: e.matmul(
                psl, r.slab[sli][:, kk, :], r.ogate[:, kk, :],
                start=(kk == 0), stop=(kk == 7)))
        ctx.inc(mm, sP)
        slab_free[sli] = sP.c
        t_mm = sP.c
        ctx.wait("vector", sP, t_mm)
        ctx.wait("vector", dS, t_xf)
        i = ctx.on("vector", lambda e, psl=psl, si=si, ot=ot: e.tensor_add(
            out=r.ypre[:, ot, :], in0=psl, in1=r.xfr[si][:]))
        ctx.inc(i, sV)
        slot_free[pi] = sV.c
        xf_cons[si] = sV.c
    t_ypre = sV.c

    ctx.wait("tensor", sV, t_ypre)
    mm = None
    for ot in range(16):
        mm = ctx.on("tensor", lambda e, ot=ot: e.matmul(
            r.PSB[0:1, 0:256], r.onesb[:, 0:1], r.ypre[:, ot, :],
            start=(ot == 0), stop=(ot == 15)))
    ctx.inc(mm, sP)
    ctx.wait("scalar", sV, t_ypre)
    sq_mm = [sP.c] * 2
    t_sqr_q = []
    for ot in range(17):
        if ot < 16:
            sqi = ot % 2
            ctx.wait("scalar", sP, sq_mm[sqi])
            i = ctx.on("scalar", lambda e, ot=ot, sqi=sqi: e.activation(
                out=r.sq[sqi][:], in_=r.ypre[:, ot, :], func=AF.Square))
            ctx.inc(i, sA)
            t_sqr_q.append(sA.c)
        if ot >= 1:
            po = ot - 1
            sqi = po % 2
            ctx.wait("tensor", sA, t_sqr_q[po])
            i = ctx.on("tensor", lambda e, sqi=sqi, po=po: e.matmul(
                r.PSB[0:1, 256:512], r.onesb[:, 0:1], r.sq[sqi][:],
                start=(po == 0), stop=(po == 15)))
            ctx.inc(i, sP)
            sq_mm[sqi] = sP.c
    t_sumsq = sP.c
    ctx.wait("vector", sP, t_sumsq)
    ctx.on("vector", lambda e: e.tensor_scalar(
        out=r.lnr[0:1, 0:256], in0=r.PSB[0:1, 0:256], scalar1=-1.0 / 2048.0,
        scalar2=None, op0=ALU.mult))
    ctx.on("vector", lambda e: e.tensor_mul(
        out=r.lnr[0:1, 512:768], in0=r.lnr[0:1, 0:256], in1=r.lnr[0:1, 0:256]))
    i = ctx.on("vector", lambda e: e.scalar_tensor_tensor(
        out=r.lnr[0:1, 256:512], in0=r.PSB[0:1, 256:512], scalar=1.0 / 2048.0,
        in1=r.lnr[0:1, 512:768], op0=ALU.mult, op1=ALU.subtract))
    ctx.inc(i, sV)
    t_var = sV.c
    ctx.wait("scalar", sV, t_var)
    i = ctx.on("scalar", lambda e: e.activation(out=r.lnr[0:1, 256:512],
                                                in_=r.lnr[0:1, 256:512], func=AF.Sqrt, bias=1e-5))
    ctx.inc(i, sA)
    ctx.wait("vector", sA, sA.c)
    i = ctx.on("vector", lambda e: e.reciprocal(out=r.lnr[0:1, 256:512],
                                                in_=r.lnr[0:1, 256:512]))
    ctx.inc(i, sV)
    ctx.wait("vector", sV, sV.c)
    i = ctx.on("vector", lambda e: e.engine_nop())
    ctx.inc(i, sV)
    t_rs = sV.c
    ctx.wait("tensor", sV, t_rs)
    ctx.on("tensor", lambda e: e.matmul(
        r.PSB[:, 512:768], r.onesf[0:1, :], r.lnr[0:1, 0:256], start=True, stop=True))
    i = ctx.on("tensor", lambda e: e.matmul(
        r.PSB[:, 768:1024], r.onesf[0:1, :], r.lnr[0:1, 256:512], start=True, stop=True))
    ctx.inc(i, sP)
    t_bc = sP.c
    ctx.wait("vector", sP, t_bc)
    ctx.wait("vector", st, r.spill_st)
    for ot in range(16):
        ctx.on("vector", lambda e, ot=ot: e.tensor_add(
            out=r.lnt[:], in0=r.ypre[:, ot, :], in1=r.PSB[:, 512:768]))
        i = ctx.on("vector", lambda e, ot=ot: e.tensor_mul(
            out=r.ystg[:, ot, :], in0=r.lnt[:], in1=r.PSB[:, 768:1024]))
        ctx.inc(i, sV)
    t_y = sV.c
    ctx.wait("sync", sV, t_y)
    ctx.dma("sync", st, d.ybf.rearrange("(k p) n -> p k n", p=128)[:, :, pc0:pc0 + 256],
            r.ystg[:])
    r.spill_st = sems["st"].c



def emit_ffn_iter(ctx, d, r, sems, it, t_p0ffn, dbg_outs):
    dS, dG, sP, sV, sA, st = (sems[n] for n in ["dS", "dG", "sP", "sV", "sA", "st"])
    pc0 = it * FIT
    t_spills = st.c

    par = it % 2
    ybf_sb = r.ybf_sb2[par]
    if it == 0:
        ctx.wait("sync", st, t_spills)
        r.y_pref[0] = ctx.dma("sync", dS, ybf_sb[:],
                              d.ybf.rearrange("(k p) n -> p k n", p=128)[:, :, 0:FIT])
    t_y = r.y_pref[par]
    if it + 1 < NFI:
        ctx.wait("sync", st, t_spills)
        ctx.wait("sync", sP, r.y_used[1 - par])
        r.y_pref[1 - par] = ctx.dma(
            "sync", dS, r.ybf_sb2[1 - par][:],
            d.ybf.rearrange("(k p) n -> p k n", p=128)[:, :, (it + 1) * FIT:(it + 2) * FIT])
    ctx.wait("vector", dS, t_y)
    ctx.wait("vector", sP, sP.c)
    ctx.wait("vector", sA, sA.c)
    i = None
    for kk in range(16):
        i = ctx.on("vector", lambda e, kk=kk: e.tensor_copy(
            out=r.z[:, kk, :], in_=ybf_sb[:, kk, :]))
    ctx.inc(i, sV)
    t_z0 = sV.c

    ctx.wait("tensor", dS, t_y)
    ctx.wait("tensor", sV, t_z0)
    ctx.wait("tensor", sA, sA.c)
    mm = None
    for kk in range(16):
        mm = ctx.on("tensor", lambda e, kk=kk: e.matmul(
            r.PSB[0:16, 0:512], r.wafn_sb[:, kk, :], ybf_sb[:, kk, :],
            start=(kk == 0), stop=(kk == 15)))
    ctx.inc(mm, sP)
    t_aff = sP.c
    ctx.wait("scalar", sP, t_aff)
    i = ctx.on("scalar", lambda e: e.activation(out=r.aff[:], in_=r.PSB[0:16, 0:512],
                                                func=AF.Sigmoid))
    ctx.inc(i, sA)
    t_affs = sA.c
    ctx.wait("tensor", sA, t_affs)
    for q4 in range(4):
        i = ctx.on("tensor", lambda e, q4=q4: e.transpose(
            r.PSB[:, 512 + q4 * 16: 512 + (q4 + 1) * 16],
            r.aff[:, q4 * 128:(q4 + 1) * 128], r.identf[:][0:16, 0:16]))
    ctx.inc(i, sP)
    t_at = sP.c
    ctx.wait("vector", sP, t_at)
    i = ctx.on("vector", lambda e: e.tensor_copy(
        out=r.affT[:].rearrange("p a b -> p (a b)"), in_=r.PSB[:, 512:576]))
    ctx.inc(i, sV)
    ctx.wait("vector", sV, sV.c)
    av = r.affT[:]
    i = ctx.on("vector", lambda e: e.tensor_reduce(out=r.m1[:], in_=av, axis=AX.X, op=ALU.max))
    ctx.inc(i, sV)
    ctx.wait("vector", sV, sV.c)
    for q4 in range(4):
        i = ctx.on("vector", lambda e, q4=q4: e.tensor_scalar(
            out=r.eq[:, q4, :], in0=av[:, q4, :], scalar1=r.m1[:, q4:q4 + 1],
            scalar2=None, op0=ALU.is_ge))
    ctx.inc(i, sV)
    ctx.wait("vector", sV, sV.c)
    ctx.on("vector", lambda e: e.scalar_tensor_tensor(
        out=r.combT[:].rearrange("p a b -> p (a b)"),
        in0=r.eq[:].rearrange("p a b -> p (a b)"), scalar=-1e30,
        in1=av.rearrange("p a b -> p (a b)"), op0=ALU.mult, op1=ALU.add))
    i = ctx.on("vector", lambda e: e.engine_nop())
    ctx.inc(i, sV)
    ctx.wait("vector", sV, sV.c)
    i = ctx.on("vector", lambda e: e.tensor_reduce(
        out=r.m2[:], in_=r.combT[:], axis=AX.X, op=ALU.max))
    ctx.inc(i, sV)
    ctx.wait("vector", sV, sV.c)
    i = ctx.on("vector", lambda e: e.tensor_add(out=r.den[:], in0=r.m1[:], in1=r.m2[:]))
    ctx.inc(i, sV)
    ctx.wait("scalar", sV, sV.c)
    i = ctx.on("scalar", lambda e: e.activation(out=r.den[:], in_=r.den[:], func=AF.Ln))
    ctx.inc(i, sA)
    ctx.wait("scalar", sA, sA.c)
    i = ctx.on("scalar", lambda e: e.activation(out=r.den[:], in_=r.den[:], func=AF.Exp,
                                                scale=-1.0))
    ctx.inc(i, sA)
    ctx.wait("vector", sA, sA.c)
    for q4 in range(4):
        i = ctx.on("vector", lambda e, q4=q4: e.tensor_scalar(
            out=r.eq[:, q4, :], in0=av[:, q4, :], scalar1=r.m2[:, q4:q4 + 1],
            scalar2=None, op0=ALU.is_ge))
    ctx.inc(i, sV)
    ctx.wait("vector", sV, sV.c)
    for q4 in range(4):
        i = ctx.on("vector", lambda e, q4=q4: e.tensor_mul(
            out=r.combT[:, q4, :], in0=av[:, q4, :], in1=r.eq[:, q4, :]))
    ctx.inc(i, sV)
    ctx.wait("vector", sV, sV.c)
    for q4 in range(4):
        i = ctx.on("vector", lambda e, q4=q4: e.tensor_scalar(
            out=r.combT[:, q4, :], in0=r.combT[:, q4, :], scalar1=r.den[:, q4:q4 + 1],
            scalar2=None, op0=ALU.mult))
    ctx.inc(i, sV)
    t_combT = sV.c
    ctx.wait("tensor", sV, t_combT)
    for q4 in range(4):
        i = ctx.on("tensor", lambda e, q4=q4: e.transpose(
            r.PSB[0:16, 512 + q4 * 128: 512 + (q4 + 1) * 128],
            r.combT[:, q4, :], r.identf[:]))
    ctx.inc(i, sP)
    t_cbt = sP.c
    ctx.wait("vector", sP, t_cbt)
    i = ctx.on("vector", lambda e: e.tensor_copy(out=r.comb[:], in_=r.PSB[0:16, 512:1024]))
    ctx.inc(i, sV)
    t_comb = sV.c

    if DBG and it == 0 and "comb" in dbg_outs:
        ctx.wait("sync", sV, t_comb)
        ctx.wait("sync", sA, sA.c)
        ctx.dma("sync", st, dbg_outs["comb"], r.comb[:])
        ctx.dma("sync", st, dbg_outs["affdbg"], r.aff[:])
        ctx.dma("sync", st, dbg_outs["affT"], r.affT[:])
        ctx.dma("sync", st, dbg_outs["m1"], r.m1[:])
        ctx.dma("sync", st, dbg_outs["m2"], r.m2[:])
        ctx.dma("sync", st, dbg_outs["den"], r.den[:])
        ctx.dma("sync", st, dbg_outs["combT"], r.combT[:])
        t_tap = sems["st"].c
        ctx.wait("vector", st, t_tap)
        ctx.wait("tensor", st, t_tap)

    wsrc = lambda ee: (d.rg_bf[ee], d.rw1_bf[ee], d.rw2_bf[ee]) if ee < E_RT else \
                      (d.sg_bf[ee - E_RT], d.sw1_bf[ee - E_RT], d.sw2_bf[ee - E_RT])
    bias_g = lambda ee, ft: r.rgb[:, ee, ft:ft + 1] if ee < E_RT else r.sgb[:, ee - E_RT, ft:ft + 1]
    bias_1 = lambda ee, ft: r.rb1[:, ee, ft:ft + 1] if ee < E_RT else r.sb1[:, ee - E_RT, ft:ft + 1]

    wa_free = [sP.c] * 3
    wb_free = [sP.c] * 3
    w2_free = [sP.c] * 3
    seqA = 0
    seqW2 = 0
    t_g1free = sV.c
    pb_v = [sV.c] * 2
    pb_a = [sA.c] * 2
    z_free = [sV.c] * 4
    t_combbc_cons = sV.c
    u_free = sP.c

    for grp in range(6):
        experts = [grp * 3, grp * 3 + 1, grp * 3 + 2]
        for gi, ee in enumerate(experts):
            srcs = wsrc(ee)
            if ee < E_RT:
                ctx.wait("tensor", sV, max(t_comb, t_combbc_cons))
                i = ctx.on("tensor", lambda e, ee=ee: e.matmul(
                    r.PSB[:, 0:512], r.sel16[:, ee * 128:(ee + 1) * 128], r.comb[:],
                    start=True, stop=True))
                ctx.inc(i, sP)
                t_cbc = sP.c
            for fp in range(4):
                sa_i = seqA % 3
                seqA += 1
                ctx.wait("sync", dG, t_p0ffn)
                ctx.wait("sync", sP, wa_free[sa_i])
                t_wa = ctx.dma("sync", dS, r.wA[sa_i][:], srcs[0][fp])
                ctx.wait("sync", sP, wb_free[sa_i])
                t_wb = ctx.dma("sync", dS, r.wB[sa_i][:], srcs[1][fp])
                for fc2 in range(2):
                    fc = fp * 2 + fc2
                    psl_g = r.PSA[:, fc2 * 1024: fc2 * 1024 + 512]
                    psl_1 = r.PSA[:, fc2 * 1024 + 512: fc2 * 1024 + 1024]
                    ctx.wait("tensor", dS, max(t_wa, t_wb))
                    ctx.wait("tensor", sV, pb_v[fc2])
                    ctx.wait("tensor", sA, pb_a[fc2])
                    mm = None
                    for kk in range(16):
                        mm = ctx.on("tensor", lambda e, kk=kk, sa_i=sa_i, fc2=fc2, psl_g=psl_g: e.matmul(
                            psl_g, r.wA[sa_i][:, kk, fc2 * 128:(fc2 + 1) * 128],
                            ybf_sb[:, kk, :], start=(kk == 0), stop=(kk == 15)))
                    for kk in range(16):
                        mm = ctx.on("tensor", lambda e, kk=kk, sa_i=sa_i, fc2=fc2, psl_1=psl_1: e.matmul(
                            psl_1, r.wB[sa_i][:, kk, fc2 * 128:(fc2 + 1) * 128],
                            ybf_sb[:, kk, :], start=(kk == 0), stop=(kk == 15)))
                    ctx.inc(mm, sP)
                    t_h = sP.c
                    ctx.wait("scalar", sP, t_h)
                    ctx.wait("scalar", sV, t_g1free)
                    i = ctx.on("scalar", lambda e, psl_g=psl_g, ee=ee, fc=fc: e.activation(
                        out=r.g1[:], in_=psl_g, func=AF.Gelu, bias=bias_g(ee, fc)))
                    ctx.inc(i, sA)
                    pb_a[fc2] = sA.c
                    ctx.wait("vector", sA, pb_a[fc2])
                    ctx.wait("vector", sP, u_free)
                    if ee < E_RT:
                        ctx.wait("vector", sP, t_cbc)
                        i = ctx.on("vector", lambda e, psl_1=psl_1, ee=ee, fc=fc: e.scalar_tensor_tensor(
                            out=r.u0[:], in0=psl_1, scalar=bias_1(ee, fc), in1=r.g1[:],
                            op0=ALU.add, op1=ALU.mult))
                        ctx.inc(i, sV)
                        t_g1free = sV.c
                        i = ctx.on("vector", lambda e, gi=gi, fc=fc: e.tensor_mul(
                            out=r.u[gi][:, fc, :], in0=r.u0[:], in1=r.PSB[:, 0:512]))
                        ctx.inc(i, sV)
                        t_combbc_cons = sV.c
                    else:
                        i = ctx.on("vector", lambda e, psl_1=psl_1, ee=ee, fc=fc, gi=gi: e.scalar_tensor_tensor(
                            out=r.u[gi][:, fc, :], in0=psl_1, scalar=bias_1(ee, fc), in1=r.g1[:],
                            op0=ALU.add, op1=ALU.mult))
                        ctx.inc(i, sV)
                        t_g1free = sV.c
                    pb_v[fc2] = sV.c
                wa_free[sa_i] = sP.c
                wb_free[sa_i] = sP.c
        t_u_grp = sV.c
        t_u_grp_a = sA.c

        for ps4 in range(4):
            rdy = []
            for gi, ee in enumerate(experts):
                w2i = seqW2 % 3
                seqW2 += 1
                ctx.wait("sync", sP, w2_free[w2i])
                t_rdy = ctx.dma("sync", dS, r.w2s[w2i][:], wsrc(ee)[2][ps4])
                rdy.append((w2i, t_rdy))
            for dc4 in range(4):
                dc = ps4 * 4 + dc4
                psl = r.PSA[:, dc4 * 512:(dc4 + 1) * 512]
                ctx.wait("tensor", sV, max(t_u_grp, z_free[dc4]))
                ctx.wait("tensor", sA, t_u_grp_a)
                ntot = 24 + (2 if grp == 0 else 0)
                nmm = 0
                mm = None
                for gi, ee in enumerate(experts):
                    w2i, t_rdy = rdy[gi]
                    ctx.wait("tensor", dS, t_rdy)
                    for fc in range(8):
                        mm = ctx.on("tensor", lambda e, w2i=w2i, fc=fc, dc4=dc4, psl=psl,
                                    gi=gi, nmm=nmm, ntot=ntot: e.matmul(
                            psl, r.w2s[w2i][:, fc, dc4 * 128:(dc4 + 1) * 128],
                            r.u[gi][:, fc, :], start=(nmm == 0), stop=(nmm == ntot - 1)))
                        nmm += 1
                if grp == 0:
                    mm = ctx.on("tensor", lambda e, dc=dc, psl=psl: e.matmul(
                        psl, r.b2r_sb[:, dc * 128:(dc + 1) * 128], r.comb[:],
                        start=False, stop=False, skip_group_check=True))
                    mm = ctx.on("tensor", lambda e, dc=dc, psl=psl: e.matmul(
                        psl, r.b2s_sb[:, dc * 128:(dc + 1) * 128], r.ones2[:],
                        start=False, stop=True, skip_group_check=True))
                ctx.inc(mm, sP)
                t_w2 = sP.c
                ctx.wait("vector", sP, t_w2)
                ctx.wait("vector", sV, t_z0)
                if grp == 0:
                    i = ctx.on("vector", lambda e, dc=dc, psl=psl: e.scalar_tensor_tensor(
                        out=r.z[:, dc, :], in0=r.z[:, dc, :], scalar=2.0, in1=psl,
                        op0=ALU.mult, op1=ALU.add))
                else:
                    i = ctx.on("vector", lambda e, dc=dc, psl=psl: e.tensor_add(
                        out=r.z[:, dc, :], in0=r.z[:, dc, :], in1=psl))
                ctx.inc(i, sV)
                z_free[dc4] = sV.c
            for gi in range(3):
                w2_free[rdy[gi][0]] = sP.c
        pb_v = [sV.c] * 2
        u_free = sP.c
    r.y_used[par] = sP.c

    t_zdone = sV.c
    if DBG and it == 0 and "zdbg" in dbg_outs:
        ctx.wait("sync", sV, t_zdone)
        ctx.dma("sync", st, dbg_outs["zdbg"], r.z[:])
    ctx.wait("tensor", sV, t_zdone)
    mm = None
    for ot in range(16):
        mm = ctx.on("tensor", lambda e, ot=ot: e.matmul(
            r.PSB[0:1, 0:512], r.onesf[:, 0:1], r.z[:, ot, :],
            start=(ot == 0), stop=(ot == 15)))
    ctx.inc(mm, sP)
    ctx.wait("scalar", sV, t_zdone)
    sq_mm = [sP.c] * 2
    scr = [r.g1, r.u0]
    for ot in range(16):
        sqi = ot % 2
        ctx.wait("scalar", sP, sq_mm[sqi])
        i = ctx.on("scalar", lambda e, ot=ot, sqi=sqi: e.activation(
            out=scr[sqi][:], in_=r.z[:, ot, :], func=AF.Square))
        ctx.inc(i, sA)
        t_sqr = sA.c
        ctx.wait("tensor", sA, t_sqr)
        i = ctx.on("tensor", lambda e, sqi=sqi, ot=ot: e.matmul(
            r.PSB[0:1, 512:1024], r.onesf[:, 0:1], scr[sqi][:],
            start=(ot == 0), stop=(ot == 15)))
        ctx.inc(i, sP)
        sq_mm[sqi] = sP.c
    t_sumsq = sP.c
    ctx.wait("vector", sP, t_sumsq)
    ctx.on("vector", lambda e: e.tensor_scalar(
        out=r.flnr[0:1, 0:512], in0=r.PSB[0:1, 0:512], scalar1=-1.0 / 2048.0,
        scalar2=None, op0=ALU.mult))
    ctx.on("vector", lambda e: e.tensor_mul(
        out=r.flnr[0:1, 1024:1536], in0=r.flnr[0:1, 0:512], in1=r.flnr[0:1, 0:512]))
    i = ctx.on("vector", lambda e: e.scalar_tensor_tensor(
        out=r.flnr[0:1, 512:1024], in0=r.PSB[0:1, 512:1024], scalar=1.0 / 2048.0,
        in1=r.flnr[0:1, 1024:1536], op0=ALU.mult, op1=ALU.subtract))
    ctx.inc(i, sV)
    t_var = sV.c
    ctx.wait("scalar", sV, t_var)
    i = ctx.on("scalar", lambda e: e.activation(out=r.flnr[0:1, 512:1024],
                                                in_=r.flnr[0:1, 512:1024], func=AF.Sqrt, bias=1e-5))
    ctx.inc(i, sA)
    ctx.wait("vector", sA, sA.c)
    i = ctx.on("vector", lambda e: e.reciprocal(out=r.flnr[0:1, 512:1024],
                                                in_=r.flnr[0:1, 512:1024]))
    ctx.inc(i, sV)
    ctx.wait("vector", sV, sV.c)
    i = ctx.on("vector", lambda e: e.engine_nop())
    ctx.inc(i, sV)
    t_rs = sV.c
    ctx.wait("tensor", sV, t_rs)
    ctx.on("tensor", lambda e: e.matmul(
        r.PSA[:, 0:512], r.onesf[0:1, :], r.flnr[0:1, 0:512], start=True, stop=True))
    i = ctx.on("tensor", lambda e: e.matmul(
        r.PSA[:, 512:1024], r.onesf[0:1, :], r.flnr[0:1, 512:1024], start=True, stop=True))
    ctx.inc(i, sP)
    t_bc = sP.c
    ctx.wait("vector", sP, t_bc)
    ost_busy = [sems["st"].c] * 2
    for ot in range(16):
        oi = ot % 2
        ctx.wait("vector", st, ost_busy[oi])
        ctx.on("vector", lambda e, ot=ot, oi=oi: e.tensor_add(
            out=r.ostg[oi][:], in0=r.z[:, ot, :], in1=r.PSA[:, 0:512]))
        i = ctx.on("vector", lambda e, ot=ot, oi=oi: e.tensor_mul(
            out=r.ostg[oi][:], in0=r.ostg[oi][:], in1=r.PSA[:, 512:1024]))
        ctx.inc(i, sV)
        t_o = sV.c
        ctx.wait("sync", sV, t_o)
        ost_busy[oi] = ctx.dma("sync", st, d.out[ot * 128:(ot + 1) * 128, pc0:pc0 + FIT],
                               r.ostg[oi][:])



def _stage_weights(inp):
    g = {}
    T = lambda a: np.ascontiguousarray(a)
    Wq = inp["L0_attn_Wq"]; Wk = inp["L0_attn_Wk"]; Wv = inp["L0_attn_Wv"]
    Wb = inp["L0_attn_Wb"]; Wa = inp["L0_attn_Wa"]; Wg = inp["L0_attn_Wg"]
    Wo = inp["L0_attn_Wo"]
    g["wq_l"] = T(Wq.T.reshape(16, 128, 8, 128).transpose(2, 1, 0, 3))
    g["wk_l"] = T(Wk.T.reshape(16, 128, 16, 128).transpose(2, 1, 0, 3))
    g["wv_l"] = T(Wv.T.reshape(16, 128, 16, 128).transpose(2, 1, 0, 3))
    g["wg_l"] = T(Wg.T.reshape(16, 128, 8, 128).transpose(2, 1, 0, 3))
    g["wo_l"] = T(Wo.T.reshape(8, 128, 16, 128).transpose(2, 1, 0, 3))
    g["wb_l"] = T(Wb.T.reshape(16, 128, 16).transpose(1, 0, 2))
    g["wa_l"] = T(Wa.T.reshape(16, 128, 8).transpose(1, 0, 2))
    g["wafn_l"] = T(inp["L0_ffn_wa"].reshape(16, 128, 16).transpose(1, 0, 2))
    g["cq_l"] = T(inp["L0_attn_conv_q"].reshape(8, 128, 4).transpose(1, 0, 2))
    g["ck_l"] = T(inp["L0_attn_conv_k"].reshape(16, 128, 4).transpose(1, 0, 2))
    g["cv_l"] = T(inp["L0_attn_conv_v"].reshape(16, 128, 4).transpose(1, 0, 2))
    g["alog"] = T(inp["L0_attn_A_log"].reshape(8, 1))
    g["dtb"] = T(inp["L0_attn_dt_bias"].reshape(8, 1))
    g["onw"] = T(inp["L0_attn_o_norm_w"].reshape(128, 1))
    g["rg_l"] = T(inp["L0_ffn_rg"].reshape(16, 16, 128, 4, 256).transpose(0, 3, 2, 1, 4))
    g["rw1_l"] = T(inp["L0_ffn_rw1"].reshape(16, 16, 128, 4, 256).transpose(0, 3, 2, 1, 4))
    g["rw2_l"] = T(inp["L0_ffn_rw2"].reshape(16, 8, 128, 4, 512).transpose(0, 3, 2, 1, 4))
    g["sg_l"] = T(inp["L0_ffn_sg"].reshape(2, 16, 128, 4, 256).transpose(0, 3, 2, 1, 4))
    g["sw1_l"] = T(inp["L0_ffn_sw1"].reshape(2, 16, 128, 4, 256).transpose(0, 3, 2, 1, 4))
    g["sw2_l"] = T(inp["L0_ffn_sw2"].reshape(2, 8, 128, 4, 512).transpose(0, 3, 2, 1, 4))
    g["rgb_l"] = T(inp["L0_ffn_rgb"].reshape(16, 8, 128).transpose(2, 0, 1))
    g["rb1_l"] = T(inp["L0_ffn_rb1"].reshape(16, 8, 128).transpose(2, 0, 1))
    g["sgb_l"] = T(inp["L0_ffn_sgb"].reshape(2, 8, 128).transpose(2, 0, 1))
    g["sb1_l"] = T(inp["L0_ffn_sb1"].reshape(2, 8, 128).transpose(2, 0, 1))
    g["b2r"] = T(inp["L0_ffn_rb2"])
    g["b2s"] = T(inp["L0_ffn_sb2"])
    sel8 = np.zeros((8, 1024), np.float32)
    for h in range(8):
        sel8[h, h * 128:(h + 1) * 128] = 1.0
    g["sel8"] = sel8
    sel16 = np.zeros((16, 2048), np.float32)
    for j in range(16):
        sel16[j, j * 128:(j + 1) * 128] = 1.0
    g["sel16"] = sel16.astype(ml_dtypes.bfloat16)
    g["maskneg"] = np.where(np.triu(np.ones((128, 128)), 0) > 0, 0.0, -1e9).astype(np.float32)
    g["identf"] = np.eye(128, dtype=np.float32)
    g["identb"] = np.eye(128).astype(ml_dtypes.bfloat16)
    return g


def _stage_x(x, core):
    xc = x[2 * core:2 * core + 2]
    xr = xc.reshape(2, NBLK, NHALF, BW, D)
    return np.ascontiguousarray(xr.transpose(4, 1, 2, 0, 3).reshape(D, NPCOL))


def _unstage_out(out_fm):
    of = out_fm.reshape(D, NBLK, NHALF, 2, BW)
    return np.ascontiguousarray(of.transpose(3, 1, 2, 4, 0).reshape(2, 2048, D))


def kernel(**inp):
    nc = build_program()
    g = _stage_weights(inp)
    x = np.asarray(inp["L0_attn_x"])
    in_maps = []
    for core in range(NCORE):
        m = dict(g)
        m["x_fm"] = _stage_x(x, core)
        in_maps.append(m)
    res = run_bass_kernel_spmd(nc, in_maps, core_ids=list(range(NCORE)))
    out = np.empty((16, 2048, 2048), np.float32)
    for core in range(NCORE):
        out[2 * core:2 * core + 2] = _unstage_out(res.results[core]["out_fm"])
    return out


# version 05
# speedup vs baseline: 1.1018x; 1.0956x over previous
import numpy as np
import ml_dtypes
from contextlib import ExitStack
import concourse.bass as bass
import concourse.mybir as mybir
from concourse.bass_utils import run_bass_kernel_spmd

F32 = mybir.dt.float32
BF16 = mybir.dt.bfloat16
AF = mybir.ActivationFunctionType
ALU = mybir.AluOpType
AX = mybir.AxisListType

ENGINES = ["sync", "gpsimd", "vector", "scalar", "tensor"]

NCORE = 8
D = 2048
KT = 16
NBLK = 8
NHALF = 2
BW = 128
HALF_COLS = 256
TOKC = 64
NCHH = 2
NPCOL = 4096
E_RT, E_SH, FF = 16, 2, 1024
NEXP = E_RT + E_SH
FIT = 512
NFI = NPCOL // FIT
SQ128 = 11.313708498984761

DBG = False
DBG_HB = 0
DBG_TB = 0
PIPE_B = 1
PIPE_D = 2


class Sem:
    def __init__(self, handle):
        self.h = handle
        self.c = 0


class Ctx:
    def __init__(self):
        self.cur = None
        self.eng = None

    def set_pass(self, name, eng, sems):
        self.cur = name
        self.eng = eng
        for s in sems:
            s.c = 0

    def on(self, engine_name, fn):
        if self.cur == engine_name:
            return fn(self.eng)
        return None

    def inc(self, inst, sem, n=1):
        sem.c += n
        if inst is not None:
            inst.then_inc(sem.h, n)

    def wait(self, engine_name, sem, value=None):
        if value is None:
            value = sem.c
        if self.cur == engine_name:
            self.eng.wait_ge(sem.h, value)

    def dma(self, engine_name, sem, out, in_, **kw):
        sem.c += 16
        if self.cur == engine_name:
            self.eng.dma_start(out=out, in_=in_, **kw).then_inc(sem.h, 16)
        return sem.c


class NS:
    pass


DBG_TAPS = [
    ("Kb", [128, NCHH, 16, 128], BF16),
    ("Vb", [128, NCHH, 16, 128], BF16),
    ("Qb", [128, NCHH, 16, 128], BF16),
    ("gate", [128, 8, HALF_COLS], BF16),
    ("beta0", [8, HALF_COLS], F32),
    ("beta1", [8, HALF_COLS], F32),
    ("g8", [8, HALF_COLS], F32),
    ("og", [128, 8, HALF_COLS], BF16),
    ("ypre", [128, 16, HALF_COLS], BF16),
    ("ystg", [128, 16, HALF_COLS], BF16),
    ("comb", [16, FIT], BF16),
    ("affdbg", [16, FIT], F32),
    ("affT", [128, 4, 16], F32),
    ("m1", [128, 4], F32),
    ("m2", [128, 4], F32),
    ("den", [128, 4], F32),
    ("combT", [128, 4, 16], F32),
    ("zdbg", [128, 16, FIT], F32),
]


def _vb_view(t, nh, h):
    v4 = t[:].rearrange("p c (bb hh) t -> p bb c hh t", bb=2)
    return v4[:, :, :, h, nh::2]



def build_program():
    nc = bass.Bass("TRN2", target_bir_lowering=False)
    d = NS()

    def din(name, shape, dt=F32):
        setattr(d, name, nc.dram_tensor(name, list(shape), dt, kind="ExternalInput").ap())

    din("x_fm", [D, NPCOL])
    din("wq_l", [8, 128, 16, 128])
    din("wk_l", [16, 128, 16, 128])
    din("wv_l", [16, 128, 16, 128])
    din("wg_l", [8, 128, 16, 128])
    din("wo_l", [16, 128, 8, 128])
    din("wb_l", [128, 16, 16])
    din("wa_l", [128, 16, 8])
    din("wafn_l", [128, 16, 16])
    din("cq_l", [128, 8, 4])
    din("ck_l", [128, 16, 4])
    din("cv_l", [128, 16, 4])
    din("alog", [8, 1])
    din("dtb", [8, 1])
    din("onw", [128, 1])
    din("rg_l", [E_RT, 4, 128, 16, 256])
    din("rw1_l", [E_RT, 4, 128, 16, 256])
    din("rw2_l", [E_RT, 4, 128, 8, 512])
    din("sg_l", [E_SH, 4, 128, 16, 256])
    din("sw1_l", [E_SH, 4, 128, 16, 256])
    din("sw2_l", [E_SH, 4, 128, 8, 512])
    din("rgb_l", [128, E_RT, 8])
    din("rb1_l", [128, E_RT, 8])
    din("sgb_l", [128, E_SH, 8])
    din("sb1_l", [128, E_SH, 8])
    din("b2r", [E_RT, D])
    din("b2s", [E_SH, D])
    din("sel8", [8, 1024])
    din("sel16", [16, 2048], BF16)
    din("maskneg", [128, 128])
    din("identf", [128, 128])
    din("identb", [128, 128], BF16)

    d.out = nc.dram_tensor("out_fm", [D, NPCOL], F32, kind="ExternalOutput").ap()

    def dint(name, shape, dt):
        setattr(d, name, nc.dram_tensor(name, list(shape), dt, kind="Internal").ap())

    dint("wbf_q", [8, 128, 16, 128], BF16)
    dint("wbf_k", [16, 128, 16, 128], BF16)
    dint("wbf_v", [16, 128, 16, 128], BF16)
    dint("wbf_g", [8, 128, 16, 128], BF16)
    dint("wbf_o", [16, 128, 8, 128], BF16)
    dint("rg_bf", [E_RT, 4, 128, 16, 256], BF16)
    dint("rw1_bf", [E_RT, 4, 128, 16, 256], BF16)
    dint("rw2_bf", [E_RT, 4, 128, 8, 512], BF16)
    dint("sg_bf", [E_SH, 4, 128, 16, 256], BF16)
    dint("sw1_bf", [E_SH, 4, 128, 16, 256], BF16)
    dint("sw2_bf", [E_SH, 4, 128, 8, 512], BF16)
    dint("ybf", [D, NPCOL], BF16)

    dbg_outs = {}
    if DBG:
        for nm, shape, dt in DBG_TAPS:
            dbg_outs[nm] = nc.dram_tensor("dbg_" + nm, list(shape), dt, kind="ExternalOutput").ap()

    stack = ExitStack()
    e = stack.enter_context
    r = NS()
    r.nc = nc

    r.sel8 = e(nc.sbuf_tensor("p_sel8", [8, 1024], F32))
    r.sel16 = e(nc.sbuf_tensor("p_sel16", [16, 2048], BF16))
    r.maskneg = e(nc.sbuf_tensor("p_mask", [128, 128], F32))
    r.identf = e(nc.sbuf_tensor("p_idf", [128, 128], F32))
    r.identb = e(nc.sbuf_tensor("p_idb", [128, 128], BF16))
    r.onesb = e(nc.sbuf_tensor("p_onesb", [128, 128], BF16))
    r.onesf = e(nc.sbuf_tensor("p_onesf", [128, 128], F32))
    r.ones2 = e(nc.sbuf_tensor("p_ones2", [2, 512], BF16))
    r.cst = e(nc.sbuf_tensor("p_cst", [128, 8], F32))
    r.wb_sb = e(nc.sbuf_tensor("p_wb", [128, 16, 16], BF16))
    r.wa_sb = e(nc.sbuf_tensor("p_wa", [128, 16, 8], BF16))
    r.wafn_sb = e(nc.sbuf_tensor("p_wafn", [128, 16, 16], BF16))
    r.b2r_sb = e(nc.sbuf_tensor("p_b2r", [16, 2048], BF16))
    r.b2s_sb = e(nc.sbuf_tensor("p_b2s", [2, 2048], BF16))
    r.cq = e(nc.sbuf_tensor("p_cq", [128, 8, 4], F32))
    r.ck = e(nc.sbuf_tensor("p_ck", [128, 16, 4], F32))
    r.cv = e(nc.sbuf_tensor("p_cv", [128, 16, 4], F32))
    r.nexpA = e(nc.sbuf_tensor("p_nexpA", [8, 1], F32))
    r.dtb = e(nc.sbuf_tensor("p_dtb", [8, 1], F32))
    r.onw = e(nc.sbuf_tensor("p_onw", [128, 1], F32))
    r.rgb = e(nc.sbuf_tensor("p_rgb", [128, E_RT, 8], F32))
    r.rb1 = e(nc.sbuf_tensor("p_rb1", [128, E_RT, 8], F32))
    r.sgb = e(nc.sbuf_tensor("p_sgb", [128, E_SH, 8], F32))
    r.sb1 = e(nc.sbuf_tensor("p_sb1", [128, E_SH, 8], F32))
    r.hist = e(nc.sbuf_tensor("p_hist", [128, 40, 2, 4], BF16))
    r.carry = e(nc.sbuf_tensor("p_carry", [8, 2], F32))
    r.carry0 = e(nc.sbuf_tensor("p_carry0", [8, 2], F32))

    nc.const_aps.register(F32, 1e-5, r.cst[:, 0:1])
    nc.const_aps.register(F32, 1.0 / 128.0, r.cst[:, 1:2])
    nc.const_aps.register(F32, 1.0, r.cst[:, 2:3])
    nc.const_aps.register(F32, -1.0, r.cst[:, 3:4])

    pa_stack = ExitStack()
    e = pa_stack.enter_context
    r.x_bf = e(nc.sbuf_tensor("a_xbf", [128, 16, HALF_COLS], BF16))
    r.slab = [e(nc.sbuf_tensor(f"a_slab{i}", [128, 16, 128], BF16)) for i in range(3)]
    r.pre = [e(nc.sbuf_tensor(f"a_pre{i}", [128, 2, 132], BF16)) for i in range(3)]
    r.cvo = [e(nc.sbuf_tensor(f"a_cvo{i}", [128, 2, 128], BF16)) for i in range(2)]
    r.qs = [e(nc.sbuf_tensor(f"a_qs{i}", [128, HALF_COLS], BF16)) for i in range(3)]
    r.sq = [e(nc.sbuf_tensor(f"a_sq{i}", [128, HALF_COLS], BF16)) for i in range(2)]
    r.srow = [e(nc.sbuf_tensor(f"a_srow{i}", [1, HALF_COLS], BF16)) for i in range(3)]
    r.Kb = e(nc.sbuf_tensor("a_Kb", [128, NCHH, 16, 128], BF16))
    r.Vb = e(nc.sbuf_tensor("a_Vb", [128, NCHH, 16, 128], BF16))
    r.Qb = e(nc.sbuf_tensor("a_Qb", [128, NCHH, 16, 128], BF16))
    r.gate = e(nc.sbuf_tensor("a_gate", [128, 8, HALF_COLS], BF16))
    r.beta_nh = [e(nc.sbuf_tensor(f"a_beta{p}", [8, HALF_COLS], F32)) for p in range(2)]
    r.g8 = e(nc.sbuf_tensor("a_g8", [8, HALF_COLS], F32))
    r.sp1 = e(nc.sbuf_tensor("a_sp1", [8, HALF_COLS], F32))
    r.sp2 = e(nc.sbuf_tensor("a_sp2", [8, HALF_COLS], F32))
    r.cumA = e(nc.sbuf_tensor("a_cumA", [8, HALF_COLS], F32))
    r.cumB = e(nc.sbuf_tensor("a_cumB", [8, HALF_COLS], F32))
    r.ogate = e(nc.sbuf_tensor("a_og", [128, 8, HALF_COLS], BF16))
    r.ypre = e(nc.sbuf_tensor("a_ypre", [128, 16, HALF_COLS], BF16))
    r.xfr = [e(nc.sbuf_tensor(f"a_xfr{i}", [128, HALF_COLS], F32)) for i in range(2)]
    r.ystg = e(nc.sbuf_tensor("a_ystg", [128, 16, HALF_COLS], BF16))
    r.lnt = e(nc.sbuf_tensor("a_lnt", [128, HALF_COLS], F32))
    r.lnr = e(nc.sbuf_tensor("a_lnr", [1, 1024], F32))
    r.S = e(nc.sbuf_tensor("s_S", [128, 2048], F32))
    r.S0bf = e(nc.sbuf_tensor("s_S0bf", [128, 2048], BF16))
    r.cumg_sub = e(nc.sbuf_tensor("s_cgs", [8, 256], F32))
    r.beta_sub = e(nc.sbuf_tensor("s_bts", [8, 256], F32))
    r.cumg_T = e(nc.sbuf_tensor("s_cgT", [128, 16], F32))
    r.beta_T = e(nc.sbuf_tensor("s_btT", [128, 16], F32))
    r.lamc = e(nc.sbuf_tensor("s_lamc", [128, 16], F32))
    r.rstd = e(nc.sbuf_tensor("s_rstd", [128, 16], F32))
    r.ssb = e(nc.sbuf_tensor("s_ssb", [128, 16], F32))
    r.edecf = e(nc.sbuf_tensor("s_edecf", [128, 16], F32))
    r.lamlast = e(nc.sbuf_tensor("s_lamlast", [128, 16], F32))
    r.lam_row = e(nc.sbuf_tensor("s_lamr", [128, 2048], BF16))
    r.E = e(nc.sbuf_tensor("s_E", [128, 2048], BF16))
    r.tmp_bf = e(nc.sbuf_tensor("s_tmpbf", [128, 2048], BF16))
    r.Nt = e(nc.sbuf_tensor("s_Nt", [128, 2048], BF16))
    r.Nn = e(nc.sbuf_tensor("s_Nn", [128, 2048], BF16))
    r.Nt2 = e(nc.sbuf_tensor("s_Nt2", [128, 2048], BF16))
    r.Nn2 = e(nc.sbuf_tensor("s_Nn2", [128, 2048], BF16))
    r.Wt = e(nc.sbuf_tensor("s_Wt", [128, 2048], BF16))
    r.Vtm = e(nc.sbuf_tensor("s_Vtm", [128, 2048], BF16))
    r.KDtm = e(nc.sbuf_tensor("s_KDtm", [128, 2048], BF16))
    r.U32 = e(nc.sbuf_tensor("s_U32", [128, 2048], F32))
    r.Ubf = e(nc.sbuf_tensor("s_Ubf", [128, 2048], BF16))
    r.t32 = e(nc.sbuf_tensor("s_t32", [128, 1024], F32))
    r.sqbf = e(nc.sbuf_tensor("s_sqbf", [128, 2048], BF16))
    r.on = e(nc.sbuf_tensor("s_on", [128, 2048], BF16))
    pa_stack.close()
    e = stack.enter_context
    r.ybf_sb2 = [e(nc.sbuf_tensor(f"f_ybf{i}", [128, 16, FIT], BF16)) for i in range(2)]
    r.z = e(nc.sbuf_tensor("f_z", [128, 16, FIT], F32))
    r.u = [e(nc.sbuf_tensor(f"f_u{i}", [128, 8, FIT], BF16)) for i in range(3)]
    r.wA = [e(nc.sbuf_tensor(f"f_wA{i}", [128, 16, 256], BF16)) for i in range(3)]
    r.wB = [e(nc.sbuf_tensor(f"f_wB{i}", [128, 16, 256], BF16)) for i in range(3)]
    r.w2s = [e(nc.sbuf_tensor(f"f_w2s{i}", [128, 8, 512], BF16)) for i in range(3)]
    r.aff = e(nc.sbuf_tensor("f_aff", [16, FIT], F32))
    r.affT = e(nc.sbuf_tensor("f_affT", [128, 4, 16], F32))
    r.combT = e(nc.sbuf_tensor("f_combT", [128, 4, 16], F32))
    r.comb = e(nc.sbuf_tensor("f_comb", [16, FIT], BF16))
    r.m1 = e(nc.sbuf_tensor("f_m1", [128, 4], F32))
    r.m2 = e(nc.sbuf_tensor("f_m2", [128, 4], F32))
    r.den = e(nc.sbuf_tensor("f_den", [128, 4], F32))
    r.eq = e(nc.sbuf_tensor("f_eq", [128, 4, 16], F32))
    r.g1 = e(nc.sbuf_tensor("f_g1", [128, FIT], F32))
    r.u0 = e(nc.sbuf_tensor("f_uzero", [128, FIT], F32))
    r.ostg = [e(nc.sbuf_tensor(f"f_ostg{i}", [128, FIT], F32)) for i in range(2)]
    r.flnr = e(nc.sbuf_tensor("f_lnr", [1, 2048], F32))

    r.PSA = e(nc.psum_tensor("ps_A", [128, 2048], F32))
    r.PSB = e(nc.psum_tensor("ps_B", [128, 1024], F32))
    r.PSC = e(nc.psum_tensor("ps_C", [128, 2048], BF16))

    sems = {n: Sem(e(nc.semaphore(n))) for n in ["dS", "dG", "sP", "sV", "sA", "st"]}
    block = e(nc.Block())
    ctx = Ctx()

    def emit():
        emit_all(ctx, nc, d, r, sems, dbg_outs)

    for en in ENGINES:
        def run_pass(eng, en=en):
            ctx.set_pass(en, eng, list(sems.values()))
            emit()
        getattr(block, en)(run_pass)
    stack.close()
    return nc



def emit_all(ctx, nc, d, r, sems, dbg_outs):
    dS, dG, sP, sV, sA, st = (sems[n] for n in ["dS", "dG", "sP", "sV", "sA", "st"])

    for name, sb in [("sel8", r.sel8), ("sel16", r.sel16), ("maskneg", r.maskneg),
                     ("identf", r.identf), ("identb", r.identb),
                     ("cq_l", r.cq), ("ck_l", r.ck), ("cv_l", r.cv),
                     ("dtb", r.dtb), ("onw", r.onw),
                     ("rgb_l", r.rgb), ("rb1_l", r.rb1), ("sgb_l", r.sgb), ("sb1_l", r.sb1)]:
        ctx.dma("sync", dS, sb[:], getattr(d, name))
    ctx.dma("sync", dS, r.nexpA[:], d.alog)
    t_cl = dS.c

    ctx.dma("gpsimd", dG, r.wb_sb[:], d.wb_l)
    ctx.dma("gpsimd", dG, r.wa_sb[:], d.wa_l)
    ctx.dma("gpsimd", dG, r.wafn_sb[:], d.wafn_l)
    ctx.dma("gpsimd", dG, r.b2r_sb[:], d.b2r)
    ctx.dma("gpsimd", dG, r.b2s_sb[:], d.b2s)
    t_smallcast = dG.c
    for (src, dst) in [(d.wq_l, d.wbf_q), (d.wk_l, d.wbf_k), (d.wv_l, d.wbf_v),
                       (d.wg_l, d.wbf_g), (d.wo_l, d.wbf_o)]:
        ctx.dma("gpsimd", dG, dst.rearrange("a p k o -> (a p k o)"),
                src.rearrange("a p k o -> (a p k o)"))
    t_p0attn = dG.c
    for (src, dst, ne) in [(d.rg_l, d.rg_bf, E_RT), (d.rw1_l, d.rw1_bf, E_RT),
                           (d.rw2_l, d.rw2_bf, E_RT), (d.sg_l, d.sg_bf, E_SH),
                           (d.sw1_l, d.sw1_bf, E_SH), (d.sw2_l, d.sw2_bf, E_SH)]:
        for ee in range(ne):
            ctx.dma("gpsimd", dG, dst[ee].rearrange("a p k o -> (a p k o)"),
                    src[ee].rearrange("a p k o -> (a p k o)"))
    t_p0ffn = dG.c

    ctx.wait("vector", dS, t_cl)
    for fn in [lambda e: e.memset(r.onesb[:], 1.0),
               lambda e: e.memset(r.onesf[:], 1.0),
               lambda e: e.memset(r.ones2[:], 1.0),
               lambda e: e.memset(r.cst[:, 0:1], 1e-5),
               lambda e: e.memset(r.cst[:, 1:2], 1.0 / 128.0),
               lambda e: e.memset(r.cst[:, 2:3], 1.0),
               lambda e: e.memset(r.cst[:, 3:4], -1.0),
               lambda e: e.memset(r.S[:], 0.0),
               lambda e: e.memset(r.carry[:, 0:2], 0.0),
               lambda e: e.memset(r.carry0[:, 0:2], 0.0),
               lambda e: e.memset(r.Qb[:], 0.0),
               lambda e: e.memset(r.hist[:], 0.0)]:
        i = ctx.on("vector", fn)
    ctx.inc(i, sV)
    t_vinit = sV.c
    ctx.wait("scalar", sV, t_vinit)
    ctx.on("scalar", lambda e: e.activation(out=r.nexpA[:], in_=r.nexpA[:], func=AF.Exp))
    i = ctx.on("scalar", lambda e: e.mul(out=r.nexpA[:], in_=r.nexpA[:], mul=-1.0))
    ctx.inc(i, sA)
    ctx.wait("tensor", dS, t_cl)
    ctx.wait("tensor", dG, t_smallcast)
    ctx.wait("tensor", sV, t_vinit)

    r.spill_st = 0
    for tb in range(NBLK):
        for hb in range(NHALF):
            emit_half(ctx, d, r, sems, tb, hb, t_p0attn, dbg_outs)

    r.y_pref = [0, 0]
    r.y_used = [0, 0]
    for it in range(NFI):
        emit_ffn_iter(ctx, d, r, sems, it, t_p0ffn, dbg_outs)

    ctx.wait("sync", st, st.c)



def emit_half(ctx, d, r, sems, tb, hb, t_p0attn, dbg_outs):
    dS, dG, sP, sV, sA, st = (sems[n] for n in ["dS", "dG", "sP", "sV", "sA", "st"])
    pc0 = tb * 512 + hb * 256
    t_entry_sV = sems["sV"].c

    t_xfree = sP.c
    ctx.wait("gpsimd", sP, t_xfree)
    xsrc = d.x_fm.rearrange("(k p) n -> p k n", p=128)[:, :, pc0:pc0 + 256]
    t_x = ctx.dma("gpsimd", dG, r.x_bf[:], xsrc)

    tiles = [("a", 0), ("b", 0), ("b", 1)] + [("q", j) for j in range(8)] + \
            [("k", j) for j in range(16)] + [("v", j) for j in range(16)] + \
            [("gg", j) for j in range(8)]
    slab_src = {"q": d.wbf_q, "k": d.wbf_k, "v": d.wbf_v, "gg": d.wbf_g}
    conv_w = {"q": r.cq, "k": r.ck, "v": r.cv}
    hist_base = {"q": 0, "k": 8, "v": 24}

    slot_free = [sV.c] * 4
    slot_free_a = [sA.c] * 4
    slab_free = [sP.c] * 3
    slab_ready = [0] * 3
    pre_free = [sA.c] * 3
    qs_free = [sA.c] * 3
    cvo_free = [sA.c] * 2
    sq_free = [sP.c] * 2
    srow_free = [sP.c] * 3
    nrm_q = 0
    ssrow_free = [sA.c] * 2
    sbc_free = [sV.c] * 2
    lnr_free = [sV.c] * 2
    bank4 = {"rd": sA.c}
    bank5 = {"rd": sV.c}

    pend = []

    def run_pending(cur_ti, flush=False):
        for ent in list(pend):
            if ent["stage"] == "B" and (flush or cur_ti - ent["ti"] >= PIPE_B):
                ent["fnB"]()
                ent["fnC"]()
                ent["stage"] = "D"
            if ent["stage"] == "D" and (flush or cur_ti - ent["ti"] >= PIPE_D):
                ent["fnD"]()
                pend.remove(ent)
        if flush and pend:
            run_pending(cur_ti, flush=True)

    for ti, (kind, j) in enumerate(tiles):
        si = ti % 3
        pi = ti % 4
        psl = r.PSA[:, pi * 512: pi * 512 + 256]

        if kind in slab_src:
            ctx.wait("sync", dG, t_p0attn)
            ctx.wait("sync", sP, slab_free[si])
            slab_ready[si] = ctx.dma("sync", dS, r.slab[si][:], slab_src[kind][j])

        ctx.wait("tensor", dG, t_x)
        ctx.wait("tensor", sV, slot_free[pi])
        ctx.wait("tensor", sA, slot_free_a[pi])
        if kind in slab_src:
            ctx.wait("tensor", dS, slab_ready[si])
            lhs = lambda kk, si=si: r.slab[si][:, kk, :]
            M = 128
        elif kind == "a":
            lhs = lambda kk: r.wa_sb[:, kk, :]
            M = 8
        else:
            lhs = lambda kk, j=j: r.wb_sb[:, kk, j * 8:(j + 1) * 8]
            M = 8
        mm = None
        for kk in range(16):
            mm = ctx.on("tensor", lambda e, kk=kk, lhs=lhs, psl=psl, M=M: e.matmul(
                psl[0:M, :] if M != 128 else psl, lhs(kk), r.x_bf[:, kk, :],
                start=(kk == 0), stop=(kk == 15)))
        ctx.inc(mm, sP)
        if kind in slab_src:
            slab_free[si] = sP.c
        t_mm = sP.c

        if kind == "a":
            ctx.wait("vector", sP, t_mm)
            i = ctx.on("vector", lambda e, pi=pi: e.tensor_scalar(
                out=r.sp1[:], in0=r.PSA[0:8, pi * 512: pi * 512 + 256], scalar1=r.dtb[:],
                scalar2=None, op0=ALU.add))
            ctx.inc(i, sV)
            slot_free[pi] = sV.c
            ctx.wait("scalar", sV, sV.c)
            i = ctx.on("scalar", lambda e: e.activation(out=r.sp2[:], in_=r.sp1[:],
                                                        func=AF.Abs))
            ctx.inc(i, sA)
            ctx.wait("scalar", sA, sA.c)
            i = ctx.on("scalar", lambda e: e.activation(out=r.sp2[:], in_=r.sp2[:],
                                                        func=AF.Exp, scale=-1.0))
            ctx.inc(i, sA)
            ctx.wait("scalar", sA, sA.c)
            i = ctx.on("scalar", lambda e: e.activation(out=r.sp2[:], in_=r.sp2[:],
                                                        func=AF.Ln, bias=1.0))
            ctx.inc(i, sA)
            ctx.wait("vector", sA, sA.c)
            ctx.on("vector", lambda e: e.tensor_scalar(
                out=r.sp1[:], in0=r.sp1[:], scalar1=0.0, scalar2=None, op0=ALU.max))
            ctx.on("vector", lambda e: e.tensor_add(out=r.g8[:], in0=r.sp1[:], in1=r.sp2[:]))
            i = ctx.on("vector", lambda e: e.tensor_scalar(
                out=r.g8[:], in0=r.g8[:], scalar1=r.nexpA[:], scalar2=None, op0=ALU.mult))
            ctx.inc(i, sV)
        elif kind == "b":
            ctx.wait("scalar", sP, t_mm)
            ctx.wait("scalar", sV, t_entry_sV)
            i = ctx.on("scalar", lambda e, j=j, pi=pi: e.activation(
                out=r.beta_nh[j][:], in_=r.PSA[0:8, pi * 512: pi * 512 + 256], func=AF.Sigmoid))
            ctx.inc(i, sA)
            slot_free_a[pi] = sA.c
        elif kind == "gg":
            ctx.wait("scalar", sP, t_mm)
            ctx.wait("scalar", sV, t_entry_sV)
            i = ctx.on("scalar", lambda e, j=j, psl=psl: e.activation(
                out=r.gate[:, j, :], in_=psl, func=AF.Silu))
            ctx.inc(i, sA)
            slot_free_a[pi] = sA.c
        else:
            pri = ti % 3
            ci = ti % 2
            hb_i = hist_base[kind] + j
            ctx.wait("vector", sA, pre_free[pri])
            ctx.on("vector", lambda e, pri=pri, hb_i=hb_i: e.tensor_copy(
                out=r.pre[pri][:, :, 0:4], in_=r.hist[:, hb_i, :, :]))
            ctx.wait("vector", sP, t_mm)
            pv = psl.rearrange("p (b t) -> p b t", b=2)
            i = ctx.on("vector", lambda e, pri=pri, pv=pv: e.tensor_copy(
                out=r.pre[pri][:, :, 4:132], in_=pv))
            ctx.inc(i, sV)
            slot_free[pi] = sV.c
            ctx.wait("vector", sV, sV.c)
            ctx.on("vector", lambda e, pri=pri, hb_i=hb_i: e.tensor_copy(
                out=r.hist[:, hb_i, :, 1:4], in_=r.pre[pri][:, :, 129:132]))
            cw = conv_w[kind]
            ctx.wait("vector", sA, cvo_free[ci])
            ctx.on("vector", lambda e, pri=pri, ci=ci, cw=cw, j=j: e.tensor_scalar(
                out=r.cvo[ci][:], in0=r.pre[pri][:, :, 1:129], scalar1=cw[:, j, 0:1],
                scalar2=None, op0=ALU.mult))
            for tap in range(1, 4):
                i = ctx.on("vector", lambda e, pri=pri, ci=ci, cw=cw, j=j, tap=tap: e.scalar_tensor_tensor(
                    out=r.cvo[ci][:], in0=r.pre[pri][:, :, 1 + tap:129 + tap],
                    scalar=cw[:, j, tap:tap + 1], in1=r.cvo[ci][:],
                    op0=ALU.mult, op1=ALU.add))
            ctx.inc(i, sV)
            t_cv = sV.c
            ctx.wait("scalar", sV, t_cv)
            if kind == "v":
                nh, h = (0, j) if j < 8 else (1, j - 8)
                vout = _vb_view(r.Vb, nh, h)
                i = ctx.on("scalar", lambda e, ci=ci, vout=vout: e.activation(
                    out=vout, in_=r.cvo[ci][:].rearrange("p b (c i) -> p b c i", c=NCHH),
                    func=AF.Silu))
                ctx.inc(i, sA)
                pre_free[pri] = sA.c
                cvo_free[ci] = sA.c
            else:
                n = nrm_q
                nrm_q += 1
                qi = n % 3
                sqi = n % 2
                sri = n % 3
                lo = 512 + (n % 2) * 256
                ro = (n % 2) * 256
                bo = 512 + (n % 2) * 256
                ctx.wait("scalar", sV, qs_free[qi])
                i = ctx.on("scalar", lambda e, ci=ci, qi=qi: e.activation(
                    out=r.qs[qi][:].rearrange("p (b t) -> p b t", b=2), in_=r.cvo[ci][:],
                    func=AF.Silu))
                ctx.inc(i, sA)
                pre_free[pri] = sA.c
                cvo_free[ci] = sA.c
                ctx.wait("scalar", sP, sq_free[sqi])
                ctx.wait("scalar", sV, lnr_free[n % 2])
                i = ctx.on("scalar", lambda e, qi=qi, sqi=sqi: e.activation(
                    out=r.sq[sqi][:], in_=r.qs[qi][:], func=AF.Square))
                ctx.inc(i, sA)
                t_sqr = sA.c
                sc2 = SQ128 if kind == "q" else 1.0
                nh, h = (0, j) if j < 8 else (1, j - 8)
                dst = _vb_view(r.Qb, 1, j) if kind == "q" else _vb_view(r.Kb, nh, h)

                ent = {"ti": ti, "stage": "B"}

                def fnB(t_sqr=t_sqr, sqi=sqi, ro=ro, n=n, ent=ent):
                    ctx.wait("tensor", sA, max(t_sqr, ssrow_free[n % 2], bank4["rd"]))
                    i = ctx.on("tensor", lambda e: e.matmul(
                        r.PSB[0:1, ro:ro + 256], r.onesb[:, 0:1], r.sq[sqi][:],
                        start=True, stop=True))
                    ctx.inc(i, sP)
                    sq_free[sqi] = sP.c
                    ent["t_ssm"] = sP.c

                def fnC(sri=sri, ro=ro, lo=lo, sc2=sc2, n=n, ent=ent):
                    ctx.wait("scalar", sP, ent["t_ssm"])
                    i = ctx.on("scalar", lambda e: e.activation(
                        out=r.lnr[0:1, lo:lo + 256], in_=r.PSB[0:1, ro:ro + 256], func=AF.Sqrt))
                    ctx.inc(i, sA)
                    ssrow_free[n % 2] = sA.c
                    bank4["rd"] = sA.c
                    t_sq2 = sA.c
                    ctx.wait("vector", sA, t_sq2)
                    ctx.wait("vector", sP, srow_free[sri])
                    ctx.on("vector", lambda e: e.tensor_scalar(
                        out=r.lnr[0:1, lo:lo + 256], in0=r.lnr[0:1, lo:lo + 256], scalar1=1e-12,
                        scalar2=sc2, op0=ALU.max, op1=ALU.mult))
                    def _recip(e):
                        with r.nc.allow_low_precision(reason="bf16 norm-scale for bf16 matmul"):
                            return e.reciprocal(out=r.srow[sri][:], in_=r.lnr[0:1, lo:lo + 256])
                    i = ctx.on("vector", _recip)
                    ctx.inc(i, sV)
                    ctx.wait("vector", sV, sV.c)
                    i = ctx.on("vector", lambda e: e.engine_nop())
                    ctx.inc(i, sV)
                    ent["t_srow"] = sV.c
                    lnr_free[n % 2] = sV.c

                def fnD(sri=sri, bo=bo, qi=qi, dst=dst, n=n, ent=ent):
                    ctx.wait("tensor", sV, max(ent["t_srow"], sbc_free[n % 2], bank5["rd"]))
                    i = ctx.on("tensor", lambda e: e.matmul(
                        r.PSB[:, bo:bo + 256], r.onesb[0:1, :], r.srow[sri][:],
                        start=True, stop=True))
                    ctx.inc(i, sP)
                    t_bc = sP.c
                    srow_free[sri] = sP.c
                    ctx.wait("vector", sP, t_bc)
                    i = ctx.on("vector", lambda e: e.tensor_mul(
                        out=dst,
                        in0=r.qs[qi][:].rearrange("p (b c i) -> p b c i", b=2, c=NCHH),
                        in1=r.PSB[:, bo:bo + 256].rearrange("p (b c i) -> p b c i", b=2, c=NCHH)))
                    ctx.inc(i, sV)
                    qs_free[qi] = sV.c
                    sbc_free[n % 2] = sV.c
                    bank5["rd"] = sV.c

                ent["fnB"], ent["fnC"], ent["fnD"] = fnB, fnC, fnD
                pend.append(ent)
        run_pending(ti)
    run_pending(len(tiles), flush=True)

    emit_cumsum_half(ctx, r, sP, sV, sA)

    consts = {"sel8": r.sel8[:], "maskneg": r.maskneg[:], "identf": r.identf[:],
              "identb": r.identb[:]}
    for c in range(NCHH):
        emit_chunk(ctx, r, c, consts, sP, sV, sA)

    emit_wo_ln(ctx, d, r, sems, pc0)

    if DBG and tb == DBG_TB and hb == DBG_HB:
        for nm, src in [("Kb", r.Kb), ("Vb", r.Vb), ("Qb", r.Qb), ("gate", r.gate),
                        ("beta0", r.beta_nh[0]), ("beta1", r.beta_nh[1]), ("g8", r.g8),
                        ("og", r.ogate), ("ypre", r.ypre), ("ystg", r.ystg)]:
            if nm in dbg_outs:
                ctx.wait("sync", sV, sV.c)
                ctx.wait("sync", sA, sA.c)
                ctx.dma("sync", st, dbg_outs[nm], src[:])



def emit_cumsum_half(ctx, r, sP, sV, sA):
    def cp(dst, src):
        ctx.on("vector", lambda e: e.tensor_copy(out=dst, in_=src))

    cp(r.carry0[:, 0:2], r.carry[:, 0:2])
    src, dst = r.cumA, r.cumB
    cp(src[:, 0:256], r.g8[:, 0:256])
    s = 1
    while s < BW:
        sv = src[:].rearrange("p (b t) -> p b t", b=2)
        dv = dst[:].rearrange("p (b t) -> p b t", b=2)
        cp(dv[:, :, 0:s], sv[:, :, 0:s])
        ctx.on("vector", lambda e, sv=sv, dv=dv, s=s: e.tensor_add(
            out=dv[:, :, s:BW], in0=sv[:, :, s:BW], in1=sv[:, :, 0:BW - s]))
        src, dst = dst, src
        s *= 2
    for b in range(2):
        ctx.on("vector", lambda e, b=b, src=src: e.tensor_scalar(
            out=src[:, b * BW:(b + 1) * BW], in0=src[:, b * BW:(b + 1) * BW],
            scalar1=r.carry[:, b:b + 1], scalar2=None, op0=ALU.add))
    if src is not r.cumA:
        cp(r.cumA[:, 0:256], src[:, 0:256])
    for b in range(2):
        cp(r.carry[:, b:b + 1], r.cumA[:, b * BW + BW - 1:b * BW + BW])
    i = ctx.on("vector", lambda e: e.engine_nop())
    ctx.inc(i, sV)



def emit_chunk(ctx, r, c, consts, sP, sV, sA):
    sel8, maskneg, identf, identb = consts["sel8"], consts["maskneg"], consts["identf"], consts["identb"]
    col0 = c * TOKC
    Kc = r.Kb[:, c]
    Vc = r.Vb[:, c]
    Qc = r.Qb[:, c]

    t_pa = sA.c
    ctx.wait("tensor", sA, t_pa)

    ctx.wait("scalar", sV, sV.c)
    i = ctx.on("scalar", lambda e: e.copy(out=r.S0bf[:], in_=r.S[:]))
    ctx.inc(i, sA)
    t_s0bf = sA.c

    for b in range(2):
        src = r.cumA[:, b * BW + col0: b * BW + col0 + TOKC]
        dst = r.cumg_sub[:, b * 128:(b + 1) * 128].rearrange("p (i two) -> p i two", two=2)
        srcb = src.unsqueeze(2).broadcast_to([8, TOKC, 2])
        ctx.on("vector", lambda e, dst=dst, srcb=srcb: e.tensor_copy(out=dst, in_=srcb))
        if col0 == 0:
            base = r.carry0[:, b:b + 1]
        else:
            base = r.cumA[:, b * BW + col0 - 1: b * BW + col0]
        ctx.on("vector", lambda e, b=b, base=base: e.tensor_scalar(
            out=r.cumg_sub[:, b * 128:(b + 1) * 128], in0=r.cumg_sub[:, b * 128:(b + 1) * 128],
            scalar1=base, scalar2=None, op0=ALU.subtract))
    for b in range(2):
        for p in range(2):
            dst = r.beta_sub[:, b * 128 + p: (b + 1) * 128: 2]
            src = r.beta_nh[p][:, b * BW + col0: b * BW + col0 + TOKC]
            ctx.on("vector", lambda e, dst=dst, src=src: e.tensor_copy(out=dst, in_=src))
    i = ctx.on("vector", lambda e: e.engine_nop())
    ctx.inc(i, sV)
    t_prep = sV.c

    ctx.wait("tensor", sV, t_prep)
    for b in range(2):
        ctx.on("tensor", lambda e, b=b: e.transpose(
            r.PSB[:, b * 8:(b + 1) * 8], r.cumg_sub[:, b * 128:(b + 1) * 128], identf[0:8, 0:8]))
    for b in range(2):
        i = ctx.on("tensor", lambda e, b=b: e.transpose(
            r.PSB[:, 16 + b * 8: 16 + (b + 1) * 8], r.beta_sub[:, b * 128:(b + 1) * 128],
            identf[0:8, 0:8]))
    ctx.inc(i, sP)
    t_tp1 = sP.c

    ctx.wait("vector", sP, t_tp1)
    ctx.on("vector", lambda e: e.tensor_copy(out=r.cumg_T[:], in_=r.PSB[:, 0:16]))
    i = ctx.on("vector", lambda e: e.tensor_copy(out=r.beta_T[:], in_=r.PSB[:, 16:32]))
    ctx.inc(i, sV)
    t_tp1e = sV.c
    ctx.wait("scalar", sV, t_tp1e)
    i = ctx.on("scalar", lambda e: e.activation(out=r.lamc[:], in_=r.cumg_T[:], func=AF.Exp))
    ctx.inc(i, sA)
    t_lamc = sA.c

    for bh in range(16):
        b, h = bh // 8, bh % 8
        i = ctx.on("tensor", lambda e, b=b, h=h, bh=bh: e.matmul(
            r.PSA[:, bh * 128:(bh + 1) * 128], sel8[:, h * 128:(h + 1) * 128],
            r.cumg_sub[:, b * 128:(b + 1) * 128], start=True, stop=True))
    ctx.inc(i, sP)
    t_cgrow = sP.c

    ctx.wait("scalar", sP, t_cgrow)
    i = ctx.on("scalar", lambda e: e.activation(out=r.lam_row[:], in_=r.PSA[:], func=AF.Exp))
    ctx.inc(i, sA)
    mview = maskneg.unsqueeze(1).broadcast_to([128, 8, 128])
    for half in range(2):
        cslice = r.cumg_T[:, half * 8:(half + 1) * 8]
        colrep = cslice.unsqueeze(2).broadcast_to([128, 8, 128])
        pa = r.PSA[:, half * 1024:(half + 1) * 1024].rearrange("p (a b) -> p a b", b=128)
        t3 = r.t32[:].rearrange("p (a b) -> p a b", b=128)
        ctx.wait("vector", sA, sA.c)
        ctx.on("vector", lambda e, pa=pa, colrep=colrep, t3=t3: e.tensor_sub(out=t3, in0=pa, in1=colrep))
        ctx.on("vector", lambda e, t3=t3, mview=mview: e.tensor_add(out=t3, in0=t3, in1=mview))
        i = ctx.on("vector", lambda e: e.engine_nop())
        ctx.inc(i, sV)
        ctx.wait("scalar", sV, sV.c)
        i = ctx.on("scalar", lambda e, half=half: e.activation(
            out=r.E[:, half * 1024:(half + 1) * 1024], in_=r.t32[:], func=AF.Exp))
        ctx.inc(i, sA)
    t_E = sA.c
    ctx.wait("vector", sA, t_E)
    ctx.on("vector", lambda e: e.tensor_copy(out=r.edecf[:], in_=r.E[:, 127::128]))
    ctx.on("vector", lambda e: e.tensor_copy(out=r.lamlast[:], in_=r.lam_row[:, 127::128]))
    t_psa_free_v = sV.c
    t_psa_free_a = sA.c

    ctx.wait("tensor", sV, t_psa_free_v)
    ctx.wait("tensor", sA, t_psa_free_a)
    for bh in range(16):
        i = ctx.on("tensor", lambda e, bh=bh: e.matmul(
            r.PSA[:, bh * 128:(bh + 1) * 128], Kc[:, bh], Kc[:, bh], start=True, stop=True))
    ctx.inc(i, sP)
    t_A = sP.c
    iview = identb.unsqueeze(1).broadcast_to([128, 16, 128])
    Ev = r.E[:].rearrange("p (a b) -> p a b", b=128)
    tbv = r.tmp_bf[:].rearrange("p (a b) -> p a b", b=128)
    ctx.wait("vector", sA, t_E)
    ctx.on("vector", lambda e: e.tensor_sub(out=tbv, in0=Ev, in1=iview))
    ctx.wait("vector", sP, t_A)
    i = ctx.on("vector", lambda e: e.tensor_mul(out=r.Nt[:], in0=r.PSA[:], in1=r.tmp_bf[:]))
    ctx.inc(i, sV)
    t_AE = sV.c
    ctx.wait("tensor", sV, t_AE)
    for bh in range(16):
        i = ctx.on("tensor", lambda e, bh=bh: e.matmul(
            r.PSA[:, bh * 128:(bh + 1) * 128], Kc[:, bh], Qc[:, bh], start=True, stop=True))
    ctx.inc(i, sP)
    t_KQ = sP.c
    ctx.wait("vector", sP, t_KQ)
    i = ctx.on("vector", lambda e: e.tensor_mul(out=r.Wt[:], in0=r.PSA[:], in1=r.E[:]))
    ctx.inc(i, sV)
    t_W = sV.c
    ctx.wait("tensor", sV, t_W)
    for bh in range(16):
        b, h = bh // 8, bh % 8
        i = ctx.on("tensor", lambda e, b=b, h=h, bh=bh: e.matmul(
            r.PSA[:, bh * 128:(bh + 1) * 128], sel8[:, h * 128:(h + 1) * 128],
            r.beta_sub[:, b * 128:(b + 1) * 128], start=True, stop=True))
    ctx.inc(i, sP)
    t_brow = sP.c
    ctx.wait("vector", sP, t_brow)
    i = ctx.on("vector", lambda e: e.scalar_tensor_tensor(
        out=r.Nt[:], in0=r.Nt[:], scalar=2.0, in1=r.PSA[:], op0=ALU.mult, op1=ALU.mult))
    ctx.inc(i, sV)
    t_Nt = sV.c

    ctx.wait("tensor", sV, t_Nt)
    for bh in range(16):
        i = ctx.on("tensor", lambda e, bh=bh: e.transpose(
            r.PSC[:, bh * 128:(bh + 1) * 128], r.Nt[:, bh * 128:(bh + 1) * 128], identb))
    ctx.inc(i, sP)
    t_Ntp = sP.c
    ctx.wait("vector", sP, t_Ntp)
    i = ctx.on("vector", lambda e: e.tensor_copy(out=r.Nn[:], in_=r.PSC[:]))
    ctx.inc(i, sV)
    t_Nn = sV.c

    ctx.wait("tensor", sV, t_Nn)
    for bh in range(16):
        i = ctx.on("tensor", lambda e, bh=bh: e.transpose(
            r.PSC[:, bh * 128:(bh + 1) * 128], Vc[:, bh], identb))
    ctx.inc(i, sP)
    t_Vtp = sP.c
    ctx.wait("vector", sP, t_Vtp)
    i = ctx.on("vector", lambda e: e.tensor_copy(out=r.Vtm[:], in_=r.PSC[:]))
    ctx.inc(i, sV)
    t_Vtm = sV.c

    ctx.wait("tensor", sA, t_s0bf)
    for bh in range(16):
        i = ctx.on("tensor", lambda e, bh=bh: e.matmul(
            r.PSA[:, bh * 128:(bh + 1) * 128], Kc[:, bh],
            r.S0bf[:, bh * 128:(bh + 1) * 128], start=True, stop=True))
    ctx.inc(i, sP)
    t_KS0 = sP.c
    ctx.wait("vector", sP, t_KS0)
    ctx.wait("vector", sA, t_lamc)
    for bh in range(16):
        sl = slice(bh * 128, (bh + 1) * 128)
        ctx.on("vector", lambda e, sl=sl, bh=bh: e.scalar_tensor_tensor(
            out=r.U32[:, sl], in0=r.PSA[:, sl], scalar=r.lamc[:, bh:bh + 1],
            in1=r.Vtm[:, sl], op0=ALU.mult, op1=ALU.subtract))
        ctx.on("vector", lambda e, sl=sl, bh=bh: e.tensor_scalar(
            out=r.U32[:, sl], in0=r.U32[:, sl], scalar1=r.beta_T[:, bh:bh + 1],
            scalar2=-2.0, op0=ALU.mult, op1=ALU.mult))
    i = ctx.on("vector", lambda e: e.engine_nop())
    ctx.inc(i, sV)
    t_R = sV.c
    ctx.wait("scalar", sV, t_R)
    i = ctx.on("scalar", lambda e: e.copy(out=r.Ubf[:], in_=r.U32[:]))
    ctx.inc(i, sA)
    t_ubf_prev = sA.c

    Acur, Bcur = r.Nt, r.Nn
    Anew, Bnew = r.Nt2, r.Nn2
    p = 1
    t_psa_free = t_R
    while p <= 64:
        ctx.wait("tensor", sA, t_ubf_prev)
        ctx.wait("tensor", sV, t_psa_free)
        for bh in range(16):
            sl = slice(bh * 128, (bh + 1) * 128)
            i = ctx.on("tensor", lambda e, sl=sl, Acur=Acur: e.matmul(
                r.PSA[:, sl], Acur[:, sl], r.Ubf[:, sl], start=True, stop=True))
        ctx.inc(i, sP)
        t_ap = sP.c
        ctx.wait("vector", sP, t_ap)
        op = ALU.subtract if p == 1 else ALU.add
        i = ctx.on("vector", lambda e, op=op: e.tensor_tensor(
            out=r.U32[:], in0=r.U32[:], in1=r.PSA[:], op=op))
        ctx.inc(i, sV)
        t_psa_free = sV.c
        ctx.wait("scalar", sV, t_psa_free)
        i = ctx.on("scalar", lambda e: e.copy(out=r.Ubf[:], in_=r.U32[:]))
        ctx.inc(i, sA)
        t_ubf_prev = sA.c
        if p < 64:
            ctx.wait("tensor", sV, t_psa_free)
            for bh in range(16):
                sl = slice(bh * 128, (bh + 1) * 128)
                i = ctx.on("tensor", lambda e, sl=sl, Acur=Acur, Bcur=Bcur: e.matmul(
                    r.PSA[:, sl], Bcur[:, sl], Acur[:, sl], start=True, stop=True))
            ctx.inc(i, sP)
            t_d1 = sP.c
            ctx.wait("vector", sP, t_d1)
            i = ctx.on("vector", lambda e, Anew=Anew: e.tensor_copy(out=Anew[:], in_=r.PSA[:]))
            ctx.inc(i, sV)
            t_Anew = sV.c
            ctx.wait("tensor", sV, t_Anew)
            for bh in range(16):
                sl = slice(bh * 128, (bh + 1) * 128)
                i = ctx.on("tensor", lambda e, sl=sl, Acur=Acur, Bcur=Bcur: e.matmul(
                    r.PSA[:, sl], Acur[:, sl], Bcur[:, sl], start=True, stop=True))
            ctx.inc(i, sP)
            t_d2 = sP.c
            ctx.wait("vector", sP, t_d2)
            i = ctx.on("vector", lambda e, Bnew=Bnew: e.tensor_copy(out=Bnew[:], in_=r.PSA[:]))
            ctx.inc(i, sV)
            t_Bnew = sV.c
            Acur, Anew = Anew, Acur
            Bcur, Bnew = Bnew, Bcur
            t_psa_free = t_Bnew
        p *= 2

    i = ctx.on("vector", lambda e: e.tensor_mul(
        out=r.tmp_bf[:].rearrange("p (a b) -> p a b", b=128), in0=Qc,
        in1=r.lam_row[:].rearrange("p (a b) -> p a b", b=128)))
    ctx.inc(i, sV)
    t_qlam = sV.c
    ctx.wait("tensor", sV, t_qlam)
    ctx.wait("tensor", sA, t_ubf_prev)
    for bh in range(16):
        sl = slice(bh * 128, (bh + 1) * 128)
        ctx.on("tensor", lambda e, sl=sl: e.matmul(
            r.PSA[:, sl], r.tmp_bf[:, sl], r.S0bf[:, sl], start=True, stop=False))
        i = ctx.on("tensor", lambda e, sl=sl: e.matmul(
            r.PSA[:, sl], r.Wt[:, sl], r.Ubf[:, sl], start=False, stop=True))
    ctx.inc(i, sP)
    t_O = sP.c

    ctx.wait("scalar", sP, t_O)
    i = ctx.on("scalar", lambda e: e.activation(out=r.sqbf[:], in_=r.PSA[:], func=AF.Square))
    ctx.inc(i, sA)
    t_sq = sA.c
    ctx.wait("vector", sA, t_sq)
    i = ctx.on("vector", lambda e: e.tensor_reduce(
        out=r.ssb[:], in_=r.sqbf[:].rearrange("p (a b) -> p a b", b=128), axis=AX.X, op=ALU.add))
    ctx.inc(i, sV)
    ctx.wait("scalar", sV, sV.c)
    i = ctx.on("scalar", lambda e: e.activation(out=r.rstd[:], in_=r.ssb[:], func=AF.Sqrt,
                                                scale=1.0 / 128.0, bias=1e-5))
    ctx.inc(i, sA)
    ctx.wait("vector", sA, sA.c)
    i = ctx.on("vector", lambda e: e.reciprocal(out=r.rstd[:], in_=r.rstd[:]))
    ctx.inc(i, sV)
    ctx.wait("vector", sV, sV.c)
    rsv = r.rstd[:].unsqueeze(2).broadcast_to([128, 16, 128])
    i = ctx.on("vector", lambda e: e.tensor_mul(
        out=r.on[:].rearrange("p (a b) -> p a b", b=128),
        in0=r.PSA[:].rearrange("p (a b) -> p a b", b=128), in1=rsv))
    ctx.inc(i, sV)
    t_on = sV.c
    ctx.wait("tensor", sV, t_on)
    for bh in range(16):
        i = ctx.on("tensor", lambda e, bh=bh: e.transpose(
            r.PSC[:, bh * 128:(bh + 1) * 128], r.on[:, bh * 128:(bh + 1) * 128], identb))
    ctx.inc(i, sP)
    t_otp = sP.c
    ctx.wait("vector", sP, t_otp)
    for bh in range(16):
        b, h = bh // 8, bh % 8
        dst = r.ogate[:, h, b * BW + col0: b * BW + col0 + TOKC]
        src = r.PSC[:, bh * 128 + 1:(bh + 1) * 128:2]
        gsl = r.gate[:, h, b * BW + col0: b * BW + col0 + TOKC]
        i = ctx.on("vector", lambda e, dst=dst, src=src, gsl=gsl: e.scalar_tensor_tensor(
            out=dst, in0=src, scalar=r.onw[:], in1=gsl, op0=ALU.mult, op1=ALU.mult))
    ctx.inc(i, sV)
    t_og = sV.c

    ctx.wait("tensor", sV, t_og)
    for bh in range(16):
        i = ctx.on("tensor", lambda e, bh=bh: e.transpose(
            r.PSC[:, bh * 128:(bh + 1) * 128], Kc[:, bh], identb))
    ctx.inc(i, sP)
    t_ktp = sP.c
    ctx.wait("vector", sP, t_ktp)
    for bh in range(16):
        sl = slice(bh * 128, (bh + 1) * 128)
        ctx.on("vector", lambda e, sl=sl, bh=bh: e.tensor_scalar(
            out=r.KDtm[:, sl], in0=r.PSC[:, sl],
            scalar1=r.edecf[:, bh:bh + 1], scalar2=None, op0=ALU.mult))
        ctx.on("vector", lambda e, sl=sl, bh=bh: e.tensor_scalar(
            out=r.S[:, sl], in0=r.S[:, sl],
            scalar1=r.lamlast[:, bh:bh + 1], scalar2=None, op0=ALU.mult))
    i = ctx.on("vector", lambda e: e.engine_nop())
    ctx.inc(i, sV)
    t_kdtm = sV.c
    ctx.wait("tensor", sV, t_kdtm)
    for bh in range(16):
        sl = slice(bh * 128, (bh + 1) * 128)
        i = ctx.on("tensor", lambda e, sl=sl: e.matmul(
            r.PSA[:, sl], r.KDtm[:, sl], r.Ubf[:, sl], start=True, stop=True))
    ctx.inc(i, sP)
    t_smm = sP.c
    ctx.wait("vector", sP, t_smm)
    i = ctx.on("vector", lambda e: e.tensor_add(out=r.S[:], in0=r.S[:], in1=r.PSA[:]))
    ctx.inc(i, sV)



def emit_wo_ln(ctx, d, r, sems, pc0):
    dS, dG, sP, sV, sA, st = (sems[n] for n in ["dS", "dG", "sP", "sV", "sA", "st"])
    t_og_all = sV.c
    slab_free = [sP.c] * 3
    slot_free = [sV.c] * 4
    xf_cons = [sV.c] * 2

    for ot in range(16):
        si = ot % 2
        pi = ot % 4
        sli = ot % 3
        psl = r.PSA[:, pi * 512: pi * 512 + 256]
        ctx.wait("sync", sV, xf_cons[si])
        t_xf = ctx.dma("sync", dS, r.xfr[si][:], d.x_fm[ot * 128:(ot + 1) * 128, pc0:pc0 + 256])
        ctx.wait("sync", sP, slab_free[sli])
        t_slab = ctx.dma("sync", dS, r.slab[sli][:, 0:8, :], d.wbf_o[ot])
        ctx.wait("tensor", dS, t_slab)
        ctx.wait("tensor", sV, max(slot_free[pi], t_og_all))
        mm = None
        for kk in range(8):
            mm = ctx.on("tensor", lambda e, kk=kk, psl=psl, sli=sli: e.matmul(
                psl, r.slab[sli][:, kk, :], r.ogate[:, kk, :],
                start=(kk == 0), stop=(kk == 7)))
        ctx.inc(mm, sP)
        slab_free[sli] = sP.c
        t_mm = sP.c
        ctx.wait("vector", sP, t_mm)
        ctx.wait("vector", dS, t_xf)
        i = ctx.on("vector", lambda e, psl=psl, si=si, ot=ot: e.tensor_add(
            out=r.ypre[:, ot, :], in0=psl, in1=r.xfr[si][:]))
        ctx.inc(i, sV)
        slot_free[pi] = sV.c
        xf_cons[si] = sV.c
    t_ypre = sV.c

    ctx.wait("tensor", sV, t_ypre)
    mm = None
    for ot in range(16):
        mm = ctx.on("tensor", lambda e, ot=ot: e.matmul(
            r.PSB[0:1, 0:256], r.onesb[:, 0:1], r.ypre[:, ot, :],
            start=(ot == 0), stop=(ot == 15)))
    ctx.inc(mm, sP)
    ctx.wait("scalar", sV, t_ypre)
    sq_mm = [sP.c] * 2
    t_sqr_q = []
    for ot in range(17):
        if ot < 16:
            sqi = ot % 2
            ctx.wait("scalar", sP, sq_mm[sqi])
            i = ctx.on("scalar", lambda e, ot=ot, sqi=sqi: e.activation(
                out=r.sq[sqi][:], in_=r.ypre[:, ot, :], func=AF.Square))
            ctx.inc(i, sA)
            t_sqr_q.append(sA.c)
        if ot >= 1:
            po = ot - 1
            sqi = po % 2
            ctx.wait("tensor", sA, t_sqr_q[po])
            i = ctx.on("tensor", lambda e, sqi=sqi, po=po: e.matmul(
                r.PSB[0:1, 256:512], r.onesb[:, 0:1], r.sq[sqi][:],
                start=(po == 0), stop=(po == 15)))
            ctx.inc(i, sP)
            sq_mm[sqi] = sP.c
    t_sumsq = sP.c
    ctx.wait("vector", sP, t_sumsq)
    ctx.on("vector", lambda e: e.tensor_scalar(
        out=r.lnr[0:1, 0:256], in0=r.PSB[0:1, 0:256], scalar1=-1.0 / 2048.0,
        scalar2=None, op0=ALU.mult))
    ctx.on("vector", lambda e: e.tensor_mul(
        out=r.lnr[0:1, 512:768], in0=r.lnr[0:1, 0:256], in1=r.lnr[0:1, 0:256]))
    i = ctx.on("vector", lambda e: e.scalar_tensor_tensor(
        out=r.lnr[0:1, 256:512], in0=r.PSB[0:1, 256:512], scalar=1.0 / 2048.0,
        in1=r.lnr[0:1, 512:768], op0=ALU.mult, op1=ALU.subtract))
    ctx.inc(i, sV)
    t_var = sV.c
    ctx.wait("scalar", sV, t_var)
    i = ctx.on("scalar", lambda e: e.activation(out=r.lnr[0:1, 256:512],
                                                in_=r.lnr[0:1, 256:512], func=AF.Sqrt, bias=1e-5))
    ctx.inc(i, sA)
    ctx.wait("vector", sA, sA.c)
    i = ctx.on("vector", lambda e: e.reciprocal(out=r.lnr[0:1, 256:512],
                                                in_=r.lnr[0:1, 256:512]))
    ctx.inc(i, sV)
    ctx.wait("vector", sV, sV.c)
    i = ctx.on("vector", lambda e: e.engine_nop())
    ctx.inc(i, sV)
    t_rs = sV.c
    ctx.wait("tensor", sV, t_rs)
    ctx.on("tensor", lambda e: e.matmul(
        r.PSB[:, 512:768], r.onesf[0:1, :], r.lnr[0:1, 0:256], start=True, stop=True))
    i = ctx.on("tensor", lambda e: e.matmul(
        r.PSB[:, 768:1024], r.onesf[0:1, :], r.lnr[0:1, 256:512], start=True, stop=True))
    ctx.inc(i, sP)
    t_bc = sP.c
    ctx.wait("vector", sP, t_bc)
    ctx.wait("vector", st, r.spill_st)
    for ot in range(16):
        ctx.on("vector", lambda e, ot=ot: e.tensor_add(
            out=r.lnt[:], in0=r.ypre[:, ot, :], in1=r.PSB[:, 512:768]))
        i = ctx.on("vector", lambda e, ot=ot: e.tensor_mul(
            out=r.ystg[:, ot, :], in0=r.lnt[:], in1=r.PSB[:, 768:1024]))
        ctx.inc(i, sV)
    t_y = sV.c
    ctx.wait("sync", sV, t_y)
    ctx.dma("sync", st, d.ybf.rearrange("(k p) n -> p k n", p=128)[:, :, pc0:pc0 + 256],
            r.ystg[:])
    r.spill_st = sems["st"].c



def emit_ffn_iter(ctx, d, r, sems, it, t_p0ffn, dbg_outs):
    dS, dG, sP, sV, sA, st = (sems[n] for n in ["dS", "dG", "sP", "sV", "sA", "st"])
    pc0 = it * FIT
    t_spills = st.c

    par = it % 2
    ybf_sb = r.ybf_sb2[par]
    if it == 0:
        ctx.wait("sync", st, t_spills)
        r.y_pref[0] = ctx.dma("sync", dS, ybf_sb[:],
                              d.ybf.rearrange("(k p) n -> p k n", p=128)[:, :, 0:FIT])
    t_y = r.y_pref[par]
    if it + 1 < NFI:
        ctx.wait("sync", st, t_spills)
        ctx.wait("sync", sP, r.y_used[1 - par])
        r.y_pref[1 - par] = ctx.dma(
            "sync", dS, r.ybf_sb2[1 - par][:],
            d.ybf.rearrange("(k p) n -> p k n", p=128)[:, :, (it + 1) * FIT:(it + 2) * FIT])
    ctx.wait("vector", dS, t_y)
    ctx.wait("vector", sP, sP.c)
    ctx.wait("vector", sA, sA.c)
    i = None
    for kk in range(16):
        i = ctx.on("vector", lambda e, kk=kk: e.tensor_copy(
            out=r.z[:, kk, :], in_=ybf_sb[:, kk, :]))
    ctx.inc(i, sV)
    t_z0 = sV.c

    ctx.wait("tensor", dS, t_y)
    ctx.wait("tensor", sV, t_z0)
    ctx.wait("tensor", sA, sA.c)
    mm = None
    for kk in range(16):
        mm = ctx.on("tensor", lambda e, kk=kk: e.matmul(
            r.PSB[0:16, 0:512], r.wafn_sb[:, kk, :], ybf_sb[:, kk, :],
            start=(kk == 0), stop=(kk == 15)))
    ctx.inc(mm, sP)
    t_aff = sP.c
    ctx.wait("scalar", sP, t_aff)
    i = ctx.on("scalar", lambda e: e.activation(out=r.aff[:], in_=r.PSB[0:16, 0:512],
                                                func=AF.Sigmoid))
    ctx.inc(i, sA)
    t_affs = sA.c
    ctx.wait("tensor", sA, t_affs)
    for q4 in range(4):
        i = ctx.on("tensor", lambda e, q4=q4: e.transpose(
            r.PSB[:, 512 + q4 * 16: 512 + (q4 + 1) * 16],
            r.aff[:, q4 * 128:(q4 + 1) * 128], r.identf[:][0:16, 0:16]))
    ctx.inc(i, sP)
    t_at = sP.c
    ctx.wait("vector", sP, t_at)
    i = ctx.on("vector", lambda e: e.tensor_copy(
        out=r.affT[:].rearrange("p a b -> p (a b)"), in_=r.PSB[:, 512:576]))
    ctx.inc(i, sV)
    ctx.wait("vector", sV, sV.c)
    av = r.affT[:]
    i = ctx.on("vector", lambda e: e.tensor_reduce(out=r.m1[:], in_=av, axis=AX.X, op=ALU.max))
    ctx.inc(i, sV)
    ctx.wait("vector", sV, sV.c)
    for q4 in range(4):
        i = ctx.on("vector", lambda e, q4=q4: e.tensor_scalar(
            out=r.eq[:, q4, :], in0=av[:, q4, :], scalar1=r.m1[:, q4:q4 + 1],
            scalar2=None, op0=ALU.is_ge))
    ctx.inc(i, sV)
    ctx.wait("vector", sV, sV.c)
    ctx.on("vector", lambda e: e.scalar_tensor_tensor(
        out=r.combT[:].rearrange("p a b -> p (a b)"),
        in0=r.eq[:].rearrange("p a b -> p (a b)"), scalar=-1e30,
        in1=av.rearrange("p a b -> p (a b)"), op0=ALU.mult, op1=ALU.add))
    i = ctx.on("vector", lambda e: e.engine_nop())
    ctx.inc(i, sV)
    ctx.wait("vector", sV, sV.c)
    i = ctx.on("vector", lambda e: e.tensor_reduce(
        out=r.m2[:], in_=r.combT[:], axis=AX.X, op=ALU.max))
    ctx.inc(i, sV)
    ctx.wait("vector", sV, sV.c)
    i = ctx.on("vector", lambda e: e.tensor_add(out=r.den[:], in0=r.m1[:], in1=r.m2[:]))
    ctx.inc(i, sV)
    ctx.wait("scalar", sV, sV.c)
    i = ctx.on("scalar", lambda e: e.activation(out=r.den[:], in_=r.den[:], func=AF.Ln))
    ctx.inc(i, sA)
    ctx.wait("scalar", sA, sA.c)
    i = ctx.on("scalar", lambda e: e.activation(out=r.den[:], in_=r.den[:], func=AF.Exp,
                                                scale=-1.0))
    ctx.inc(i, sA)
    ctx.wait("vector", sA, sA.c)
    for q4 in range(4):
        i = ctx.on("vector", lambda e, q4=q4: e.tensor_scalar(
            out=r.eq[:, q4, :], in0=av[:, q4, :], scalar1=r.m2[:, q4:q4 + 1],
            scalar2=None, op0=ALU.is_ge))
    ctx.inc(i, sV)
    ctx.wait("vector", sV, sV.c)
    for q4 in range(4):
        i = ctx.on("vector", lambda e, q4=q4: e.tensor_mul(
            out=r.combT[:, q4, :], in0=av[:, q4, :], in1=r.eq[:, q4, :]))
    ctx.inc(i, sV)
    ctx.wait("vector", sV, sV.c)
    for q4 in range(4):
        i = ctx.on("vector", lambda e, q4=q4: e.tensor_scalar(
            out=r.combT[:, q4, :], in0=r.combT[:, q4, :], scalar1=r.den[:, q4:q4 + 1],
            scalar2=None, op0=ALU.mult))
    ctx.inc(i, sV)
    t_combT = sV.c
    ctx.wait("tensor", sV, t_combT)
    for q4 in range(4):
        i = ctx.on("tensor", lambda e, q4=q4: e.transpose(
            r.PSB[0:16, 512 + q4 * 128: 512 + (q4 + 1) * 128],
            r.combT[:, q4, :], r.identf[:]))
    ctx.inc(i, sP)
    t_cbt = sP.c
    ctx.wait("vector", sP, t_cbt)
    i = ctx.on("vector", lambda e: e.tensor_copy(out=r.comb[:], in_=r.PSB[0:16, 512:1024]))
    ctx.inc(i, sV)
    t_comb = sV.c

    if DBG and it == 0 and "comb" in dbg_outs:
        ctx.wait("sync", sV, t_comb)
        ctx.wait("sync", sA, sA.c)
        ctx.dma("sync", st, dbg_outs["comb"], r.comb[:])
        ctx.dma("sync", st, dbg_outs["affdbg"], r.aff[:])
        ctx.dma("sync", st, dbg_outs["affT"], r.affT[:])
        ctx.dma("sync", st, dbg_outs["m1"], r.m1[:])
        ctx.dma("sync", st, dbg_outs["m2"], r.m2[:])
        ctx.dma("sync", st, dbg_outs["den"], r.den[:])
        ctx.dma("sync", st, dbg_outs["combT"], r.combT[:])
        t_tap = sems["st"].c
        ctx.wait("vector", st, t_tap)
        ctx.wait("tensor", st, t_tap)

    wsrc = lambda ee: (d.rg_bf[ee], d.rw1_bf[ee], d.rw2_bf[ee]) if ee < E_RT else \
                      (d.sg_bf[ee - E_RT], d.sw1_bf[ee - E_RT], d.sw2_bf[ee - E_RT])
    bias_g = lambda ee, ft: r.rgb[:, ee, ft:ft + 1] if ee < E_RT else r.sgb[:, ee - E_RT, ft:ft + 1]
    bias_1 = lambda ee, ft: r.rb1[:, ee, ft:ft + 1] if ee < E_RT else r.sb1[:, ee - E_RT, ft:ft + 1]

    wa_free = [sP.c] * 3
    wb_free = [sP.c] * 3
    w2_free = [sP.c] * 3
    seqA = 0
    seqW2 = 0
    t_g1free = sV.c
    pb_v = [sV.c] * 2
    pb_a = [sA.c] * 2
    z_free = [sV.c] * 4
    t_combbc_cons = sV.c
    u_free = sP.c

    for grp in range(6):
        experts = [grp * 3, grp * 3 + 1, grp * 3 + 2]
        for gi, ee in enumerate(experts):
            srcs = wsrc(ee)
            if ee < E_RT:
                ctx.wait("tensor", sV, max(t_comb, t_combbc_cons))
                i = ctx.on("tensor", lambda e, ee=ee: e.matmul(
                    r.PSB[:, 0:512], r.sel16[:, ee * 128:(ee + 1) * 128], r.comb[:],
                    start=True, stop=True))
                ctx.inc(i, sP)
                t_cbc = sP.c
            for fp in range(4):
                sa_i = seqA % 3
                seqA += 1
                ctx.wait("sync", dG, t_p0ffn)
                ctx.wait("sync", sP, wa_free[sa_i])
                t_wa = ctx.dma("sync", dS, r.wA[sa_i][:], srcs[0][fp])
                ctx.wait("sync", sP, wb_free[sa_i])
                t_wb = ctx.dma("sync", dS, r.wB[sa_i][:], srcs[1][fp])
                for fc2 in range(2):
                    fc = fp * 2 + fc2
                    psl_g = r.PSA[:, fc2 * 1024: fc2 * 1024 + 512]
                    psl_1 = r.PSA[:, fc2 * 1024 + 512: fc2 * 1024 + 1024]
                    ctx.wait("tensor", dS, max(t_wa, t_wb))
                    ctx.wait("tensor", sV, pb_v[fc2])
                    ctx.wait("tensor", sA, pb_a[fc2])
                    mm = None
                    for kk in range(16):
                        mm = ctx.on("tensor", lambda e, kk=kk, sa_i=sa_i, fc2=fc2, psl_g=psl_g: e.matmul(
                            psl_g, r.wA[sa_i][:, kk, fc2 * 128:(fc2 + 1) * 128],
                            ybf_sb[:, kk, :], start=(kk == 0), stop=(kk == 15)))
                    for kk in range(16):
                        mm = ctx.on("tensor", lambda e, kk=kk, sa_i=sa_i, fc2=fc2, psl_1=psl_1: e.matmul(
                            psl_1, r.wB[sa_i][:, kk, fc2 * 128:(fc2 + 1) * 128],
                            ybf_sb[:, kk, :], start=(kk == 0), stop=(kk == 15)))
                    ctx.inc(mm, sP)
                    t_h = sP.c
                    ctx.wait("scalar", sP, t_h)
                    ctx.wait("scalar", sV, t_g1free)
                    i = ctx.on("scalar", lambda e, psl_g=psl_g, ee=ee, fc=fc: e.activation(
                        out=r.g1[:], in_=psl_g, func=AF.Gelu, bias=bias_g(ee, fc)))
                    ctx.inc(i, sA)
                    pb_a[fc2] = sA.c
                    ctx.wait("vector", sA, pb_a[fc2])
                    ctx.wait("vector", sP, u_free)
                    if ee < E_RT:
                        ctx.wait("vector", sP, t_cbc)
                        i = ctx.on("vector", lambda e, psl_1=psl_1, ee=ee, fc=fc: e.scalar_tensor_tensor(
                            out=r.u0[:], in0=psl_1, scalar=bias_1(ee, fc), in1=r.g1[:],
                            op0=ALU.add, op1=ALU.mult))
                        ctx.inc(i, sV)
                        t_g1free = sV.c
                        i = ctx.on("vector", lambda e, gi=gi, fc=fc: e.tensor_mul(
                            out=r.u[gi][:, fc, :], in0=r.u0[:], in1=r.PSB[:, 0:512]))
                        ctx.inc(i, sV)
                        t_combbc_cons = sV.c
                    else:
                        i = ctx.on("vector", lambda e, psl_1=psl_1, ee=ee, fc=fc, gi=gi: e.scalar_tensor_tensor(
                            out=r.u[gi][:, fc, :], in0=psl_1, scalar=bias_1(ee, fc), in1=r.g1[:],
                            op0=ALU.add, op1=ALU.mult))
                        ctx.inc(i, sV)
                        t_g1free = sV.c
                    pb_v[fc2] = sV.c
                wa_free[sa_i] = sP.c
                wb_free[sa_i] = sP.c
        t_u_grp = sV.c
        t_u_grp_a = sA.c

        for ps4 in range(4):
            rdy = []
            for gi, ee in enumerate(experts):
                w2i = seqW2 % 3
                seqW2 += 1
                ctx.wait("sync", sP, w2_free[w2i])
                t_rdy = ctx.dma("sync", dS, r.w2s[w2i][:], wsrc(ee)[2][ps4])
                rdy.append((w2i, t_rdy))
            for dc4 in range(4):
                dc = ps4 * 4 + dc4
                psl = r.PSA[:, dc4 * 512:(dc4 + 1) * 512]
                ctx.wait("tensor", sV, max(t_u_grp, z_free[dc4]))
                ctx.wait("tensor", sA, t_u_grp_a)
                ntot = 24 + (2 if grp == 0 else 0)
                nmm = 0
                mm = None
                for gi, ee in enumerate(experts):
                    w2i, t_rdy = rdy[gi]
                    ctx.wait("tensor", dS, t_rdy)
                    for fc in range(8):
                        mm = ctx.on("tensor", lambda e, w2i=w2i, fc=fc, dc4=dc4, psl=psl,
                                    gi=gi, nmm=nmm, ntot=ntot: e.matmul(
                            psl, r.w2s[w2i][:, fc, dc4 * 128:(dc4 + 1) * 128],
                            r.u[gi][:, fc, :], start=(nmm == 0), stop=(nmm == ntot - 1)))
                        nmm += 1
                if grp == 0:
                    mm = ctx.on("tensor", lambda e, dc=dc, psl=psl: e.matmul(
                        psl, r.b2r_sb[:, dc * 128:(dc + 1) * 128], r.comb[:],
                        start=False, stop=False, skip_group_check=True))
                    mm = ctx.on("tensor", lambda e, dc=dc, psl=psl: e.matmul(
                        psl, r.b2s_sb[:, dc * 128:(dc + 1) * 128], r.ones2[:],
                        start=False, stop=True, skip_group_check=True))
                ctx.inc(mm, sP)
                t_w2 = sP.c
                ctx.wait("vector", sP, t_w2)
                ctx.wait("vector", sV, t_z0)
                if grp == 0:
                    i = ctx.on("vector", lambda e, dc=dc, psl=psl: e.scalar_tensor_tensor(
                        out=r.z[:, dc, :], in0=r.z[:, dc, :], scalar=2.0, in1=psl,
                        op0=ALU.mult, op1=ALU.add))
                else:
                    i = ctx.on("vector", lambda e, dc=dc, psl=psl: e.tensor_add(
                        out=r.z[:, dc, :], in0=r.z[:, dc, :], in1=psl))
                ctx.inc(i, sV)
                z_free[dc4] = sV.c
            for gi in range(3):
                w2_free[rdy[gi][0]] = sP.c
        pb_v = [sV.c] * 2
        u_free = sP.c
    r.y_used[par] = sP.c

    t_zdone = sV.c
    if DBG and it == 0 and "zdbg" in dbg_outs:
        ctx.wait("sync", sV, t_zdone)
        ctx.dma("sync", st, dbg_outs["zdbg"], r.z[:])
    ctx.wait("tensor", sV, t_zdone)
    mm = None
    for ot in range(16):
        mm = ctx.on("tensor", lambda e, ot=ot: e.matmul(
            r.PSB[0:1, 0:512], r.onesf[:, 0:1], r.z[:, ot, :],
            start=(ot == 0), stop=(ot == 15)))
    ctx.inc(mm, sP)
    ctx.wait("scalar", sV, t_zdone)
    sq_mm = [sP.c] * 2
    scr = [r.g1, r.u0]
    for ot in range(16):
        sqi = ot % 2
        ctx.wait("scalar", sP, sq_mm[sqi])
        i = ctx.on("scalar", lambda e, ot=ot, sqi=sqi: e.activation(
            out=scr[sqi][:], in_=r.z[:, ot, :], func=AF.Square))
        ctx.inc(i, sA)
        t_sqr = sA.c
        ctx.wait("tensor", sA, t_sqr)
        i = ctx.on("tensor", lambda e, sqi=sqi, ot=ot: e.matmul(
            r.PSB[0:1, 512:1024], r.onesf[:, 0:1], scr[sqi][:],
            start=(ot == 0), stop=(ot == 15)))
        ctx.inc(i, sP)
        sq_mm[sqi] = sP.c
    t_sumsq = sP.c
    ctx.wait("vector", sP, t_sumsq)
    ctx.on("vector", lambda e: e.tensor_scalar(
        out=r.flnr[0:1, 0:512], in0=r.PSB[0:1, 0:512], scalar1=-1.0 / 2048.0,
        scalar2=None, op0=ALU.mult))
    ctx.on("vector", lambda e: e.tensor_mul(
        out=r.flnr[0:1, 1024:1536], in0=r.flnr[0:1, 0:512], in1=r.flnr[0:1, 0:512]))
    i = ctx.on("vector", lambda e: e.scalar_tensor_tensor(
        out=r.flnr[0:1, 512:1024], in0=r.PSB[0:1, 512:1024], scalar=1.0 / 2048.0,
        in1=r.flnr[0:1, 1024:1536], op0=ALU.mult, op1=ALU.subtract))
    ctx.inc(i, sV)
    t_var = sV.c
    ctx.wait("scalar", sV, t_var)
    i = ctx.on("scalar", lambda e: e.activation(out=r.flnr[0:1, 512:1024],
                                                in_=r.flnr[0:1, 512:1024], func=AF.Sqrt, bias=1e-5))
    ctx.inc(i, sA)
    ctx.wait("vector", sA, sA.c)
    i = ctx.on("vector", lambda e: e.reciprocal(out=r.flnr[0:1, 512:1024],
                                                in_=r.flnr[0:1, 512:1024]))
    ctx.inc(i, sV)
    ctx.wait("vector", sV, sV.c)
    i = ctx.on("vector", lambda e: e.engine_nop())
    ctx.inc(i, sV)
    t_rs = sV.c
    ctx.wait("tensor", sV, t_rs)
    ctx.on("tensor", lambda e: e.matmul(
        r.PSA[:, 0:512], r.onesf[0:1, :], r.flnr[0:1, 0:512], start=True, stop=True))
    i = ctx.on("tensor", lambda e: e.matmul(
        r.PSA[:, 512:1024], r.onesf[0:1, :], r.flnr[0:1, 512:1024], start=True, stop=True))
    ctx.inc(i, sP)
    t_bc = sP.c
    ctx.wait("vector", sP, t_bc)
    ost_busy = [sems["st"].c] * 2
    for ot in range(16):
        oi = ot % 2
        ctx.wait("vector", st, ost_busy[oi])
        ctx.on("vector", lambda e, ot=ot, oi=oi: e.tensor_add(
            out=r.ostg[oi][:], in0=r.z[:, ot, :], in1=r.PSA[:, 0:512]))
        i = ctx.on("vector", lambda e, ot=ot, oi=oi: e.tensor_mul(
            out=r.ostg[oi][:], in0=r.ostg[oi][:], in1=r.PSA[:, 512:1024]))
        ctx.inc(i, sV)
        t_o = sV.c
        ctx.wait("sync", sV, t_o)
        ost_busy[oi] = ctx.dma("sync", st, d.out[ot * 128:(ot + 1) * 128, pc0:pc0 + FIT],
                               r.ostg[oi][:])



def _stage_weights(inp):
    g = {}
    T = lambda a: np.ascontiguousarray(a)
    Wq = inp["L0_attn_Wq"]; Wk = inp["L0_attn_Wk"]; Wv = inp["L0_attn_Wv"]
    Wb = inp["L0_attn_Wb"]; Wa = inp["L0_attn_Wa"]; Wg = inp["L0_attn_Wg"]
    Wo = inp["L0_attn_Wo"]
    g["wq_l"] = T(Wq.T.reshape(16, 128, 8, 128).transpose(2, 1, 0, 3))
    g["wk_l"] = T(Wk.T.reshape(16, 128, 16, 128).transpose(2, 1, 0, 3))
    g["wv_l"] = T(Wv.T.reshape(16, 128, 16, 128).transpose(2, 1, 0, 3))
    g["wg_l"] = T(Wg.T.reshape(16, 128, 8, 128).transpose(2, 1, 0, 3))
    g["wo_l"] = T(Wo.T.reshape(8, 128, 16, 128).transpose(2, 1, 0, 3))
    g["wb_l"] = T(Wb.T.reshape(16, 128, 16).transpose(1, 0, 2))
    g["wa_l"] = T(Wa.T.reshape(16, 128, 8).transpose(1, 0, 2))
    g["wafn_l"] = T(inp["L0_ffn_wa"].reshape(16, 128, 16).transpose(1, 0, 2))
    g["cq_l"] = T(inp["L0_attn_conv_q"].reshape(8, 128, 4).transpose(1, 0, 2))
    g["ck_l"] = T(inp["L0_attn_conv_k"].reshape(16, 128, 4).transpose(1, 0, 2))
    g["cv_l"] = T(inp["L0_attn_conv_v"].reshape(16, 128, 4).transpose(1, 0, 2))
    g["alog"] = T(inp["L0_attn_A_log"].reshape(8, 1))
    g["dtb"] = T(inp["L0_attn_dt_bias"].reshape(8, 1))
    g["onw"] = T(inp["L0_attn_o_norm_w"].reshape(128, 1))
    g["rg_l"] = T(inp["L0_ffn_rg"].reshape(16, 16, 128, 4, 256).transpose(0, 3, 2, 1, 4))
    g["rw1_l"] = T(inp["L0_ffn_rw1"].reshape(16, 16, 128, 4, 256).transpose(0, 3, 2, 1, 4))
    g["rw2_l"] = T(inp["L0_ffn_rw2"].reshape(16, 8, 128, 4, 512).transpose(0, 3, 2, 1, 4))
    g["sg_l"] = T(inp["L0_ffn_sg"].reshape(2, 16, 128, 4, 256).transpose(0, 3, 2, 1, 4))
    g["sw1_l"] = T(inp["L0_ffn_sw1"].reshape(2, 16, 128, 4, 256).transpose(0, 3, 2, 1, 4))
    g["sw2_l"] = T(inp["L0_ffn_sw2"].reshape(2, 8, 128, 4, 512).transpose(0, 3, 2, 1, 4))
    g["rgb_l"] = T(inp["L0_ffn_rgb"].reshape(16, 8, 128).transpose(2, 0, 1))
    g["rb1_l"] = T(inp["L0_ffn_rb1"].reshape(16, 8, 128).transpose(2, 0, 1))
    g["sgb_l"] = T(inp["L0_ffn_sgb"].reshape(2, 8, 128).transpose(2, 0, 1))
    g["sb1_l"] = T(inp["L0_ffn_sb1"].reshape(2, 8, 128).transpose(2, 0, 1))
    g["b2r"] = T(inp["L0_ffn_rb2"])
    g["b2s"] = T(inp["L0_ffn_sb2"])
    sel8 = np.zeros((8, 1024), np.float32)
    for h in range(8):
        sel8[h, h * 128:(h + 1) * 128] = 1.0
    g["sel8"] = sel8
    sel16 = np.zeros((16, 2048), np.float32)
    for j in range(16):
        sel16[j, j * 128:(j + 1) * 128] = 1.0
    g["sel16"] = sel16.astype(ml_dtypes.bfloat16)
    g["maskneg"] = np.where(np.triu(np.ones((128, 128)), 0) > 0, 0.0, -1e9).astype(np.float32)
    g["identf"] = np.eye(128, dtype=np.float32)
    g["identb"] = np.eye(128).astype(ml_dtypes.bfloat16)
    return g


def _stage_x(x, core):
    xc = x[2 * core:2 * core + 2]
    xr = xc.reshape(2, NBLK, NHALF, BW, D)
    return np.ascontiguousarray(xr.transpose(4, 1, 2, 0, 3).reshape(D, NPCOL))


def _unstage_out(out_fm):
    of = out_fm.reshape(D, NBLK, NHALF, 2, BW)
    return np.ascontiguousarray(of.transpose(3, 1, 2, 4, 0).reshape(2, 2048, D))


def kernel(**inp):
    nc = build_program()
    g = _stage_weights(inp)
    x = np.asarray(inp["L0_attn_x"])
    in_maps = []
    for core in range(NCORE):
        m = dict(g)
        m["x_fm"] = _stage_x(x, core)
        in_maps.append(m)
    res = run_bass_kernel_spmd(nc, in_maps, core_ids=list(range(NCORE)))
    out = np.empty((16, 2048, 2048), np.float32)
    for core in range(NCORE):
        out[2 * core:2 * core + 2] = _unstage_out(res.results[core]["out_fm"])
    return out


# version 06
# speedup vs baseline: 1.1030x; 1.0011x over previous
import numpy as np
import ml_dtypes
from contextlib import ExitStack
import concourse.bass as bass
import concourse.mybir as mybir
from concourse.bass_utils import run_bass_kernel_spmd

F32 = mybir.dt.float32
BF16 = mybir.dt.bfloat16
AF = mybir.ActivationFunctionType
ALU = mybir.AluOpType
AX = mybir.AxisListType

ENGINES = ["sync", "gpsimd", "vector", "scalar", "tensor"]

NCORE = 8
D = 2048
KT = 16
NBLK = 8
NHALF = 2
BW = 128
HALF_COLS = 256
TOKC = 64
NCHH = 2
NPCOL = 4096
E_RT, E_SH, FF = 16, 2, 1024
NEXP = E_RT + E_SH
FIT = 512
NFI = NPCOL // FIT
SQ128 = 11.313708498984761

DBG = False
DBG_HB = 0
DBG_TB = 0
PIPE_B = 1
PIPE_D = 2


class Sem:
    def __init__(self, handle):
        self.h = handle
        self.c = 0


class Ctx:
    def __init__(self):
        self.cur = None
        self.eng = None

    def set_pass(self, name, eng, sems):
        self.cur = name
        self.eng = eng
        for s in sems:
            s.c = 0

    def on(self, engine_name, fn):
        if self.cur == engine_name:
            return fn(self.eng)
        return None

    def inc(self, inst, sem, n=1):
        sem.c += n
        if inst is not None:
            inst.then_inc(sem.h, n)

    def wait(self, engine_name, sem, value=None):
        if value is None:
            value = sem.c
        if self.cur == engine_name:
            self.eng.wait_ge(sem.h, value)

    def dma(self, engine_name, sem, out, in_, **kw):
        sem.c += 16
        if self.cur == engine_name:
            self.eng.dma_start(out=out, in_=in_, **kw).then_inc(sem.h, 16)
        return sem.c


class NS:
    pass


DBG_TAPS = [
    ("Kb", [128, NCHH, 16, 128], BF16),
    ("Vb", [128, NCHH, 16, 128], BF16),
    ("Qb", [128, NCHH, 16, 128], BF16),
    ("gate", [128, 8, HALF_COLS], BF16),
    ("beta0", [8, HALF_COLS], F32),
    ("beta1", [8, HALF_COLS], F32),
    ("g8", [8, HALF_COLS], F32),
    ("og", [128, 8, HALF_COLS], BF16),
    ("ypre", [128, 16, HALF_COLS], BF16),
    ("ystg", [128, 16, HALF_COLS], BF16),
    ("comb", [16, FIT], BF16),
    ("affdbg", [16, FIT], F32),
    ("affT", [128, 4, 16], F32),
    ("m1", [128, 4], F32),
    ("m2", [128, 4], F32),
    ("den", [128, 4], F32),
    ("combT", [128, 4, 16], F32),
    ("zdbg", [128, 16, FIT], F32),
]


def _vb_view(t, nh, h):
    v4 = t[:].rearrange("p c (bb hh) t -> p bb c hh t", bb=2)
    return v4[:, :, :, h, nh::2]



def build_program():
    nc = bass.Bass("TRN2", target_bir_lowering=False)
    d = NS()

    def din(name, shape, dt=F32):
        setattr(d, name, nc.dram_tensor(name, list(shape), dt, kind="ExternalInput").ap())

    din("x_fm", [D, NPCOL])
    din("wq_l", [8, 128, 16, 128])
    din("wk_l", [16, 128, 16, 128])
    din("wv_l", [16, 128, 16, 128])
    din("wg_l", [8, 128, 16, 128])
    din("wo_l", [16, 128, 8, 128])
    din("wb_l", [128, 16, 16])
    din("wa_l", [128, 16, 8])
    din("wafn_l", [128, 16, 16])
    din("cq_l", [128, 8, 4])
    din("ck_l", [128, 16, 4])
    din("cv_l", [128, 16, 4])
    din("alog", [8, 1])
    din("dtb", [8, 1])
    din("onw", [128, 1])
    din("rg_l", [E_RT, 4, 128, 16, 256])
    din("rw1_l", [E_RT, 4, 128, 16, 256])
    din("rw2_l", [E_RT, 4, 128, 8, 512])
    din("sg_l", [E_SH, 4, 128, 16, 256])
    din("sw1_l", [E_SH, 4, 128, 16, 256])
    din("sw2_l", [E_SH, 4, 128, 8, 512])
    din("rgb_l", [128, E_RT, 8])
    din("rb1_l", [128, E_RT, 8])
    din("sgb_l", [128, E_SH, 8])
    din("sb1_l", [128, E_SH, 8])
    din("b2r", [E_RT, D])
    din("b2s", [E_SH, D])
    din("sel8", [8, 1024])
    din("sel16", [16, 2048], BF16)
    din("maskneg", [128, 128])
    din("identf", [128, 128])
    din("identb", [128, 128], BF16)

    d.out = nc.dram_tensor("out_fm", [D, NPCOL], F32, kind="ExternalOutput").ap()

    def dint(name, shape, dt):
        setattr(d, name, nc.dram_tensor(name, list(shape), dt, kind="Internal").ap())

    dint("wbf_q", [8, 128, 16, 128], BF16)
    dint("wbf_k", [16, 128, 16, 128], BF16)
    dint("wbf_v", [16, 128, 16, 128], BF16)
    dint("wbf_g", [8, 128, 16, 128], BF16)
    dint("wbf_o", [16, 128, 8, 128], BF16)
    dint("rg_bf", [E_RT, 4, 128, 16, 256], BF16)
    dint("rw1_bf", [E_RT, 4, 128, 16, 256], BF16)
    dint("rw2_bf", [E_RT, 4, 128, 8, 512], BF16)
    dint("sg_bf", [E_SH, 4, 128, 16, 256], BF16)
    dint("sw1_bf", [E_SH, 4, 128, 16, 256], BF16)
    dint("sw2_bf", [E_SH, 4, 128, 8, 512], BF16)
    dint("ybf", [D, NPCOL], BF16)

    dbg_outs = {}
    if DBG:
        for nm, shape, dt in DBG_TAPS:
            dbg_outs[nm] = nc.dram_tensor("dbg_" + nm, list(shape), dt, kind="ExternalOutput").ap()

    stack = ExitStack()
    e = stack.enter_context
    r = NS()
    r.nc = nc

    r.sel8 = e(nc.sbuf_tensor("p_sel8", [8, 1024], F32))
    r.sel16 = e(nc.sbuf_tensor("p_sel16", [16, 2048], BF16))
    r.maskneg = e(nc.sbuf_tensor("p_mask", [128, 128], F32))
    r.identf = e(nc.sbuf_tensor("p_idf", [128, 128], F32))
    r.identb = e(nc.sbuf_tensor("p_idb", [128, 128], BF16))
    r.onesb = e(nc.sbuf_tensor("p_onesb", [128, 128], BF16))
    r.onesf = e(nc.sbuf_tensor("p_onesf", [128, 128], F32))
    r.ones2 = e(nc.sbuf_tensor("p_ones2", [2, 512], BF16))
    r.cst = e(nc.sbuf_tensor("p_cst", [128, 8], F32))
    r.wb_sb = e(nc.sbuf_tensor("p_wb", [128, 16, 16], BF16))
    r.wa_sb = e(nc.sbuf_tensor("p_wa", [128, 16, 8], BF16))
    r.wafn_sb = e(nc.sbuf_tensor("p_wafn", [128, 16, 16], BF16))
    r.b2r_sb = e(nc.sbuf_tensor("p_b2r", [16, 2048], BF16))
    r.b2s_sb = e(nc.sbuf_tensor("p_b2s", [2, 2048], BF16))
    r.cq = e(nc.sbuf_tensor("p_cq", [128, 8, 4], F32))
    r.ck = e(nc.sbuf_tensor("p_ck", [128, 16, 4], F32))
    r.cv = e(nc.sbuf_tensor("p_cv", [128, 16, 4], F32))
    r.nexpA = e(nc.sbuf_tensor("p_nexpA", [8, 1], F32))
    r.dtb = e(nc.sbuf_tensor("p_dtb", [8, 1], F32))
    r.onw = e(nc.sbuf_tensor("p_onw", [128, 1], F32))
    r.rgb = e(nc.sbuf_tensor("p_rgb", [128, E_RT, 8], F32))
    r.rb1 = e(nc.sbuf_tensor("p_rb1", [128, E_RT, 8], F32))
    r.sgb = e(nc.sbuf_tensor("p_sgb", [128, E_SH, 8], F32))
    r.sb1 = e(nc.sbuf_tensor("p_sb1", [128, E_SH, 8], F32))
    r.hist = e(nc.sbuf_tensor("p_hist", [128, 40, 2, 4], BF16))
    r.carry = e(nc.sbuf_tensor("p_carry", [8, 2], F32))
    r.carry0 = e(nc.sbuf_tensor("p_carry0", [8, 2], F32))

    nc.const_aps.register(F32, 1e-5, r.cst[:, 0:1])
    nc.const_aps.register(F32, 1.0 / 128.0, r.cst[:, 1:2])
    nc.const_aps.register(F32, 1.0, r.cst[:, 2:3])
    nc.const_aps.register(F32, -1.0, r.cst[:, 3:4])

    pa_stack = ExitStack()
    e = pa_stack.enter_context
    r.x_bf = e(nc.sbuf_tensor("a_xbf", [128, 16, HALF_COLS], BF16))
    r.slab = [e(nc.sbuf_tensor(f"a_slab{i}", [128, 16, 128], BF16)) for i in range(3)]
    r.pre = [e(nc.sbuf_tensor(f"a_pre{i}", [128, 2, 132], BF16)) for i in range(3)]
    r.cvo = [e(nc.sbuf_tensor(f"a_cvo{i}", [128, 2, 128], BF16)) for i in range(2)]
    r.qs = [e(nc.sbuf_tensor(f"a_qs{i}", [128, HALF_COLS], BF16)) for i in range(3)]
    r.sq = [e(nc.sbuf_tensor(f"a_sq{i}", [128, HALF_COLS], BF16)) for i in range(2)]
    r.srow = [e(nc.sbuf_tensor(f"a_srow{i}", [1, HALF_COLS], BF16)) for i in range(3)]
    r.Kb = e(nc.sbuf_tensor("a_Kb", [128, NCHH, 16, 128], BF16))
    r.Vb = e(nc.sbuf_tensor("a_Vb", [128, NCHH, 16, 128], BF16))
    r.Qb = e(nc.sbuf_tensor("a_Qb", [128, NCHH, 16, 128], BF16))
    r.gate = e(nc.sbuf_tensor("a_gate", [128, 8, HALF_COLS], BF16))
    r.beta_nh = [e(nc.sbuf_tensor(f"a_beta{p}", [8, HALF_COLS], F32)) for p in range(2)]
    r.g8 = e(nc.sbuf_tensor("a_g8", [8, HALF_COLS], F32))
    r.sp1 = e(nc.sbuf_tensor("a_sp1", [8, HALF_COLS], F32))
    r.sp2 = e(nc.sbuf_tensor("a_sp2", [8, HALF_COLS], F32))
    r.cumA = e(nc.sbuf_tensor("a_cumA", [8, HALF_COLS], F32))
    r.cumB = e(nc.sbuf_tensor("a_cumB", [8, HALF_COLS], F32))
    r.ogate = e(nc.sbuf_tensor("a_og", [128, 8, HALF_COLS], BF16))
    r.ypre = e(nc.sbuf_tensor("a_ypre", [128, 16, HALF_COLS], BF16))
    r.xfr = [e(nc.sbuf_tensor(f"a_xfr{i}", [128, HALF_COLS], F32)) for i in range(2)]
    r.ystg = e(nc.sbuf_tensor("a_ystg", [128, 16, HALF_COLS], BF16))
    r.lnt = e(nc.sbuf_tensor("a_lnt", [128, HALF_COLS], F32))
    r.lnr = e(nc.sbuf_tensor("a_lnr", [1, 1024], F32))
    r.S = e(nc.sbuf_tensor("s_S", [128, 2048], F32))
    r.S0bf = e(nc.sbuf_tensor("s_S0bf", [128, 2048], BF16))
    r.cumg_sub = e(nc.sbuf_tensor("s_cgs", [8, 256], F32))
    r.beta_sub = e(nc.sbuf_tensor("s_bts", [8, 256], F32))
    r.cumg_T = e(nc.sbuf_tensor("s_cgT", [128, 16], F32))
    r.beta_T = e(nc.sbuf_tensor("s_btT", [128, 16], F32))
    r.lamc = e(nc.sbuf_tensor("s_lamc", [128, 16], F32))
    r.rstd = e(nc.sbuf_tensor("s_rstd", [128, 16], F32))
    r.ssb = e(nc.sbuf_tensor("s_ssb", [128, 16], F32))
    r.edecf = e(nc.sbuf_tensor("s_edecf", [128, 16], F32))
    r.lamlast = e(nc.sbuf_tensor("s_lamlast", [128, 16], F32))
    r.lam_row = e(nc.sbuf_tensor("s_lamr", [128, 2048], BF16))
    r.E = e(nc.sbuf_tensor("s_E", [128, 2048], BF16))
    r.tmp_bf = e(nc.sbuf_tensor("s_tmpbf", [128, 2048], BF16))
    r.Nt = e(nc.sbuf_tensor("s_Nt", [128, 2048], BF16))
    r.Nn = e(nc.sbuf_tensor("s_Nn", [128, 2048], BF16))
    r.Nt2 = e(nc.sbuf_tensor("s_Nt2", [128, 2048], BF16))
    r.Nn2 = e(nc.sbuf_tensor("s_Nn2", [128, 2048], BF16))
    r.Wt = e(nc.sbuf_tensor("s_Wt", [128, 2048], BF16))
    r.Vtm = e(nc.sbuf_tensor("s_Vtm", [128, 2048], BF16))
    r.KDtm = e(nc.sbuf_tensor("s_KDtm", [128, 2048], BF16))
    r.U32 = e(nc.sbuf_tensor("s_U32", [128, 2048], F32))
    r.Ubf = e(nc.sbuf_tensor("s_Ubf", [128, 2048], BF16))
    r.t32 = e(nc.sbuf_tensor("s_t32", [128, 1024], F32))
    r.sqbf = e(nc.sbuf_tensor("s_sqbf", [128, 2048], BF16))
    r.on = e(nc.sbuf_tensor("s_on", [128, 2048], BF16))
    pa_stack.close()
    e = stack.enter_context
    r.ybf_sb2 = [e(nc.sbuf_tensor(f"f_ybf{i}", [128, 16, FIT], BF16)) for i in range(2)]
    r.z = e(nc.sbuf_tensor("f_z", [128, 16, FIT], F32))
    r.u = [e(nc.sbuf_tensor(f"f_u{i}", [128, 8, FIT], BF16)) for i in range(3)]
    r.wA = [e(nc.sbuf_tensor(f"f_wA{i}", [128, 16, 256], BF16)) for i in range(3)]
    r.wB = [e(nc.sbuf_tensor(f"f_wB{i}", [128, 16, 256], BF16)) for i in range(3)]
    r.w2s = [e(nc.sbuf_tensor(f"f_w2s{i}", [128, 8, 512], BF16)) for i in range(3)]
    r.aff = e(nc.sbuf_tensor("f_aff", [16, FIT], F32))
    r.affT = e(nc.sbuf_tensor("f_affT", [128, 4, 16], F32))
    r.combT = e(nc.sbuf_tensor("f_combT", [128, 4, 16], F32))
    r.comb = e(nc.sbuf_tensor("f_comb", [16, FIT], BF16))
    r.m1 = e(nc.sbuf_tensor("f_m1", [128, 4], F32))
    r.m2 = e(nc.sbuf_tensor("f_m2", [128, 4], F32))
    r.den = e(nc.sbuf_tensor("f_den", [128, 4], F32))
    r.eq = e(nc.sbuf_tensor("f_eq", [128, 4, 16], F32))
    r.g1 = e(nc.sbuf_tensor("f_g1", [128, FIT], F32))
    r.u0 = e(nc.sbuf_tensor("f_uzero", [128, FIT], F32))
    r.ostg = [e(nc.sbuf_tensor(f"f_ostg{i}", [128, FIT], F32)) for i in range(2)]
    r.flnr = e(nc.sbuf_tensor("f_lnr", [1, 2048], F32))

    r.PSA = e(nc.psum_tensor("ps_A", [128, 2048], F32))
    r.PSB = e(nc.psum_tensor("ps_B", [128, 1024], F32))
    r.PSC = e(nc.psum_tensor("ps_C", [128, 2048], BF16))

    sems = {n: Sem(e(nc.semaphore(n))) for n in ["dS", "dG", "sP", "sV", "sA", "st", "sG"]}
    block = e(nc.Block())
    ctx = Ctx()

    def emit():
        emit_all(ctx, nc, d, r, sems, dbg_outs)

    for en in ENGINES:
        def run_pass(eng, en=en):
            ctx.set_pass(en, eng, list(sems.values()))
            emit()
        getattr(block, en)(run_pass)
    stack.close()
    return nc



def emit_all(ctx, nc, d, r, sems, dbg_outs):
    dS, dG, sP, sV, sA, st, sG = (sems[n] for n in ["dS", "dG", "sP", "sV", "sA", "st", "sG"])

    for name, sb in [("sel8", r.sel8), ("sel16", r.sel16), ("maskneg", r.maskneg),
                     ("identf", r.identf), ("identb", r.identb),
                     ("cq_l", r.cq), ("ck_l", r.ck), ("cv_l", r.cv),
                     ("dtb", r.dtb), ("onw", r.onw),
                     ("rgb_l", r.rgb), ("rb1_l", r.rb1), ("sgb_l", r.sgb), ("sb1_l", r.sb1)]:
        ctx.dma("sync", dS, sb[:], getattr(d, name))
    ctx.dma("sync", dS, r.nexpA[:], d.alog)
    t_cl = dS.c

    ctx.dma("gpsimd", dG, r.wb_sb[:], d.wb_l)
    ctx.dma("gpsimd", dG, r.wa_sb[:], d.wa_l)
    ctx.dma("gpsimd", dG, r.wafn_sb[:], d.wafn_l)
    ctx.dma("gpsimd", dG, r.b2r_sb[:], d.b2r)
    ctx.dma("gpsimd", dG, r.b2s_sb[:], d.b2s)
    t_smallcast = dG.c
    for (src, dst) in [(d.wq_l, d.wbf_q), (d.wk_l, d.wbf_k), (d.wv_l, d.wbf_v),
                       (d.wg_l, d.wbf_g), (d.wo_l, d.wbf_o)]:
        ctx.dma("gpsimd", dG, dst.rearrange("a p k o -> (a p k o)"),
                src.rearrange("a p k o -> (a p k o)"))
    t_p0attn = dG.c
    for (src, dst, ne) in [(d.rg_l, d.rg_bf, E_RT), (d.rw1_l, d.rw1_bf, E_RT),
                           (d.rw2_l, d.rw2_bf, E_RT), (d.sg_l, d.sg_bf, E_SH),
                           (d.sw1_l, d.sw1_bf, E_SH), (d.sw2_l, d.sw2_bf, E_SH)]:
        for ee in range(ne):
            ctx.dma("gpsimd", dG, dst[ee].rearrange("a p k o -> (a p k o)"),
                    src[ee].rearrange("a p k o -> (a p k o)"))
    t_p0ffn = dG.c

    ctx.wait("vector", dS, t_cl)
    for fn in [lambda e: e.memset(r.onesb[:], 1.0),
               lambda e: e.memset(r.onesf[:], 1.0),
               lambda e: e.memset(r.ones2[:], 1.0),
               lambda e: e.memset(r.cst[:, 0:1], 1e-5),
               lambda e: e.memset(r.cst[:, 1:2], 1.0 / 128.0),
               lambda e: e.memset(r.cst[:, 2:3], 1.0),
               lambda e: e.memset(r.cst[:, 3:4], -1.0),
               lambda e: e.memset(r.S[:], 0.0),
               lambda e: e.memset(r.carry[:, 0:2], 0.0),
               lambda e: e.memset(r.carry0[:, 0:2], 0.0),
               lambda e: e.memset(r.Qb[:], 0.0),
               lambda e: e.memset(r.hist[:], 0.0)]:
        i = ctx.on("vector", fn)
    ctx.inc(i, sV)
    t_vinit = sV.c
    ctx.wait("scalar", sV, t_vinit)
    ctx.on("scalar", lambda e: e.activation(out=r.nexpA[:], in_=r.nexpA[:], func=AF.Exp))
    i = ctx.on("scalar", lambda e: e.mul(out=r.nexpA[:], in_=r.nexpA[:], mul=-1.0))
    ctx.inc(i, sA)
    ctx.wait("tensor", dS, t_cl)
    ctx.wait("tensor", dG, t_smallcast)
    ctx.wait("tensor", sV, t_vinit)

    r.spill_st = 0
    for tb in range(NBLK):
        for hb in range(NHALF):
            emit_half(ctx, d, r, sems, tb, hb, t_p0attn, dbg_outs)

    r.y_pref = [0, 0]
    r.y_used = [0, 0]
    for it in range(NFI):
        emit_ffn_iter(ctx, d, r, sems, it, t_p0ffn, dbg_outs)

    ctx.wait("sync", st, st.c)



def emit_half(ctx, d, r, sems, tb, hb, t_p0attn, dbg_outs):
    dS, dG, sP, sV, sA, st, sG = (sems[n] for n in ["dS", "dG", "sP", "sV", "sA", "st", "sG"])
    pc0 = tb * 512 + hb * 256
    t_entry_sV = sems["sV"].c

    t_xfree = sP.c
    ctx.wait("gpsimd", sP, t_xfree)
    xsrc = d.x_fm.rearrange("(k p) n -> p k n", p=128)[:, :, pc0:pc0 + 256]
    t_x = ctx.dma("gpsimd", dG, r.x_bf[:], xsrc)

    tiles = [("a", 0), ("b", 0), ("b", 1)] + [("q", j) for j in range(8)] + \
            [("k", j) for j in range(16)] + [("v", j) for j in range(16)] + \
            [("gg", j) for j in range(8)]
    slab_src = {"q": d.wbf_q, "k": d.wbf_k, "v": d.wbf_v, "gg": d.wbf_g}
    conv_w = {"q": r.cq, "k": r.ck, "v": r.cv}
    hist_base = {"q": 0, "k": 8, "v": 24}

    slot_free = [sV.c] * 4
    slot_free_a = [sA.c] * 4
    slab_free = [sP.c] * 3
    slab_ready = [0] * 3
    pre_free = [sA.c] * 3
    qs_free = [sA.c] * 3
    cvo_free = [sA.c] * 2
    sq_free = [sP.c] * 2
    srow_free = [sP.c] * 3
    nrm_q = 0
    ssrow_free = [sA.c] * 2
    sbc_free = [sV.c] * 2
    lnr_free = [sV.c] * 2
    bank4 = {"rd": sA.c}
    bank5 = {"rd": sV.c}

    pend = []

    def run_pending(cur_ti, flush=False):
        for ent in list(pend):
            if ent["stage"] == "B" and (flush or cur_ti - ent["ti"] >= PIPE_B):
                ent["fnB"]()
                ent["fnC"]()
                ent["stage"] = "D"
            if ent["stage"] == "D" and (flush or cur_ti - ent["ti"] >= PIPE_D):
                ent["fnD"]()
                pend.remove(ent)
        if flush and pend:
            run_pending(cur_ti, flush=True)

    for ti, (kind, j) in enumerate(tiles):
        si = ti % 3
        pi = ti % 4
        psl = r.PSA[:, pi * 512: pi * 512 + 256]

        if kind in slab_src:
            ctx.wait("sync", dG, t_p0attn)
            ctx.wait("sync", sP, slab_free[si])
            slab_ready[si] = ctx.dma("sync", dS, r.slab[si][:], slab_src[kind][j])

        ctx.wait("tensor", dG, t_x)
        ctx.wait("tensor", sV, slot_free[pi])
        ctx.wait("tensor", sA, slot_free_a[pi])
        if kind in slab_src:
            ctx.wait("tensor", dS, slab_ready[si])
            lhs = lambda kk, si=si: r.slab[si][:, kk, :]
            M = 128
        elif kind == "a":
            lhs = lambda kk: r.wa_sb[:, kk, :]
            M = 8
        else:
            lhs = lambda kk, j=j: r.wb_sb[:, kk, j * 8:(j + 1) * 8]
            M = 8
        mm = None
        for kk in range(16):
            mm = ctx.on("tensor", lambda e, kk=kk, lhs=lhs, psl=psl, M=M: e.matmul(
                psl[0:M, :] if M != 128 else psl, lhs(kk), r.x_bf[:, kk, :],
                start=(kk == 0), stop=(kk == 15)))
        ctx.inc(mm, sP)
        if kind in slab_src:
            slab_free[si] = sP.c
        t_mm = sP.c

        if kind == "a":
            ctx.wait("vector", sP, t_mm)
            i = ctx.on("vector", lambda e, pi=pi: e.tensor_scalar(
                out=r.sp1[:], in0=r.PSA[0:8, pi * 512: pi * 512 + 256], scalar1=r.dtb[:],
                scalar2=None, op0=ALU.add))
            ctx.inc(i, sV)
            slot_free[pi] = sV.c
            ctx.wait("scalar", sV, sV.c)
            i = ctx.on("scalar", lambda e: e.activation(out=r.sp2[:], in_=r.sp1[:],
                                                        func=AF.Abs))
            ctx.inc(i, sA)
            ctx.wait("scalar", sA, sA.c)
            i = ctx.on("scalar", lambda e: e.activation(out=r.sp2[:], in_=r.sp2[:],
                                                        func=AF.Exp, scale=-1.0))
            ctx.inc(i, sA)
            ctx.wait("scalar", sA, sA.c)
            i = ctx.on("scalar", lambda e: e.activation(out=r.sp2[:], in_=r.sp2[:],
                                                        func=AF.Ln, bias=1.0))
            ctx.inc(i, sA)
            ctx.wait("vector", sA, sA.c)
            ctx.on("vector", lambda e: e.tensor_scalar(
                out=r.sp1[:], in0=r.sp1[:], scalar1=0.0, scalar2=None, op0=ALU.max))
            ctx.on("vector", lambda e: e.tensor_add(out=r.g8[:], in0=r.sp1[:], in1=r.sp2[:]))
            i = ctx.on("vector", lambda e: e.tensor_scalar(
                out=r.g8[:], in0=r.g8[:], scalar1=r.nexpA[:], scalar2=None, op0=ALU.mult))
            ctx.inc(i, sV)
        elif kind == "b":
            ctx.wait("scalar", sP, t_mm)
            ctx.wait("scalar", sV, t_entry_sV)
            i = ctx.on("scalar", lambda e, j=j, pi=pi: e.activation(
                out=r.beta_nh[j][:], in_=r.PSA[0:8, pi * 512: pi * 512 + 256], func=AF.Sigmoid))
            ctx.inc(i, sA)
            slot_free_a[pi] = sA.c
        elif kind == "gg":
            ctx.wait("scalar", sP, t_mm)
            ctx.wait("scalar", sV, t_entry_sV)
            i = ctx.on("scalar", lambda e, j=j, psl=psl: e.activation(
                out=r.gate[:, j, :], in_=psl, func=AF.Silu))
            ctx.inc(i, sA)
            slot_free_a[pi] = sA.c
        else:
            pri = ti % 3
            ci = ti % 2
            hb_i = hist_base[kind] + j
            ctx.wait("vector", sA, pre_free[pri])
            ctx.on("vector", lambda e, pri=pri, hb_i=hb_i: e.tensor_copy(
                out=r.pre[pri][:, :, 0:4], in_=r.hist[:, hb_i, :, :]))
            ctx.wait("vector", sP, t_mm)
            pv = psl.rearrange("p (b t) -> p b t", b=2)
            i = ctx.on("vector", lambda e, pri=pri, pv=pv: e.tensor_copy(
                out=r.pre[pri][:, :, 4:132], in_=pv))
            ctx.inc(i, sV)
            slot_free[pi] = sV.c
            ctx.wait("vector", sV, sV.c)
            ctx.on("vector", lambda e, pri=pri, hb_i=hb_i: e.tensor_copy(
                out=r.hist[:, hb_i, :, 1:4], in_=r.pre[pri][:, :, 129:132]))
            cw = conv_w[kind]
            ctx.wait("vector", sA, cvo_free[ci])
            ctx.on("vector", lambda e, pri=pri, ci=ci, cw=cw, j=j: e.tensor_scalar(
                out=r.cvo[ci][:], in0=r.pre[pri][:, :, 1:129], scalar1=cw[:, j, 0:1],
                scalar2=None, op0=ALU.mult))
            for tap in range(1, 4):
                i = ctx.on("vector", lambda e, pri=pri, ci=ci, cw=cw, j=j, tap=tap: e.scalar_tensor_tensor(
                    out=r.cvo[ci][:], in0=r.pre[pri][:, :, 1 + tap:129 + tap],
                    scalar=cw[:, j, tap:tap + 1], in1=r.cvo[ci][:],
                    op0=ALU.mult, op1=ALU.add))
            ctx.inc(i, sV)
            t_cv = sV.c
            ctx.wait("scalar", sV, t_cv)
            if kind == "v":
                nh, h = (0, j) if j < 8 else (1, j - 8)
                vout = _vb_view(r.Vb, nh, h)
                i = ctx.on("scalar", lambda e, ci=ci, vout=vout: e.activation(
                    out=vout, in_=r.cvo[ci][:].rearrange("p b (c i) -> p b c i", c=NCHH),
                    func=AF.Silu))
                ctx.inc(i, sA)
                pre_free[pri] = sA.c
                cvo_free[ci] = sA.c
            else:
                n = nrm_q
                nrm_q += 1
                qi = n % 3
                sqi = n % 2
                sri = n % 3
                lo = 512 + (n % 2) * 256
                ro = (n % 2) * 256
                bo = 512 + (n % 2) * 256
                ctx.wait("scalar", sV, qs_free[qi])
                i = ctx.on("scalar", lambda e, ci=ci, qi=qi: e.activation(
                    out=r.qs[qi][:].rearrange("p (b t) -> p b t", b=2), in_=r.cvo[ci][:],
                    func=AF.Silu))
                ctx.inc(i, sA)
                pre_free[pri] = sA.c
                cvo_free[ci] = sA.c
                ctx.wait("scalar", sP, sq_free[sqi])
                ctx.wait("scalar", sV, lnr_free[n % 2])
                i = ctx.on("scalar", lambda e, qi=qi, sqi=sqi: e.activation(
                    out=r.sq[sqi][:], in_=r.qs[qi][:], func=AF.Square))
                ctx.inc(i, sA)
                t_sqr = sA.c
                sc2 = SQ128 if kind == "q" else 1.0
                nh, h = (0, j) if j < 8 else (1, j - 8)
                dst = _vb_view(r.Qb, 1, j) if kind == "q" else _vb_view(r.Kb, nh, h)

                ent = {"ti": ti, "stage": "B"}

                def fnB(t_sqr=t_sqr, sqi=sqi, ro=ro, n=n, ent=ent):
                    ctx.wait("tensor", sA, max(t_sqr, ssrow_free[n % 2], bank4["rd"]))
                    i = ctx.on("tensor", lambda e: e.matmul(
                        r.PSB[0:1, ro:ro + 256], r.onesb[:, 0:1], r.sq[sqi][:],
                        start=True, stop=True))
                    ctx.inc(i, sP)
                    sq_free[sqi] = sP.c
                    ent["t_ssm"] = sP.c

                def fnC(sri=sri, ro=ro, lo=lo, sc2=sc2, n=n, ent=ent):
                    ctx.wait("scalar", sP, ent["t_ssm"])
                    i = ctx.on("scalar", lambda e: e.activation(
                        out=r.lnr[0:1, lo:lo + 256], in_=r.PSB[0:1, ro:ro + 256], func=AF.Sqrt))
                    ctx.inc(i, sA)
                    ssrow_free[n % 2] = sA.c
                    bank4["rd"] = sA.c
                    t_sq2 = sA.c
                    ctx.wait("vector", sA, t_sq2)
                    ctx.wait("vector", sP, srow_free[sri])
                    ctx.on("vector", lambda e: e.tensor_scalar(
                        out=r.lnr[0:1, lo:lo + 256], in0=r.lnr[0:1, lo:lo + 256], scalar1=1e-12,
                        scalar2=sc2, op0=ALU.max, op1=ALU.mult))
                    def _recip(e):
                        with r.nc.allow_low_precision(reason="bf16 norm-scale for bf16 matmul"):
                            return e.reciprocal(out=r.srow[sri][:], in_=r.lnr[0:1, lo:lo + 256])
                    i = ctx.on("vector", _recip)
                    ctx.inc(i, sV)
                    ctx.wait("vector", sV, sV.c)
                    i = ctx.on("vector", lambda e: e.engine_nop())
                    ctx.inc(i, sV)
                    ent["t_srow"] = sV.c
                    lnr_free[n % 2] = sV.c

                def fnD(sri=sri, bo=bo, qi=qi, dst=dst, n=n, ent=ent):
                    ctx.wait("tensor", sV, max(ent["t_srow"], sbc_free[n % 2], bank5["rd"]))
                    i = ctx.on("tensor", lambda e: e.matmul(
                        r.PSB[:, bo:bo + 256], r.onesb[0:1, :], r.srow[sri][:],
                        start=True, stop=True))
                    ctx.inc(i, sP)
                    t_bc = sP.c
                    srow_free[sri] = sP.c
                    ctx.wait("vector", sP, t_bc)
                    i = ctx.on("vector", lambda e: e.tensor_mul(
                        out=dst,
                        in0=r.qs[qi][:].rearrange("p (b c i) -> p b c i", b=2, c=NCHH),
                        in1=r.PSB[:, bo:bo + 256].rearrange("p (b c i) -> p b c i", b=2, c=NCHH)))
                    ctx.inc(i, sV)
                    qs_free[qi] = sV.c
                    sbc_free[n % 2] = sV.c
                    bank5["rd"] = sV.c

                ent["fnB"], ent["fnC"], ent["fnD"] = fnB, fnC, fnD
                pend.append(ent)
        run_pending(ti)
    run_pending(len(tiles), flush=True)

    emit_cumsum_half(ctx, r, sP, sV, sA)

    consts = {"sel8": r.sel8[:], "maskneg": r.maskneg[:], "identf": r.identf[:],
              "identb": r.identb[:]}
    for c in range(NCHH):
        emit_chunk(ctx, r, c, consts, sP, sV, sA)

    emit_wo_ln(ctx, d, r, sems, pc0)

    if DBG and tb == DBG_TB and hb == DBG_HB:
        for nm, src in [("Kb", r.Kb), ("Vb", r.Vb), ("Qb", r.Qb), ("gate", r.gate),
                        ("beta0", r.beta_nh[0]), ("beta1", r.beta_nh[1]), ("g8", r.g8),
                        ("og", r.ogate), ("ypre", r.ypre), ("ystg", r.ystg)]:
            if nm in dbg_outs:
                ctx.wait("sync", sV, sV.c)
                ctx.wait("sync", sA, sA.c)
                ctx.dma("sync", st, dbg_outs[nm], src[:])



def emit_cumsum_half(ctx, r, sP, sV, sA):
    def cp(dst, src):
        ctx.on("vector", lambda e: e.tensor_copy(out=dst, in_=src))

    cp(r.carry0[:, 0:2], r.carry[:, 0:2])
    src, dst = r.cumA, r.cumB
    cp(src[:, 0:256], r.g8[:, 0:256])
    s = 1
    while s < BW:
        sv = src[:].rearrange("p (b t) -> p b t", b=2)
        dv = dst[:].rearrange("p (b t) -> p b t", b=2)
        cp(dv[:, :, 0:s], sv[:, :, 0:s])
        ctx.on("vector", lambda e, sv=sv, dv=dv, s=s: e.tensor_add(
            out=dv[:, :, s:BW], in0=sv[:, :, s:BW], in1=sv[:, :, 0:BW - s]))
        src, dst = dst, src
        s *= 2
    for b in range(2):
        ctx.on("vector", lambda e, b=b, src=src: e.tensor_scalar(
            out=src[:, b * BW:(b + 1) * BW], in0=src[:, b * BW:(b + 1) * BW],
            scalar1=r.carry[:, b:b + 1], scalar2=None, op0=ALU.add))
    if src is not r.cumA:
        cp(r.cumA[:, 0:256], src[:, 0:256])
    for b in range(2):
        cp(r.carry[:, b:b + 1], r.cumA[:, b * BW + BW - 1:b * BW + BW])
    i = ctx.on("vector", lambda e: e.engine_nop())
    ctx.inc(i, sV)



def emit_chunk(ctx, r, c, consts, sP, sV, sA):
    sel8, maskneg, identf, identb = consts["sel8"], consts["maskneg"], consts["identf"], consts["identb"]
    col0 = c * TOKC
    Kc = r.Kb[:, c]
    Vc = r.Vb[:, c]
    Qc = r.Qb[:, c]

    t_pa = sA.c
    ctx.wait("tensor", sA, t_pa)

    ctx.wait("scalar", sV, sV.c)
    i = ctx.on("scalar", lambda e: e.copy(out=r.S0bf[:], in_=r.S[:]))
    ctx.inc(i, sA)
    t_s0bf = sA.c

    for b in range(2):
        src = r.cumA[:, b * BW + col0: b * BW + col0 + TOKC]
        dst = r.cumg_sub[:, b * 128:(b + 1) * 128].rearrange("p (i two) -> p i two", two=2)
        srcb = src.unsqueeze(2).broadcast_to([8, TOKC, 2])
        ctx.on("vector", lambda e, dst=dst, srcb=srcb: e.tensor_copy(out=dst, in_=srcb))
        if col0 == 0:
            base = r.carry0[:, b:b + 1]
        else:
            base = r.cumA[:, b * BW + col0 - 1: b * BW + col0]
        ctx.on("vector", lambda e, b=b, base=base: e.tensor_scalar(
            out=r.cumg_sub[:, b * 128:(b + 1) * 128], in0=r.cumg_sub[:, b * 128:(b + 1) * 128],
            scalar1=base, scalar2=None, op0=ALU.subtract))
    for b in range(2):
        for p in range(2):
            dst = r.beta_sub[:, b * 128 + p: (b + 1) * 128: 2]
            src = r.beta_nh[p][:, b * BW + col0: b * BW + col0 + TOKC]
            ctx.on("vector", lambda e, dst=dst, src=src: e.tensor_copy(out=dst, in_=src))
    i = ctx.on("vector", lambda e: e.engine_nop())
    ctx.inc(i, sV)
    t_prep = sV.c

    ctx.wait("tensor", sV, t_prep)
    for b in range(2):
        ctx.on("tensor", lambda e, b=b: e.transpose(
            r.PSB[:, b * 8:(b + 1) * 8], r.cumg_sub[:, b * 128:(b + 1) * 128], identf[0:8, 0:8]))
    for b in range(2):
        i = ctx.on("tensor", lambda e, b=b: e.transpose(
            r.PSB[:, 16 + b * 8: 16 + (b + 1) * 8], r.beta_sub[:, b * 128:(b + 1) * 128],
            identf[0:8, 0:8]))
    ctx.inc(i, sP)
    t_tp1 = sP.c

    ctx.wait("vector", sP, t_tp1)
    ctx.on("vector", lambda e: e.tensor_copy(out=r.cumg_T[:], in_=r.PSB[:, 0:16]))
    i = ctx.on("vector", lambda e: e.tensor_copy(out=r.beta_T[:], in_=r.PSB[:, 16:32]))
    ctx.inc(i, sV)
    t_tp1e = sV.c
    ctx.wait("scalar", sV, t_tp1e)
    i = ctx.on("scalar", lambda e: e.activation(out=r.lamc[:], in_=r.cumg_T[:], func=AF.Exp))
    ctx.inc(i, sA)
    t_lamc = sA.c

    for bh in range(16):
        b, h = bh // 8, bh % 8
        i = ctx.on("tensor", lambda e, b=b, h=h, bh=bh: e.matmul(
            r.PSA[:, bh * 128:(bh + 1) * 128], sel8[:, h * 128:(h + 1) * 128],
            r.cumg_sub[:, b * 128:(b + 1) * 128], start=True, stop=True))
    ctx.inc(i, sP)
    t_cgrow = sP.c

    ctx.wait("scalar", sP, t_cgrow)
    i = ctx.on("scalar", lambda e: e.activation(out=r.lam_row[:], in_=r.PSA[:], func=AF.Exp))
    ctx.inc(i, sA)
    mview = maskneg.unsqueeze(1).broadcast_to([128, 8, 128])
    for half in range(2):
        cslice = r.cumg_T[:, half * 8:(half + 1) * 8]
        colrep = cslice.unsqueeze(2).broadcast_to([128, 8, 128])
        pa = r.PSA[:, half * 1024:(half + 1) * 1024].rearrange("p (a b) -> p a b", b=128)
        t3 = r.t32[:].rearrange("p (a b) -> p a b", b=128)
        ctx.wait("vector", sA, sA.c)
        ctx.on("vector", lambda e, pa=pa, colrep=colrep, t3=t3: e.tensor_sub(out=t3, in0=pa, in1=colrep))
        ctx.on("vector", lambda e, t3=t3, mview=mview: e.tensor_add(out=t3, in0=t3, in1=mview))
        i = ctx.on("vector", lambda e: e.engine_nop())
        ctx.inc(i, sV)
        ctx.wait("scalar", sV, sV.c)
        i = ctx.on("scalar", lambda e, half=half: e.activation(
            out=r.E[:, half * 1024:(half + 1) * 1024], in_=r.t32[:], func=AF.Exp))
        ctx.inc(i, sA)
    t_E = sA.c
    ctx.wait("vector", sA, t_E)
    ctx.on("vector", lambda e: e.tensor_copy(out=r.edecf[:], in_=r.E[:, 127::128]))
    ctx.on("vector", lambda e: e.tensor_copy(out=r.lamlast[:], in_=r.lam_row[:, 127::128]))
    t_psa_free_v = sV.c
    t_psa_free_a = sA.c

    ctx.wait("tensor", sV, t_psa_free_v)
    ctx.wait("tensor", sA, t_psa_free_a)
    for bh in range(16):
        i = ctx.on("tensor", lambda e, bh=bh: e.matmul(
            r.PSA[:, bh * 128:(bh + 1) * 128], Kc[:, bh], Kc[:, bh], start=True, stop=True))
    ctx.inc(i, sP)
    t_A = sP.c
    iview = identb.unsqueeze(1).broadcast_to([128, 16, 128])
    Ev = r.E[:].rearrange("p (a b) -> p a b", b=128)
    tbv = r.tmp_bf[:].rearrange("p (a b) -> p a b", b=128)
    ctx.wait("vector", sA, t_E)
    ctx.on("vector", lambda e: e.tensor_sub(out=tbv, in0=Ev, in1=iview))
    ctx.wait("vector", sP, t_A)
    i = ctx.on("vector", lambda e: e.tensor_mul(out=r.Nt[:], in0=r.PSA[:], in1=r.tmp_bf[:]))
    ctx.inc(i, sV)
    t_AE = sV.c
    ctx.wait("tensor", sV, t_AE)
    for bh in range(16):
        i = ctx.on("tensor", lambda e, bh=bh: e.matmul(
            r.PSA[:, bh * 128:(bh + 1) * 128], Kc[:, bh], Qc[:, bh], start=True, stop=True))
    ctx.inc(i, sP)
    t_KQ = sP.c
    ctx.wait("vector", sP, t_KQ)
    i = ctx.on("vector", lambda e: e.tensor_mul(out=r.Wt[:], in0=r.PSA[:], in1=r.E[:]))
    ctx.inc(i, sV)
    t_W = sV.c
    ctx.wait("tensor", sV, t_W)
    for bh in range(16):
        b, h = bh // 8, bh % 8
        i = ctx.on("tensor", lambda e, b=b, h=h, bh=bh: e.matmul(
            r.PSA[:, bh * 128:(bh + 1) * 128], sel8[:, h * 128:(h + 1) * 128],
            r.beta_sub[:, b * 128:(b + 1) * 128], start=True, stop=True))
    ctx.inc(i, sP)
    t_brow = sP.c
    ctx.wait("vector", sP, t_brow)
    i = ctx.on("vector", lambda e: e.scalar_tensor_tensor(
        out=r.Nt[:], in0=r.Nt[:], scalar=2.0, in1=r.PSA[:], op0=ALU.mult, op1=ALU.mult))
    ctx.inc(i, sV)
    t_Nt = sV.c

    ctx.wait("tensor", sV, t_Nt)
    for bh in range(16):
        i = ctx.on("tensor", lambda e, bh=bh: e.transpose(
            r.PSC[:, bh * 128:(bh + 1) * 128], r.Nt[:, bh * 128:(bh + 1) * 128], identb))
    ctx.inc(i, sP)
    t_Ntp = sP.c
    ctx.wait("vector", sP, t_Ntp)
    i = ctx.on("vector", lambda e: e.tensor_copy(out=r.Nn[:], in_=r.PSC[:]))
    ctx.inc(i, sV)
    t_Nn = sV.c

    ctx.wait("tensor", sV, t_Nn)
    for bh in range(16):
        i = ctx.on("tensor", lambda e, bh=bh: e.transpose(
            r.PSC[:, bh * 128:(bh + 1) * 128], Vc[:, bh], identb))
    ctx.inc(i, sP)
    t_Vtp = sP.c
    ctx.wait("vector", sP, t_Vtp)
    i = ctx.on("vector", lambda e: e.tensor_copy(out=r.Vtm[:], in_=r.PSC[:]))
    ctx.inc(i, sV)
    t_Vtm = sV.c

    ctx.wait("tensor", sA, t_s0bf)
    for bh in range(16):
        i = ctx.on("tensor", lambda e, bh=bh: e.matmul(
            r.PSA[:, bh * 128:(bh + 1) * 128], Kc[:, bh],
            r.S0bf[:, bh * 128:(bh + 1) * 128], start=True, stop=True))
    ctx.inc(i, sP)
    t_KS0 = sP.c
    ctx.wait("vector", sP, t_KS0)
    ctx.wait("vector", sA, t_lamc)
    for bh in range(16):
        sl = slice(bh * 128, (bh + 1) * 128)
        ctx.on("vector", lambda e, sl=sl, bh=bh: e.scalar_tensor_tensor(
            out=r.U32[:, sl], in0=r.PSA[:, sl], scalar=r.lamc[:, bh:bh + 1],
            in1=r.Vtm[:, sl], op0=ALU.mult, op1=ALU.subtract))
        ctx.on("vector", lambda e, sl=sl, bh=bh: e.tensor_scalar(
            out=r.U32[:, sl], in0=r.U32[:, sl], scalar1=r.beta_T[:, bh:bh + 1],
            scalar2=-2.0, op0=ALU.mult, op1=ALU.mult))
    i = ctx.on("vector", lambda e: e.engine_nop())
    ctx.inc(i, sV)
    t_R = sV.c
    ctx.wait("scalar", sV, t_R)
    i = ctx.on("scalar", lambda e: e.copy(out=r.Ubf[:], in_=r.U32[:]))
    ctx.inc(i, sA)
    t_ubf_prev = sA.c

    Acur, Bcur = r.Nt, r.Nn
    Anew, Bnew = r.Nt2, r.Nn2
    p = 1
    t_psa_free = t_R
    psb_free = t_tp1e
    while p <= 64:
        ctx.wait("tensor", sA, t_ubf_prev)
        ctx.wait("tensor", sV, t_psa_free)
        for bh in range(16):
            sl = slice(bh * 128, (bh + 1) * 128)
            i = ctx.on("tensor", lambda e, sl=sl, Acur=Acur: e.matmul(
                r.PSA[:, sl], Acur[:, sl], r.Ubf[:, sl], start=True, stop=True))
        ctx.inc(i, sP)
        t_ap = sP.c
        ctx.wait("vector", sP, t_ap)
        op = ALU.subtract if p == 1 else ALU.add
        i = ctx.on("vector", lambda e, op=op: e.tensor_tensor(
            out=r.U32[:], in0=r.U32[:], in1=r.PSA[:], op=op))
        ctx.inc(i, sV)
        t_psa_free = sV.c
        ctx.wait("scalar", sV, t_psa_free)
        i = ctx.on("scalar", lambda e: e.copy(out=r.Ubf[:], in_=r.U32[:]))
        ctx.inc(i, sA)
        t_ubf_prev = sA.c
        if p < 64:
            for half, (src_l, src_r, dstb) in enumerate(
                    [(Bcur, Acur, Anew), (Bcur, Acur, Anew), (Acur, Bcur, Bnew), (Acur, Bcur, Bnew)]):
                hsl = slice((half % 2) * 1024, (half % 2) * 1024 + 1024)
                ctx.wait("tensor", sV, psb_free)
                for bh8 in range(8):
                    bh = (half % 2) * 8 + bh8
                    sl = slice(bh * 128, (bh + 1) * 128)
                    psl2 = r.PSB[:, bh8 * 128:(bh8 + 1) * 128]
                    i = ctx.on("tensor", lambda e, sl=sl, psl2=psl2, src_l=src_l, src_r=src_r: e.matmul(
                        psl2, src_l[:, sl], src_r[:, sl], start=True, stop=True))
                ctx.inc(i, sP)
                t_d = sP.c
                ctx.wait("vector", sP, t_d)
                i = ctx.on("vector", lambda e, dstb=dstb, hsl=hsl: e.tensor_copy(
                    out=dstb[:, hsl], in_=r.PSB[:]))
                ctx.inc(i, sV)
                psb_free = sV.c
            Acur, Anew = Anew, Acur
            Bcur, Bnew = Bnew, Bcur
            t_psa_free = max(t_psa_free, psb_free)
        p *= 2

    i = ctx.on("vector", lambda e: e.tensor_mul(
        out=r.tmp_bf[:].rearrange("p (a b) -> p a b", b=128), in0=Qc,
        in1=r.lam_row[:].rearrange("p (a b) -> p a b", b=128)))
    ctx.inc(i, sV)
    t_qlam = sV.c
    ctx.wait("tensor", sV, t_qlam)
    ctx.wait("tensor", sA, t_ubf_prev)
    for bh in range(16):
        sl = slice(bh * 128, (bh + 1) * 128)
        ctx.on("tensor", lambda e, sl=sl: e.matmul(
            r.PSA[:, sl], r.tmp_bf[:, sl], r.S0bf[:, sl], start=True, stop=False))
        i = ctx.on("tensor", lambda e, sl=sl: e.matmul(
            r.PSA[:, sl], r.Wt[:, sl], r.Ubf[:, sl], start=False, stop=True))
    ctx.inc(i, sP)
    t_O = sP.c

    ctx.wait("scalar", sP, t_O)
    i = ctx.on("scalar", lambda e: e.activation(out=r.sqbf[:], in_=r.PSA[:], func=AF.Square))
    ctx.inc(i, sA)
    t_sq = sA.c
    ctx.wait("vector", sA, t_sq)
    i = ctx.on("vector", lambda e: e.tensor_reduce(
        out=r.ssb[:], in_=r.sqbf[:].rearrange("p (a b) -> p a b", b=128), axis=AX.X, op=ALU.add))
    ctx.inc(i, sV)
    ctx.wait("scalar", sV, sV.c)
    i = ctx.on("scalar", lambda e: e.activation(out=r.rstd[:], in_=r.ssb[:], func=AF.Sqrt,
                                                scale=1.0 / 128.0, bias=1e-5))
    ctx.inc(i, sA)
    ctx.wait("vector", sA, sA.c)
    i = ctx.on("vector", lambda e: e.reciprocal(out=r.rstd[:], in_=r.rstd[:]))
    ctx.inc(i, sV)
    ctx.wait("vector", sV, sV.c)
    rsv = r.rstd[:].unsqueeze(2).broadcast_to([128, 16, 128])
    i = ctx.on("vector", lambda e: e.tensor_mul(
        out=r.on[:].rearrange("p (a b) -> p a b", b=128),
        in0=r.PSA[:].rearrange("p (a b) -> p a b", b=128), in1=rsv))
    ctx.inc(i, sV)
    t_on = sV.c
    ctx.wait("tensor", sV, t_on)
    for bh in range(16):
        i = ctx.on("tensor", lambda e, bh=bh: e.transpose(
            r.PSC[:, bh * 128:(bh + 1) * 128], r.on[:, bh * 128:(bh + 1) * 128], identb))
    ctx.inc(i, sP)
    t_otp = sP.c
    ctx.wait("vector", sP, t_otp)
    for bh in range(16):
        b, h = bh // 8, bh % 8
        dst = r.ogate[:, h, b * BW + col0: b * BW + col0 + TOKC]
        src = r.PSC[:, bh * 128 + 1:(bh + 1) * 128:2]
        gsl = r.gate[:, h, b * BW + col0: b * BW + col0 + TOKC]
        i = ctx.on("vector", lambda e, dst=dst, src=src, gsl=gsl: e.scalar_tensor_tensor(
            out=dst, in0=src, scalar=r.onw[:], in1=gsl, op0=ALU.mult, op1=ALU.mult))
    ctx.inc(i, sV)
    t_og = sV.c

    ctx.wait("tensor", sV, t_og)
    for bh in range(16):
        i = ctx.on("tensor", lambda e, bh=bh: e.transpose(
            r.PSC[:, bh * 128:(bh + 1) * 128], Kc[:, bh], identb))
    ctx.inc(i, sP)
    t_ktp = sP.c
    ctx.wait("vector", sP, t_ktp)
    for bh in range(16):
        sl = slice(bh * 128, (bh + 1) * 128)
        ctx.on("vector", lambda e, sl=sl, bh=bh: e.tensor_scalar(
            out=r.KDtm[:, sl], in0=r.PSC[:, sl],
            scalar1=r.edecf[:, bh:bh + 1], scalar2=None, op0=ALU.mult))
        ctx.on("vector", lambda e, sl=sl, bh=bh: e.tensor_scalar(
            out=r.S[:, sl], in0=r.S[:, sl],
            scalar1=r.lamlast[:, bh:bh + 1], scalar2=None, op0=ALU.mult))
    i = ctx.on("vector", lambda e: e.engine_nop())
    ctx.inc(i, sV)
    t_kdtm = sV.c
    ctx.wait("tensor", sV, t_kdtm)
    for bh in range(16):
        sl = slice(bh * 128, (bh + 1) * 128)
        i = ctx.on("tensor", lambda e, sl=sl: e.matmul(
            r.PSA[:, sl], r.KDtm[:, sl], r.Ubf[:, sl], start=True, stop=True))
    ctx.inc(i, sP)
    t_smm = sP.c
    ctx.wait("vector", sP, t_smm)
    i = ctx.on("vector", lambda e: e.tensor_add(out=r.S[:], in0=r.S[:], in1=r.PSA[:]))
    ctx.inc(i, sV)



def emit_wo_ln(ctx, d, r, sems, pc0):
    dS, dG, sP, sV, sA, st = (sems[n] for n in ["dS", "dG", "sP", "sV", "sA", "st"])
    t_og_all = sV.c
    slab_free = [sP.c] * 3
    slot_free = [sV.c] * 4
    xf_cons = [sV.c] * 2

    for ot in range(16):
        si = ot % 2
        pi = ot % 4
        sli = ot % 3
        psl = r.PSA[:, pi * 512: pi * 512 + 256]
        ctx.wait("sync", sV, xf_cons[si])
        t_xf = ctx.dma("sync", dS, r.xfr[si][:], d.x_fm[ot * 128:(ot + 1) * 128, pc0:pc0 + 256])
        ctx.wait("sync", sP, slab_free[sli])
        t_slab = ctx.dma("sync", dS, r.slab[sli][:, 0:8, :], d.wbf_o[ot])
        ctx.wait("tensor", dS, t_slab)
        ctx.wait("tensor", sV, max(slot_free[pi], t_og_all))
        mm = None
        for kk in range(8):
            mm = ctx.on("tensor", lambda e, kk=kk, psl=psl, sli=sli: e.matmul(
                psl, r.slab[sli][:, kk, :], r.ogate[:, kk, :],
                start=(kk == 0), stop=(kk == 7)))
        ctx.inc(mm, sP)
        slab_free[sli] = sP.c
        t_mm = sP.c
        ctx.wait("vector", sP, t_mm)
        ctx.wait("vector", dS, t_xf)
        i = ctx.on("vector", lambda e, psl=psl, si=si, ot=ot: e.tensor_add(
            out=r.ypre[:, ot, :], in0=psl, in1=r.xfr[si][:]))
        ctx.inc(i, sV)
        slot_free[pi] = sV.c
        xf_cons[si] = sV.c
    t_ypre = sV.c

    ctx.wait("tensor", sV, t_ypre)
    mm = None
    for ot in range(16):
        mm = ctx.on("tensor", lambda e, ot=ot: e.matmul(
            r.PSB[0:1, 0:256], r.onesb[:, 0:1], r.ypre[:, ot, :],
            start=(ot == 0), stop=(ot == 15)))
    ctx.inc(mm, sP)
    ctx.wait("scalar", sV, t_ypre)
    sq_mm = [sP.c] * 2
    t_sqr_q = []
    for ot in range(17):
        if ot < 16:
            sqi = ot % 2
            ctx.wait("scalar", sP, sq_mm[sqi])
            i = ctx.on("scalar", lambda e, ot=ot, sqi=sqi: e.activation(
                out=r.sq[sqi][:], in_=r.ypre[:, ot, :], func=AF.Square))
            ctx.inc(i, sA)
            t_sqr_q.append(sA.c)
        if ot >= 1:
            po = ot - 1
            sqi = po % 2
            ctx.wait("tensor", sA, t_sqr_q[po])
            i = ctx.on("tensor", lambda e, sqi=sqi, po=po: e.matmul(
                r.PSB[0:1, 256:512], r.onesb[:, 0:1], r.sq[sqi][:],
                start=(po == 0), stop=(po == 15)))
            ctx.inc(i, sP)
            sq_mm[sqi] = sP.c
    t_sumsq = sP.c
    ctx.wait("vector", sP, t_sumsq)
    ctx.on("vector", lambda e: e.tensor_scalar(
        out=r.lnr[0:1, 0:256], in0=r.PSB[0:1, 0:256], scalar1=-1.0 / 2048.0,
        scalar2=None, op0=ALU.mult))
    ctx.on("vector", lambda e: e.tensor_mul(
        out=r.lnr[0:1, 512:768], in0=r.lnr[0:1, 0:256], in1=r.lnr[0:1, 0:256]))
    i = ctx.on("vector", lambda e: e.scalar_tensor_tensor(
        out=r.lnr[0:1, 256:512], in0=r.PSB[0:1, 256:512], scalar=1.0 / 2048.0,
        in1=r.lnr[0:1, 512:768], op0=ALU.mult, op1=ALU.subtract))
    ctx.inc(i, sV)
    t_var = sV.c
    ctx.wait("scalar", sV, t_var)
    i = ctx.on("scalar", lambda e: e.activation(out=r.lnr[0:1, 256:512],
                                                in_=r.lnr[0:1, 256:512], func=AF.Sqrt, bias=1e-5))
    ctx.inc(i, sA)
    ctx.wait("vector", sA, sA.c)
    i = ctx.on("vector", lambda e: e.reciprocal(out=r.lnr[0:1, 256:512],
                                                in_=r.lnr[0:1, 256:512]))
    ctx.inc(i, sV)
    ctx.wait("vector", sV, sV.c)
    i = ctx.on("vector", lambda e: e.engine_nop())
    ctx.inc(i, sV)
    t_rs = sV.c
    ctx.wait("tensor", sV, t_rs)
    ctx.on("tensor", lambda e: e.matmul(
        r.PSB[:, 512:768], r.onesf[0:1, :], r.lnr[0:1, 0:256], start=True, stop=True))
    i = ctx.on("tensor", lambda e: e.matmul(
        r.PSB[:, 768:1024], r.onesf[0:1, :], r.lnr[0:1, 256:512], start=True, stop=True))
    ctx.inc(i, sP)
    t_bc = sP.c
    ctx.wait("vector", sP, t_bc)
    ctx.wait("vector", st, r.spill_st)
    for ot in range(16):
        ctx.on("vector", lambda e, ot=ot: e.tensor_add(
            out=r.lnt[:], in0=r.ypre[:, ot, :], in1=r.PSB[:, 512:768]))
        i = ctx.on("vector", lambda e, ot=ot: e.tensor_mul(
            out=r.ystg[:, ot, :], in0=r.lnt[:], in1=r.PSB[:, 768:1024]))
        ctx.inc(i, sV)
    t_y = sV.c
    ctx.wait("sync", sV, t_y)
    ctx.dma("sync", st, d.ybf.rearrange("(k p) n -> p k n", p=128)[:, :, pc0:pc0 + 256],
            r.ystg[:])
    r.spill_st = sems["st"].c



def emit_ffn_iter(ctx, d, r, sems, it, t_p0ffn, dbg_outs):
    dS, dG, sP, sV, sA, st = (sems[n] for n in ["dS", "dG", "sP", "sV", "sA", "st"])
    pc0 = it * FIT
    t_spills = st.c

    par = it % 2
    ybf_sb = r.ybf_sb2[par]
    if it == 0:
        ctx.wait("sync", st, t_spills)
        r.y_pref[0] = ctx.dma("sync", dS, ybf_sb[:],
                              d.ybf.rearrange("(k p) n -> p k n", p=128)[:, :, 0:FIT])
    t_y = r.y_pref[par]
    if it + 1 < NFI:
        ctx.wait("sync", st, t_spills)
        ctx.wait("sync", sP, r.y_used[1 - par])
        r.y_pref[1 - par] = ctx.dma(
            "sync", dS, r.ybf_sb2[1 - par][:],
            d.ybf.rearrange("(k p) n -> p k n", p=128)[:, :, (it + 1) * FIT:(it + 2) * FIT])
    ctx.wait("vector", dS, t_y)
    ctx.wait("vector", sP, sP.c)
    ctx.wait("vector", sA, sA.c)
    i = None
    for kk in range(16):
        i = ctx.on("vector", lambda e, kk=kk: e.tensor_copy(
            out=r.z[:, kk, :], in_=ybf_sb[:, kk, :]))
    ctx.inc(i, sV)
    t_z0 = sV.c

    ctx.wait("tensor", dS, t_y)
    ctx.wait("tensor", sV, t_z0)
    ctx.wait("tensor", sA, sA.c)
    mm = None
    for kk in range(16):
        mm = ctx.on("tensor", lambda e, kk=kk: e.matmul(
            r.PSB[0:16, 0:512], r.wafn_sb[:, kk, :], ybf_sb[:, kk, :],
            start=(kk == 0), stop=(kk == 15)))
    ctx.inc(mm, sP)
    t_aff = sP.c
    ctx.wait("scalar", sP, t_aff)
    i = ctx.on("scalar", lambda e: e.activation(out=r.aff[:], in_=r.PSB[0:16, 0:512],
                                                func=AF.Sigmoid))
    ctx.inc(i, sA)
    t_affs = sA.c
    ctx.wait("tensor", sA, t_affs)
    for q4 in range(4):
        i = ctx.on("tensor", lambda e, q4=q4: e.transpose(
            r.PSB[:, 512 + q4 * 16: 512 + (q4 + 1) * 16],
            r.aff[:, q4 * 128:(q4 + 1) * 128], r.identf[:][0:16, 0:16]))
    ctx.inc(i, sP)
    t_at = sP.c
    ctx.wait("vector", sP, t_at)
    i = ctx.on("vector", lambda e: e.tensor_copy(
        out=r.affT[:].rearrange("p a b -> p (a b)"), in_=r.PSB[:, 512:576]))
    ctx.inc(i, sV)
    ctx.wait("vector", sV, sV.c)
    av = r.affT[:]
    i = ctx.on("vector", lambda e: e.tensor_reduce(out=r.m1[:], in_=av, axis=AX.X, op=ALU.max))
    ctx.inc(i, sV)
    ctx.wait("vector", sV, sV.c)
    for q4 in range(4):
        i = ctx.on("vector", lambda e, q4=q4: e.tensor_scalar(
            out=r.eq[:, q4, :], in0=av[:, q4, :], scalar1=r.m1[:, q4:q4 + 1],
            scalar2=None, op0=ALU.is_ge))
    ctx.inc(i, sV)
    ctx.wait("vector", sV, sV.c)
    ctx.on("vector", lambda e: e.scalar_tensor_tensor(
        out=r.combT[:].rearrange("p a b -> p (a b)"),
        in0=r.eq[:].rearrange("p a b -> p (a b)"), scalar=-1e30,
        in1=av.rearrange("p a b -> p (a b)"), op0=ALU.mult, op1=ALU.add))
    i = ctx.on("vector", lambda e: e.engine_nop())
    ctx.inc(i, sV)
    ctx.wait("vector", sV, sV.c)
    i = ctx.on("vector", lambda e: e.tensor_reduce(
        out=r.m2[:], in_=r.combT[:], axis=AX.X, op=ALU.max))
    ctx.inc(i, sV)
    ctx.wait("vector", sV, sV.c)
    i = ctx.on("vector", lambda e: e.tensor_add(out=r.den[:], in0=r.m1[:], in1=r.m2[:]))
    ctx.inc(i, sV)
    ctx.wait("scalar", sV, sV.c)
    i = ctx.on("scalar", lambda e: e.activation(out=r.den[:], in_=r.den[:], func=AF.Ln))
    ctx.inc(i, sA)
    ctx.wait("scalar", sA, sA.c)
    i = ctx.on("scalar", lambda e: e.activation(out=r.den[:], in_=r.den[:], func=AF.Exp,
                                                scale=-1.0))
    ctx.inc(i, sA)
    ctx.wait("vector", sA, sA.c)
    for q4 in range(4):
        i = ctx.on("vector", lambda e, q4=q4: e.tensor_scalar(
            out=r.eq[:, q4, :], in0=av[:, q4, :], scalar1=r.m2[:, q4:q4 + 1],
            scalar2=None, op0=ALU.is_ge))
    ctx.inc(i, sV)
    ctx.wait("vector", sV, sV.c)
    for q4 in range(4):
        i = ctx.on("vector", lambda e, q4=q4: e.tensor_mul(
            out=r.combT[:, q4, :], in0=av[:, q4, :], in1=r.eq[:, q4, :]))
    ctx.inc(i, sV)
    ctx.wait("vector", sV, sV.c)
    for q4 in range(4):
        i = ctx.on("vector", lambda e, q4=q4: e.tensor_scalar(
            out=r.combT[:, q4, :], in0=r.combT[:, q4, :], scalar1=r.den[:, q4:q4 + 1],
            scalar2=None, op0=ALU.mult))
    ctx.inc(i, sV)
    t_combT = sV.c
    ctx.wait("tensor", sV, t_combT)
    for q4 in range(4):
        i = ctx.on("tensor", lambda e, q4=q4: e.transpose(
            r.PSB[0:16, 512 + q4 * 128: 512 + (q4 + 1) * 128],
            r.combT[:, q4, :], r.identf[:]))
    ctx.inc(i, sP)
    t_cbt = sP.c
    ctx.wait("vector", sP, t_cbt)
    i = ctx.on("vector", lambda e: e.tensor_copy(out=r.comb[:], in_=r.PSB[0:16, 512:1024]))
    ctx.inc(i, sV)
    t_comb = sV.c

    if DBG and it == 0 and "comb" in dbg_outs:
        ctx.wait("sync", sV, t_comb)
        ctx.wait("sync", sA, sA.c)
        ctx.dma("sync", st, dbg_outs["comb"], r.comb[:])
        ctx.dma("sync", st, dbg_outs["affdbg"], r.aff[:])
        ctx.dma("sync", st, dbg_outs["affT"], r.affT[:])
        ctx.dma("sync", st, dbg_outs["m1"], r.m1[:])
        ctx.dma("sync", st, dbg_outs["m2"], r.m2[:])
        ctx.dma("sync", st, dbg_outs["den"], r.den[:])
        ctx.dma("sync", st, dbg_outs["combT"], r.combT[:])
        t_tap = sems["st"].c
        ctx.wait("vector", st, t_tap)
        ctx.wait("tensor", st, t_tap)

    wsrc = lambda ee: (d.rg_bf[ee], d.rw1_bf[ee], d.rw2_bf[ee]) if ee < E_RT else \
                      (d.sg_bf[ee - E_RT], d.sw1_bf[ee - E_RT], d.sw2_bf[ee - E_RT])
    bias_g = lambda ee, ft: r.rgb[:, ee, ft:ft + 1] if ee < E_RT else r.sgb[:, ee - E_RT, ft:ft + 1]
    bias_1 = lambda ee, ft: r.rb1[:, ee, ft:ft + 1] if ee < E_RT else r.sb1[:, ee - E_RT, ft:ft + 1]

    wa_free = [sP.c] * 3
    wb_free = [sP.c] * 3
    w2_free = [sP.c] * 3
    seqA = 0
    seqW2 = 0
    t_g1free = sV.c
    pb_v = [sV.c] * 2
    pb_a = [sA.c] * 2
    z_free = [sV.c] * 4
    t_combbc_cons = sV.c
    u_free = sP.c

    for grp in range(6):
        experts = [grp * 3, grp * 3 + 1, grp * 3 + 2]
        for gi, ee in enumerate(experts):
            srcs = wsrc(ee)
            if ee < E_RT:
                ctx.wait("tensor", sV, max(t_comb, t_combbc_cons))
                i = ctx.on("tensor", lambda e, ee=ee: e.matmul(
                    r.PSB[:, 0:512], r.sel16[:, ee * 128:(ee + 1) * 128], r.comb[:],
                    start=True, stop=True))
                ctx.inc(i, sP)
                t_cbc = sP.c
            for fp in range(4):
                sa_i = seqA % 3
                seqA += 1
                ctx.wait("sync", dG, t_p0ffn)
                ctx.wait("sync", sP, wa_free[sa_i])
                t_wa = ctx.dma("sync", dS, r.wA[sa_i][:], srcs[0][fp])
                ctx.wait("sync", sP, wb_free[sa_i])
                t_wb = ctx.dma("sync", dS, r.wB[sa_i][:], srcs[1][fp])
                for fc2 in range(2):
                    fc = fp * 2 + fc2
                    psl_g = r.PSA[:, fc2 * 1024: fc2 * 1024 + 512]
                    psl_1 = r.PSA[:, fc2 * 1024 + 512: fc2 * 1024 + 1024]
                    ctx.wait("tensor", dS, max(t_wa, t_wb))
                    ctx.wait("tensor", sV, pb_v[fc2])
                    ctx.wait("tensor", sA, pb_a[fc2])
                    mm = None
                    for kk in range(16):
                        mm = ctx.on("tensor", lambda e, kk=kk, sa_i=sa_i, fc2=fc2, psl_g=psl_g: e.matmul(
                            psl_g, r.wA[sa_i][:, kk, fc2 * 128:(fc2 + 1) * 128],
                            ybf_sb[:, kk, :], start=(kk == 0), stop=(kk == 15)))
                    for kk in range(16):
                        mm = ctx.on("tensor", lambda e, kk=kk, sa_i=sa_i, fc2=fc2, psl_1=psl_1: e.matmul(
                            psl_1, r.wB[sa_i][:, kk, fc2 * 128:(fc2 + 1) * 128],
                            ybf_sb[:, kk, :], start=(kk == 0), stop=(kk == 15)))
                    ctx.inc(mm, sP)
                    t_h = sP.c
                    ctx.wait("scalar", sP, t_h)
                    ctx.wait("scalar", sV, t_g1free)
                    i = ctx.on("scalar", lambda e, psl_g=psl_g, ee=ee, fc=fc: e.activation(
                        out=r.g1[:], in_=psl_g, func=AF.Gelu, bias=bias_g(ee, fc)))
                    ctx.inc(i, sA)
                    pb_a[fc2] = sA.c
                    ctx.wait("vector", sA, pb_a[fc2])
                    ctx.wait("vector", sP, u_free)
                    if ee < E_RT:
                        ctx.wait("vector", sP, t_cbc)
                        i = ctx.on("vector", lambda e, psl_1=psl_1, ee=ee, fc=fc: e.scalar_tensor_tensor(
                            out=r.u0[:], in0=psl_1, scalar=bias_1(ee, fc), in1=r.g1[:],
                            op0=ALU.add, op1=ALU.mult))
                        ctx.inc(i, sV)
                        t_g1free = sV.c
                        i = ctx.on("vector", lambda e, gi=gi, fc=fc: e.tensor_mul(
                            out=r.u[gi][:, fc, :], in0=r.u0[:], in1=r.PSB[:, 0:512]))
                        ctx.inc(i, sV)
                        t_combbc_cons = sV.c
                    else:
                        i = ctx.on("vector", lambda e, psl_1=psl_1, ee=ee, fc=fc, gi=gi: e.scalar_tensor_tensor(
                            out=r.u[gi][:, fc, :], in0=psl_1, scalar=bias_1(ee, fc), in1=r.g1[:],
                            op0=ALU.add, op1=ALU.mult))
                        ctx.inc(i, sV)
                        t_g1free = sV.c
                    pb_v[fc2] = sV.c
                wa_free[sa_i] = sP.c
                wb_free[sa_i] = sP.c
        t_u_grp = sV.c
        t_u_grp_a = sA.c

        for ps4 in range(4):
            rdy = []
            for gi, ee in enumerate(experts):
                w2i = seqW2 % 3
                seqW2 += 1
                ctx.wait("sync", sP, w2_free[w2i])
                t_rdy = ctx.dma("sync", dS, r.w2s[w2i][:], wsrc(ee)[2][ps4])
                rdy.append((w2i, t_rdy))
            for dc4 in range(4):
                dc = ps4 * 4 + dc4
                psl = r.PSA[:, dc4 * 512:(dc4 + 1) * 512]
                ctx.wait("tensor", sV, max(t_u_grp, z_free[dc4]))
                ctx.wait("tensor", sA, t_u_grp_a)
                ntot = 24 + (2 if grp == 0 else 0)
                nmm = 0
                mm = None
                for gi, ee in enumerate(experts):
                    w2i, t_rdy = rdy[gi]
                    ctx.wait("tensor", dS, t_rdy)
                    for fc in range(8):
                        mm = ctx.on("tensor", lambda e, w2i=w2i, fc=fc, dc4=dc4, psl=psl,
                                    gi=gi, nmm=nmm, ntot=ntot: e.matmul(
                            psl, r.w2s[w2i][:, fc, dc4 * 128:(dc4 + 1) * 128],
                            r.u[gi][:, fc, :], start=(nmm == 0), stop=(nmm == ntot - 1)))
                        nmm += 1
                if grp == 0:
                    mm = ctx.on("tensor", lambda e, dc=dc, psl=psl: e.matmul(
                        psl, r.b2r_sb[:, dc * 128:(dc + 1) * 128], r.comb[:],
                        start=False, stop=False, skip_group_check=True))
                    mm = ctx.on("tensor", lambda e, dc=dc, psl=psl: e.matmul(
                        psl, r.b2s_sb[:, dc * 128:(dc + 1) * 128], r.ones2[:],
                        start=False, stop=True, skip_group_check=True))
                ctx.inc(mm, sP)
                t_w2 = sP.c
                ctx.wait("vector", sP, t_w2)
                ctx.wait("vector", sV, t_z0)
                if grp == 0:
                    i = ctx.on("vector", lambda e, dc=dc, psl=psl: e.scalar_tensor_tensor(
                        out=r.z[:, dc, :], in0=r.z[:, dc, :], scalar=2.0, in1=psl,
                        op0=ALU.mult, op1=ALU.add))
                else:
                    i = ctx.on("vector", lambda e, dc=dc, psl=psl: e.tensor_add(
                        out=r.z[:, dc, :], in0=r.z[:, dc, :], in1=psl))
                ctx.inc(i, sV)
                z_free[dc4] = sV.c
            for gi in range(3):
                w2_free[rdy[gi][0]] = sP.c
        pb_v = [sV.c] * 2
        u_free = sP.c
    r.y_used[par] = sP.c

    t_zdone = sV.c
    if DBG and it == 0 and "zdbg" in dbg_outs:
        ctx.wait("sync", sV, t_zdone)
        ctx.dma("sync", st, dbg_outs["zdbg"], r.z[:])
    ctx.wait("tensor", sV, t_zdone)
    mm = None
    for ot in range(16):
        mm = ctx.on("tensor", lambda e, ot=ot: e.matmul(
            r.PSB[0:1, 0:512], r.onesf[:, 0:1], r.z[:, ot, :],
            start=(ot == 0), stop=(ot == 15)))
    ctx.inc(mm, sP)
    ctx.wait("scalar", sV, t_zdone)
    sq_mm = [sP.c] * 2
    scr = [r.g1, r.u0]
    for ot in range(16):
        sqi = ot % 2
        ctx.wait("scalar", sP, sq_mm[sqi])
        i = ctx.on("scalar", lambda e, ot=ot, sqi=sqi: e.activation(
            out=scr[sqi][:], in_=r.z[:, ot, :], func=AF.Square))
        ctx.inc(i, sA)
        t_sqr = sA.c
        ctx.wait("tensor", sA, t_sqr)
        i = ctx.on("tensor", lambda e, sqi=sqi, ot=ot: e.matmul(
            r.PSB[0:1, 512:1024], r.onesf[:, 0:1], scr[sqi][:],
            start=(ot == 0), stop=(ot == 15)))
        ctx.inc(i, sP)
        sq_mm[sqi] = sP.c
    t_sumsq = sP.c
    ctx.wait("vector", sP, t_sumsq)
    ctx.on("vector", lambda e: e.tensor_scalar(
        out=r.flnr[0:1, 0:512], in0=r.PSB[0:1, 0:512], scalar1=-1.0 / 2048.0,
        scalar2=None, op0=ALU.mult))
    ctx.on("vector", lambda e: e.tensor_mul(
        out=r.flnr[0:1, 1024:1536], in0=r.flnr[0:1, 0:512], in1=r.flnr[0:1, 0:512]))
    i = ctx.on("vector", lambda e: e.scalar_tensor_tensor(
        out=r.flnr[0:1, 512:1024], in0=r.PSB[0:1, 512:1024], scalar=1.0 / 2048.0,
        in1=r.flnr[0:1, 1024:1536], op0=ALU.mult, op1=ALU.subtract))
    ctx.inc(i, sV)
    t_var = sV.c
    ctx.wait("scalar", sV, t_var)
    i = ctx.on("scalar", lambda e: e.activation(out=r.flnr[0:1, 512:1024],
                                                in_=r.flnr[0:1, 512:1024], func=AF.Sqrt, bias=1e-5))
    ctx.inc(i, sA)
    ctx.wait("vector", sA, sA.c)
    i = ctx.on("vector", lambda e: e.reciprocal(out=r.flnr[0:1, 512:1024],
                                                in_=r.flnr[0:1, 512:1024]))
    ctx.inc(i, sV)
    ctx.wait("vector", sV, sV.c)
    i = ctx.on("vector", lambda e: e.engine_nop())
    ctx.inc(i, sV)
    t_rs = sV.c
    ctx.wait("tensor", sV, t_rs)
    ctx.on("tensor", lambda e: e.matmul(
        r.PSA[:, 0:512], r.onesf[0:1, :], r.flnr[0:1, 0:512], start=True, stop=True))
    i = ctx.on("tensor", lambda e: e.matmul(
        r.PSA[:, 512:1024], r.onesf[0:1, :], r.flnr[0:1, 512:1024], start=True, stop=True))
    ctx.inc(i, sP)
    t_bc = sP.c
    ctx.wait("vector", sP, t_bc)
    ost_busy = [sems["st"].c] * 2
    for ot in range(16):
        oi = ot % 2
        ctx.wait("vector", st, ost_busy[oi])
        ctx.on("vector", lambda e, ot=ot, oi=oi: e.tensor_add(
            out=r.ostg[oi][:], in0=r.z[:, ot, :], in1=r.PSA[:, 0:512]))
        i = ctx.on("vector", lambda e, ot=ot, oi=oi: e.tensor_mul(
            out=r.ostg[oi][:], in0=r.ostg[oi][:], in1=r.PSA[:, 512:1024]))
        ctx.inc(i, sV)
        t_o = sV.c
        ctx.wait("sync", sV, t_o)
        ost_busy[oi] = ctx.dma("sync", st, d.out[ot * 128:(ot + 1) * 128, pc0:pc0 + FIT],
                               r.ostg[oi][:])



def _stage_weights(inp):
    g = {}
    T = lambda a: np.ascontiguousarray(a)
    Wq = inp["L0_attn_Wq"]; Wk = inp["L0_attn_Wk"]; Wv = inp["L0_attn_Wv"]
    Wb = inp["L0_attn_Wb"]; Wa = inp["L0_attn_Wa"]; Wg = inp["L0_attn_Wg"]
    Wo = inp["L0_attn_Wo"]
    g["wq_l"] = T(Wq.T.reshape(16, 128, 8, 128).transpose(2, 1, 0, 3))
    g["wk_l"] = T(Wk.T.reshape(16, 128, 16, 128).transpose(2, 1, 0, 3))
    g["wv_l"] = T(Wv.T.reshape(16, 128, 16, 128).transpose(2, 1, 0, 3))
    g["wg_l"] = T(Wg.T.reshape(16, 128, 8, 128).transpose(2, 1, 0, 3))
    g["wo_l"] = T(Wo.T.reshape(8, 128, 16, 128).transpose(2, 1, 0, 3))
    g["wb_l"] = T(Wb.T.reshape(16, 128, 16).transpose(1, 0, 2))
    g["wa_l"] = T(Wa.T.reshape(16, 128, 8).transpose(1, 0, 2))
    g["wafn_l"] = T(inp["L0_ffn_wa"].reshape(16, 128, 16).transpose(1, 0, 2))
    g["cq_l"] = T(inp["L0_attn_conv_q"].reshape(8, 128, 4).transpose(1, 0, 2))
    g["ck_l"] = T(inp["L0_attn_conv_k"].reshape(16, 128, 4).transpose(1, 0, 2))
    g["cv_l"] = T(inp["L0_attn_conv_v"].reshape(16, 128, 4).transpose(1, 0, 2))
    g["alog"] = T(inp["L0_attn_A_log"].reshape(8, 1))
    g["dtb"] = T(inp["L0_attn_dt_bias"].reshape(8, 1))
    g["onw"] = T(inp["L0_attn_o_norm_w"].reshape(128, 1))
    g["rg_l"] = T(inp["L0_ffn_rg"].reshape(16, 16, 128, 4, 256).transpose(0, 3, 2, 1, 4))
    g["rw1_l"] = T(inp["L0_ffn_rw1"].reshape(16, 16, 128, 4, 256).transpose(0, 3, 2, 1, 4))
    g["rw2_l"] = T(inp["L0_ffn_rw2"].reshape(16, 8, 128, 4, 512).transpose(0, 3, 2, 1, 4))
    g["sg_l"] = T(inp["L0_ffn_sg"].reshape(2, 16, 128, 4, 256).transpose(0, 3, 2, 1, 4))
    g["sw1_l"] = T(inp["L0_ffn_sw1"].reshape(2, 16, 128, 4, 256).transpose(0, 3, 2, 1, 4))
    g["sw2_l"] = T(inp["L0_ffn_sw2"].reshape(2, 8, 128, 4, 512).transpose(0, 3, 2, 1, 4))
    g["rgb_l"] = T(inp["L0_ffn_rgb"].reshape(16, 8, 128).transpose(2, 0, 1))
    g["rb1_l"] = T(inp["L0_ffn_rb1"].reshape(16, 8, 128).transpose(2, 0, 1))
    g["sgb_l"] = T(inp["L0_ffn_sgb"].reshape(2, 8, 128).transpose(2, 0, 1))
    g["sb1_l"] = T(inp["L0_ffn_sb1"].reshape(2, 8, 128).transpose(2, 0, 1))
    g["b2r"] = T(inp["L0_ffn_rb2"])
    g["b2s"] = T(inp["L0_ffn_sb2"])
    sel8 = np.zeros((8, 1024), np.float32)
    for h in range(8):
        sel8[h, h * 128:(h + 1) * 128] = 1.0
    g["sel8"] = sel8
    sel16 = np.zeros((16, 2048), np.float32)
    for j in range(16):
        sel16[j, j * 128:(j + 1) * 128] = 1.0
    g["sel16"] = sel16.astype(ml_dtypes.bfloat16)
    g["maskneg"] = np.where(np.triu(np.ones((128, 128)), 0) > 0, 0.0, -1e9).astype(np.float32)
    g["identf"] = np.eye(128, dtype=np.float32)
    g["identb"] = np.eye(128).astype(ml_dtypes.bfloat16)
    return g


def _stage_x(x, core):
    xc = x[2 * core:2 * core + 2]
    xr = xc.reshape(2, NBLK, NHALF, BW, D)
    return np.ascontiguousarray(xr.transpose(4, 1, 2, 0, 3).reshape(D, NPCOL))


def _unstage_out(out_fm):
    of = out_fm.reshape(D, NBLK, NHALF, 2, BW)
    return np.ascontiguousarray(of.transpose(3, 1, 2, 4, 0).reshape(2, 2048, D))


def kernel(**inp):
    nc = build_program()
    g = _stage_weights(inp)
    x = np.asarray(inp["L0_attn_x"])
    in_maps = []
    for core in range(NCORE):
        m = dict(g)
        m["x_fm"] = _stage_x(x, core)
        in_maps.append(m)
    res = run_bass_kernel_spmd(nc, in_maps, core_ids=list(range(NCORE)))
    out = np.empty((16, 2048, 2048), np.float32)
    for core in range(NCORE):
        out[2 * core:2 * core + 2] = _unstage_out(res.results[core]["out_fm"])
    return out


# version 07
# speedup vs baseline: 1.1432x; 1.0365x over previous
import numpy as np
import ml_dtypes
from contextlib import ExitStack
import concourse.bass as bass
import concourse.mybir as mybir
from concourse.bass_utils import run_bass_kernel_spmd

F32 = mybir.dt.float32
BF16 = mybir.dt.bfloat16
AF = mybir.ActivationFunctionType
ALU = mybir.AluOpType
AX = mybir.AxisListType

ENGINES = ["sync", "gpsimd", "vector", "scalar", "tensor"]

NCORE = 8
D = 2048
KT = 16
NBLK = 8
NHALF = 2
BW = 128
HALF_COLS = 256
TOKC = 64
NCHH = 2
NPCOL = 4096
E_RT, E_SH, FF = 16, 2, 1024
NEXP = E_RT + E_SH
FIT = 512
NFI = NPCOL // FIT
SQ128 = 11.313708498984761

DBG = False
DBG_HB = 0
DBG_TB = 0
PIPE_B = 1
PIPE_D = 2


class Sem:
    def __init__(self, handle):
        self.h = handle
        self.c = 0


class Ctx:
    def __init__(self):
        self.cur = None
        self.eng = None

    def set_pass(self, name, eng, sems):
        self.cur = name
        self.eng = eng
        for s in sems:
            s.c = 0

    def on(self, engine_name, fn):
        if self.cur == engine_name:
            return fn(self.eng)
        return None

    def inc(self, inst, sem, n=1):
        sem.c += n
        if inst is not None:
            inst.then_inc(sem.h, n)

    def wait(self, engine_name, sem, value=None):
        if value is None:
            value = sem.c
        if self.cur == engine_name:
            self.eng.wait_ge(sem.h, value)

    def dma(self, engine_name, sem, out, in_, **kw):
        sem.c += 16
        if self.cur == engine_name:
            self.eng.dma_start(out=out, in_=in_, **kw).then_inc(sem.h, 16)
        return sem.c


class NS:
    pass


DBG_TAPS = [
    ("Kb", [128, NCHH, 16, 128], BF16),
    ("Vb", [128, NCHH, 16, 128], BF16),
    ("Qb", [128, NCHH, 16, 128], BF16),
    ("gate", [128, 8, HALF_COLS], BF16),
    ("beta0", [8, HALF_COLS], F32),
    ("beta1", [8, HALF_COLS], F32),
    ("g8", [8, HALF_COLS], F32),
    ("og", [128, 8, HALF_COLS], BF16),
    ("ypre", [128, 16, HALF_COLS], BF16),
    ("ystg", [128, 16, HALF_COLS], BF16),
    ("comb", [16, FIT], BF16),
    ("affdbg", [16, FIT], F32),
    ("affT", [128, 4, 16], F32),
    ("m1", [128, 4], F32),
    ("m2", [128, 4], F32),
    ("den", [128, 4], F32),
    ("combT", [128, 4, 16], F32),
    ("zdbg", [128, 16, FIT], F32),
]


def _vb_view(t, nh, h):
    v4 = t[:].rearrange("p c (bb hh) t -> p bb c hh t", bb=2)
    return v4[:, :, :, h, nh::2]



def build_program():
    nc = bass.Bass("TRN2", target_bir_lowering=False)
    d = NS()

    def din(name, shape, dt=F32):
        setattr(d, name, nc.dram_tensor(name, list(shape), dt, kind="ExternalInput").ap())

    din("x_fm", [D, NPCOL])
    din("wq_l", [8, 128, 16, 128])
    din("wk_l", [16, 128, 16, 128])
    din("wv_l", [16, 128, 16, 128])
    din("wg_l", [8, 128, 16, 128])
    din("wo_l", [16, 128, 8, 128])
    din("wb_l", [128, 16, 16])
    din("wa_l", [128, 16, 8])
    din("wafn_l", [128, 16, 16])
    din("cq_l", [128, 8, 4])
    din("ck_l", [128, 16, 4])
    din("cv_l", [128, 16, 4])
    din("alog", [8, 1])
    din("dtb", [8, 1])
    din("onw", [128, 1])
    din("rg_l", [E_RT, 4, 128, 16, 256])
    din("rw1_l", [E_RT, 4, 128, 16, 256])
    din("rw2_l", [E_RT, 4, 128, 8, 512])
    din("sg_l", [E_SH, 4, 128, 16, 256])
    din("sw1_l", [E_SH, 4, 128, 16, 256])
    din("sw2_l", [E_SH, 4, 128, 8, 512])
    din("rgb_l", [128, E_RT, 8])
    din("rb1_l", [128, E_RT, 8])
    din("sgb_l", [128, E_SH, 8])
    din("sb1_l", [128, E_SH, 8])
    din("b2r", [E_RT, D])
    din("b2s", [E_SH, D])
    din("sel8", [8, 1024])
    din("sel16", [16, 2048], BF16)
    din("maskneg", [128, 128])
    din("identf", [128, 128])
    din("identb", [128, 128], BF16)

    d.out = nc.dram_tensor("out_fm", [D, NPCOL], F32, kind="ExternalOutput").ap()

    def dint(name, shape, dt):
        setattr(d, name, nc.dram_tensor(name, list(shape), dt, kind="Internal").ap())

    dint("wbf_q", [8, 128, 16, 128], BF16)
    dint("wbf_k", [16, 128, 16, 128], BF16)
    dint("wbf_v", [16, 128, 16, 128], BF16)
    dint("wbf_g", [8, 128, 16, 128], BF16)
    dint("wbf_o", [16, 128, 8, 128], BF16)
    dint("rg_bf", [E_RT, 4, 128, 16, 256], BF16)
    dint("rw1_bf", [E_RT, 4, 128, 16, 256], BF16)
    dint("rw2_bf", [E_RT, 4, 128, 8, 512], BF16)
    dint("sg_bf", [E_SH, 4, 128, 16, 256], BF16)
    dint("sw1_bf", [E_SH, 4, 128, 16, 256], BF16)
    dint("sw2_bf", [E_SH, 4, 128, 8, 512], BF16)
    dint("ybf", [D, NPCOL], BF16)

    dbg_outs = {}
    if DBG:
        for nm, shape, dt in DBG_TAPS:
            dbg_outs[nm] = nc.dram_tensor("dbg_" + nm, list(shape), dt, kind="ExternalOutput").ap()

    stack = ExitStack()
    e = stack.enter_context
    r = NS()
    r.nc = nc

    r.sel8 = e(nc.sbuf_tensor("p_sel8", [8, 1024], F32))
    r.sel16 = e(nc.sbuf_tensor("p_sel16", [16, 2048], BF16))
    r.maskneg = e(nc.sbuf_tensor("p_mask", [128, 128], F32))
    r.identf = e(nc.sbuf_tensor("p_idf", [128, 128], F32))
    r.identb = e(nc.sbuf_tensor("p_idb", [128, 128], BF16))
    r.onesb = e(nc.sbuf_tensor("p_onesb", [128, 128], BF16))
    r.onesf = e(nc.sbuf_tensor("p_onesf", [128, 128], F32))
    r.ones2 = e(nc.sbuf_tensor("p_ones2", [2, 512], BF16))
    r.cst = e(nc.sbuf_tensor("p_cst", [128, 8], F32))
    r.wb_sb = e(nc.sbuf_tensor("p_wb", [128, 16, 16], BF16))
    r.wa_sb = e(nc.sbuf_tensor("p_wa", [128, 16, 8], BF16))
    r.wafn_sb = e(nc.sbuf_tensor("p_wafn", [128, 16, 16], BF16))
    r.b2r_sb = e(nc.sbuf_tensor("p_b2r", [16, 2048], BF16))
    r.b2s_sb = e(nc.sbuf_tensor("p_b2s", [2, 2048], BF16))
    r.cq = e(nc.sbuf_tensor("p_cq", [128, 8, 4], F32))
    r.ck = e(nc.sbuf_tensor("p_ck", [128, 16, 4], F32))
    r.cv = e(nc.sbuf_tensor("p_cv", [128, 16, 4], F32))
    r.nexpA = e(nc.sbuf_tensor("p_nexpA", [8, 1], F32))
    r.dtb = e(nc.sbuf_tensor("p_dtb", [8, 1], F32))
    r.onw = e(nc.sbuf_tensor("p_onw", [128, 1], F32))
    r.rgb = e(nc.sbuf_tensor("p_rgb", [128, E_RT, 8], F32))
    r.rb1 = e(nc.sbuf_tensor("p_rb1", [128, E_RT, 8], F32))
    r.sgb = e(nc.sbuf_tensor("p_sgb", [128, E_SH, 8], F32))
    r.sb1 = e(nc.sbuf_tensor("p_sb1", [128, E_SH, 8], F32))
    r.hist = e(nc.sbuf_tensor("p_hist", [128, 40, 2, 4], BF16))
    r.carry = e(nc.sbuf_tensor("p_carry", [8, 2], F32))
    r.carry0 = e(nc.sbuf_tensor("p_carry0", [8, 2], F32))

    nc.const_aps.register(F32, 1e-5, r.cst[:, 0:1])
    nc.const_aps.register(F32, 1.0 / 128.0, r.cst[:, 1:2])
    nc.const_aps.register(F32, 1.0, r.cst[:, 2:3])
    nc.const_aps.register(F32, -1.0, r.cst[:, 3:4])

    pa_stack = ExitStack()
    e = pa_stack.enter_context
    r.x_bf = e(nc.sbuf_tensor("a_xbf", [128, 16, HALF_COLS], BF16))
    r.slab = [e(nc.sbuf_tensor(f"a_slab{i}", [128, 16, 128], BF16)) for i in range(3)]
    r.pre = [e(nc.sbuf_tensor(f"a_pre{i}", [128, 2, 132], BF16)) for i in range(3)]
    r.cvo = [e(nc.sbuf_tensor(f"a_cvo{i}", [128, 2, 128], BF16)) for i in range(2)]
    r.qs = [e(nc.sbuf_tensor(f"a_qs{i}", [128, HALF_COLS], BF16)) for i in range(3)]
    r.sq = [e(nc.sbuf_tensor(f"a_sq{i}", [128, HALF_COLS], BF16)) for i in range(2)]
    r.srow = [e(nc.sbuf_tensor(f"a_srow{i}", [1, HALF_COLS], BF16)) for i in range(3)]
    r.Kb = e(nc.sbuf_tensor("a_Kb", [128, NCHH, 16, 128], BF16))
    r.Vb = e(nc.sbuf_tensor("a_Vb", [128, NCHH, 16, 128], BF16))
    r.Qb = e(nc.sbuf_tensor("a_Qb", [128, NCHH, 16, 128], BF16))
    r.gate = e(nc.sbuf_tensor("a_gate", [128, 8, HALF_COLS], BF16))
    r.beta_nh = [e(nc.sbuf_tensor(f"a_beta{p}", [8, HALF_COLS], F32)) for p in range(2)]
    r.g8 = e(nc.sbuf_tensor("a_g8", [8, HALF_COLS], F32))
    r.sp1 = e(nc.sbuf_tensor("a_sp1", [8, HALF_COLS], F32))
    r.sp2 = e(nc.sbuf_tensor("a_sp2", [8, HALF_COLS], F32))
    r.cumA = e(nc.sbuf_tensor("a_cumA", [8, HALF_COLS], F32))
    r.cumB = e(nc.sbuf_tensor("a_cumB", [8, HALF_COLS], F32))
    r.ogate = e(nc.sbuf_tensor("a_og", [128, 8, HALF_COLS], BF16))
    r.ypre = e(nc.sbuf_tensor("a_ypre", [128, 16, HALF_COLS], BF16))
    r.xfr = [e(nc.sbuf_tensor(f"a_xfr{i}", [128, HALF_COLS], F32)) for i in range(2)]
    r.ystg = e(nc.sbuf_tensor("a_ystg", [128, 16, HALF_COLS], BF16))
    r.lnt = e(nc.sbuf_tensor("a_lnt", [128, HALF_COLS], F32))
    r.lnr = e(nc.sbuf_tensor("a_lnr", [1, 1024], F32))
    r.S = e(nc.sbuf_tensor("s_S", [128, 2048], F32))
    r.S0bf = e(nc.sbuf_tensor("s_S0bf", [128, 2048], BF16))
    r.cumg_sub = e(nc.sbuf_tensor("s_cgs", [8, 256], F32))
    r.beta_sub = e(nc.sbuf_tensor("s_bts", [8, 256], F32))
    r.cumg_T = e(nc.sbuf_tensor("s_cgT", [128, 16], F32))
    r.beta_T = e(nc.sbuf_tensor("s_btT", [128, 16], F32))
    r.lamc = e(nc.sbuf_tensor("s_lamc", [128, 16], F32))
    r.rstd = e(nc.sbuf_tensor("s_rstd", [128, 16], F32))
    r.ssb = e(nc.sbuf_tensor("s_ssb", [128, 16], F32))
    r.edecf = e(nc.sbuf_tensor("s_edecf", [128, 16], F32))
    r.lamlast = e(nc.sbuf_tensor("s_lamlast", [128, 16], F32))
    r.lam_row = e(nc.sbuf_tensor("s_lamr", [128, 2048], BF16))
    r.E = e(nc.sbuf_tensor("s_E", [128, 2048], BF16))
    r.tmp_bf = e(nc.sbuf_tensor("s_tmpbf", [128, 2048], BF16))
    r.Nt = e(nc.sbuf_tensor("s_Nt", [128, 2048], BF16))
    r.Nn = e(nc.sbuf_tensor("s_Nn", [128, 2048], BF16))
    r.Nt2 = e(nc.sbuf_tensor("s_Nt2", [128, 2048], BF16))
    r.Nn2 = e(nc.sbuf_tensor("s_Nn2", [128, 2048], BF16))
    r.Wt = e(nc.sbuf_tensor("s_Wt", [128, 2048], BF16))
    r.Vtm = e(nc.sbuf_tensor("s_Vtm", [128, 2048], BF16))
    r.KDtm = e(nc.sbuf_tensor("s_KDtm", [128, 2048], BF16))
    r.U32 = e(nc.sbuf_tensor("s_U32", [128, 2048], F32))
    r.Ubf = e(nc.sbuf_tensor("s_Ubf", [128, 2048], BF16))
    r.t32 = e(nc.sbuf_tensor("s_t32", [128, 1024], F32))
    r.sqbf = e(nc.sbuf_tensor("s_sqbf", [128, 2048], BF16))
    r.on = e(nc.sbuf_tensor("s_on", [128, 2048], BF16))
    pa_stack.close()
    e = stack.enter_context
    r.ybf_sb2 = [e(nc.sbuf_tensor(f"f_ybf{i}", [128, 16, FIT], BF16)) for i in range(2)]
    r.z = e(nc.sbuf_tensor("f_z", [128, 16, FIT], F32))
    r.u = [e(nc.sbuf_tensor(f"f_u{i}", [128, 8, FIT], BF16)) for i in range(3)]
    r.wA = [e(nc.sbuf_tensor(f"f_wA{i}", [128, 16, 256], BF16)) for i in range(3)]
    r.wB = [e(nc.sbuf_tensor(f"f_wB{i}", [128, 16, 256], BF16)) for i in range(3)]
    r.w2s = [e(nc.sbuf_tensor(f"f_w2s{i}", [128, 8, 512], BF16)) for i in range(3)]
    r.aff = e(nc.sbuf_tensor("f_aff", [16, FIT], F32))
    r.affT = e(nc.sbuf_tensor("f_affT", [128, 4, 16], F32))
    r.combT = e(nc.sbuf_tensor("f_combT", [128, 4, 16], F32))
    r.comb = e(nc.sbuf_tensor("f_comb", [16, FIT], BF16))
    r.m1 = e(nc.sbuf_tensor("f_m1", [128, 4], F32))
    r.m2 = e(nc.sbuf_tensor("f_m2", [128, 4], F32))
    r.den = e(nc.sbuf_tensor("f_den", [128, 4], F32))
    r.eq = e(nc.sbuf_tensor("f_eq", [128, 4, 16], F32))
    r.g1 = e(nc.sbuf_tensor("f_g1", [128, FIT], F32))
    r.u0 = e(nc.sbuf_tensor("f_uzero", [128, FIT], F32))
    r.ostg = [e(nc.sbuf_tensor(f"f_ostg{i}", [128, FIT], F32)) for i in range(2)]
    r.flnr = e(nc.sbuf_tensor("f_lnr", [1, 2048], F32))

    r.PSA = e(nc.psum_tensor("ps_A", [128, 2048], F32))
    r.PSB = e(nc.psum_tensor("ps_B", [128, 1024], F32))
    r.PSC = e(nc.psum_tensor("ps_C", [128, 2048], BF16))

    sems = {n: Sem(e(nc.semaphore(n))) for n in ["dS", "dG", "sP", "sV", "sA", "st", "sG"]}
    block = e(nc.Block())
    ctx = Ctx()

    def emit():
        emit_all(ctx, nc, d, r, sems, dbg_outs)

    for en in ENGINES:
        def run_pass(eng, en=en):
            ctx.set_pass(en, eng, list(sems.values()))
            emit()
        getattr(block, en)(run_pass)
    stack.close()
    return nc



def emit_all(ctx, nc, d, r, sems, dbg_outs):
    dS, dG, sP, sV, sA, st, sG = (sems[n] for n in ["dS", "dG", "sP", "sV", "sA", "st", "sG"])

    for name, sb in [("sel8", r.sel8), ("sel16", r.sel16), ("maskneg", r.maskneg),
                     ("identf", r.identf), ("identb", r.identb),
                     ("cq_l", r.cq), ("ck_l", r.ck), ("cv_l", r.cv),
                     ("dtb", r.dtb), ("onw", r.onw),
                     ("rgb_l", r.rgb), ("rb1_l", r.rb1), ("sgb_l", r.sgb), ("sb1_l", r.sb1)]:
        ctx.dma("sync", dS, sb[:], getattr(d, name))
    ctx.dma("sync", dS, r.nexpA[:], d.alog)
    t_cl = dS.c

    ctx.dma("gpsimd", dG, r.wb_sb[:], d.wb_l)
    ctx.dma("gpsimd", dG, r.wa_sb[:], d.wa_l)
    ctx.dma("gpsimd", dG, r.wafn_sb[:], d.wafn_l)
    ctx.dma("gpsimd", dG, r.b2r_sb[:], d.b2r)
    ctx.dma("gpsimd", dG, r.b2s_sb[:], d.b2s)
    t_smallcast = dG.c
    for (src, dst) in [(d.wq_l, d.wbf_q), (d.wk_l, d.wbf_k), (d.wv_l, d.wbf_v),
                       (d.wg_l, d.wbf_g), (d.wo_l, d.wbf_o)]:
        ctx.dma("gpsimd", dG, dst.rearrange("a p k o -> (a p k o)"),
                src.rearrange("a p k o -> (a p k o)"))
    t_p0attn = dG.c
    for (src, dst, ne) in [(d.rg_l, d.rg_bf, E_RT), (d.rw1_l, d.rw1_bf, E_RT),
                           (d.rw2_l, d.rw2_bf, E_RT), (d.sg_l, d.sg_bf, E_SH),
                           (d.sw1_l, d.sw1_bf, E_SH), (d.sw2_l, d.sw2_bf, E_SH)]:
        for ee in range(ne):
            ctx.dma("gpsimd", dG, dst[ee].rearrange("a p k o -> (a p k o)"),
                    src[ee].rearrange("a p k o -> (a p k o)"))
    t_p0ffn = dG.c

    ctx.wait("vector", dS, t_cl)
    for fn in [lambda e: e.memset(r.onesb[:], 1.0),
               lambda e: e.memset(r.onesf[:], 1.0),
               lambda e: e.memset(r.ones2[:], 1.0),
               lambda e: e.memset(r.cst[:, 0:1], 1e-5),
               lambda e: e.memset(r.cst[:, 1:2], 1.0 / 128.0),
               lambda e: e.memset(r.cst[:, 2:3], 1.0),
               lambda e: e.memset(r.cst[:, 3:4], -1.0),
               lambda e: e.memset(r.S[:], 0.0),
               lambda e: e.memset(r.carry[:, 0:2], 0.0),
               lambda e: e.memset(r.carry0[:, 0:2], 0.0),
               lambda e: e.memset(r.Qb[:], 0.0),
               lambda e: e.memset(r.hist[:], 0.0)]:
        i = ctx.on("vector", fn)
    ctx.inc(i, sV)
    t_vinit = sV.c
    ctx.wait("scalar", sV, t_vinit)
    ctx.on("scalar", lambda e: e.activation(out=r.nexpA[:], in_=r.nexpA[:], func=AF.Exp))
    i = ctx.on("scalar", lambda e: e.mul(out=r.nexpA[:], in_=r.nexpA[:], mul=-1.0))
    ctx.inc(i, sA)
    ctx.wait("tensor", dS, t_cl)
    ctx.wait("tensor", dG, t_smallcast)
    ctx.wait("tensor", sV, t_vinit)

    r.spill_st = 0
    for tb in range(NBLK):
        for hb in range(NHALF):
            emit_half(ctx, d, r, sems, tb, hb, t_p0attn, dbg_outs)

    r.y_pref = [0, 0]
    r.y_used = [0, 0]
    for it in range(NFI):
        emit_ffn_iter(ctx, d, r, sems, it, t_p0ffn, dbg_outs)

    ctx.wait("sync", st, st.c)



def emit_half(ctx, d, r, sems, tb, hb, t_p0attn, dbg_outs):
    dS, dG, sP, sV, sA, st, sG = (sems[n] for n in ["dS", "dG", "sP", "sV", "sA", "st", "sG"])
    pc0 = tb * 512 + hb * 256
    t_entry_sV = sems["sV"].c

    t_xfree = sP.c
    ctx.wait("gpsimd", sP, t_xfree)
    xsrc = d.x_fm.rearrange("(k p) n -> p k n", p=128)[:, :, pc0:pc0 + 256]
    t_x = ctx.dma("gpsimd", dG, r.x_bf[:], xsrc)

    tiles = [("a", 0), ("b", 0), ("b", 1)] + [("q", j) for j in range(8)] + \
            [("k", j) for j in range(16)] + [("v", j) for j in range(16)] + \
            [("gg", j) for j in range(8)]
    slab_src = {"q": d.wbf_q, "k": d.wbf_k, "v": d.wbf_v, "gg": d.wbf_g}
    conv_w = {"q": r.cq, "k": r.ck, "v": r.cv}
    hist_base = {"q": 0, "k": 8, "v": 24}

    slot_free = [sV.c] * 4
    slot_free_a = [sA.c] * 4
    slab_free = [sP.c] * 3
    slab_ready = [0] * 3
    pre_free = [sV.c] * 3
    qs_free = [sA.c] * 3
    cvo_free = [sA.c] * 2
    sq_free = [sP.c] * 2
    srow_free = [sP.c] * 3
    nrm_q = 0
    ssrow_free = [sA.c] * 2
    sbc_free = [sV.c] * 2
    lnr_free = [sV.c] * 2
    bank4 = {"rd": sA.c}
    bank5 = {"rd": sV.c}

    pend = []

    def run_pending(cur_ti, flush=False):
        for ent in list(pend):
            if ent["stage"] == "B" and (flush or cur_ti - ent["ti"] >= PIPE_B):
                ent["fnB"]()
                ent["fnC"]()
                ent["stage"] = "D"
            if ent["stage"] == "D" and (flush or cur_ti - ent["ti"] >= PIPE_D):
                ent["fnD"]()
                pend.remove(ent)
        if flush and pend:
            run_pending(cur_ti, flush=True)

    for ti, (kind, j) in enumerate(tiles):
        si = ti % 3
        pi = ti % 4
        psl = r.PSA[:, pi * 512: pi * 512 + 256]

        if kind in slab_src:
            ctx.wait("sync", dG, t_p0attn)
            ctx.wait("sync", sP, slab_free[si])
            slab_ready[si] = ctx.dma("sync", dS, r.slab[si][:], slab_src[kind][j])

        ctx.wait("tensor", dG, t_x)
        ctx.wait("tensor", sV, slot_free[pi])
        ctx.wait("tensor", sA, slot_free_a[pi])
        if kind in slab_src:
            ctx.wait("tensor", dS, slab_ready[si])
            lhs = lambda kk, si=si: r.slab[si][:, kk, :]
            M = 128
        elif kind == "a":
            lhs = lambda kk: r.wa_sb[:, kk, :]
            M = 8
        else:
            lhs = lambda kk, j=j: r.wb_sb[:, kk, j * 8:(j + 1) * 8]
            M = 8
        mm = None
        for kk in range(16):
            mm = ctx.on("tensor", lambda e, kk=kk, lhs=lhs, psl=psl, M=M: e.matmul(
                psl[0:M, :] if M != 128 else psl, lhs(kk), r.x_bf[:, kk, :],
                start=(kk == 0), stop=(kk == 15)))
        ctx.inc(mm, sP)
        if kind in slab_src:
            slab_free[si] = sP.c
        t_mm = sP.c

        if kind == "a":
            ctx.wait("vector", sP, t_mm)
            i = ctx.on("vector", lambda e, pi=pi: e.tensor_scalar(
                out=r.sp1[:], in0=r.PSA[0:8, pi * 512: pi * 512 + 256], scalar1=r.dtb[:],
                scalar2=None, op0=ALU.add))
            ctx.inc(i, sV)
            slot_free[pi] = sV.c
            ctx.wait("scalar", sV, sV.c)
            i = ctx.on("scalar", lambda e: e.activation(out=r.sp2[:], in_=r.sp1[:],
                                                        func=AF.Abs))
            ctx.inc(i, sA)
            ctx.wait("scalar", sA, sA.c)
            i = ctx.on("scalar", lambda e: e.activation(out=r.sp2[:], in_=r.sp2[:],
                                                        func=AF.Exp, scale=-1.0))
            ctx.inc(i, sA)
            ctx.wait("scalar", sA, sA.c)
            i = ctx.on("scalar", lambda e: e.activation(out=r.sp2[:], in_=r.sp2[:],
                                                        func=AF.Ln, bias=1.0))
            ctx.inc(i, sA)
            ctx.wait("vector", sA, sA.c)
            ctx.on("vector", lambda e: e.tensor_scalar(
                out=r.sp1[:], in0=r.sp1[:], scalar1=0.0, scalar2=None, op0=ALU.max))
            ctx.on("vector", lambda e: e.tensor_add(out=r.g8[:], in0=r.sp1[:], in1=r.sp2[:]))
            i = ctx.on("vector", lambda e: e.tensor_scalar(
                out=r.g8[:], in0=r.g8[:], scalar1=r.nexpA[:], scalar2=None, op0=ALU.mult))
            ctx.inc(i, sV)
        elif kind == "b":
            ctx.wait("scalar", sP, t_mm)
            ctx.wait("scalar", sV, t_entry_sV)
            i = ctx.on("scalar", lambda e, j=j, pi=pi: e.activation(
                out=r.beta_nh[j][:], in_=r.PSA[0:8, pi * 512: pi * 512 + 256], func=AF.Sigmoid))
            ctx.inc(i, sA)
            slot_free_a[pi] = sA.c
        elif kind == "gg":
            ctx.wait("scalar", sP, t_mm)
            ctx.wait("scalar", sV, t_entry_sV)
            i = ctx.on("scalar", lambda e, j=j, psl=psl: e.activation(
                out=r.gate[:, j, :], in_=psl, func=AF.Silu))
            ctx.inc(i, sA)
            slot_free_a[pi] = sA.c
        else:
            pri = ti % 3
            ci = ti % 2
            hb_i = hist_base[kind] + j
            ctx.wait("scalar", sP, t_mm)
            ctx.wait("scalar", sV, pre_free[pri])
            pv = psl.rearrange("p (b t) -> p b t", b=2)
            i = ctx.on("scalar", lambda e, pri=pri, pv=pv: e.copy(
                out=r.pre[pri][:, :, 4:132], in_=pv))
            ctx.inc(i, sA)
            slot_free_a[pi] = sA.c
            t_evac = sA.c
            ctx.wait("vector", sA, t_evac)
            ctx.on("vector", lambda e, pri=pri, hb_i=hb_i: e.tensor_copy(
                out=r.pre[pri][:, :, 0:4], in_=r.hist[:, hb_i, :, :]))
            ctx.on("vector", lambda e, pri=pri, hb_i=hb_i: e.tensor_copy(
                out=r.hist[:, hb_i, :, 1:4], in_=r.pre[pri][:, :, 129:132]))
            cw = conv_w[kind]
            ctx.wait("vector", sA, cvo_free[ci])
            ctx.on("vector", lambda e, pri=pri, ci=ci, cw=cw, j=j: e.tensor_scalar(
                out=r.cvo[ci][:], in0=r.pre[pri][:, :, 1:129], scalar1=cw[:, j, 0:1],
                scalar2=None, op0=ALU.mult))
            for tap in range(1, 4):
                i = ctx.on("vector", lambda e, pri=pri, ci=ci, cw=cw, j=j, tap=tap: e.scalar_tensor_tensor(
                    out=r.cvo[ci][:], in0=r.pre[pri][:, :, 1 + tap:129 + tap],
                    scalar=cw[:, j, tap:tap + 1], in1=r.cvo[ci][:],
                    op0=ALU.mult, op1=ALU.add))
            ctx.inc(i, sV)
            t_cv = sV.c
            pre_free[pri] = t_cv
            ctx.wait("scalar", sV, t_cv)
            if kind == "v":
                nh, h = (0, j) if j < 8 else (1, j - 8)
                vout = _vb_view(r.Vb, nh, h)
                i = ctx.on("scalar", lambda e, ci=ci, vout=vout: e.activation(
                    out=vout, in_=r.cvo[ci][:].rearrange("p b (c i) -> p b c i", c=NCHH),
                    func=AF.Silu))
                ctx.inc(i, sA)
                cvo_free[ci] = sA.c
            else:
                n = nrm_q
                nrm_q += 1
                qi = n % 3
                sqi = n % 2
                sri = n % 3
                lo = 512 + (n % 2) * 256
                ro = (n % 2) * 256
                bo = 512 + (n % 2) * 256
                ctx.wait("scalar", sV, qs_free[qi])
                i = ctx.on("scalar", lambda e, ci=ci, qi=qi: e.activation(
                    out=r.qs[qi][:].rearrange("p (b t) -> p b t", b=2), in_=r.cvo[ci][:],
                    func=AF.Silu))
                ctx.inc(i, sA)
                cvo_free[ci] = sA.c
                ctx.wait("scalar", sP, sq_free[sqi])
                ctx.wait("scalar", sV, lnr_free[n % 2])
                i = ctx.on("scalar", lambda e, qi=qi, sqi=sqi: e.activation(
                    out=r.sq[sqi][:], in_=r.qs[qi][:], func=AF.Square))
                ctx.inc(i, sA)
                t_sqr = sA.c
                sc2 = SQ128 if kind == "q" else 1.0
                nh, h = (0, j) if j < 8 else (1, j - 8)
                dst = _vb_view(r.Qb, 1, j) if kind == "q" else _vb_view(r.Kb, nh, h)

                ent = {"ti": ti, "stage": "B"}

                def fnB(t_sqr=t_sqr, sqi=sqi, ro=ro, n=n, ent=ent):
                    ctx.wait("tensor", sA, max(t_sqr, ssrow_free[n % 2], bank4["rd"]))
                    i = ctx.on("tensor", lambda e: e.matmul(
                        r.PSB[0:1, ro:ro + 256], r.onesb[:, 0:1], r.sq[sqi][:],
                        start=True, stop=True))
                    ctx.inc(i, sP)
                    sq_free[sqi] = sP.c
                    ent["t_ssm"] = sP.c

                def fnC(sri=sri, ro=ro, lo=lo, sc2=sc2, n=n, ent=ent):
                    ctx.wait("scalar", sP, ent["t_ssm"])
                    i = ctx.on("scalar", lambda e: e.activation(
                        out=r.lnr[0:1, lo:lo + 256], in_=r.PSB[0:1, ro:ro + 256], func=AF.Sqrt))
                    ctx.inc(i, sA)
                    ssrow_free[n % 2] = sA.c
                    bank4["rd"] = sA.c
                    t_sq2 = sA.c
                    ctx.wait("vector", sA, t_sq2)
                    ctx.wait("vector", sP, srow_free[sri])
                    ctx.on("vector", lambda e: e.tensor_scalar(
                        out=r.lnr[0:1, lo:lo + 256], in0=r.lnr[0:1, lo:lo + 256], scalar1=1e-12,
                        scalar2=sc2, op0=ALU.max, op1=ALU.mult))
                    def _recip(e):
                        with r.nc.allow_low_precision(reason="bf16 norm-scale for bf16 matmul"):
                            return e.reciprocal(out=r.srow[sri][:], in_=r.lnr[0:1, lo:lo + 256])
                    i = ctx.on("vector", _recip)
                    ctx.inc(i, sV)
                    ctx.wait("vector", sV, sV.c)
                    i = ctx.on("vector", lambda e: e.engine_nop())
                    ctx.inc(i, sV)
                    ent["t_srow"] = sV.c
                    lnr_free[n % 2] = sV.c

                def fnD(sri=sri, bo=bo, qi=qi, dst=dst, n=n, ent=ent):
                    ctx.wait("tensor", sV, max(ent["t_srow"], sbc_free[n % 2], bank5["rd"]))
                    i = ctx.on("tensor", lambda e: e.matmul(
                        r.PSB[:, bo:bo + 256], r.onesb[0:1, :], r.srow[sri][:],
                        start=True, stop=True))
                    ctx.inc(i, sP)
                    t_bc = sP.c
                    srow_free[sri] = sP.c
                    ctx.wait("vector", sP, t_bc)
                    i = ctx.on("vector", lambda e: e.tensor_mul(
                        out=dst,
                        in0=r.qs[qi][:].rearrange("p (b c i) -> p b c i", b=2, c=NCHH),
                        in1=r.PSB[:, bo:bo + 256].rearrange("p (b c i) -> p b c i", b=2, c=NCHH)))
                    ctx.inc(i, sV)
                    qs_free[qi] = sV.c
                    sbc_free[n % 2] = sV.c
                    bank5["rd"] = sV.c

                ent["fnB"], ent["fnC"], ent["fnD"] = fnB, fnC, fnD
                pend.append(ent)
        run_pending(ti)
    run_pending(len(tiles), flush=True)

    emit_cumsum_half(ctx, r, sP, sV, sA)

    consts = {"sel8": r.sel8[:], "maskneg": r.maskneg[:], "identf": r.identf[:],
              "identb": r.identb[:]}
    for c in range(NCHH):
        emit_chunk(ctx, r, c, consts, sP, sV, sA)

    emit_wo_ln(ctx, d, r, sems, pc0)

    if DBG and tb == DBG_TB and hb == DBG_HB:
        for nm, src in [("Kb", r.Kb), ("Vb", r.Vb), ("Qb", r.Qb), ("gate", r.gate),
                        ("beta0", r.beta_nh[0]), ("beta1", r.beta_nh[1]), ("g8", r.g8),
                        ("og", r.ogate), ("ypre", r.ypre), ("ystg", r.ystg)]:
            if nm in dbg_outs:
                ctx.wait("sync", sV, sV.c)
                ctx.wait("sync", sA, sA.c)
                ctx.dma("sync", st, dbg_outs[nm], src[:])



def emit_cumsum_half(ctx, r, sP, sV, sA):
    def cp(dst, src):
        ctx.on("vector", lambda e: e.tensor_copy(out=dst, in_=src))

    cp(r.carry0[:, 0:2], r.carry[:, 0:2])
    src, dst = r.cumA, r.cumB
    cp(src[:, 0:256], r.g8[:, 0:256])
    s = 1
    while s < BW:
        sv = src[:].rearrange("p (b t) -> p b t", b=2)
        dv = dst[:].rearrange("p (b t) -> p b t", b=2)
        cp(dv[:, :, 0:s], sv[:, :, 0:s])
        ctx.on("vector", lambda e, sv=sv, dv=dv, s=s: e.tensor_add(
            out=dv[:, :, s:BW], in0=sv[:, :, s:BW], in1=sv[:, :, 0:BW - s]))
        src, dst = dst, src
        s *= 2
    for b in range(2):
        ctx.on("vector", lambda e, b=b, src=src: e.tensor_scalar(
            out=src[:, b * BW:(b + 1) * BW], in0=src[:, b * BW:(b + 1) * BW],
            scalar1=r.carry[:, b:b + 1], scalar2=None, op0=ALU.add))
    if src is not r.cumA:
        cp(r.cumA[:, 0:256], src[:, 0:256])
    for b in range(2):
        cp(r.carry[:, b:b + 1], r.cumA[:, b * BW + BW - 1:b * BW + BW])
    i = ctx.on("vector", lambda e: e.engine_nop())
    ctx.inc(i, sV)



def emit_chunk(ctx, r, c, consts, sP, sV, sA):
    sel8, maskneg, identf, identb = consts["sel8"], consts["maskneg"], consts["identf"], consts["identb"]
    col0 = c * TOKC
    Kc = r.Kb[:, c]
    Vc = r.Vb[:, c]
    Qc = r.Qb[:, c]

    t_pa = sA.c
    ctx.wait("tensor", sA, t_pa)

    ctx.wait("scalar", sV, sV.c)
    i = ctx.on("scalar", lambda e: e.copy(out=r.S0bf[:], in_=r.S[:]))
    ctx.inc(i, sA)
    t_s0bf = sA.c

    for b in range(2):
        src = r.cumA[:, b * BW + col0: b * BW + col0 + TOKC]
        dst = r.cumg_sub[:, b * 128:(b + 1) * 128].rearrange("p (i two) -> p i two", two=2)
        srcb = src.unsqueeze(2).broadcast_to([8, TOKC, 2])
        ctx.on("vector", lambda e, dst=dst, srcb=srcb: e.tensor_copy(out=dst, in_=srcb))
        if col0 == 0:
            base = r.carry0[:, b:b + 1]
        else:
            base = r.cumA[:, b * BW + col0 - 1: b * BW + col0]
        ctx.on("vector", lambda e, b=b, base=base: e.tensor_scalar(
            out=r.cumg_sub[:, b * 128:(b + 1) * 128], in0=r.cumg_sub[:, b * 128:(b + 1) * 128],
            scalar1=base, scalar2=None, op0=ALU.subtract))
    for b in range(2):
        for p in range(2):
            dst = r.beta_sub[:, b * 128 + p: (b + 1) * 128: 2]
            src = r.beta_nh[p][:, b * BW + col0: b * BW + col0 + TOKC]
            ctx.on("vector", lambda e, dst=dst, src=src: e.tensor_copy(out=dst, in_=src))
    i = ctx.on("vector", lambda e: e.engine_nop())
    ctx.inc(i, sV)
    t_prep = sV.c

    ctx.wait("tensor", sV, t_prep)
    for b in range(2):
        ctx.on("tensor", lambda e, b=b: e.transpose(
            r.PSB[:, b * 8:(b + 1) * 8], r.cumg_sub[:, b * 128:(b + 1) * 128], identf[0:8, 0:8]))
    for b in range(2):
        i = ctx.on("tensor", lambda e, b=b: e.transpose(
            r.PSB[:, 16 + b * 8: 16 + (b + 1) * 8], r.beta_sub[:, b * 128:(b + 1) * 128],
            identf[0:8, 0:8]))
    ctx.inc(i, sP)
    t_tp1 = sP.c

    ctx.wait("vector", sP, t_tp1)
    ctx.on("vector", lambda e: e.tensor_copy(out=r.cumg_T[:], in_=r.PSB[:, 0:16]))
    i = ctx.on("vector", lambda e: e.tensor_copy(out=r.beta_T[:], in_=r.PSB[:, 16:32]))
    ctx.inc(i, sV)
    t_tp1e = sV.c
    ctx.wait("scalar", sV, t_tp1e)
    i = ctx.on("scalar", lambda e: e.activation(out=r.lamc[:], in_=r.cumg_T[:], func=AF.Exp))
    ctx.inc(i, sA)
    t_lamc = sA.c

    for bh in range(16):
        b, h = bh // 8, bh % 8
        i = ctx.on("tensor", lambda e, b=b, h=h, bh=bh: e.matmul(
            r.PSA[:, bh * 128:(bh + 1) * 128], sel8[:, h * 128:(h + 1) * 128],
            r.cumg_sub[:, b * 128:(b + 1) * 128], start=True, stop=True))
    ctx.inc(i, sP)
    t_cgrow = sP.c

    ctx.wait("scalar", sP, t_cgrow)
    i = ctx.on("scalar", lambda e: e.activation(out=r.lam_row[:], in_=r.PSA[:], func=AF.Exp))
    ctx.inc(i, sA)
    mview = maskneg.unsqueeze(1).broadcast_to([128, 8, 128])
    for half in range(2):
        cslice = r.cumg_T[:, half * 8:(half + 1) * 8]
        colrep = cslice.unsqueeze(2).broadcast_to([128, 8, 128])
        pa = r.PSA[:, half * 1024:(half + 1) * 1024].rearrange("p (a b) -> p a b", b=128)
        t3 = r.t32[:].rearrange("p (a b) -> p a b", b=128)
        ctx.wait("vector", sA, sA.c)
        ctx.on("vector", lambda e, pa=pa, colrep=colrep, t3=t3: e.tensor_sub(out=t3, in0=pa, in1=colrep))
        ctx.on("vector", lambda e, t3=t3, mview=mview: e.tensor_add(out=t3, in0=t3, in1=mview))
        i = ctx.on("vector", lambda e: e.engine_nop())
        ctx.inc(i, sV)
        ctx.wait("scalar", sV, sV.c)
        i = ctx.on("scalar", lambda e, half=half: e.activation(
            out=r.E[:, half * 1024:(half + 1) * 1024], in_=r.t32[:], func=AF.Exp))
        ctx.inc(i, sA)
    t_E = sA.c
    ctx.wait("vector", sA, t_E)
    ctx.on("vector", lambda e: e.tensor_copy(out=r.edecf[:], in_=r.E[:, 127::128]))
    ctx.on("vector", lambda e: e.tensor_copy(out=r.lamlast[:], in_=r.lam_row[:, 127::128]))
    t_psa_free_v = sV.c
    t_psa_free_a = sA.c

    ctx.wait("tensor", sV, t_psa_free_v)
    ctx.wait("tensor", sA, t_psa_free_a)
    for bh in range(16):
        i = ctx.on("tensor", lambda e, bh=bh: e.matmul(
            r.PSA[:, bh * 128:(bh + 1) * 128], Kc[:, bh], Kc[:, bh], start=True, stop=True))
    ctx.inc(i, sP)
    t_A = sP.c
    iview = identb.unsqueeze(1).broadcast_to([128, 16, 128])
    Ev = r.E[:].rearrange("p (a b) -> p a b", b=128)
    tbv = r.tmp_bf[:].rearrange("p (a b) -> p a b", b=128)
    ctx.wait("vector", sA, t_E)
    ctx.on("vector", lambda e: e.tensor_sub(out=tbv, in0=Ev, in1=iview))
    ctx.wait("vector", sP, t_A)
    i = ctx.on("vector", lambda e: e.tensor_mul(out=r.Nt[:], in0=r.PSA[:], in1=r.tmp_bf[:]))
    ctx.inc(i, sV)
    t_AE = sV.c
    ctx.wait("tensor", sV, t_AE)
    for bh in range(16):
        i = ctx.on("tensor", lambda e, bh=bh: e.matmul(
            r.PSA[:, bh * 128:(bh + 1) * 128], Kc[:, bh], Qc[:, bh], start=True, stop=True))
    ctx.inc(i, sP)
    t_KQ = sP.c
    ctx.wait("vector", sP, t_KQ)
    i = ctx.on("vector", lambda e: e.tensor_mul(out=r.Wt[:], in0=r.PSA[:], in1=r.E[:]))
    ctx.inc(i, sV)
    t_W = sV.c
    ctx.wait("tensor", sV, t_W)
    for bh in range(16):
        b, h = bh // 8, bh % 8
        i = ctx.on("tensor", lambda e, b=b, h=h, bh=bh: e.matmul(
            r.PSA[:, bh * 128:(bh + 1) * 128], sel8[:, h * 128:(h + 1) * 128],
            r.beta_sub[:, b * 128:(b + 1) * 128], start=True, stop=True))
    ctx.inc(i, sP)
    t_brow = sP.c
    ctx.wait("vector", sP, t_brow)
    i = ctx.on("vector", lambda e: e.scalar_tensor_tensor(
        out=r.Nt[:], in0=r.Nt[:], scalar=2.0, in1=r.PSA[:], op0=ALU.mult, op1=ALU.mult))
    ctx.inc(i, sV)
    t_Nt = sV.c

    ctx.wait("tensor", sV, t_Nt)
    for bh in range(16):
        i = ctx.on("tensor", lambda e, bh=bh: e.transpose(
            r.PSC[:, bh * 128:(bh + 1) * 128], r.Nt[:, bh * 128:(bh + 1) * 128], identb))
    ctx.inc(i, sP)
    t_Ntp = sP.c
    ctx.wait("vector", sP, t_Ntp)
    i = ctx.on("vector", lambda e: e.tensor_copy(out=r.Nn[:], in_=r.PSC[:]))
    ctx.inc(i, sV)
    t_Nn = sV.c

    ctx.wait("tensor", sV, t_Nn)
    for bh in range(16):
        i = ctx.on("tensor", lambda e, bh=bh: e.transpose(
            r.PSC[:, bh * 128:(bh + 1) * 128], Vc[:, bh], identb))
    ctx.inc(i, sP)
    t_Vtp = sP.c
    ctx.wait("vector", sP, t_Vtp)
    i = ctx.on("vector", lambda e: e.tensor_copy(out=r.Vtm[:], in_=r.PSC[:]))
    ctx.inc(i, sV)
    t_Vtm = sV.c

    ctx.wait("tensor", sA, t_s0bf)
    for bh in range(16):
        i = ctx.on("tensor", lambda e, bh=bh: e.matmul(
            r.PSA[:, bh * 128:(bh + 1) * 128], Kc[:, bh],
            r.S0bf[:, bh * 128:(bh + 1) * 128], start=True, stop=True))
    ctx.inc(i, sP)
    t_KS0 = sP.c
    ctx.wait("vector", sP, t_KS0)
    ctx.wait("vector", sA, t_lamc)
    for bh in range(16):
        sl = slice(bh * 128, (bh + 1) * 128)
        ctx.on("vector", lambda e, sl=sl, bh=bh: e.scalar_tensor_tensor(
            out=r.U32[:, sl], in0=r.PSA[:, sl], scalar=r.lamc[:, bh:bh + 1],
            in1=r.Vtm[:, sl], op0=ALU.mult, op1=ALU.subtract))
        ctx.on("vector", lambda e, sl=sl, bh=bh: e.tensor_scalar(
            out=r.U32[:, sl], in0=r.U32[:, sl], scalar1=r.beta_T[:, bh:bh + 1],
            scalar2=-2.0, op0=ALU.mult, op1=ALU.mult))
    i = ctx.on("vector", lambda e: e.engine_nop())
    ctx.inc(i, sV)
    t_R = sV.c
    ctx.wait("scalar", sV, t_R)
    i = ctx.on("scalar", lambda e: e.copy(out=r.Ubf[:], in_=r.U32[:]))
    ctx.inc(i, sA)
    t_ubf_prev = sA.c

    Acur, Bcur = r.Nt, r.Nn
    Anew, Bnew = r.Nt2, r.Nn2
    p = 1
    t_psa_free = t_R
    while p <= 64:
        ctx.wait("tensor", sA, t_ubf_prev)
        ctx.wait("tensor", sV, t_psa_free)
        for bh in range(16):
            sl = slice(bh * 128, (bh + 1) * 128)
            i = ctx.on("tensor", lambda e, sl=sl, Acur=Acur: e.matmul(
                r.PSA[:, sl], Acur[:, sl], r.Ubf[:, sl], start=True, stop=True))
        ctx.inc(i, sP)
        t_ap = sP.c
        ctx.wait("vector", sP, t_ap)
        op = ALU.subtract if p == 1 else ALU.add
        i = ctx.on("vector", lambda e, op=op: e.tensor_tensor(
            out=r.U32[:], in0=r.U32[:], in1=r.PSA[:], op=op))
        ctx.inc(i, sV)
        t_psa_free = sV.c
        ctx.wait("scalar", sV, t_psa_free)
        i = ctx.on("scalar", lambda e: e.copy(out=r.Ubf[:], in_=r.U32[:]))
        ctx.inc(i, sA)
        t_ubf_prev = sA.c
        if p < 64:
            ctx.wait("tensor", sV, t_psa_free)
            for bh in range(16):
                sl = slice(bh * 128, (bh + 1) * 128)
                i = ctx.on("tensor", lambda e, sl=sl, Acur=Acur, Bcur=Bcur: e.matmul(
                    r.PSA[:, sl], Bcur[:, sl], Acur[:, sl], start=True, stop=True))
            ctx.inc(i, sP)
            t_d1 = sP.c
            ctx.wait("vector", sP, t_d1)
            i = ctx.on("vector", lambda e, Anew=Anew: e.tensor_copy(out=Anew[:], in_=r.PSA[:]))
            ctx.inc(i, sV)
            t_Anew = sV.c
            ctx.wait("tensor", sV, t_Anew)
            for bh in range(16):
                sl = slice(bh * 128, (bh + 1) * 128)
                i = ctx.on("tensor", lambda e, sl=sl, Acur=Acur, Bcur=Bcur: e.matmul(
                    r.PSA[:, sl], Acur[:, sl], Bcur[:, sl], start=True, stop=True))
            ctx.inc(i, sP)
            t_d2 = sP.c
            ctx.wait("vector", sP, t_d2)
            i = ctx.on("vector", lambda e, Bnew=Bnew: e.tensor_copy(out=Bnew[:], in_=r.PSA[:]))
            ctx.inc(i, sV)
            t_Bnew = sV.c
            Acur, Anew = Anew, Acur
            Bcur, Bnew = Bnew, Bcur
            t_psa_free = t_Bnew
        p *= 2

    i = ctx.on("vector", lambda e: e.tensor_mul(
        out=r.tmp_bf[:].rearrange("p (a b) -> p a b", b=128), in0=Qc,
        in1=r.lam_row[:].rearrange("p (a b) -> p a b", b=128)))
    ctx.inc(i, sV)
    t_qlam = sV.c
    ctx.wait("tensor", sV, t_qlam)
    ctx.wait("tensor", sA, t_ubf_prev)
    for bh in range(16):
        sl = slice(bh * 128, (bh + 1) * 128)
        ctx.on("tensor", lambda e, sl=sl: e.matmul(
            r.PSA[:, sl], r.tmp_bf[:, sl], r.S0bf[:, sl], start=True, stop=False))
        i = ctx.on("tensor", lambda e, sl=sl: e.matmul(
            r.PSA[:, sl], r.Wt[:, sl], r.Ubf[:, sl], start=False, stop=True))
    ctx.inc(i, sP)
    t_O = sP.c

    ctx.wait("scalar", sP, t_O)
    i = ctx.on("scalar", lambda e: e.activation(out=r.sqbf[:], in_=r.PSA[:], func=AF.Square))
    ctx.inc(i, sA)
    t_sq = sA.c
    ctx.wait("vector", sA, t_sq)
    i = ctx.on("vector", lambda e: e.tensor_reduce(
        out=r.ssb[:], in_=r.sqbf[:].rearrange("p (a b) -> p a b", b=128), axis=AX.X, op=ALU.add))
    ctx.inc(i, sV)
    ctx.wait("scalar", sV, sV.c)
    i = ctx.on("scalar", lambda e: e.activation(out=r.rstd[:], in_=r.ssb[:], func=AF.Sqrt,
                                                scale=1.0 / 128.0, bias=1e-5))
    ctx.inc(i, sA)
    ctx.wait("vector", sA, sA.c)
    i = ctx.on("vector", lambda e: e.reciprocal(out=r.rstd[:], in_=r.rstd[:]))
    ctx.inc(i, sV)
    ctx.wait("vector", sV, sV.c)
    rsv = r.rstd[:].unsqueeze(2).broadcast_to([128, 16, 128])
    i = ctx.on("vector", lambda e: e.tensor_mul(
        out=r.on[:].rearrange("p (a b) -> p a b", b=128),
        in0=r.PSA[:].rearrange("p (a b) -> p a b", b=128), in1=rsv))
    ctx.inc(i, sV)
    t_on = sV.c
    ctx.wait("tensor", sV, t_on)
    for bh in range(16):
        i = ctx.on("tensor", lambda e, bh=bh: e.transpose(
            r.PSC[:, bh * 128:(bh + 1) * 128], r.on[:, bh * 128:(bh + 1) * 128], identb))
    ctx.inc(i, sP)
    t_otp = sP.c
    ctx.wait("vector", sP, t_otp)
    for bh in range(16):
        b, h = bh // 8, bh % 8
        dst = r.ogate[:, h, b * BW + col0: b * BW + col0 + TOKC]
        src = r.PSC[:, bh * 128 + 1:(bh + 1) * 128:2]
        gsl = r.gate[:, h, b * BW + col0: b * BW + col0 + TOKC]
        i = ctx.on("vector", lambda e, dst=dst, src=src, gsl=gsl: e.scalar_tensor_tensor(
            out=dst, in0=src, scalar=r.onw[:], in1=gsl, op0=ALU.mult, op1=ALU.mult))
    ctx.inc(i, sV)
    t_og = sV.c

    ctx.wait("tensor", sV, t_og)
    for bh in range(16):
        i = ctx.on("tensor", lambda e, bh=bh: e.transpose(
            r.PSC[:, bh * 128:(bh + 1) * 128], Kc[:, bh], identb))
    ctx.inc(i, sP)
    t_ktp = sP.c
    ctx.wait("vector", sP, t_ktp)
    for bh in range(16):
        sl = slice(bh * 128, (bh + 1) * 128)
        ctx.on("vector", lambda e, sl=sl, bh=bh: e.tensor_scalar(
            out=r.KDtm[:, sl], in0=r.PSC[:, sl],
            scalar1=r.edecf[:, bh:bh + 1], scalar2=None, op0=ALU.mult))
        ctx.on("vector", lambda e, sl=sl, bh=bh: e.tensor_scalar(
            out=r.S[:, sl], in0=r.S[:, sl],
            scalar1=r.lamlast[:, bh:bh + 1], scalar2=None, op0=ALU.mult))
    i = ctx.on("vector", lambda e: e.engine_nop())
    ctx.inc(i, sV)
    t_kdtm = sV.c
    ctx.wait("tensor", sV, t_kdtm)
    for bh in range(16):
        sl = slice(bh * 128, (bh + 1) * 128)
        i = ctx.on("tensor", lambda e, sl=sl: e.matmul(
            r.PSA[:, sl], r.KDtm[:, sl], r.Ubf[:, sl], start=True, stop=True))
    ctx.inc(i, sP)
    t_smm = sP.c
    ctx.wait("vector", sP, t_smm)
    i = ctx.on("vector", lambda e: e.tensor_add(out=r.S[:], in0=r.S[:], in1=r.PSA[:]))
    ctx.inc(i, sV)



def emit_wo_ln(ctx, d, r, sems, pc0):
    dS, dG, sP, sV, sA, st = (sems[n] for n in ["dS", "dG", "sP", "sV", "sA", "st"])
    t_og_all = sV.c
    slab_free = [sP.c] * 3
    slot_free = [sV.c] * 4
    xf_cons = [sV.c] * 2

    for ot in range(16):
        si = ot % 2
        pi = ot % 4
        sli = ot % 3
        psl = r.PSA[:, pi * 512: pi * 512 + 256]
        ctx.wait("sync", sV, xf_cons[si])
        t_xf = ctx.dma("sync", dS, r.xfr[si][:], d.x_fm[ot * 128:(ot + 1) * 128, pc0:pc0 + 256])
        ctx.wait("sync", sP, slab_free[sli])
        t_slab = ctx.dma("sync", dS, r.slab[sli][:, 0:8, :], d.wbf_o[ot])
        ctx.wait("tensor", dS, t_slab)
        ctx.wait("tensor", sV, max(slot_free[pi], t_og_all))
        mm = None
        for kk in range(8):
            mm = ctx.on("tensor", lambda e, kk=kk, psl=psl, sli=sli: e.matmul(
                psl, r.slab[sli][:, kk, :], r.ogate[:, kk, :],
                start=(kk == 0), stop=(kk == 7)))
        ctx.inc(mm, sP)
        slab_free[sli] = sP.c
        t_mm = sP.c
        ctx.wait("vector", sP, t_mm)
        ctx.wait("vector", dS, t_xf)
        i = ctx.on("vector", lambda e, psl=psl, si=si, ot=ot: e.tensor_add(
            out=r.ypre[:, ot, :], in0=psl, in1=r.xfr[si][:]))
        ctx.inc(i, sV)
        slot_free[pi] = sV.c
        xf_cons[si] = sV.c
    t_ypre = sV.c

    ctx.wait("tensor", sV, t_ypre)
    mm = None
    for ot in range(16):
        mm = ctx.on("tensor", lambda e, ot=ot: e.matmul(
            r.PSB[0:1, 0:256], r.onesb[:, 0:1], r.ypre[:, ot, :],
            start=(ot == 0), stop=(ot == 15)))
    ctx.inc(mm, sP)
    ctx.wait("scalar", sV, t_ypre)
    sq_mm = [sP.c] * 2
    t_sqr_q = []
    for ot in range(17):
        if ot < 16:
            sqi = ot % 2
            ctx.wait("scalar", sP, sq_mm[sqi])
            i = ctx.on("scalar", lambda e, ot=ot, sqi=sqi: e.activation(
                out=r.sq[sqi][:], in_=r.ypre[:, ot, :], func=AF.Square))
            ctx.inc(i, sA)
            t_sqr_q.append(sA.c)
        if ot >= 1:
            po = ot - 1
            sqi = po % 2
            ctx.wait("tensor", sA, t_sqr_q[po])
            i = ctx.on("tensor", lambda e, sqi=sqi, po=po: e.matmul(
                r.PSB[0:1, 256:512], r.onesb[:, 0:1], r.sq[sqi][:],
                start=(po == 0), stop=(po == 15)))
            ctx.inc(i, sP)
            sq_mm[sqi] = sP.c
    t_sumsq = sP.c
    ctx.wait("vector", sP, t_sumsq)
    ctx.on("vector", lambda e: e.tensor_scalar(
        out=r.lnr[0:1, 0:256], in0=r.PSB[0:1, 0:256], scalar1=-1.0 / 2048.0,
        scalar2=None, op0=ALU.mult))
    ctx.on("vector", lambda e: e.tensor_mul(
        out=r.lnr[0:1, 512:768], in0=r.lnr[0:1, 0:256], in1=r.lnr[0:1, 0:256]))
    i = ctx.on("vector", lambda e: e.scalar_tensor_tensor(
        out=r.lnr[0:1, 256:512], in0=r.PSB[0:1, 256:512], scalar=1.0 / 2048.0,
        in1=r.lnr[0:1, 512:768], op0=ALU.mult, op1=ALU.subtract))
    ctx.inc(i, sV)
    t_var = sV.c
    ctx.wait("scalar", sV, t_var)
    i = ctx.on("scalar", lambda e: e.activation(out=r.lnr[0:1, 256:512],
                                                in_=r.lnr[0:1, 256:512], func=AF.Sqrt, bias=1e-5))
    ctx.inc(i, sA)
    ctx.wait("vector", sA, sA.c)
    i = ctx.on("vector", lambda e: e.reciprocal(out=r.lnr[0:1, 256:512],
                                                in_=r.lnr[0:1, 256:512]))
    ctx.inc(i, sV)
    ctx.wait("vector", sV, sV.c)
    i = ctx.on("vector", lambda e: e.engine_nop())
    ctx.inc(i, sV)
    t_rs = sV.c
    ctx.wait("tensor", sV, t_rs)
    ctx.on("tensor", lambda e: e.matmul(
        r.PSB[:, 512:768], r.onesf[0:1, :], r.lnr[0:1, 0:256], start=True, stop=True))
    i = ctx.on("tensor", lambda e: e.matmul(
        r.PSB[:, 768:1024], r.onesf[0:1, :], r.lnr[0:1, 256:512], start=True, stop=True))
    ctx.inc(i, sP)
    t_bc = sP.c
    ctx.wait("vector", sP, t_bc)
    ctx.wait("vector", st, r.spill_st)
    for ot in range(16):
        ctx.on("vector", lambda e, ot=ot: e.tensor_add(
            out=r.lnt[:], in0=r.ypre[:, ot, :], in1=r.PSB[:, 512:768]))
        i = ctx.on("vector", lambda e, ot=ot: e.tensor_mul(
            out=r.ystg[:, ot, :], in0=r.lnt[:], in1=r.PSB[:, 768:1024]))
        ctx.inc(i, sV)
    t_y = sV.c
    ctx.wait("sync", sV, t_y)
    ctx.dma("sync", st, d.ybf.rearrange("(k p) n -> p k n", p=128)[:, :, pc0:pc0 + 256],
            r.ystg[:])
    r.spill_st = sems["st"].c



def emit_ffn_iter(ctx, d, r, sems, it, t_p0ffn, dbg_outs):
    dS, dG, sP, sV, sA, st = (sems[n] for n in ["dS", "dG", "sP", "sV", "sA", "st"])
    pc0 = it * FIT
    t_spills = st.c

    par = it % 2
    ybf_sb = r.ybf_sb2[par]
    if it == 0:
        ctx.wait("sync", st, t_spills)
        r.y_pref[0] = ctx.dma("sync", dS, ybf_sb[:],
                              d.ybf.rearrange("(k p) n -> p k n", p=128)[:, :, 0:FIT])
    t_y = r.y_pref[par]
    if it + 1 < NFI:
        ctx.wait("sync", st, t_spills)
        ctx.wait("sync", sP, r.y_used[1 - par])
        r.y_pref[1 - par] = ctx.dma(
            "sync", dS, r.ybf_sb2[1 - par][:],
            d.ybf.rearrange("(k p) n -> p k n", p=128)[:, :, (it + 1) * FIT:(it + 2) * FIT])
    ctx.wait("vector", dS, t_y)
    ctx.wait("vector", sP, sP.c)
    ctx.wait("vector", sA, sA.c)
    i = None
    for kk in range(16):
        i = ctx.on("vector", lambda e, kk=kk: e.tensor_copy(
            out=r.z[:, kk, :], in_=ybf_sb[:, kk, :]))
    ctx.inc(i, sV)
    t_z0 = sV.c

    ctx.wait("tensor", dS, t_y)
    ctx.wait("tensor", sV, t_z0)
    ctx.wait("tensor", sA, sA.c)
    mm = None
    for kk in range(16):
        mm = ctx.on("tensor", lambda e, kk=kk: e.matmul(
            r.PSB[0:16, 0:512], r.wafn_sb[:, kk, :], ybf_sb[:, kk, :],
            start=(kk == 0), stop=(kk == 15)))
    ctx.inc(mm, sP)
    t_aff = sP.c
    ctx.wait("scalar", sP, t_aff)
    i = ctx.on("scalar", lambda e: e.activation(out=r.aff[:], in_=r.PSB[0:16, 0:512],
                                                func=AF.Sigmoid))
    ctx.inc(i, sA)
    t_affs = sA.c
    ctx.wait("tensor", sA, t_affs)
    for q4 in range(4):
        i = ctx.on("tensor", lambda e, q4=q4: e.transpose(
            r.PSB[:, 512 + q4 * 16: 512 + (q4 + 1) * 16],
            r.aff[:, q4 * 128:(q4 + 1) * 128], r.identf[:][0:16, 0:16]))
    ctx.inc(i, sP)
    t_at = sP.c
    ctx.wait("vector", sP, t_at)
    i = ctx.on("vector", lambda e: e.tensor_copy(
        out=r.affT[:].rearrange("p a b -> p (a b)"), in_=r.PSB[:, 512:576]))
    ctx.inc(i, sV)
    ctx.wait("vector", sV, sV.c)
    av = r.affT[:]
    i = ctx.on("vector", lambda e: e.tensor_reduce(out=r.m1[:], in_=av, axis=AX.X, op=ALU.max))
    ctx.inc(i, sV)
    ctx.wait("vector", sV, sV.c)
    for q4 in range(4):
        i = ctx.on("vector", lambda e, q4=q4: e.tensor_scalar(
            out=r.eq[:, q4, :], in0=av[:, q4, :], scalar1=r.m1[:, q4:q4 + 1],
            scalar2=None, op0=ALU.is_ge))
    ctx.inc(i, sV)
    ctx.wait("vector", sV, sV.c)
    ctx.on("vector", lambda e: e.scalar_tensor_tensor(
        out=r.combT[:].rearrange("p a b -> p (a b)"),
        in0=r.eq[:].rearrange("p a b -> p (a b)"), scalar=-1e30,
        in1=av.rearrange("p a b -> p (a b)"), op0=ALU.mult, op1=ALU.add))
    i = ctx.on("vector", lambda e: e.engine_nop())
    ctx.inc(i, sV)
    ctx.wait("vector", sV, sV.c)
    i = ctx.on("vector", lambda e: e.tensor_reduce(
        out=r.m2[:], in_=r.combT[:], axis=AX.X, op=ALU.max))
    ctx.inc(i, sV)
    ctx.wait("vector", sV, sV.c)
    i = ctx.on("vector", lambda e: e.tensor_add(out=r.den[:], in0=r.m1[:], in1=r.m2[:]))
    ctx.inc(i, sV)
    ctx.wait("scalar", sV, sV.c)
    i = ctx.on("scalar", lambda e: e.activation(out=r.den[:], in_=r.den[:], func=AF.Ln))
    ctx.inc(i, sA)
    ctx.wait("scalar", sA, sA.c)
    i = ctx.on("scalar", lambda e: e.activation(out=r.den[:], in_=r.den[:], func=AF.Exp,
                                                scale=-1.0))
    ctx.inc(i, sA)
    ctx.wait("vector", sA, sA.c)
    for q4 in range(4):
        i = ctx.on("vector", lambda e, q4=q4: e.tensor_scalar(
            out=r.eq[:, q4, :], in0=av[:, q4, :], scalar1=r.m2[:, q4:q4 + 1],
            scalar2=None, op0=ALU.is_ge))
    ctx.inc(i, sV)
    ctx.wait("vector", sV, sV.c)
    for q4 in range(4):
        i = ctx.on("vector", lambda e, q4=q4: e.tensor_mul(
            out=r.combT[:, q4, :], in0=av[:, q4, :], in1=r.eq[:, q4, :]))
    ctx.inc(i, sV)
    ctx.wait("vector", sV, sV.c)
    for q4 in range(4):
        i = ctx.on("vector", lambda e, q4=q4: e.tensor_scalar(
            out=r.combT[:, q4, :], in0=r.combT[:, q4, :], scalar1=r.den[:, q4:q4 + 1],
            scalar2=None, op0=ALU.mult))
    ctx.inc(i, sV)
    t_combT = sV.c
    ctx.wait("tensor", sV, t_combT)
    for q4 in range(4):
        i = ctx.on("tensor", lambda e, q4=q4: e.transpose(
            r.PSB[0:16, 512 + q4 * 128: 512 + (q4 + 1) * 128],
            r.combT[:, q4, :], r.identf[:]))
    ctx.inc(i, sP)
    t_cbt = sP.c
    ctx.wait("vector", sP, t_cbt)
    i = ctx.on("vector", lambda e: e.tensor_copy(out=r.comb[:], in_=r.PSB[0:16, 512:1024]))
    ctx.inc(i, sV)
    t_comb = sV.c

    if DBG and it == 0 and "comb" in dbg_outs:
        ctx.wait("sync", sV, t_comb)
        ctx.wait("sync", sA, sA.c)
        ctx.dma("sync", st, dbg_outs["comb"], r.comb[:])
        ctx.dma("sync", st, dbg_outs["affdbg"], r.aff[:])
        ctx.dma("sync", st, dbg_outs["affT"], r.affT[:])
        ctx.dma("sync", st, dbg_outs["m1"], r.m1[:])
        ctx.dma("sync", st, dbg_outs["m2"], r.m2[:])
        ctx.dma("sync", st, dbg_outs["den"], r.den[:])
        ctx.dma("sync", st, dbg_outs["combT"], r.combT[:])
        t_tap = sems["st"].c
        ctx.wait("vector", st, t_tap)
        ctx.wait("tensor", st, t_tap)

    wsrc = lambda ee: (d.rg_bf[ee], d.rw1_bf[ee], d.rw2_bf[ee]) if ee < E_RT else \
                      (d.sg_bf[ee - E_RT], d.sw1_bf[ee - E_RT], d.sw2_bf[ee - E_RT])
    bias_g = lambda ee, ft: r.rgb[:, ee, ft:ft + 1] if ee < E_RT else r.sgb[:, ee - E_RT, ft:ft + 1]
    bias_1 = lambda ee, ft: r.rb1[:, ee, ft:ft + 1] if ee < E_RT else r.sb1[:, ee - E_RT, ft:ft + 1]

    wa_free = [sP.c] * 3
    wb_free = [sP.c] * 3
    w2_free = [sP.c] * 3
    seqA = 0
    seqW2 = 0
    t_g1free = sV.c
    pb_v = [sV.c] * 2
    pb_a = [sA.c] * 2
    z_free = [sV.c] * 4
    t_combbc_cons = sV.c
    u_free = sP.c

    for grp in range(6):
        experts = [grp * 3, grp * 3 + 1, grp * 3 + 2]
        for gi, ee in enumerate(experts):
            srcs = wsrc(ee)
            if ee < E_RT:
                ctx.wait("tensor", sV, max(t_comb, t_combbc_cons))
                i = ctx.on("tensor", lambda e, ee=ee: e.matmul(
                    r.PSB[:, 0:512], r.sel16[:, ee * 128:(ee + 1) * 128], r.comb[:],
                    start=True, stop=True))
                ctx.inc(i, sP)
                t_cbc = sP.c
            for fp in range(4):
                sa_i = seqA % 3
                seqA += 1
                ctx.wait("sync", dG, t_p0ffn)
                ctx.wait("sync", sP, wa_free[sa_i])
                t_wa = ctx.dma("sync", dS, r.wA[sa_i][:], srcs[0][fp])
                ctx.wait("sync", sP, wb_free[sa_i])
                t_wb = ctx.dma("sync", dS, r.wB[sa_i][:], srcs[1][fp])
                for fc2 in range(2):
                    fc = fp * 2 + fc2
                    psl_g = r.PSA[:, fc2 * 1024: fc2 * 1024 + 512]
                    psl_1 = r.PSA[:, fc2 * 1024 + 512: fc2 * 1024 + 1024]
                    ctx.wait("tensor", dS, max(t_wa, t_wb))
                    ctx.wait("tensor", sV, pb_v[fc2])
                    ctx.wait("tensor", sA, pb_a[fc2])
                    mm = None
                    for kk in range(16):
                        mm = ctx.on("tensor", lambda e, kk=kk, sa_i=sa_i, fc2=fc2, psl_g=psl_g: e.matmul(
                            psl_g, r.wA[sa_i][:, kk, fc2 * 128:(fc2 + 1) * 128],
                            ybf_sb[:, kk, :], start=(kk == 0), stop=(kk == 15)))
                    for kk in range(16):
                        mm = ctx.on("tensor", lambda e, kk=kk, sa_i=sa_i, fc2=fc2, psl_1=psl_1: e.matmul(
                            psl_1, r.wB[sa_i][:, kk, fc2 * 128:(fc2 + 1) * 128],
                            ybf_sb[:, kk, :], start=(kk == 0), stop=(kk == 15)))
                    ctx.inc(mm, sP)
                    t_h = sP.c
                    ctx.wait("scalar", sP, t_h)
                    ctx.wait("scalar", sV, t_g1free)
                    i = ctx.on("scalar", lambda e, psl_g=psl_g, ee=ee, fc=fc: e.activation(
                        out=r.g1[:], in_=psl_g, func=AF.Gelu, bias=bias_g(ee, fc)))
                    ctx.inc(i, sA)
                    pb_a[fc2] = sA.c
                    ctx.wait("vector", sA, pb_a[fc2])
                    ctx.wait("vector", sP, u_free)
                    if ee < E_RT:
                        ctx.wait("vector", sP, t_cbc)
                        i = ctx.on("vector", lambda e, psl_1=psl_1, ee=ee, fc=fc: e.scalar_tensor_tensor(
                            out=r.u0[:], in0=psl_1, scalar=bias_1(ee, fc), in1=r.g1[:],
                            op0=ALU.add, op1=ALU.mult))
                        ctx.inc(i, sV)
                        t_g1free = sV.c
                        i = ctx.on("vector", lambda e, gi=gi, fc=fc: e.tensor_mul(
                            out=r.u[gi][:, fc, :], in0=r.u0[:], in1=r.PSB[:, 0:512]))
                        ctx.inc(i, sV)
                        t_combbc_cons = sV.c
                    else:
                        i = ctx.on("vector", lambda e, psl_1=psl_1, ee=ee, fc=fc, gi=gi: e.scalar_tensor_tensor(
                            out=r.u[gi][:, fc, :], in0=psl_1, scalar=bias_1(ee, fc), in1=r.g1[:],
                            op0=ALU.add, op1=ALU.mult))
                        ctx.inc(i, sV)
                        t_g1free = sV.c
                    pb_v[fc2] = sV.c
                wa_free[sa_i] = sP.c
                wb_free[sa_i] = sP.c
        t_u_grp = sV.c
        t_u_grp_a = sA.c

        for ps4 in range(4):
            rdy = []
            for gi, ee in enumerate(experts):
                w2i = seqW2 % 3
                seqW2 += 1
                ctx.wait("sync", sP, w2_free[w2i])
                t_rdy = ctx.dma("sync", dS, r.w2s[w2i][:], wsrc(ee)[2][ps4])
                rdy.append((w2i, t_rdy))
            for dc4 in range(4):
                dc = ps4 * 4 + dc4
                psl = r.PSA[:, dc4 * 512:(dc4 + 1) * 512]
                ctx.wait("tensor", sV, max(t_u_grp, z_free[dc4]))
                ctx.wait("tensor", sA, t_u_grp_a)
                ntot = 24 + (2 if grp == 0 else 0)
                nmm = 0
                mm = None
                for gi, ee in enumerate(experts):
                    w2i, t_rdy = rdy[gi]
                    ctx.wait("tensor", dS, t_rdy)
                    for fc in range(8):
                        mm = ctx.on("tensor", lambda e, w2i=w2i, fc=fc, dc4=dc4, psl=psl,
                                    gi=gi, nmm=nmm, ntot=ntot: e.matmul(
                            psl, r.w2s[w2i][:, fc, dc4 * 128:(dc4 + 1) * 128],
                            r.u[gi][:, fc, :], start=(nmm == 0), stop=(nmm == ntot - 1)))
                        nmm += 1
                if grp == 0:
                    mm = ctx.on("tensor", lambda e, dc=dc, psl=psl: e.matmul(
                        psl, r.b2r_sb[:, dc * 128:(dc + 1) * 128], r.comb[:],
                        start=False, stop=False, skip_group_check=True))
                    mm = ctx.on("tensor", lambda e, dc=dc, psl=psl: e.matmul(
                        psl, r.b2s_sb[:, dc * 128:(dc + 1) * 128], r.ones2[:],
                        start=False, stop=True, skip_group_check=True))
                ctx.inc(mm, sP)
                t_w2 = sP.c
                ctx.wait("vector", sP, t_w2)
                ctx.wait("vector", sV, t_z0)
                if grp == 0:
                    i = ctx.on("vector", lambda e, dc=dc, psl=psl: e.scalar_tensor_tensor(
                        out=r.z[:, dc, :], in0=r.z[:, dc, :], scalar=2.0, in1=psl,
                        op0=ALU.mult, op1=ALU.add))
                else:
                    i = ctx.on("vector", lambda e, dc=dc, psl=psl: e.tensor_add(
                        out=r.z[:, dc, :], in0=r.z[:, dc, :], in1=psl))
                ctx.inc(i, sV)
                z_free[dc4] = sV.c
            for gi in range(3):
                w2_free[rdy[gi][0]] = sP.c
        pb_v = [sV.c] * 2
        u_free = sP.c
    r.y_used[par] = sP.c

    t_zdone = sV.c
    if DBG and it == 0 and "zdbg" in dbg_outs:
        ctx.wait("sync", sV, t_zdone)
        ctx.dma("sync", st, dbg_outs["zdbg"], r.z[:])
    ctx.wait("tensor", sV, t_zdone)
    mm = None
    for ot in range(16):
        mm = ctx.on("tensor", lambda e, ot=ot: e.matmul(
            r.PSB[0:1, 0:512], r.onesf[:, 0:1], r.z[:, ot, :],
            start=(ot == 0), stop=(ot == 15)))
    ctx.inc(mm, sP)
    ctx.wait("scalar", sV, t_zdone)
    sq_mm = [sP.c] * 2
    scr = [r.g1, r.u0]
    for ot in range(16):
        sqi = ot % 2
        ctx.wait("scalar", sP, sq_mm[sqi])
        i = ctx.on("scalar", lambda e, ot=ot, sqi=sqi: e.activation(
            out=scr[sqi][:], in_=r.z[:, ot, :], func=AF.Square))
        ctx.inc(i, sA)
        t_sqr = sA.c
        ctx.wait("tensor", sA, t_sqr)
        i = ctx.on("tensor", lambda e, sqi=sqi, ot=ot: e.matmul(
            r.PSB[0:1, 512:1024], r.onesf[:, 0:1], scr[sqi][:],
            start=(ot == 0), stop=(ot == 15)))
        ctx.inc(i, sP)
        sq_mm[sqi] = sP.c
    t_sumsq = sP.c
    ctx.wait("vector", sP, t_sumsq)
    ctx.on("vector", lambda e: e.tensor_scalar(
        out=r.flnr[0:1, 0:512], in0=r.PSB[0:1, 0:512], scalar1=-1.0 / 2048.0,
        scalar2=None, op0=ALU.mult))
    ctx.on("vector", lambda e: e.tensor_mul(
        out=r.flnr[0:1, 1024:1536], in0=r.flnr[0:1, 0:512], in1=r.flnr[0:1, 0:512]))
    i = ctx.on("vector", lambda e: e.scalar_tensor_tensor(
        out=r.flnr[0:1, 512:1024], in0=r.PSB[0:1, 512:1024], scalar=1.0 / 2048.0,
        in1=r.flnr[0:1, 1024:1536], op0=ALU.mult, op1=ALU.subtract))
    ctx.inc(i, sV)
    t_var = sV.c
    ctx.wait("scalar", sV, t_var)
    i = ctx.on("scalar", lambda e: e.activation(out=r.flnr[0:1, 512:1024],
                                                in_=r.flnr[0:1, 512:1024], func=AF.Sqrt, bias=1e-5))
    ctx.inc(i, sA)
    ctx.wait("vector", sA, sA.c)
    i = ctx.on("vector", lambda e: e.reciprocal(out=r.flnr[0:1, 512:1024],
                                                in_=r.flnr[0:1, 512:1024]))
    ctx.inc(i, sV)
    ctx.wait("vector", sV, sV.c)
    i = ctx.on("vector", lambda e: e.engine_nop())
    ctx.inc(i, sV)
    t_rs = sV.c
    ctx.wait("tensor", sV, t_rs)
    ctx.on("tensor", lambda e: e.matmul(
        r.PSA[:, 0:512], r.onesf[0:1, :], r.flnr[0:1, 0:512], start=True, stop=True))
    i = ctx.on("tensor", lambda e: e.matmul(
        r.PSA[:, 512:1024], r.onesf[0:1, :], r.flnr[0:1, 512:1024], start=True, stop=True))
    ctx.inc(i, sP)
    t_bc = sP.c
    ctx.wait("vector", sP, t_bc)
    ost_busy = [sems["st"].c] * 2
    for ot in range(16):
        oi = ot % 2
        ctx.wait("vector", st, ost_busy[oi])
        ctx.on("vector", lambda e, ot=ot, oi=oi: e.tensor_add(
            out=r.ostg[oi][:], in0=r.z[:, ot, :], in1=r.PSA[:, 0:512]))
        i = ctx.on("vector", lambda e, ot=ot, oi=oi: e.tensor_mul(
            out=r.ostg[oi][:], in0=r.ostg[oi][:], in1=r.PSA[:, 512:1024]))
        ctx.inc(i, sV)
        t_o = sV.c
        ctx.wait("sync", sV, t_o)
        ost_busy[oi] = ctx.dma("sync", st, d.out[ot * 128:(ot + 1) * 128, pc0:pc0 + FIT],
                               r.ostg[oi][:])



def _stage_weights(inp):
    g = {}
    T = lambda a: np.ascontiguousarray(a)
    Wq = inp["L0_attn_Wq"]; Wk = inp["L0_attn_Wk"]; Wv = inp["L0_attn_Wv"]
    Wb = inp["L0_attn_Wb"]; Wa = inp["L0_attn_Wa"]; Wg = inp["L0_attn_Wg"]
    Wo = inp["L0_attn_Wo"]
    g["wq_l"] = T(Wq.T.reshape(16, 128, 8, 128).transpose(2, 1, 0, 3))
    g["wk_l"] = T(Wk.T.reshape(16, 128, 16, 128).transpose(2, 1, 0, 3))
    g["wv_l"] = T(Wv.T.reshape(16, 128, 16, 128).transpose(2, 1, 0, 3))
    g["wg_l"] = T(Wg.T.reshape(16, 128, 8, 128).transpose(2, 1, 0, 3))
    g["wo_l"] = T(Wo.T.reshape(8, 128, 16, 128).transpose(2, 1, 0, 3))
    g["wb_l"] = T(Wb.T.reshape(16, 128, 16).transpose(1, 0, 2))
    g["wa_l"] = T(Wa.T.reshape(16, 128, 8).transpose(1, 0, 2))
    g["wafn_l"] = T(inp["L0_ffn_wa"].reshape(16, 128, 16).transpose(1, 0, 2))
    g["cq_l"] = T(inp["L0_attn_conv_q"].reshape(8, 128, 4).transpose(1, 0, 2))
    g["ck_l"] = T(inp["L0_attn_conv_k"].reshape(16, 128, 4).transpose(1, 0, 2))
    g["cv_l"] = T(inp["L0_attn_conv_v"].reshape(16, 128, 4).transpose(1, 0, 2))
    g["alog"] = T(inp["L0_attn_A_log"].reshape(8, 1))
    g["dtb"] = T(inp["L0_attn_dt_bias"].reshape(8, 1))
    g["onw"] = T(inp["L0_attn_o_norm_w"].reshape(128, 1))
    g["rg_l"] = T(inp["L0_ffn_rg"].reshape(16, 16, 128, 4, 256).transpose(0, 3, 2, 1, 4))
    g["rw1_l"] = T(inp["L0_ffn_rw1"].reshape(16, 16, 128, 4, 256).transpose(0, 3, 2, 1, 4))
    g["rw2_l"] = T(inp["L0_ffn_rw2"].reshape(16, 8, 128, 4, 512).transpose(0, 3, 2, 1, 4))
    g["sg_l"] = T(inp["L0_ffn_sg"].reshape(2, 16, 128, 4, 256).transpose(0, 3, 2, 1, 4))
    g["sw1_l"] = T(inp["L0_ffn_sw1"].reshape(2, 16, 128, 4, 256).transpose(0, 3, 2, 1, 4))
    g["sw2_l"] = T(inp["L0_ffn_sw2"].reshape(2, 8, 128, 4, 512).transpose(0, 3, 2, 1, 4))
    g["rgb_l"] = T(inp["L0_ffn_rgb"].reshape(16, 8, 128).transpose(2, 0, 1))
    g["rb1_l"] = T(inp["L0_ffn_rb1"].reshape(16, 8, 128).transpose(2, 0, 1))
    g["sgb_l"] = T(inp["L0_ffn_sgb"].reshape(2, 8, 128).transpose(2, 0, 1))
    g["sb1_l"] = T(inp["L0_ffn_sb1"].reshape(2, 8, 128).transpose(2, 0, 1))
    g["b2r"] = T(inp["L0_ffn_rb2"])
    g["b2s"] = T(inp["L0_ffn_sb2"])
    sel8 = np.zeros((8, 1024), np.float32)
    for h in range(8):
        sel8[h, h * 128:(h + 1) * 128] = 1.0
    g["sel8"] = sel8
    sel16 = np.zeros((16, 2048), np.float32)
    for j in range(16):
        sel16[j, j * 128:(j + 1) * 128] = 1.0
    g["sel16"] = sel16.astype(ml_dtypes.bfloat16)
    g["maskneg"] = np.where(np.triu(np.ones((128, 128)), 0) > 0, 0.0, -1e9).astype(np.float32)
    g["identf"] = np.eye(128, dtype=np.float32)
    g["identb"] = np.eye(128).astype(ml_dtypes.bfloat16)
    return g


def _stage_x(x, core):
    xc = x[2 * core:2 * core + 2]
    xr = xc.reshape(2, NBLK, NHALF, BW, D)
    return np.ascontiguousarray(xr.transpose(4, 1, 2, 0, 3).reshape(D, NPCOL))


def _unstage_out(out_fm):
    of = out_fm.reshape(D, NBLK, NHALF, 2, BW)
    return np.ascontiguousarray(of.transpose(3, 1, 2, 4, 0).reshape(2, 2048, D))


def kernel(**inp):
    nc = build_program()
    g = _stage_weights(inp)
    x = np.asarray(inp["L0_attn_x"])
    in_maps = []
    for core in range(NCORE):
        m = dict(g)
        m["x_fm"] = _stage_x(x, core)
        in_maps.append(m)
    res = run_bass_kernel_spmd(nc, in_maps, core_ids=list(range(NCORE)))
    out = np.empty((16, 2048, 2048), np.float32)
    for core in range(NCORE):
        out[2 * core:2 * core + 2] = _unstage_out(res.results[core]["out_fm"])
    return out
